# version 01
# baseline (speedup 1.0000x reference)
import numpy as np
import ml_dtypes
from contextlib import ExitStack

import concourse.bass as bass
import concourse.mybir as mybir
from concourse.bass_utils import run_bass_kernel_spmd

F32 = mybir.dt.float32
BF16 = mybir.dt.bfloat16
AF = mybir.ActivationFunctionType
ALU = mybir.AluOpType
AXX = mybir.AxisListType.X

T = 2048
D = 1024
NT = T // 128
ND = D // 128
RND_C = float(3 * 2.0 ** 22)
QMAX16 = 32767.0
EPS_LN = 1e-5

ENGINES = ("sync", "vector", "scalar", "tensor", "gpsimd")


class Prog:

    def __init__(self, nc):
        self.nc = nc
        self.q = {e: [] for e in ENGINES}
        self.ctx = ExitStack()
        self.sems = {}
        self.counts = {}
        self.const_vals = [0.0, 1.0]
        self.const_sb = self.nc.alloc_sbuf_tensor("const_sb", [128, 16], F32)

    def want_const(self, *vals):
        for v in vals:
            v = float(v)
            if v not in self.const_vals:
                self.const_vals.append(v)

    def const_ap(self, v):
        v = float(v)
        assert self.const_sb is not None, "const_ap only valid inside thunks (finish)"
        i = self.const_vals.index(v)
        return self.const_sb[:, i:i + 1]

    def sem(self, name):
        if name not in self.sems:
            self.sems[name] = self.ctx.enter_context(self.nc.semaphore(name))
            self.counts[name] = 0
        return self.sems[name]

    SELF_SYNC = ("vector", "scalar")

    def op(self, engine, fn, sem=None, by=1, tick=True):
        do_tick = tick and engine in self.SELF_SYNC
        if do_tick:
            prev = getattr(self, "_last_evt", {}).get(engine)
            if prev is not None:
                self.wait(engine, prev[0], prev[1])
        ret = None
        if sem is not None:
            s = self.sem(sem)
            self.counts[sem] += by
            ret = self.counts[sem]
            self.q[engine].append(lambda e, fn=fn, s=s, by=by: fn(e).then_inc(s, by))
            evt = (sem, ret)
        else:
            tname = f"tick_{engine}"
            if do_tick:
                ts = self.sem(tname)
                self.counts[tname] += 1
                tcur = self.counts[tname]
                self.q[engine].append(lambda e, fn=fn, ts=ts: fn(e).then_inc(ts, 1))
                evt = (tname, tcur)
            else:
                self.q[engine].append(fn)
                evt = None
        if do_tick and evt is not None:
            if not hasattr(self, "_last_evt"):
                self._last_evt = {}
            self._last_evt[engine] = evt
        return ret

    def wait(self, engine, sem, value):
        s = self.sem(sem)
        self.q[engine].append(lambda e, s=s, value=value: e.wait_ge(s, value))

    def barrier(self, engines=ENGINES):
        name = "bar"
        for eng in engines:
            self.op(eng, lambda e: e.nop(), sem=name)
        tot = self.counts[name]
        for eng in engines:
            self.wait(eng, name, tot)

    def wait_cur(self, engine, sem):
        self.wait(engine, sem, self.counts[sem])

    def sb(self, name, shape, dtype=F32, **kw):
        if getattr(self, "_scopes", None):
            return self._scopes[-1].enter_context(
                self.nc.sbuf_tensor(name, shape, dtype, **kw))
        return self.nc.alloc_sbuf_tensor(name, shape, dtype, **kw)

    def push_scope(self):
        if not hasattr(self, "_scopes"):
            self._scopes = []
        self._scopes.append(ExitStack())

    def pop_scope(self):
        self._scopes.pop().close()

    def psum(self, name, shape, dtype=F32):
        return self.nc.alloc_psum_tensor(name, shape, dtype)

    def finish(self):
        assert len(self.const_vals) <= 16
        vals = np.zeros(16, dtype=np.float32)
        vals[:len(self.const_vals)] = self.const_vals
        ctab = np.broadcast_to(vals, (128, 16)).copy()
        ctab_h = self.nc.inline_tensor(ctab, "const_tab").ap()
        for i, v in enumerate(self.const_vals):
            self.nc.const_aps.register(F32, float(v), self.const_sb[:, i:i + 1])
        s = self.sem("ld_const")
        self.q["sync"].insert(0, lambda e, s=s, h=ctab_h, d=self.const_sb: e.dma_start(
            out=d[:], in_=h).then_inc(s, 16))
        for eng in ENGINES:
            if self.q[eng] and eng != "sync":
                self.q[eng].insert(0, lambda e, s=s: e.wait_ge(s, 16))
        blk = self.ctx.enter_context(self.nc.Block())
        for eng in ENGINES:
            thunks = self.q[eng]
            if not thunks:
                continue
            getattr(blk, eng)(lambda e, th=thunks: [f(e) for f in th])
        self.ctx.close()
        return self.nc


class PsumPool:

    def __init__(self, P, banks):
        self.P = P
        self.banks = banks
        self.free_at = [None] * len(banks)

    def acquire(self, idx, engine):
        fa = self.free_at[idx]
        if fa is not None:
            self.P.wait(engine, fa[0], fa[1])
            self.free_at[idx] = None
        return self.banks[idx]

    def release(self, idx, sem, count):
        self.free_at[idx] = (sem, count)


def new_nc():
    return bass.Bass("TRN2", target_bir_lowering=False, detect_race_conditions=False)


def load_colvec(P, src_ap, dst_col, n_chunks, ld_sem, queue="scalar"):
    src = src_ap
    if len(src.shape) == 1:
        src = src.rearrange("(a n) -> a n", a=1)
    total = None
    for c in range(n_chunks):
        total = P.op(queue,
                     lambda e, c=c, src=src: e.dma_start(
                         out=dst_col[:, c:c + 1],
                         in_=src[0:1, c * 128:(c + 1) * 128].rearrange("a p -> p a")),
                     sem=ld_sem, by=16, tick=False)
    return total


def emit_ln_stats(P, x_tiles, mu, rstd, junk_bf, sumsq, ssum, vs, as_, eps=EPS_LN):
    for i in range(NT):
        P.op("vector", lambda e, i=i: e.reduce_sum(
            out=ssum[:, i:i + 1], in_=x_tiles[i][:], axis=AXX), sem=vs)
        P.op("scalar", lambda e, i=i: e.activation(
            out=junk_bf[:], in_=x_tiles[i][:], func=AF.Square,
            accum_out=sumsq[:, i:i + 1]), sem=as_)
    P.wait_cur("vector", as_)
    v = P.op("vector", lambda e: e.tensor_scalar(
        out=mu[:], in0=ssum[:], scalar1=1.0 / D, scalar2=None, op0=ALU.mult), sem=vs)
    P.wait("vector", vs, v)
    v = P.op("vector", lambda e: e.tensor_mul(out=rstd[:], in0=mu[:], in1=mu[:]), sem=vs)
    P.wait("vector", vs, v)
    v_mid = P.op("vector", lambda e: e.scalar_tensor_tensor(
        out=rstd[:], in0=sumsq[:], scalar=1.0 / D, in1=rstd[:],
        op0=ALU.mult, op1=ALU.subtract), sem=vs)
    P.wait("vector", vs, v_mid)
    P.wait("scalar", vs, v_mid)
    a_sqrt = P.op("scalar", lambda e: e.activation(
        out=rstd[:], in_=rstd[:], func=AF.Sqrt, bias=float(eps), scale=1.0), sem=as_)
    P.wait("vector", as_, a_sqrt)
    done = P.op("vector", lambda e: e.reciprocal(out=rstd[:], in_=rstd[:]), sem=vs)
    return done


def emit_partition_absmax(P, col_ap, row_sb, out_sb, vs, dsem, tag):
    d = P.op("gpsimd", lambda e: e.dma_start(out=row_sb[:], in_=col_ap), sem=dsem, by=16)
    P.wait("vector", dsem, d)
    return P.op("vector", lambda e: e.tensor_reduce(
        out=out_sb[:], in_=row_sb[:], axis=AXX, op=ALU.max), sem=vs)



def build_launch1(dbg=False, bars=()):
    nc = new_nc()
    P = Prog(nc)
    P.want_const(EPS_LN)
    ident_h = nc.inline_tensor(np.eye(128, dtype=ml_dtypes.bfloat16), "ident_bf16").ap()

    x_h = nc.dram_tensor("x", [T, D], F32, kind="ExternalInput").ap()
    win_h = nc.dram_tensor("W_in", [D, 2 * D], F32, kind="ExternalInput").ap()
    bin_h = nc.dram_tensor("b_in", [2 * D], F32, kind="ExternalInput").ap()
    a_h = nc.dram_tensor("A", [D], F32, kind="ExternalInput").ap()
    c_h = nc.dram_tensor("C", [D], F32, kind="ExternalInput").ap()
    wout_h = nc.dram_tensor("W_out", [D, D], F32, kind="ExternalInput").ap()
    bout_h = nc.dram_tensor("b_out", [D], F32, kind="ExternalInput").ap()

    x1_h = nc.dram_tensor("x1", [T, D], F32, kind="ExternalOutput").ap()
    if dbg:
        d_xn0_h = nc.dram_tensor("d_xn0", [128, T], BF16, kind="ExternalOutput").ap()
        d_x0_h = nc.dram_tensor("d_x0", [128, D], F32, kind="ExternalOutput").ap()
        d_st_h = nc.dram_tensor("d_st", [128, 4 * NT], F32, kind="ExternalOutput").ap()
        d_yall_h = [nc.dram_tensor(f"d_yall{c}", [128, T], BF16, kind="ExternalOutput").ap()
                    for c in range(ND)]
        d_xnT_h = nc.dram_tensor("d_xnT", [128, T], BF16, kind="ExternalOutput").ap()
        d_v0_h = nc.dram_tensor("d_v0", [128, T], BF16, kind="ExternalOutput").ap()
        d_h0_h = nc.dram_tensor("d_h0", [128, T], BF16, kind="ExternalOutput").ap()
        d_g0_h = nc.dram_tensor("d_g0", [128, T], BF16, kind="ExternalOutput").ap()
        d_y0_h = nc.dram_tensor("d_y0", [128, T], BF16, kind="ExternalOutput").ap()
    xn1_h = nc.dram_tensor("xn1", [T, D], BF16, kind="ExternalOutput").ap()
    m1_h = nc.dram_tensor("m1", [1, 1], F32, kind="ExternalOutput").ap()

    x_sb = [P.sb(f"xt{i}", [128, D], F32) for i in range(NT)]
    xnT = [P.sb(f"xnT{c}", [128, T], BF16) for c in range(ND)]
    big8 = [P.sb(f"big8_{c}", [128, T], BF16) for c in range(ND)]
    vtmp = [P.sb(f"vtmp{k}", [128, T], BF16) for k in range(2)]
    htmp = [P.sb(f"htmp{k}", [128, T], BF16) for k in range(2)]
    gtmp = [P.sb(f"gtmp{k}", [128, T], BF16) for k in range(2)]
    wstage = [P.sb(f"wstage{k}", [128, D], F32) for k in range(2)]
    wbf = [P.sb(f"wbf{c}", [128, 2 * D], BF16) for c in range(ND)]
    ident_sb = P.sb("ident_sb", [128, 128], BF16)
    bin_col = P.sb("bin_col", [128, 2 * ND], F32)
    dec_col = P.sb("dec_col", [128, ND], F32)
    c_col = P.sb("c_col", [128, ND], F32)
    bout_row = P.sb("bout_row", [1, D], F32)
    bout_row_bf = P.sb("bout_row_bf", [1, D], BF16)
    ones_row_bf = P.sb("ones_row_bf", [1, 128], BF16)
    stat_mu = P.sb("stat_mu", [128, NT], F32)
    stat_rstd = P.sb("stat_rstd", [128, NT], F32)
    stat_ssum = P.sb("stat_ssum", [128, NT], F32)
    stat_sumsq = P.sb("stat_sumsq", [128, NT], F32)
    junk_bf = P.sb("junk_bf", [128, D], BF16)
    amax_cols = P.sb("amax_cols", [128, NT], F32)
    amax_col = P.sb("amax_col", [128, 1], F32)
    amax_row = P.sb("amax_row", [1, 128], F32)
    m1_sb = P.sb("m1_sb", [1, 1], F32)

    ps_tr = [P.psum(f"ps_tr{k}", [128, 128], BF16) for k in range(2)]
    ps_mm = [P.psum(f"ps_mm{k}", [128, 512], F32) for k in range(6)]
    pool = PsumPool(P, ps_mm)

    for i in range(NT):
        P.op("sync", lambda e, i=i: e.dma_start(
            out=x_sb[i][:], in_=x_h[i * 128:(i + 1) * 128, :]), sem="ld_x", by=16)
    ld_x_total = P.counts["ld_x"]
    P.op("sync", lambda e: e.dma_start(out=ident_sb[:], in_=ident_h), sem="ld_misc", by=16)
    P.op("sync", lambda e: e.dma_start(
        out=bout_row[:], in_=bout_h.rearrange("(a n) -> a n", a=1)), sem="ld_misc", by=16)
    ld_misc_total = P.counts["ld_misc"]
    load_colvec(P, bin_h, bin_col, 2 * ND, "ld_vec")
    load_colvec(P, a_h, dec_col, ND, "ld_vec")
    ld_vec_total = load_colvec(P, c_h, c_col, ND, "ld_vec")
    P.wait("scalar", "ld_vec", ld_vec_total)
    dec_done = P.op("scalar", lambda e: e.activation(
        out=dec_col[:], in_=dec_col[:], func=AF.Sigmoid), sem="a_misc")
    P.op("vector", lambda e: e.memset(ones_row_bf[:], 1.0), sem="v_misc")
    P.wait("vector", "ld_misc", ld_misc_total)
    misc_done = P.op("vector", lambda e: e.tensor_copy(
        out=bout_row_bf[:], in_=bout_row[:]), sem="v_misc")

    nw = 0
    for ic in range(ND):
        for h in range(2):
            k = nw % 2
            if nw >= 2:
                P.wait("sync", "wcast1", nw - 1)
            v = P.op("sync", lambda e, ic=ic, h=h, k=k: e.dma_start(
                out=wstage[k][:], in_=win_h[ic * 128:(ic + 1) * 128, h * D:(h + 1) * D]),
                sem=f"wld1_{k}", by=16)
            P.wait("scalar", f"wld1_{k}", v)
            P.op("scalar", lambda e, ic=ic, h=h, k=k: e.copy(
                out=wbf[ic][:, h * D:(h + 1) * D], in_=wstage[k][:]), sem="wcast1")
            nw += 1
    wcast1_total = P.counts["wcast1"]

    if 0 in bars:
        P.barrier()
    P.wait("vector", "ld_x", ld_x_total)
    P.wait("scalar", "ld_x", ld_x_total)
    ln0_done = emit_ln_stats(P, x_sb, stat_mu, stat_rstd, junk_bf,
                             stat_sumsq, stat_ssum, "v1", "a1")
    P.wait("vector", "v1", ln0_done)
    if dbg:
        P.wait("gpsimd", "v1", ln0_done)
        P.op("gpsimd", lambda e: e.dma_start(out=d_st_h[:, 0:NT], in_=stat_mu[:]), sem="st_dbg", by=16)
        P.op("gpsimd", lambda e: e.dma_start(out=d_st_h[:, NT:2*NT], in_=stat_rstd[:]), sem="st_dbg", by=16)
        P.op("gpsimd", lambda e: e.dma_start(out=d_st_h[:, 2*NT:3*NT], in_=stat_ssum[:]), sem="st_dbg", by=16)
        P.op("gpsimd", lambda e: e.dma_start(out=d_st_h[:, 3*NT:4*NT], in_=stat_sumsq[:]), sem="st_dbg", by=16)
    for i in range(NT):
        dst = big8[i // 2]
        off = (i % 2) * D
        P.op("vector", lambda e, i=i, dst=dst, off=off: e.tensor_scalar(
            out=dst[:, off:off + D], in0=x_sb[i][:],
            scalar1=stat_mu[:, i:i + 1], scalar2=stat_rstd[:, i:i + 1],
            op0=ALU.subtract, op1=ALU.mult), sem="xn0_done")
    if dbg:
        P.wait("gpsimd", "ld_x", ld_x_total)
        P.op("gpsimd", lambda e: e.dma_start(out=d_x0_h, in_=x_sb[0][:]), sem="st_dbg", by=16)
        P.wait("gpsimd", "xn0_done", P.counts["xn0_done"])
        P.op("gpsimd", lambda e: e.dma_start(out=d_xn0_h, in_=big8[0][:]), sem="st_dbg", by=16)
    P.wait("tensor", "ld_misc", ld_misc_total)
    ntr = 0
    for i in range(NT):
        P.wait("tensor", "xn0_done", i + 1)
        src = big8[i // 2]
        soff = (i % 2) * D
        for c in range(ND):
            k = ntr % 2
            if ntr >= 2:
                P.wait("tensor", "trev1", ntr - 1)
            P.op("tensor", lambda e, i=i, c=c, k=k, src=src, soff=soff: e.transpose(
                out=ps_tr[k][:], in_=src[:, soff + c * 128:soff + (c + 1) * 128],
                identity=ident_sb[:]), sem="tr1")
            P.wait("scalar", "tr1", P.counts["tr1"])
            P.op("scalar", lambda e, i=i, c=c, k=k: e.copy(
                out=xnT[c][:, i * 128:(i + 1) * 128], in_=ps_tr[k][:]), sem="trev1")
            ntr += 1
    trev1_total = P.counts["trev1"]

    if 1 in bars:
        P.barrier()
    if dbg:
        P.wait("gpsimd", "trev1", trev1_total)
        P.op("gpsimd", lambda e: e.dma_start(out=d_xnT_h, in_=xnT[0][:]), sem="st_dbg", by=16)
    P.wait("tensor", "trev1", trev1_total)
    P.wait("scalar", "ld_vec", ld_vec_total)
    P.wait("vector", "ld_vec", ld_vec_total)
    scan_counts = []
    y_counts = []
    for s in range(16):
        c = s // 2
        is_gate = (s % 2 == 1)
        o = c + ND if is_gate else c
        banks = [0, 1, 2, 3] if s % 2 == 0 else [4, 5, 0, 1]
        dstslot = gtmp[c % 2] if is_gate else vtmp[c % 2]
        if c >= 2:
            if is_gate:
                P.wait("scalar", "y1", y_counts[c - 2])
            else:
                P.wait("scalar", "scan1", scan_counts[c - 2])
        for ic in range(ND):
            P.wait("tensor", "wcast1", 2 * (ic + 1))
            for t in range(4):
                bk = banks[t]
                if ic == 0:
                    pool.acquire(bk, "tensor")
                last = (ic == ND - 1)
                P.op("tensor", lambda e, o=o, ic=ic, t=t, bk=bk, last=last: e.matmul(
                    out=ps_mm[bk][:],
                    lhsT=wbf[ic][:, o * 128:(o + 1) * 128],
                    rhs=xnT[ic][:, t * 512:(t + 1) * 512],
                    start=(ic == 0), stop=last),
                    sem="mm1" if last else None)
        mm_after = P.counts["mm1"]
        for t in range(4):
            bk = banks[t]
            P.wait("scalar", "mm1", mm_after)
            if is_gate:
                ev = P.op("scalar", lambda e, c=c, t=t, bk=bk, dstslot=dstslot: e.activation(
                    out=dstslot[:, t * 512:(t + 1) * 512], in_=ps_mm[bk][:],
                    func=AF.Sigmoid, bias=bin_col[:, (c + ND):(c + ND) + 1], scale=1.0),
                    sem="ev1")
            else:
                ev = P.op("scalar", lambda e, c=c, t=t, bk=bk, dstslot=dstslot: e.activation(
                    out=dstslot[:, t * 512:(t + 1) * 512], in_=ps_mm[bk][:],
                    func=AF.Identity, bias=bin_col[:, c:c + 1], scale=1.0), sem="ev1")
            pool.release(bk, "ev1", ev)
        if not is_gate:
            P.wait("vector", "ev1", P.counts["ev1"])
            P.wait("vector", "a_misc", dec_done)
            sc = P.op("vector", lambda e, c=c: e.tensor_tensor_scan(
                out=htmp[c % 2][:], data0=dec_col[:, c:c + 1].broadcast_to((128, T)),
                data1=vtmp[c % 2][:], initial=0.0, op0=ALU.mult, op1=ALU.add),
                sem="scan1")
            scan_counts.append(sc)
        else:
            P.wait("vector", "ev1", P.counts["ev1"])
            P.wait("vector", "scan1", scan_counts[c])
            yc = P.op("vector", lambda e, c=c: e.scalar_tensor_tensor(
                out=big8[c][:], in0=htmp[c % 2][:], scalar=c_col[:, c:c + 1],
                in1=gtmp[c % 2][:], op0=ALU.mult, op1=ALU.mult), sem="y1")
            y_counts.append(yc)
            if dbg and c == 0:
                P.wait("gpsimd", "y1", yc)
                P.op("gpsimd", lambda e: e.dma_start(out=d_v0_h, in_=vtmp[0][:]), sem="st_dbg", by=16)
                P.op("gpsimd", lambda e: e.dma_start(out=d_h0_h, in_=htmp[0][:]), sem="st_dbg", by=16)
                P.op("gpsimd", lambda e: e.dma_start(out=d_g0_h, in_=gtmp[0][:]), sem="st_dbg", by=16)
                P.op("gpsimd", lambda e: e.dma_start(out=d_y0_h, in_=big8[0][:]), sem="st_dbg", by=16)
    y1_total = P.counts["y1"]

    if dbg:
        P.wait("gpsimd", "y1", y1_total)
        for c in range(ND):
            P.op("gpsimd", lambda e, c=c: e.dma_start(out=d_yall_h[c], in_=big8[c][:]),
                 sem="st_dbg", by=16)

    mm1_total = P.counts["mm1"]
    P.wait("sync", "mm1", mm1_total)
    P.wait("scalar", "mm1", mm1_total)
    for ic in range(ND):
        k = ic % 2
        if ic >= 2:
            P.wait("sync", "wcast2", ic - 1)
        v = P.op("sync", lambda e, ic=ic, k=k: e.dma_start(
            out=wstage[k][:], in_=wout_h[ic * 128:(ic + 1) * 128, :]),
            sem=f"wld2_{k}", by=16)
        P.wait("scalar", f"wld2_{k}", v)
        P.op("scalar", lambda e, ic=ic, k=k: e.copy(
            out=wbf[ic][:, 0:D], in_=wstage[k][:]), sem="wcast2")

    P.wait("tensor", "y1", y1_total)
    P.wait("tensor", "v_misc", misc_done)
    for t in range(NT):
        banks = [0, 1] if t % 2 == 0 else [2, 3]
        for ic in range(ND):
            P.wait("tensor", "wcast2", ic + 1)
            for o2 in range(2):
                bk = banks[o2]
                if ic == 0:
                    pool.acquire(bk, "tensor")
                P.op("tensor", lambda e, t=t, ic=ic, o2=o2, bk=bk: e.matmul(
                    out=ps_mm[bk][:],
                    lhsT=big8[ic][:, t * 128:(t + 1) * 128],
                    rhs=wbf[ic][:, o2 * 512:(o2 + 1) * 512],
                    start=(ic == 0), stop=False))
        for o2 in range(2):
            bk = banks[o2]
            P.op("tensor", lambda e, t=t, o2=o2, bk=bk: e.matmul(
                out=ps_mm[bk][:],
                lhsT=ones_row_bf[0:1, :],
                rhs=bout_row_bf[0:1, o2 * 512:(o2 + 1) * 512],
                start=False, stop=True), sem="mm2")
        mm2_after = P.counts["mm2"]
        for o2 in range(2):
            bk = banks[o2]
            P.wait("vector", "mm2", mm2_after)
            ev = P.op("vector", lambda e, t=t, o2=o2, bk=bk: e.tensor_add(
                out=x_sb[t][:, o2 * 512:(o2 + 1) * 512],
                in0=x_sb[t][:, o2 * 512:(o2 + 1) * 512],
                in1=ps_mm[bk][:]), sem="ev2")
            pool.release(bk, "ev2", ev)
        P.wait("gpsimd", "ev2", P.counts["ev2"])
        P.op("gpsimd", lambda e, t=t: e.dma_start(
            out=x1_h[t * 128:(t + 1) * 128, :], in_=x_sb[t][:]), sem="st_x1", by=16)

    if 3 in bars:
        P.barrier()
    ev2_total = P.counts["ev2"]
    P.wait("vector", "ev2", ev2_total)
    P.wait("scalar", "ev2", ev2_total)
    ln1_done = emit_ln_stats(P, x_sb, stat_mu, stat_rstd, junk_bf,
                             stat_sumsq, stat_ssum, "v2", "a2")
    P.wait("vector", "v2", ln1_done)
    for i in range(NT):
        dst = xnT[i // 2]
        off = (i % 2) * D
        P.op("vector", lambda e, i=i, dst=dst, off=off: e.tensor_scalar(
            out=dst[:, off:off + D], in0=x_sb[i][:],
            scalar1=stat_mu[:, i:i + 1], scalar2=stat_rstd[:, i:i + 1],
            op0=ALU.subtract, op1=ALU.mult), sem="xn1_done")
        P.op("vector", lambda e, i=i, dst=dst, off=off: e.tensor_reduce(
            out=amax_cols[:, i:i + 1], in_=dst[:, off:off + D], axis=AXX,
            op=ALU.max, apply_absolute_value=True), sem="xn1_done")
        P.wait("gpsimd", "xn1_done", 2 * (i + 1))
        P.op("gpsimd", lambda e, i=i, dst=dst, off=off: e.dma_start(
            out=xn1_h[i * 128:(i + 1) * 128, :], in_=dst[:, off:off + D]),
            sem="st_xn1", by=16)
    am = P.op("vector", lambda e: e.tensor_reduce(
        out=amax_col[:], in_=amax_cols[:], axis=AXX, op=ALU.max), sem="amax_v")
    P.wait("gpsimd", "amax_v", am)
    m_fin = emit_partition_absmax(P, amax_col[:], amax_row, m1_sb, "amax_v", "amax_d", "m1")
    P.wait("gpsimd", "amax_v", m_fin)
    P.op("gpsimd", lambda e: e.dma_start(out=m1_h[:], in_=m1_sb[:]), sem="st_m1", by=16)

    if dbg:
        P.wait("gpsimd", "st_dbg", P.counts["st_dbg"])
    P.wait("gpsimd", "st_x1", P.counts["st_x1"])
    P.wait("gpsimd", "st_xn1", P.counts["st_xn1"])
    P.wait("gpsimd", "st_m1", P.counts["st_m1"])
    return P.finish()


def kernel(**inputs):
    B = 8
    cores = list(range(B))

    def contig(a):
        return np.ascontiguousarray(np.asarray(a))

    x = contig(inputs["L0_attn_x"])

    nc1 = build_launch1()
    wk1 = {k: contig(inputs["L0_attn_" + k])
           for k in ("W_in", "b_in", "A", "C", "W_out", "b_out")}
    r1 = run_bass_kernel_spmd(
        nc1, [dict(wk1, x=contig(x[b])) for b in range(B)], core_ids=cores)
    x1 = [contig(r1.results[b]["x1"]) for b in range(B)]
    xn1 = [contig(r1.results[b]["xn1"]) for b in range(B)]
    m1_all = contig(np.concatenate(
        [np.asarray(r1.results[b]["m1"]).reshape(1, 1) for b in range(B)], axis=1))

    nc2 = build_launch2()
    wk2 = {"w_up": contig(inputs["L0_ffn_w_up"]),
           "w_gate": contig(inputs["L0_ffn_w_gate"]), "m1_all": m1_all}
    r2 = run_bass_kernel_spmd(
        nc2, [dict(wk2, xn1=xn1[b]) for b in range(B)], core_ids=cores)
    up_raw = [contig(r2.results[b]["up_raw"]) for b in range(B)]
    gate_raw = [contig(r2.results[b]["gate_raw"]) for b in range(B)]
    mup_all = contig(np.concatenate(
        [np.asarray(r2.results[b]["m_up"]).reshape(1, 1) for b in range(B)], axis=1))
    mg_all = contig(np.concatenate(
        [np.asarray(r2.results[b]["m_g"]).reshape(1, 1) for b in range(B)], axis=1))

    nc3 = build_launch3()
    r3 = run_bass_kernel_spmd(
        nc3, [{"gate_raw": gate_raw[b], "m_g_all": mg_all} for b in range(B)],
        core_ids=cores)
    mg2_all = contig(np.concatenate(
        [np.asarray(r3.results[b]["m_g2"]).reshape(1, 1) for b in range(B)], axis=1))

    nc4 = build_launch4()
    r4 = run_bass_kernel_spmd(
        nc4, [{"gate_raw": gate_raw[b], "up_raw": up_raw[b], "m_g_all": mg_all,
               "m_g2_all": mg2_all, "m_up_all": mup_all} for b in range(B)],
        core_ids=cores)
    t_q = [contig(r4.results[b]["t_q"]) for b in range(B)]
    mo_all = contig(np.concatenate(
        [np.asarray(r4.results[b]["m_o"]).reshape(1, 1) for b in range(B)], axis=1))

    nc5 = build_launch5_full()
    wk5 = {"w_down": contig(inputs["L0_ffn_w_down"]), "m_o_all": mo_all}
    for k in ("wq", "bq", "wk", "bk", "wv", "bv", "wo", "bo",
              "g1", "b1", "g2", "b2", "w1", "bf1", "w2", "bf2"):
        wk5[k] = contig(inputs["L1_attn_" + k])
    for src, dst in (("w1", "f_w1"), ("b1", "f_b1"), ("w2", "f_w2"), ("b2", "f_b2")):
        wk5[dst] = contig(inputs["L1_ffn_" + src])
    r5 = run_bass_kernel_spmd(
        nc5, [dict(wk5, x1=x1[b], t_q=t_q[b]) for b in range(B)], core_ids=cores)
    return np.stack([np.asarray(r5.results[b]["out"], dtype=np.float32)
                     for b in range(B)], axis=0)


C32 = float(3 * 2.0 ** 22)
C64 = float(3 * 2.0 ** 51)
G_A = float(2.0 ** -75)
G_B = float(2.0 ** 75)


def emit_g_gadget(P, g_col):
    P.op("vector", lambda e: e.memset(g_col[:], G_A))
    P.op("vector", lambda e: e.tensor_scalar(
        out=g_col[:], in0=g_col[:], scalar1=G_A, scalar2=G_B,
        op0=ALU.mult, op1=ALU.mult))
    return P.op("vector", lambda e: e.tensor_scalar(
        out=g_col[:], in0=g_col[:], scalar1=G_B, scalar2=None,
        op0=ALU.mult), sem="g_done")


def emit_round(P, dst_fn, src_ap, tb, ta, g_col, pre=None):
    if pre is not None:
        pre(tb)
        tsrc = tb[:]
    else:
        tsrc = src_ap
    P.op("vector", lambda e, tsrc=tsrc: e.tensor_scalar(
        out=tb[:], in0=tsrc, scalar1=C32, scalar2=C32,
        op0=ALU.add, op1=ALU.subtract))
    P.op("vector", lambda e: e.tensor_scalar(
        out=ta[:], in0=tb[:], scalar1=C64, scalar2=C64,
        op0=ALU.add, op1=ALU.subtract))
    P.op("vector", lambda e: e.tensor_sub(out=ta[:], in0=ta[:], in1=tb[:]))
    npart = ta[:].shape[0]
    P.op("vector", lambda e, npart=npart: e.scalar_tensor_tensor(
        out=ta[:], in0=ta[:], scalar=g_col[0:npart, 0:1], in1=tb[:],
        op0=ALU.mult, op1=ALU.add))
    return dst_fn(ta)



def build_launch2():
    nc = new_nc()
    P = Prog(nc)
    ident_h = nc.inline_tensor(np.eye(128, dtype=ml_dtypes.bfloat16), "ident_bf16").ap()

    xn1_h = nc.dram_tensor("xn1", [T, D], BF16, kind="ExternalInput").ap()
    m1a_h = nc.dram_tensor("m1_all", [1, 8], F32, kind="ExternalInput").ap()
    wup_h = nc.dram_tensor("w_up", [3 * D, D], F32, kind="ExternalInput").ap()
    wgate_h = nc.dram_tensor("w_gate", [3 * D, D], F32, kind="ExternalInput").ap()

    up_h = nc.dram_tensor("up_raw", [3 * D, T], BF16, kind="ExternalOutput").ap()
    gate_h = nc.dram_tensor("gate_raw", [3 * D, T], BF16, kind="ExternalOutput").ap()
    mup_h = nc.dram_tensor("m_up", [1, 1], F32, kind="ExternalOutput").ap()
    mg_h = nc.dram_tensor("m_g", [1, 1], F32, kind="ExternalOutput").ap()

    NO = 24

    xn1 = [P.sb(f"xn1_{i}", [128, D], BF16) for i in range(NT)]
    xqT = [P.sb(f"xqT{c}", [128, T], BF16) for c in range(ND)]
    xq_tm = [P.sb(f"xq{i}", [128, D], BF16) for i in range(NT)]
    tmp1 = P.sb("tmp1", [128, D], F32)
    tmp2 = P.sb("tmp2", [128, D], F32)
    wstage = [P.sb(f"wstage{k}", [128, D], F32) for k in range(2)]
    wq_nat = [P.sb(f"wq_nat{k}", [128, D], BF16) for k in range(2)]
    wT_sl = [[P.sb(f"wT_{k}_{ic}", [128, 128], BF16) for ic in range(ND)] for k in range(2)]
    strip = [P.sb(f"strip{k}", [128, T], BF16) for k in range(2)]
    ident_sb = P.sb("ident_sb", [128, 128], BF16)
    g_col = P.sb("g_col", [128, 1], F32)
    mrow = P.sb("mrow", [128, 8], F32)
    s_col = P.sb("s_col", [128, 1], F32)
    invs_col = P.sb("invs_col", [128, 1], F32)
    bmax = P.sb("bmax", [128, 32], F32)
    sblk = P.sb("sblk", [128, 32], F32)
    invblk = P.sb("invblk", [128, 32], F32)
    amax_acc = P.sb("amax_acc", [128, 2], F32)
    amax_row = P.sb("amax_row", [1, 128], F32)
    m_sb = P.sb("m_sb", [1, 2], F32)

    ps_tr = [P.psum(f"ps_tr{k}", [128, 128], BF16) for k in range(2)]
    ps_mm = [P.psum(f"ps_mm{k}", [128, 512], F32) for k in range(6)]
    pool = PsumPool(P, ps_mm)

    for i in range(NT):
        P.op("sync", lambda e, i=i: e.dma_start(
            out=xn1[i][:], in_=xn1_h[i * 128:(i + 1) * 128, :]), sem="ld_x", by=16)
    ld_x_total = P.counts["ld_x"]
    P.op("sync", lambda e: e.dma_start(out=ident_sb[:], in_=ident_h), sem="ld_misc", by=16)
    P.op("sync", lambda e: e.dma_start(
        out=mrow[:], in_=m1a_h.partition_broadcast(128)), sem="ld_misc", by=16)
    ld_misc_total = P.counts["ld_misc"]

    emit_g_gadget(P, g_col)
    P.wait("vector", "ld_misc", ld_misc_total)
    P.op("vector", lambda e: e.tensor_reduce(
        out=s_col[:], in_=mrow[:], axis=AXX, op=ALU.max))
    P.op("vector", lambda e: e.tensor_scalar(
        out=s_col[:], in0=s_col[:], scalar1=1.0 / QMAX16, scalar2=1e-12,
        op0=ALU.mult, op1=ALU.max))
    sc_done = P.op("vector", lambda e: e.reciprocal(
        out=invs_col[:], in_=s_col[:]), sem="sc_done")

    P.wait("vector", "ld_x", ld_x_total)
    for i in range(NT):
        def pre(tb, i=i):
            P.op("vector", lambda e, i=i, tb=tb: e.tensor_scalar(
                out=tb[:], in0=xn1[i][:], scalar1=invs_col[:, 0:1], scalar2=None,
                op0=ALU.mult))
        def fin(ta, i=i):
            return P.op("vector", lambda e, i=i, ta=ta: e.tensor_scalar(
                out=xq_tm[i][:], in0=ta[:], scalar1=s_col[:, 0:1], scalar2=None,
                op0=ALU.mult), sem="xq_done")
        emit_round(P, fin, None, tmp1, tmp2, g_col, pre=pre)
    P.wait("tensor", "ld_misc", ld_misc_total)
    ntr = 0
    for i in range(NT):
        P.wait("tensor", "xq_done", i + 1)
        for c in range(ND):
            k = ntr % 2
            if ntr >= 2:
                P.wait("tensor", "trev", ntr - 1)
            P.op("tensor", lambda e, i=i, c=c, k=k: e.transpose(
                out=ps_tr[k][:], in_=xq_tm[i][:, c * 128:(c + 1) * 128],
                identity=ident_sb[:]), sem="tr")
            P.wait("scalar", "tr", P.counts["tr"])
            P.op("scalar", lambda e, i=i, c=c, k=k: e.copy(
                out=xqT[c][:, i * 128:(i + 1) * 128], in_=ps_tr[k][:]), sem="trev")
            ntr += 1
    trev_total = P.counts["trev"]
    P.op("vector", lambda e: e.memset(amax_acc[:], 0.0), sem="amax_init")

    P.wait("tensor", "trev", trev_total)
    for wi, (w_h, out_h, acol) in enumerate(((wup_h, up_h, 0), (wgate_h, gate_h, 1))):
        for r in range(NO):
            k = r % 2
            nw = wi * NO + r
            if nw >= 2:
                P.wait("sync", "wdq", nw - 1)
            v = P.op("sync", lambda e, w_h=w_h, r=r, k=k: e.dma_start(
                out=wstage[k][:], in_=w_h[r * 128:(r + 1) * 128, :]),
                sem=f"wld_{k}", by=16)
            P.wait("vector", f"wld_{k}", v)
            P.op("vector", lambda e, k=k: e.tensor_reduce(
                out=bmax[:], in_=wstage[k][:].rearrange("p (b i) -> p b i", i=32),
                axis=AXX, op=ALU.max, apply_absolute_value=True))
            P.op("vector", lambda e: e.tensor_scalar(
                out=sblk[:], in0=bmax[:], scalar1=1.0 / 7.0, scalar2=1e-12,
                op0=ALU.mult, op1=ALU.max))
            P.op("vector", lambda e: e.reciprocal(out=invblk[:], in_=sblk[:]))
            def pre(tb, k=k):
                P.op("vector", lambda e, k=k, tb=tb: e.tensor_tensor(
                    out=tb[:].rearrange("p (b i) -> p b i", i=32),
                    in0=wstage[k][:].rearrange("p (b i) -> p b i", i=32),
                    in1=invblk[:].rearrange("p b -> p b ()").broadcast_to((128, 32, 32)),
                    op=ALU.mult))
            def fin(ta, k=k):
                return P.op("vector", lambda e, k=k, ta=ta: e.tensor_tensor(
                    out=wq_nat[k][:].rearrange("p (b i) -> p b i", i=32),
                    in0=ta[:].rearrange("p (b i) -> p b i", i=32),
                    in1=sblk[:].rearrange("p b -> p b ()").broadcast_to((128, 32, 32)),
                    op=ALU.mult), sem="wdq")
            emit_round(P, fin, None, tmp1, tmp2, g_col, pre=pre)
            wdq_c = P.counts["wdq"]
            P.wait("scalar", "wdq", wdq_c)
            for ic in range(ND):
                P.op("scalar", lambda e, k=k, ic=ic: e.dma_start_transpose(
                    out=wT_sl[k][ic][:], in_=wq_nat[k][:, ic * 128:(ic + 1) * 128]),
                    sem=f"wtr_{k}", by=16, tick=False)
            wtr_c = P.counts[f"wtr_{k}"]
            banks = [0, 1, 2, 3] if nw % 2 == 0 else [4, 5, 0, 1]
            P.wait("tensor", f"wtr_{k}", wtr_c)
            for ic in range(ND):
                for t in range(4):
                    bk = banks[t]
                    if ic == 0:
                        pool.acquire(bk, "tensor")
                    P.op("tensor", lambda e, k=k, ic=ic, t=t, bk=bk: e.matmul(
                        out=ps_mm[bk][:],
                        lhsT=wT_sl[k][ic][:],
                        rhs=xqT[ic][:, t * 512:(t + 1) * 512],
                        start=(ic == 0), stop=(ic == ND - 1)),
                        sem="mm" if ic == ND - 1 else None)
            mm_c = P.counts["mm"]
            if nw >= 2:
                P.wait("scalar", "strip_st", (nw - 2) * 16 + 16)
            for t in range(4):
                bk = banks[t]
                P.wait("scalar", "mm", mm_c)
                ev = P.op("scalar", lambda e, k=k, t=t, bk=bk: e.copy(
                    out=strip[k][:, t * 512:(t + 1) * 512], in_=ps_mm[bk][:]), sem="ev")
                pool.release(bk, "ev", ev)
            ev_c = P.counts["ev"]
            P.wait("vector", "ev", ev_c)
            P.wait("vector", "amax_init", 1)
            P.op("vector", lambda e, k=k, acol=acol: e.tensor_reduce(
                out=bmax[:, 0:1], in_=strip[k][:], axis=AXX, op=ALU.max,
                apply_absolute_value=True))
            vm = P.op("vector", lambda e, acol=acol: e.tensor_tensor(
                out=amax_acc[:, acol:acol + 1], in0=amax_acc[:, acol:acol + 1],
                in1=bmax[:, 0:1], op=ALU.max), sem="amax_v")
            P.wait("gpsimd", "ev", ev_c)
            P.op("gpsimd", lambda e, out_h=out_h, r=r, k=k: e.dma_start(
                out=out_h[r * 128:(r + 1) * 128, :], in_=strip[k][:]),
                sem="strip_st", by=16)

    P.wait("gpsimd", "amax_v", P.counts["amax_v"])
    d = P.op("gpsimd", lambda e: e.dma_start(
        out=amax_row[:], in_=amax_acc[:, 0:1]), sem="amax_d", by=16)
    P.wait("vector", "amax_d", d)
    r1 = P.op("vector", lambda e: e.tensor_reduce(
        out=m_sb[:, 0:1], in_=amax_row[:], axis=AXX, op=ALU.max), sem="amax_f")
    P.wait("gpsimd", "amax_f", r1)
    d = P.op("gpsimd", lambda e: e.dma_start(
        out=amax_row[:], in_=amax_acc[:, 1:2]), sem="amax_d", by=16)
    P.wait("vector", "amax_d", d)
    r2 = P.op("vector", lambda e: e.tensor_reduce(
        out=m_sb[:, 1:2], in_=amax_row[:], axis=AXX, op=ALU.max), sem="amax_f")
    P.wait("gpsimd", "amax_f", r2)
    P.op("gpsimd", lambda e: e.dma_start(out=mup_h, in_=m_sb[0:1, 0:1]), sem="st_m", by=16)
    P.op("gpsimd", lambda e: e.dma_start(out=mg_h, in_=m_sb[0:1, 1:2]), sem="st_m", by=16)

    P.wait("gpsimd", "strip_st", P.counts["strip_st"])
    P.wait("gpsimd", "st_m", P.counts["st_m"])
    return P.finish()


def emit_scale_cols(P, rows_ap, s_col, invs_col, tmp_col, qmax, ld_sem_val, name):
    P.op("vector", lambda e: e.tensor_reduce(
        out=s_col[:], in_=tmp_col[:], axis=AXX, op=ALU.max))
    P.op("vector", lambda e: e.tensor_scalar(
        out=s_col[:], in0=s_col[:], scalar1=1.0 / qmax, scalar2=1e-12,
        op0=ALU.mult, op1=ALU.max))
    return P.op("vector", lambda e: e.reciprocal(
        out=invs_col[:], in_=s_col[:]), sem=name)



def build_launch3():
    nc = new_nc()
    P = Prog(nc)
    gate_h = nc.dram_tensor("gate_raw", [3 * D, T], BF16, kind="ExternalInput").ap()
    mga_h = nc.dram_tensor("m_g_all", [1, 8], F32, kind="ExternalInput").ap()
    mg2_h = nc.dram_tensor("m_g2", [1, 1], F32, kind="ExternalOutput").ap()

    NO = 24
    chunk = [P.sb(f"chunk{k}", [128, T], BF16) for k in range(2)]
    g1 = P.sb("g1", [128, T], F32)
    sg = P.sb("sg", [128, T], F32)
    tb = P.sb("tb", [128, T], F32)
    ta = P.sb("ta", [128, T], F32)
    g_col = P.sb("g_col", [128, 1], F32)
    mrow8 = P.sb("mrow8", [128, 8], F32)
    s_col = P.sb("s_col", [128, 1], F32)
    invs_col = P.sb("invs_col", [128, 1], F32)
    amax_acc = P.sb("amax_acc", [128, 2], F32)
    amax_row = P.sb("amax_row", [1, 128], F32)
    m_sb = P.sb("m_sb", [1, 1], F32)

    P.op("sync", lambda e: e.dma_start(
        out=mrow8[:], in_=mga_h.partition_broadcast(128)), sem="ld_misc", by=16)
    emit_g_gadget(P, g_col)
    P.wait("vector", "ld_misc", 16)
    emit_scale_cols(P, None, s_col, invs_col, mrow8, QMAX16, None, "sc_done")
    P.op("vector", lambda e: e.memset(amax_acc[:], 0.0))

    for r in range(NO):
        k = r % 2
        if r >= 2:
            P.wait("sync", "ch_done", r - 1)
        v = P.op("sync", lambda e, r=r, k=k: e.dma_start(
            out=chunk[k][:], in_=gate_h[r * 128:(r + 1) * 128, :]),
            sem=f"ld_{k}", by=16)
        P.wait("vector", f"ld_{k}", v)
        def pre(tbx, k=k):
            P.op("vector", lambda e, k=k, tbx=tbx: e.tensor_scalar(
                out=tbx[:], in0=chunk[k][:], scalar1=invs_col[:, 0:1], scalar2=None,
                op0=ALU.mult))
        def fin(tax):
            return P.op("vector", lambda e, tax=tax: e.tensor_scalar(
                out=g1[:], in0=tax[:], scalar1=s_col[:, 0:1], scalar2=None,
                op0=ALU.mult), sem="g1_done")
        emit_round(P, fin, None, tb, ta, g_col, pre=pre)
        P.wait("scalar", "g1_done", P.counts["g1_done"])
        P.op("scalar", lambda e: e.activation(out=sg[:], in_=g1[:], func=AF.Sigmoid),
             sem="sg_done")
        P.wait("vector", "sg_done", P.counts["sg_done"])
        def pre2(tbx):
            P.op("vector", lambda e, tbx=tbx: e.tensor_scalar(
                out=tbx[:], in0=sg[:], scalar1=65536.0, scalar2=None, op0=ALU.mult))
        def fin2(tax):
            return P.op("vector", lambda e, tax=tax: e.tensor_scalar(
                out=sg[:], in0=tax[:], scalar1=65535.0, scalar2=float(2.0 ** -16),
                op0=ALU.min, op1=ALU.mult), sem="sig_done")
        emit_round(P, fin2, None, tb, ta, g_col, pre=pre2)
        P.op("vector", lambda e: e.tensor_mul(out=g1[:], in0=g1[:], in1=sg[:]))
        P.op("vector", lambda e: e.tensor_reduce(
            out=amax_acc[:, 1:2], in_=g1[:], axis=AXX, op=ALU.max,
            apply_absolute_value=True))
        cd = P.op("vector", lambda e: e.tensor_tensor(
            out=amax_acc[:, 0:1], in0=amax_acc[:, 0:1], in1=amax_acc[:, 1:2],
            op=ALU.max), sem="ch_done")

    P.wait("gpsimd", "ch_done", P.counts["ch_done"])
    d = P.op("gpsimd", lambda e: e.dma_start(
        out=amax_row[:], in_=amax_acc[:, 0:1]), sem="amax_d", by=16)
    P.wait("vector", "amax_d", d)
    rf = P.op("vector", lambda e: e.tensor_reduce(
        out=m_sb[:], in_=amax_row[:], axis=AXX, op=ALU.max), sem="amax_f")
    P.wait("gpsimd", "amax_f", rf)
    P.op("gpsimd", lambda e: e.dma_start(out=mg2_h, in_=m_sb[:]), sem="st_m", by=16)
    P.wait("gpsimd", "st_m", 16)
    return P.finish()



def build_launch4():
    nc = new_nc()
    P = Prog(nc)
    gate_h = nc.dram_tensor("gate_raw", [3 * D, T], BF16, kind="ExternalInput").ap()
    up_h = nc.dram_tensor("up_raw", [3 * D, T], BF16, kind="ExternalInput").ap()
    mga_h = nc.dram_tensor("m_g_all", [1, 8], F32, kind="ExternalInput").ap()
    mg2a_h = nc.dram_tensor("m_g2_all", [1, 8], F32, kind="ExternalInput").ap()
    mua_h = nc.dram_tensor("m_up_all", [1, 8], F32, kind="ExternalInput").ap()
    tq_h = nc.dram_tensor("t_q", [3 * D, T], BF16, kind="ExternalOutput").ap()
    mo_h = nc.dram_tensor("m_o", [1, 1], F32, kind="ExternalOutput").ap()

    NO = 24
    gch = [P.sb(f"gch{k}", [128, T], BF16) for k in range(2)]
    uch = [P.sb(f"uch{k}", [128, T], BF16) for k in range(2)]
    tqb = [P.sb(f"tqb{k}", [128, T], BF16) for k in range(2)]
    g1 = P.sb("g1", [128, T], F32)
    sg = P.sb("sg", [128, T], F32)
    tb = P.sb("tb", [128, T], F32)
    ta = P.sb("ta", [128, T], F32)
    g_col = P.sb("g_col", [128, 1], F32)
    mrow8 = P.sb("mrow8", [128, 8], F32)
    sg1_col = P.sb("sg1_col", [128, 1], F32)
    invg1_col = P.sb("invg1_col", [128, 1], F32)
    sg2_col = P.sb("sg2_col", [128, 1], F32)
    invg2_col = P.sb("invg2_col", [128, 1], F32)
    su_col = P.sb("su_col", [128, 1], F32)
    invu_col = P.sb("invu_col", [128, 1], F32)
    amax_acc = P.sb("amax_acc", [128, 2], F32)
    amax_row = P.sb("amax_row", [1, 128], F32)
    m_sb = P.sb("m_sb", [1, 1], F32)

    P.op("sync", lambda e: e.dma_start(
        out=mrow8[:], in_=mga_h.partition_broadcast(128)), sem="ld_m1", by=16)
    emit_g_gadget(P, g_col)
    P.wait("vector", "ld_m1", 16)
    emit_scale_cols(P, None, sg1_col, invg1_col, mrow8, QMAX16, None, "sc1")
    P.op("sync", lambda e: e.dma_start(
        out=mrow8[:], in_=mg2a_h.partition_broadcast(128)), sem="ld_m2", by=16)
    P.wait("sync", "sc1", P.counts["sc1"])
    P.wait("vector", "ld_m2", 16)
    emit_scale_cols(P, None, sg2_col, invg2_col, mrow8, QMAX16, None, "sc2")
    P.op("sync", lambda e: e.dma_start(
        out=mrow8[:], in_=mua_h.partition_broadcast(128)), sem="ld_m3", by=16)
    P.wait("sync", "sc2", P.counts["sc2"])
    P.wait("vector", "ld_m3", 16)
    emit_scale_cols(P, None, su_col, invu_col, mrow8, QMAX16, None, "sc3")
    P.op("vector", lambda e: e.memset(amax_acc[:], 0.0))

    for r in range(NO):
        k = r % 2
        if r >= 2:
            P.wait("sync", "ch_done", r - 1)
        v = P.op("sync", lambda e, r=r, k=k: e.dma_start(
            out=gch[k][:], in_=gate_h[r * 128:(r + 1) * 128, :]),
            sem=f"ldg_{k}", by=16)
        v2 = P.op("sync", lambda e, r=r, k=k: e.dma_start(
            out=uch[k][:], in_=up_h[r * 128:(r + 1) * 128, :]),
            sem=f"ldu_{k}", by=16)
        P.wait("vector", f"ldg_{k}", v)
        P.wait("vector", f"ldu_{k}", v2)
        def pre(tbx, k=k):
            P.op("vector", lambda e, k=k, tbx=tbx: e.tensor_scalar(
                out=tbx[:], in0=gch[k][:], scalar1=invg1_col[:, 0:1], scalar2=None,
                op0=ALU.mult))
        def fin(tax):
            return P.op("vector", lambda e, tax=tax: e.tensor_scalar(
                out=g1[:], in0=tax[:], scalar1=sg1_col[:, 0:1], scalar2=None,
                op0=ALU.mult), sem="g1_done")
        emit_round(P, fin, None, tb, ta, g_col, pre=pre)
        P.wait("scalar", "g1_done", P.counts["g1_done"])
        P.op("scalar", lambda e: e.activation(out=sg[:], in_=g1[:], func=AF.Sigmoid),
             sem="sg_done")
        P.wait("vector", "sg_done", P.counts["sg_done"])
        def pre2(tbx):
            P.op("vector", lambda e, tbx=tbx: e.tensor_scalar(
                out=tbx[:], in0=sg[:], scalar1=65536.0, scalar2=None, op0=ALU.mult))
        def fin2(tax):
            return P.op("vector", lambda e, tax=tax: e.tensor_scalar(
                out=sg[:], in0=tax[:], scalar1=65535.0, scalar2=float(2.0 ** -16),
                op0=ALU.min, op1=ALU.mult), sem="sig_done")
        emit_round(P, fin2, None, tb, ta, g_col, pre=pre2)
        P.op("vector", lambda e: e.tensor_mul(out=g1[:], in0=g1[:], in1=sg[:]))
        def pre3(tbx):
            P.op("vector", lambda e, tbx=tbx: e.tensor_scalar(
                out=tbx[:], in0=g1[:], scalar1=invg2_col[:, 0:1], scalar2=None,
                op0=ALU.mult))
        def fin3(tax):
            return P.op("vector", lambda e, tax=tax: e.tensor_scalar(
                out=g1[:], in0=tax[:], scalar1=sg2_col[:, 0:1], scalar2=None,
                op0=ALU.mult), sem="g2_done")
        emit_round(P, fin3, None, tb, ta, g_col, pre=pre3)
        def pre4(tbx, k=k):
            P.op("vector", lambda e, k=k, tbx=tbx: e.tensor_scalar(
                out=tbx[:], in0=uch[k][:], scalar1=invu_col[:, 0:1], scalar2=None,
                op0=ALU.mult))
        def fin4(tax):
            return P.op("vector", lambda e, tax=tax: e.tensor_scalar(
                out=sg[:], in0=tax[:], scalar1=su_col[:, 0:1], scalar2=None,
                op0=ALU.mult), sem="u1_done")
        emit_round(P, fin4, None, tb, ta, g_col, pre=pre4)
        if r >= 2:
            P.wait("vector", "tq_st", (r - 2) * 16 + 16)
        P.op("vector", lambda e, k=k: e.tensor_mul(out=tqb[k][:], in0=g1[:], in1=sg[:]))
        P.op("vector", lambda e, k=k: e.tensor_reduce(
            out=amax_acc[:, 1:2], in_=tqb[k][:], axis=AXX, op=ALU.max,
            apply_absolute_value=True))
        cd = P.op("vector", lambda e: e.tensor_tensor(
            out=amax_acc[:, 0:1], in0=amax_acc[:, 0:1], in1=amax_acc[:, 1:2],
            op=ALU.max), sem="ch_done")
        P.wait("gpsimd", "ch_done", cd)
        P.op("gpsimd", lambda e, r=r, k=k: e.dma_start(
            out=tq_h[r * 128:(r + 1) * 128, :], in_=tqb[k][:]), sem="tq_st", by=16)

    P.wait("gpsimd", "ch_done", P.counts["ch_done"])
    d = P.op("gpsimd", lambda e: e.dma_start(
        out=amax_row[:], in_=amax_acc[:, 0:1]), sem="amax_d", by=16)
    P.wait("vector", "amax_d", d)
    rf = P.op("vector", lambda e: e.tensor_reduce(
        out=m_sb[:], in_=amax_row[:], axis=AXX, op=ALU.max), sem="amax_f")
    P.wait("gpsimd", "amax_f", rf)
    P.op("gpsimd", lambda e: e.dma_start(out=mo_h, in_=m_sb[:]), sem="st_m", by=16)
    P.wait("gpsimd", "tq_st", P.counts["tq_st"])
    P.wait("gpsimd", "st_m", 16)
    return P.finish()


def emit_wprep(P, w_h, O, I, scratch_h, stage, stage_bf, tt_sl, ident_unused,
               gain_col=None, lpbq=False, fixq=None, g_col=None,
               bmax=None, sblk=None, invblk=None, tb=None, ta=None, tag=""):
    NR, NI = O // 128, I // 128
    base_cast = P.counts.get("wp_cast", 0)
    base_st = [P.counts.get(f"wp_st{sl}", 0) for sl in range(2)]
    sc4 = scratch_h.rearrange("(r a) (ic b) -> r ic a b", a=128, b=128) if False else scratch_h
    if not hasattr(P, "_wp_hist"):
        P._wp_hist = []
    for r in range(NR):
        nw = r
        g_row = base_cast + r
        ldsem = f"wp_ld{r % 2}"
        if g_row >= 2:
            P.wait("sync", "wp_cast", g_row - 1)
        v = P.op("sync", lambda e, r=r: e.dma_start(
            out=stage[r % 2][:, 0:I], in_=w_h[r * 128:(r + 1) * 128, :]), sem=ldsem, by=16)
        P.wait("vector", ldsem, v)
        if g_row >= 2:
            h0, h1 = P._wp_hist[g_row - 2]
            if h0:
                P.wait("vector", "wp_tr0", h0)
            if h1:
                P.wait("vector", "wp_tr1", h1)
        if lpbq:
            nb = I // 32
            P.op("vector", lambda e, r=r: e.tensor_reduce(
                out=bmax[:, 0:nb], in_=stage[r % 2][:].rearrange("p (b i) -> p b i", i=32),
                axis=AXX, op=ALU.max, apply_absolute_value=True))
            P.op("vector", lambda e: e.tensor_scalar(
                out=sblk[:, 0:nb], in0=bmax[:, 0:nb], scalar1=1.0 / 7.0, scalar2=1e-12,
                op0=ALU.mult, op1=ALU.max))
            P.op("vector", lambda e: e.reciprocal(out=invblk[:, 0:nb], in_=sblk[:, 0:nb]))
            def pre(tbx, r=r):
                P.op("vector", lambda e, r=r, tbx=tbx: e.tensor_tensor(
                    out=tbx[:].rearrange("p (b i) -> p b i", i=32),
                    in0=stage[r % 2][:].rearrange("p (b i) -> p b i", i=32),
                    in1=invblk[:, 0:nb].rearrange("p b -> p b ()").broadcast_to((128, nb, 32)),
                    op=ALU.mult))
            def fin(tax, r=r):
                return P.op("vector", lambda e, r=r, tax=tax: e.tensor_tensor(
                    out=stage_bf[r % 2][:].rearrange("p (b i) -> p b i", i=32),
                    in0=tax[:].rearrange("p (b i) -> p b i", i=32),
                    in1=sblk[:, 0:nb].rearrange("p b -> p b ()").broadcast_to((128, nb, 32)),
                    op=ALU.mult), sem="wp_cast")
            emit_round(P, fin, None, tb, ta, g_col, pre=pre)
        elif fixq is not None:
            width, frac = fixq
            sc = float(2.0 ** frac)
            imax, imin = float(2 ** (width - 1) - 1), float(-2 ** (width - 1))
            def pre(tbx, r=r):
                P.op("vector", lambda e, r=r: e.tensor_scalar(
                    out=tb[:, 0:I], in0=stage[r % 2][:, 0:I], scalar1=sc, scalar2=None,
                    op0=ALU.mult))
            def fin(tax, r=r):
                P.op("vector", lambda e: e.tensor_scalar(
                    out=ta[:, 0:I], in0=ta[:, 0:I], scalar1=imax, scalar2=imin,
                    op0=ALU.min, op1=ALU.max))
                return P.op("vector", lambda e, r=r: e.tensor_scalar(
                    out=stage_bf[r % 2][:, 0:I], in0=ta[:, 0:I], scalar1=1.0 / sc,
                    scalar2=None, op0=ALU.mult), sem="wp_cast")
            tbv = type("S", (), {"__getitem__": lambda s_, k_: tb[:, 0:I]})()
            tav = type("S", (), {"__getitem__": lambda s_, k_: ta[:, 0:I]})()
            emit_round(P, fin, None, tbv, tav, g_col, pre=pre)
        else:
            P.op("vector", lambda e, r=r: e.tensor_copy(
                out=stage_bf[r % 2][:, 0:I], in_=stage[r % 2][:, 0:I]), sem="wp_cast")
        cast_c = P.counts["wp_cast"]
        for ic in range(NI):
            sl = (r * NI + ic) % 2
            trsem = f"wp_tr{sl}"
            stsem = f"wp_st{sl}"
            n_use = (r * NI + ic) // 2
            if base_st[sl] + n_use * 16 > 0:
                P.wait("scalar", stsem, base_st[sl] + n_use * 16)
            P.wait("scalar", "wp_cast", cast_c)
            v = P.op("scalar", lambda e, r=r, ic=ic, sl=sl: e.dma_start_transpose(
                out=tt_sl[sl][:], in_=stage_bf[r % 2][:, ic * 128:(ic + 1) * 128]),
                sem=trsem, by=16, tick=False)
            if gain_col is not None:
                P.wait("vector", trsem, v)
                P.op("vector", lambda e, ic=ic, sl=sl: e.tensor_scalar(
                    out=tt_sl[sl][:], in0=tt_sl[sl][:], scalar1=gain_col[:, ic:ic + 1],
                    scalar2=None, op0=ALU.mult), sem="wp_g")
                P.wait("gpsimd", "wp_g", P.counts["wp_g"])
            else:
                P.wait("gpsimd", trsem, v)
            P.op("gpsimd", lambda e, r=r, ic=ic, sl=sl: e.dma_start(
                out=scratch_h[(r * NI + ic) * 128:(r * NI + ic + 1) * 128, :],
                in_=tt_sl[sl][:]), sem=stsem, by=16)
        P._wp_hist.append((P.counts.get("wp_tr0", 0), P.counts.get("wp_tr1", 0)))
    for sl in range(2):
        nm = f"wp_st{sl}"
        if nm in P.counts:
            P.wait("gpsimd", nm, P.counts[nm])


def _qdq_tile(P, out_fn, src_fn, invs, s, tb, ta, g_col, sem=None):
    def pre(tbx):
        P.op("vector", lambda e, tbx=tbx: src_fn(e, tbx))
    def fin(tax):
        return P.op("vector", lambda e, tax=tax: out_fn(e, tax), sem=sem)
    return emit_round(P, fin, None, tb, ta, g_col, pre=pre)


def build_launch5(dbg=False):
    nc = new_nc()
    P = Prog(nc)
    P.want_const(EPS_LN)
    ident_h = nc.inline_tensor(np.eye(128, dtype=ml_dtypes.bfloat16), "ident_bf16").ap()

    x1_h = nc.dram_tensor("x1", [T, D], F32, kind="ExternalInput").ap()
    tq_h = nc.dram_tensor("t_q", [3 * D, T], BF16, kind="ExternalInput").ap()
    moa_h = nc.dram_tensor("m_o_all", [1, 8], F32, kind="ExternalInput").ap()
    wdown_h = nc.dram_tensor("w_down", [D, 3 * D], F32, kind="ExternalInput").ap()
    wq_h = nc.dram_tensor("wq", [D, D], F32, kind="ExternalInput").ap()
    bq_h = nc.dram_tensor("bq", [D], F32, kind="ExternalInput").ap()
    wk_h = nc.dram_tensor("wk", [D, D], F32, kind="ExternalInput").ap()
    bk_h = nc.dram_tensor("bk", [D], F32, kind="ExternalInput").ap()
    wv_h = nc.dram_tensor("wv", [D, D], F32, kind="ExternalInput").ap()
    bv_h = nc.dram_tensor("bv", [D], F32, kind="ExternalInput").ap()
    wo_h = nc.dram_tensor("wo", [D, D], F32, kind="ExternalInput").ap()
    bo_h = nc.dram_tensor("bo", [D], F32, kind="ExternalInput").ap()
    g1_h = nc.dram_tensor("g1", [D], F32, kind="ExternalInput").ap()
    b1_h = nc.dram_tensor("b1", [D], F32, kind="ExternalInput").ap()
    g2_h = nc.dram_tensor("g2", [D], F32, kind="ExternalInput").ap()
    b2_h = nc.dram_tensor("b2", [D], F32, kind="ExternalInput").ap()
    w1_h = nc.dram_tensor("w1", [4 * D, D], F32, kind="ExternalInput").ap()
    bf1_h = nc.dram_tensor("bf1", [4 * D], F32, kind="ExternalInput").ap()
    w2_h = nc.dram_tensor("w2", [D, 4 * D], F32, kind="ExternalInput").ap()
    bf2_h = nc.dram_tensor("bf2", [D], F32, kind="ExternalInput").ap()
    fw1_h = nc.dram_tensor("f_w1", [4 * D, D], F32, kind="ExternalInput").ap()
    fb1_h = nc.dram_tensor("f_b1", [4 * D], F32, kind="ExternalInput").ap()
    fw2_h = nc.dram_tensor("f_w2", [D, 4 * D], F32, kind="ExternalInput").ap()
    fb2_h = nc.dram_tensor("f_b2", [D], F32, kind="ExternalInput").ap()
    out_h = nc.dram_tensor("out", [T, D], F32, kind="ExternalOutput").ap()

    x2_hbm = nc.dram_tensor("x2_hbm", [T, D], F32, kind="Internal").ap()
    xin_hbm = nc.dram_tensor("xin_hbm", [T, D], BF16, kind="Internal").ap()
    x3_hbm = nc.dram_tensor("x3_hbm", [T, D], F32, kind="Internal").ap()
    wqT_s = nc.dram_tensor("wqT_s", [D * D // 128, 128], BF16, kind="Internal").ap()
    wkT_s = nc.dram_tensor("wkT_s", [D * D // 128, 128], BF16, kind="Internal").ap()
    wvT_s = nc.dram_tensor("wvT_s", [D * D // 128, 128], BF16, kind="Internal").ap()
    woT_s = nc.dram_tensor("woT_s", [D * D // 128, 128], BF16, kind="Internal").ap()
    w1T_s = nc.dram_tensor("w1T_s", [4 * D * D // 128, 128], BF16, kind="Internal").ap()
    w2T_s = nc.dram_tensor("w2T_s", [4 * D * D // 128, 128], BF16, kind="Internal").ap()
    fw1T_s = nc.dram_tensor("fw1T_s", [4 * D * D // 128, 128], BF16, kind="Internal").ap()
    fw2T_s = nc.dram_tensor("fw2T_s", [4 * D * D // 128, 128], BF16, kind="Internal").ap()
    if dbg:
        dx2_h = nc.dram_tensor("d_x2", [T, D], F32, kind="ExternalOutput").ap()
        dxx2_h = nc.dram_tensor("d_xx2", [T, D], F32, kind="ExternalOutput").ap()
        dqT_h = nc.dram_tensor("d_qT", [128, T], BF16, kind="ExternalOutput").ap()
        dkT_h = nc.dram_tensor("d_kT", [128, T], BF16, kind="ExternalOutput").ap()
        dvx_h = nc.dram_tensor("d_vx", [128, 1040], BF16, kind="ExternalOutput").ap()
        dat_h = nc.dram_tensor("d_at", [128, T], BF16, kind="ExternalOutput").ap()
        dk2_h = nc.dram_tensor("d_k2", [128, 256], F32, kind="ExternalOutput").ap()
        dcps_h = nc.dram_tensor("d_cps", [128, D], F32, kind="ExternalOutput").ap()
        dct1_h = nc.dram_tensor("d_ct1", [128, D], F32, kind="ExternalOutput").ap()
        dcn2_h = nc.dram_tensor("d_cn2", [128, 4], F32, kind="ExternalOutput").ap()

    ident_sb = P.sb("ident_sb", [128, 128], BF16)
    g_col = P.sb("g_col", [128, 1], F32)
    mrow8 = P.sb("mrow8", [128, 8], F32)
    so_col = P.sb("so_col", [128, 1], F32)
    invo_col = P.sb("invo_col", [128, 1], F32)
    g1c = P.sb("g1c", [128, ND], F32)
    b1c = P.sb("b1c", [128, ND], F32)
    g2c = P.sb("g2c", [128, ND], F32)
    b2c = P.sb("b2c", [128, ND], F32)
    b1g = P.sb("b1g", [128, ND], F32)
    b2g = P.sb("b2g", [128, ND], F32)
    b1g_bf = P.sb("b1g_bf", [128, ND], BF16)
    b2g_bf = P.sb("b2g_bf", [128, ND], BF16)
    ones_row = P.sb("ones_row", [1, 128], BF16)
    bf1col = P.sb("bf1col", [128, 32], F32)
    fb1col = P.sb("fb1col", [128, 32], F32)
    stat = P.sb("stat", [128, NT * 8], F32)
    k2s = P.sb("k2s", [128, NT * 16], F32)
    f16 = P.sb("f16", [128, 16], F32)
    n2t = P.sb("n2t", [128, 16], F32)
    junk_bf = P.sb("junk_bf", [128, D], BF16)
    amax_row = P.sb("amax_row", [1, 128], F32)

    ps_tr = [P.psum(f"ps_tr{k}", [128, 128], BF16) for k in range(2)]
    ps_mm = [P.psum(f"ps_mm{k}", [128, 512], F32) for k in range(6)]
    pool = PsumPool(P, ps_mm)

    MU0, RSTD0, MU1, RSTD1, SQ, SS = 0, NT, 2 * NT, 3 * NT, 4 * NT, 5 * NT
    MU2, RSTD2 = 6 * NT, 7 * NT

    P.op("sync", lambda e: e.dma_start(out=ident_sb[:], in_=ident_h), sem="ld_misc", by=16)
    P.op("sync", lambda e: e.dma_start(
        out=mrow8[:], in_=moa_h.partition_broadcast(128)), sem="ld_misc", by=16)
    ld_misc_total = P.counts["ld_misc"]
    emit_g_gadget(P, g_col)
    P.op("vector", lambda e: e.memset(ones_row[:], 1.0), sem="v_misc")
    load_colvec(P, g1_h, g1c, ND, "ld_vec")
    load_colvec(P, b1_h, b1c, ND, "ld_vec")
    load_colvec(P, g2_h, g2c, ND, "ld_vec")
    ld_vec_total = load_colvec(P, b2_h, b2c, ND, "ld_vec")
    P.wait("vector", "ld_vec", ld_vec_total)
    P.op("vector", lambda e: e.reciprocal(out=b1g[:], in_=g1c[:]))
    P.op("vector", lambda e: e.tensor_mul(out=b1g[:], in0=b1g[:], in1=b1c[:]))
    P.op("vector", lambda e: e.reciprocal(out=b2g[:], in_=g2c[:]))
    P.op("vector", lambda e: e.tensor_mul(out=b2g[:], in0=b2g[:], in1=b2c[:]))
    P.op("vector", lambda e: e.tensor_copy(out=b1g_bf[:], in_=b1g[:]))
    P.op("vector", lambda e: e.tensor_copy(out=b2g_bf[:], in_=b2g[:]), sem="bg_done")
    P.wait("vector", "ld_misc", ld_misc_total)
    emit_scale_cols(P, None, so_col, invo_col, mrow8, QMAX16, None, "sc_o")

    P.push_scope()
    wstage = [P.sb(f"wstage{k}", [128, 4 * D], F32) for k in range(2)]
    wstage_bf = [P.sb(f"wstage_bf{k}", [128, 4 * D], BF16) for k in range(2)]
    tt_sl = [P.sb(f"tt_sl{k}", [128, 128], BF16) for k in range(2)]
    tbw = P.sb("tbw", [128, 4 * D], F32)
    taw = P.sb("taw", [128, 4 * D], F32)
    P.wait("vector", "bg_done", P.counts["bg_done"])
    emit_wprep(P, wq_h, D, D, wqT_s, wstage, wstage_bf, tt_sl, None,
               gain_col=g1c, tag="q")
    emit_wprep(P, wk_h, D, D, wkT_s, wstage, wstage_bf, tt_sl, None,
               gain_col=g1c, tag="k")
    emit_wprep(P, wv_h, D, D, wvT_s, wstage, wstage_bf, tt_sl, None,
               gain_col=g1c, tag="v")
    emit_wprep(P, wo_h, D, D, woT_s, wstage, wstage_bf, tt_sl, None, tag="o")
    emit_wprep(P, w1_h, 4 * D, D, w1T_s, wstage, wstage_bf, tt_sl, None,
               gain_col=g2c, tag="w1")
    emit_wprep(P, w2_h, D, 4 * D, w2T_s, wstage, wstage_bf, tt_sl, None, tag="w2")
    emit_wprep(P, fw1_h, 4 * D, D, fw1T_s, wstage, wstage_bf, tt_sl, None,
               fixq=(8, 6), g_col=g_col, tb=tbw, ta=taw, tag="f1")
    emit_wprep(P, fw2_h, D, 4 * D, fw2T_s, wstage, wstage_bf, tt_sl, None,
               fixq=(8, 6), g_col=g_col, tb=tbw, ta=taw, tag="f2")
    P.pop_scope()
    P.barrier()

    P.push_scope()
    x_sb = [P.sb(f"xA{i}", [128, D], F32) for i in range(NT)]
    wdT = [P.sb(f"wdT{ic}", [128, D], BF16) for ic in range(24)]
    oq = [[P.sb(f"oq0_{ic}", [128, 512], BF16) for ic in range(24)]]
    HW = 3 * D // 2
    stA = [P.sb(f"stA{k}", [128, HW], F32) for k in range(2)]
    stA_bf = [P.sb(f"stA_bf{k}", [128, HW], BF16) for k in range(2)]
    ttA = [P.sb(f"ttA{k}", [128, 128], BF16) for k in range(2)]
    tqst = [P.sb(f"tqst{k}", [128, 512], BF16) for k in range(2)]
    tbA = P.sb("tbA", [128, HW], F32)
    taA = P.sb("taA", [128, HW], F32)
    xinb = P.sb("xinb", [128, D], BF16)

    for i in range(NT):
        P.op("sync", lambda e, i=i: e.dma_start(
            out=x_sb[i][:], in_=x1_h[i * 128:(i + 1) * 128, :]), sem="ld_x", by=16)
    ld_x_total = P.counts["ld_x"]

    bmaxA = P.sb("bmaxA", [128, 48], F32)
    sblkA = P.sb("sblkA", [128, 48], F32)
    invblkA = P.sb("invblkA", [128, 48], F32)
    nwch = 0
    wd_trhist = []
    for r in range(ND):
        for h in range(2):
            k = nwch % 2
            if nwch >= 2:
                P.wait("sync", "wd_cast", nwch - 1)
            v = P.op("sync", lambda e, r=r, h=h, k=k: e.dma_start(
                out=stA[k][:], in_=wdown_h[r * 128:(r + 1) * 128, h * HW:(h + 1) * HW]),
                sem=f"wd_ld{k}", by=16)
            P.wait("vector", f"wd_ld{k}", v)
            if nwch >= 2:
                h0, h1 = wd_trhist[nwch - 2]
                if h0:
                    P.wait("vector", "wd_trd0", h0)
                if h1:
                    P.wait("vector", "wd_trd1", h1)
            P.op("vector", lambda e, k=k: e.tensor_reduce(
                out=bmaxA[:], in_=stA[k][:].rearrange("p (b i) -> p b i", i=32),
                axis=AXX, op=ALU.max, apply_absolute_value=True))
            P.op("vector", lambda e: e.tensor_scalar(
                out=sblkA[:], in0=bmaxA[:], scalar1=1.0 / 7.0, scalar2=1e-12,
                op0=ALU.mult, op1=ALU.max))
            P.op("vector", lambda e: e.reciprocal(out=invblkA[:], in_=sblkA[:]))
            def pre(tbx, k=k):
                P.op("vector", lambda e, k=k, tbx=tbx: e.tensor_tensor(
                    out=tbx[:].rearrange("p (b i) -> p b i", i=32),
                    in0=stA[k][:].rearrange("p (b i) -> p b i", i=32),
                    in1=invblkA[:].rearrange("p b -> p b ()").broadcast_to((128, 48, 32)),
                    op=ALU.mult))
            def fin(tax, k=k):
                return P.op("vector", lambda e, k=k, tax=tax: e.tensor_tensor(
                    out=stA_bf[k][:].rearrange("p (b i) -> p b i", i=32),
                    in0=tax[:].rearrange("p (b i) -> p b i", i=32),
                    in1=sblkA[:].rearrange("p b -> p b ()").broadcast_to((128, 48, 32)),
                    op=ALU.mult), sem="wd_cast")
            emit_round(P, fin, None, tbA, taA, g_col, pre=pre)
            cast_c = P.counts["wd_cast"]
            for icl in range(12):
                ic = h * 12 + icl
                ntr = nwch * 12 + icl
                sl = ntr % 2
                if ntr >= 2:
                    P.wait("scalar", f"wd_tr{sl}", (ntr // 2))
                P.wait("scalar", "wd_cast", cast_c)
                v = P.op("scalar", lambda e, icl=icl, sl=sl, k=k: e.dma_start_transpose(
                    out=ttA[sl][:], in_=stA_bf[k][:, icl * 128:(icl + 1) * 128]),
                    sem=f"wd_trd{sl}", by=16, tick=False)
                P.wait("scalar", f"wd_trd{sl}", v)
                P.op("scalar", lambda e, r=r, ic=ic, sl=sl: e.copy(
                    out=wdT[ic][:, r * 128:(r + 1) * 128], in_=ttA[sl][:]),
                    sem=f"wd_tr{sl}")
            wd_trhist.append((P.counts.get("wd_trd0", 0), P.counts.get("wd_trd1", 0)))
            nwch += 1
    for sl in range(2):
        P.wait("tensor", f"wd_tr{sl}", P.counts[f"wd_tr{sl}"])

    P.wait("vector", "sc_o", P.counts["sc_o"])
    for qt in range(4):
        kq = 0
        if qt >= 1:
            P.wait("vector", "mmA", P.counts["mmA"])
        for ic in range(24):
            sl = ic % 2
            nld = qt * 24 + ic
            if nld >= 2:
                P.wait("sync", "oq_done", nld - 1)
            v = P.op("sync", lambda e, qt=qt, ic=ic, sl=sl: e.dma_start(
                out=tqst[sl][:], in_=tq_h[ic * 128:(ic + 1) * 128,
                                          qt * 512:(qt + 1) * 512]),
                sem=f"tq_ld{sl}", by=16)
            P.wait("vector", f"tq_ld{sl}", v)
            def src_fn(e, tbx, sl=sl):
                return e.tensor_scalar(out=tbx[:, 0:512], in0=tqst[sl][:],
                                       scalar1=invo_col[:, 0:1], scalar2=None,
                                       op0=ALU.mult)
            def out_fn(e, tax, qt=qt, ic=ic, kq=kq):
                return e.tensor_scalar(out=oq[kq][ic][:], in0=tax[:, 0:512],
                                       scalar1=so_col[:, 0:1], scalar2=None,
                                       op0=ALU.mult)
            def pre(tbx, sl=sl):
                P.op("vector", lambda e, tbx=tbx, sl=sl: e.tensor_scalar(
                    out=tbx[:, 0:512], in0=tqst[sl][:], scalar1=invo_col[:, 0:1],
                    scalar2=None, op0=ALU.mult))
            def fin(tax, qt=qt, ic=ic, kq=kq):
                return P.op("vector", lambda e, tax=tax, qt=qt, ic=ic, kq=kq: e.tensor_scalar(
                    out=oq[kq][ic][:], in0=tax[:, 0:512], scalar1=so_col[:, 0:1],
                    scalar2=None, op0=ALU.mult), sem="oq_done")
            class _V:
                pass
            tbv = type("S", (), {"__getitem__": lambda s, k: tbA[:, 0:512]})()
            tav = type("S", (), {"__getitem__": lambda s, k: taA[:, 0:512]})()
            emit_round(P, fin, None, tbv, tav, g_col, pre=pre)
        oq_ready = P.counts["oq_done"]
        P.wait("tensor", "oq_done", oq_ready)
        for tt in range(4):
            t_glob = qt * 4 + tt
            banks = [0, 1] if tt % 2 == 0 else [2, 3]
            for ic in range(24):
                for o2 in range(2):
                    bk = banks[o2]
                    if ic == 0:
                        pool.acquire(bk, "tensor")
                    P.op("tensor", lambda e, qt=qt, tt=tt, ic=ic, o2=o2, bk=bk, kq=kq: e.matmul(
                        out=ps_mm[bk][:],
                        lhsT=oq[kq][ic][:, tt * 128:(tt + 1) * 128],
                        rhs=wdT[ic][:, o2 * 512:(o2 + 1) * 512],
                        start=(ic == 0), stop=(ic == 23)),
                        sem="mmA" if ic == 23 else None)
            mm_c = P.counts["mmA"]
            P.wait("vector", "ld_x", ld_x_total)
            for o2 in range(2):
                bk = banks[o2]
                P.wait("vector", "mmA", mm_c)
                ev = P.op("vector", lambda e, t_glob=t_glob, o2=o2, bk=bk: e.tensor_add(
                    out=x_sb[t_glob][:, o2 * 512:(o2 + 1) * 512],
                    in0=x_sb[t_glob][:, o2 * 512:(o2 + 1) * 512],
                    in1=ps_mm[bk][:]), sem="evA")
                pool.release(bk, "evA", ev)
            P.op("vector", lambda e, t_glob=t_glob: e.reduce_sum(
                out=stat[:, SS + t_glob:SS + t_glob + 1], in_=x_sb[t_glob][:],
                axis=AXX), sem="sv")
            P.wait("scalar", "evA", P.counts["evA"])
            P.op("scalar", lambda e, t_glob=t_glob: e.activation(
                out=junk_bf[:], in_=x_sb[t_glob][:], func=AF.Square,
                accum_out=stat[:, SQ + t_glob:SQ + t_glob + 1]), sem="sa")
            P.wait("gpsimd", "evA", P.counts["evA"])
            P.op("gpsimd", lambda e, t_glob=t_glob: e.dma_start(
                out=x2_hbm[t_glob * 128:(t_glob + 1) * 128, :], in_=x_sb[t_glob][:]),
                sem="st_x2", by=16)
            if dbg:
                P.op("gpsimd", lambda e, t_glob=t_glob: e.dma_start(
                    out=dx2_h[t_glob * 128:(t_glob + 1) * 128, :], in_=x_sb[t_glob][:]),
                    sem="st_x2", by=16)

    P.wait("vector", "sa", P.counts["sa"])
    P.wait("vector", "sv", P.counts["sv"])
    P.op("vector", lambda e: e.tensor_scalar(
        out=stat[:, MU0:MU0 + NT], in0=stat[:, SS:SS + NT], scalar1=1.0 / D,
        scalar2=None, op0=ALU.mult))
    P.op("vector", lambda e: e.tensor_mul(
        out=stat[:, RSTD0:RSTD0 + NT], in0=stat[:, MU0:MU0 + NT],
        in1=stat[:, MU0:MU0 + NT]))
    vm = P.op("vector", lambda e: e.scalar_tensor_tensor(
        out=stat[:, RSTD0:RSTD0 + NT], in0=stat[:, SQ:SQ + NT], scalar=1.0 / D,
        in1=stat[:, RSTD0:RSTD0 + NT], op0=ALU.mult, op1=ALU.subtract), sem="sv")
    P.wait("scalar", "sv", vm)
    va = P.op("scalar", lambda e: e.activation(
        out=stat[:, RSTD0:RSTD0 + NT], in_=stat[:, RSTD0:RSTD0 + NT],
        func=AF.Sqrt, bias=EPS_LN, scale=1.0), sem="sa")
    P.wait("vector", "sa", va)
    P.op("vector", lambda e: e.reciprocal(
        out=stat[:, RSTD0:RSTD0 + NT], in_=stat[:, RSTD0:RSTD0 + NT]), sem="sv")
    for i in range(NT):
        P.op("vector", lambda e, i=i: e.tensor_scalar(
            out=xinb[:], in0=x_sb[i][:],
            scalar1=stat[:, MU0 + i:MU0 + i + 1], scalar2=stat[:, RSTD0 + i:RSTD0 + i + 1],
            op0=ALU.subtract, op1=ALU.mult))
        P.op("vector", lambda e, i=i: e.reduce_sum(
            out=stat[:, SS + i:SS + i + 1], in_=xinb[:], axis=AXX), sem="sv")
        P.wait("scalar", "sv", P.counts["sv"])
        P.op("scalar", lambda e, i=i: e.activation(
            out=junk_bf[:], in_=xinb[:], func=AF.Square,
            accum_out=stat[:, SQ + i:SQ + i + 1]), sem="sa")
        P.wait("gpsimd", "sa", P.counts["sa"])
        P.op("gpsimd", lambda e, i=i: e.dma_start(
            out=xin_hbm[i * 128:(i + 1) * 128, :], in_=xinb[:]), sem="st_xin", by=16)
        P.wait("vector", "st_xin", P.counts["st_xin"])
    P.op("vector", lambda e: e.tensor_scalar(
        out=stat[:, MU1:MU1 + NT], in0=stat[:, SS:SS + NT], scalar1=1.0 / D,
        scalar2=None, op0=ALU.mult))
    P.op("vector", lambda e: e.tensor_mul(
        out=stat[:, RSTD1:RSTD1 + NT], in0=stat[:, MU1:MU1 + NT],
        in1=stat[:, MU1:MU1 + NT]))
    vm = P.op("vector", lambda e: e.scalar_tensor_tensor(
        out=stat[:, RSTD1:RSTD1 + NT], in0=stat[:, SQ:SQ + NT], scalar=1.0 / D,
        in1=stat[:, RSTD1:RSTD1 + NT], op0=ALU.mult, op1=ALU.subtract), sem="sv")
    P.wait("scalar", "sv", vm)
    va = P.op("scalar", lambda e: e.activation(
        out=stat[:, RSTD1:RSTD1 + NT], in_=stat[:, RSTD1:RSTD1 + NT],
        func=AF.Sqrt, bias=EPS_LN, scale=1.0), sem="sa")
    P.wait("vector", "sa", va)
    P.op("vector", lambda e: e.reciprocal(
        out=stat[:, RSTD1:RSTD1 + NT], in_=stat[:, RSTD1:RSTD1 + NT]), sem="stats1_done")
    P.wait("gpsimd", "st_x2", P.counts["st_x2"])
    P.wait("gpsimd", "st_xin", P.counts["st_xin"])
    P.pop_scope()
    P.barrier()
    return P, dict(locals())


def build_launch5_full(dbg=False):
    P, env = build_launch5(dbg=dbg)
    _emit_l5_attention(P, env, dbg=dbg)
    _emit_l5_tail(P, env, dbg=dbg)
    return P.finish()


def _emit_l5_attention(P, env, dbg=False):
    nc = P.nc
    g = env
    stat, g_col, ident_sb, k2s, f16, n2t = g["stat"], g["g_col"], g["ident_sb"], g["k2s"], g["f16"], g["n2t"]
    ones_row, junk_bf = g["ones_row"], g["junk_bf"]
    ps_tr, ps_mm, pool = g["ps_tr"], g["ps_mm"], g["pool"]
    b1g = g["b1g"]
    b1g_bf = g["b1g_bf"]
    MU1, RSTD1, SQ, SS = g["MU1"], g["RSTD1"], g["SQ"], g["SS"]
    MU2, RSTD2 = g["MU2"], g["RSTD2"]
    xin_hbm = g["xin_hbm"]
    MAXN = 1.0 - 1e-05

    P.push_scope()
    xb = [P.sb(f"xb{i}", [128, D], BF16) for i in range(NT)]
    g["xb"] = xb
    brow_f = [P.sb("brow_f0", [1, D], F32)] * 4
    brow_b = [P.sb("brow_b0", [1, D], BF16)] * 4
    P.push_scope()
    qT = [P.sb(f"qT{p}", [128, T], BF16) for p in range(8)]
    kT = [P.sb(f"kT{p}", [128, T], BF16) for p in range(8)]
    v_ext = [P.sb(f"vext{i}", [128, 16 * 65], BF16) for i in range(NT)]

    P.push_scope()
    xnT = [P.sb(f"xnT{c}", [128, T], BF16) for c in range(ND)]
    wT_res = [P.sb(f"wTr{ic}", [128, D], BF16) for ic in range(ND)]
    qk_tm = xb
    xtile = [P.sb(f"xtile{k}", [128, D], BF16) for k in range(2)]

    P.wait("vector", "stats1_done", P.counts["stats1_done"])
    for i in range(NT):
        k = i % 2
        if i >= 2:
            P.wait("sync", "xnstage", i - 1)
        v = P.op("sync", lambda e, i=i, k=k: e.dma_start(
            out=xtile[k][:], in_=xin_hbm[i * 128:(i + 1) * 128, :]),
            sem=f"xin_ld{k}", by=16)
        P.wait("vector", f"xin_ld{k}", v)
        P.op("vector", lambda e, i=i, k=k: e.tensor_scalar(
            out=qk_tm[i][:], in0=xtile[k][:],
            scalar1=stat[:, MU1 + i:MU1 + i + 1], scalar2=stat[:, RSTD1 + i:RSTD1 + i + 1],
            op0=ALU.subtract, op1=ALU.mult), sem="xnstage")
    xn_c = P.counts["xnstage"]
    P.wait("tensor", "xnstage", xn_c)
    P.wait("tensor", "ld_misc", 16)
    ntr = 0
    for i in range(NT):
        for c in range(ND):
            kk = ntr % 2
            if ntr >= 2:
                P.wait("tensor", "trB1", ntr - 1)
            P.op("tensor", lambda e, i=i, c=c, kk=kk: e.transpose(
                out=ps_tr[kk][:], in_=qk_tm[i][:, c * 128:(c + 1) * 128],
                identity=ident_sb[:]), sem="trp")
            P.wait("scalar", "trp", P.counts["trp"])
            P.op("scalar", lambda e, i=i, c=c, kk=kk: e.copy(
                out=xnT[c][:, i * 128:(i + 1) * 128], in_=ps_tr[kk][:]), sem="trB1")
            ntr += 1
    trB1_total = P.counts["trB1"]

    def load_wres(scr, tag, dst=None):
        if dst is None:
            dst = wT_res
        for ic in range(ND):
            for r in range(ND):
                P.op("sync", lambda e, ic=ic, r=r, scr=scr, dst=dst: e.dma_start(
                    out=dst[ic][:, r * 128:(r + 1) * 128],
                    in_=scr[(r * ND + ic) * 128:(r * ND + ic + 1) * 128, :]),
                    sem="wres", by=16)
        return P.counts["wres"]

    def bias_matvec(bvec_col, bias_h, w_row, tag, wres_c=None):
        P.op("sync", lambda e, bias_h=bias_h, w_row=w_row: e.dma_start(
            out=brow_f[w_row][:], in_=bias_h.rearrange("(a n) -> a n", a=1)),
            sem="bld", by=16)
        if wres_c is not None:
            P.wait("tensor", "wres", wres_c)
        for o2 in range(2):
            bk = o2
            pool.acquire(bk, "tensor")
            for ic in range(ND):
                P.op("tensor", lambda e, ic=ic, o2=o2, bk=bk: e.matmul(
                    out=ps_mm[bk][0:1, :],
                    lhsT=b1g_bf[:, ic:ic + 1],
                    rhs=wT_res[ic][:, o2 * 512:(o2 + 1) * 512],
                    start=(ic == 0), stop=(ic == ND - 1)),
                    sem="bmv" if ic == ND - 1 else None)
        P.wait("vector", "bmv", P.counts["bmv"])
        P.wait("vector", "bld", P.counts["bld"])
        for o2 in range(2):
            ev = P.op("vector", lambda e, o2=o2, w_row=w_row: e.tensor_add(
                out=brow_b[w_row][0:1, o2 * 512:(o2 + 1) * 512],
                in0=ps_mm[o2][0:1, :],
                in1=brow_f[w_row][0:1, o2 * 512:(o2 + 1) * 512]), sem="bev")
            pool.release(o2, "bev", ev)
        return P.counts["bev"]

    def qkv_mm(w_row, tag, evac):
        for t in range(NT):
            banks = [0, 1] if t % 2 == 0 else [2, 3]
            for ic in range(ND):
                for o2 in range(2):
                    bk = banks[o2]
                    if ic == 0:
                        pool.acquire(bk, "tensor")
                    P.op("tensor", lambda e, t=t, ic=ic, o2=o2, bk=bk: e.matmul(
                        out=ps_mm[bk][:],
                        lhsT=xnT[ic][:, t * 128:(t + 1) * 128],
                        rhs=wT_res[ic][:, o2 * 512:(o2 + 1) * 512],
                        start=(ic == 0), stop=False))
            for o2 in range(2):
                bk = banks[o2]
                P.op("tensor", lambda e, t=t, o2=o2, bk=bk, w_row=w_row: e.matmul(
                    out=ps_mm[bk][:],
                    lhsT=ones_row[0:1, :],
                    rhs=brow_b[w_row][0:1, o2 * 512:(o2 + 1) * 512],
                    start=False, stop=True), sem="mm_qkv")
            evac(t, P.counts["mm_qkv"], banks)

    wres_c = load_wres(g["wqT_s"], "q")
    bias_matvec(b1g, g["bq_h"], 0, "q", wres_c=wres_c)
    P.wait("tensor", "trB1", trB1_total)
    P.wait("tensor", "wres", wres_c)
    sq_scr = P.sb("sq_scr", [128, D], BF16)

    def evac_qk(which):
        def evac(t, mm_c, banks):
            for o2 in range(2):
                bk = banks[o2]
                P.wait("scalar", "mm_qkv", mm_c)
                ev = P.op("scalar", lambda e, t=t, o2=o2, bk=bk: e.copy(
                    out=qk_tm[t][:, o2 * 512:(o2 + 1) * 512], in_=ps_mm[bk][:]),
                    sem="ev_qkv")
                pool.release(bk, "ev_qkv", ev)
            P.wait("vector", "ev_qkv", P.counts["ev_qkv"])
            P.op("vector", lambda e, t=t: e.tensor_mul(
                out=sq_scr[:], in0=qk_tm[t][:], in1=qk_tm[t][:]))
            nv = P.op("vector", lambda e: e.tensor_reduce(
                out=n2t[:], in_=sq_scr[:].rearrange("p (h d) -> p h d", d=64),
                axis=AXX, op=ALU.add), sem="n2_qk")
            P.wait("scalar", "n2_qk", nv)
            na = P.op("scalar", lambda e: e.activation(
                out=f16[:], in_=n2t[:], func=AF.Sqrt), sem="sq_qk")
            P.wait("vector", "sq_qk", na)
            P.op("vector", lambda e: e.tensor_scalar(
                out=f16[:], in0=f16[:], scalar1=1e-12, scalar2=None, op0=ALU.max))
            P.op("vector", lambda e: e.reciprocal(out=f16[:], in_=f16[:]))
            P.op("vector", lambda e: e.tensor_scalar(
                out=f16[:], in0=f16[:], scalar1=MAXN, scalar2=1.0,
                op0=ALU.mult, op1=ALU.min))
            if which == "k":
                P.op("vector", lambda e: e.tensor_mul(
                    out=n2t[:], in0=n2t[:], in1=f16[:]))
                P.op("vector", lambda e: e.tensor_mul(
                    out=n2t[:], in0=n2t[:], in1=f16[:]))
                P.op("vector", lambda e, t=t: e.tensor_scalar(
                    out=k2s[:, t * 16:(t + 1) * 16], in0=n2t[:], scalar1=-0.125,
                    scalar2=None, op0=ALU.mult), sem="k2s_done")
            P.op("vector", lambda e, t=t: e.tensor_tensor(
                out=qk_tm[t][:].rearrange("p (h d) -> p h d", d=64),
                in0=qk_tm[t][:].rearrange("p (h d) -> p h d", d=64),
                in1=f16[:].rearrange("p h -> p h ()").broadcast_to((128, 16, 64)),
                op=ALU.mult), sem="proj_qk")
        return evac
    qkv_mm(0, "q", evac_qk("q"))
    P.wait("tensor", "proj_qk", P.counts["proj_qk"])
    ntr = 0
    for i in range(NT):
        for c in range(ND):
            kk = ntr % 2
            if ntr >= 2:
                P.wait("tensor", "trQ", ntr - 1)
            P.op("tensor", lambda e, i=i, c=c, kk=kk: e.transpose(
                out=ps_tr[kk][:], in_=qk_tm[i][:, c * 128:(c + 1) * 128],
                identity=ident_sb[:]), sem="trp")
            P.wait("scalar", "trp", P.counts["trp"])
            P.op("scalar", lambda e, i=i, c=c, kk=kk: e.copy(
                out=qT[c][:, i * 128:(i + 1) * 128], in_=ps_tr[kk][:]), sem="trQ")
            ntr += 1
    trQ_total = P.counts["trQ"]
    P.wait("sync", "trQ", trQ_total)
    wres_k = load_wres(g["wkT_s"], "k")
    bias_matvec(b1g, g["bk_h"], 1, "k", wres_c=wres_k)
    P.wait("tensor", "wres", wres_k)
    qkv_mm(1, "k", evac_qk("k"))
    P.wait("tensor", "proj_qk", P.counts["proj_qk"])
    ntr = 0
    for i in range(NT):
        for c in range(ND):
            kk = ntr % 2
            if ntr >= 2:
                P.wait("tensor", "trK", ntr - 1)
            P.op("tensor", lambda e, i=i, c=c, kk=kk: e.transpose(
                out=ps_tr[kk][:], in_=qk_tm[i][:, c * 128:(c + 1) * 128],
                identity=ident_sb[:]), sem="trp")
            P.wait("scalar", "trp", P.counts["trp"])
            P.op("scalar", lambda e, i=i, c=c, kk=kk: e.copy(
                out=kT[c][:, i * 128:(i + 1) * 128], in_=ps_tr[kk][:]), sem="trK")
            ntr += 1
    for i in range(NT):
        P.op("vector", lambda e, i=i: e.memset(
            v_ext[i][:].rearrange("p (h c) -> p h c", c=65)[:, :, 64:65], 1.0),
            sem="vone")
    P.wait("sync", "trK", P.counts["trK"])
    wres_v = load_wres(g["wvT_s"], "v")
    bias_matvec(b1g, g["bv_h"], 2, "v", wres_c=wres_v)
    P.wait("tensor", "wres", wres_v)

    def evac_v(t, mm_c, banks):
        for o2 in range(2):
            bk = banks[o2]
            P.wait("scalar", "mm_qkv", mm_c)
            ev = P.op("scalar", lambda e, t=t, o2=o2, bk=bk: e.copy(
                out=v_ext[t][:].rearrange("p (h c) -> p h c", c=65)[:, o2 * 8:(o2 + 1) * 8, 0:64],
                in_=ps_mm[bk][:].rearrange("p (h d) -> p h d", d=64)), sem="ev_v")
            pool.release(bk, "ev_v", ev)
    qkv_mm(2, "v", evac_v)
    P.wait("tensor", "ev_v", P.counts["ev_v"])
    P.wait("tensor", "vone", P.counts["vone"])
    P.pop_scope()
    P.barrier()

    P.push_scope()
    attn = [P.sb(f"attn{p}", [128, T], BF16) for p in range(8)]
    P.push_scope()
    e_sl = [P.sb(f"esl{k}", [128, 512], BF16) for k in range(4)]
    rb_sb = [P.sb(f"rbsb{k}", [64, 512], BF16) for k in range(2)]
    rs_row = P.sb("rs_row", [1, 512], F32)
    rsb_row = P.sb("rsb_row", [1, 512], BF16)

    P.wait("scalar", "k2s_done", P.counts["k2s_done"])
    P.wait("tensor", "trK", P.counts["trK"])
    pv_hist = []
    rb_hist = [0, 0]
    for p in range(8):
        for qt in range(4):
            pool.acquire(4, "tensor")
            pool.acquire(5, "tensor")
            for kt in range(16):
                sA = 0 if kt % 2 == 0 else 2
                sB = 1 if kt % 2 == 0 else 3
                pool.acquire(sA, "tensor")
                pool.acquire(sB, "tensor")
                P.op("tensor", lambda e, p=p, qt=qt, kt=kt, sA=sA: e.matmul(
                    out=ps_mm[sA][:],
                    lhsT=kT[p][0:64, kt * 128:(kt + 1) * 128],
                    rhs=qT[p][0:64, qt * 512:(qt + 1) * 512],
                    start=True, stop=True), sem="mm_s")
                P.op("tensor", lambda e, p=p, qt=qt, kt=kt, sB=sB: e.matmul(
                    out=ps_mm[sB][:],
                    lhsT=kT[p][64:128, kt * 128:(kt + 1) * 128],
                    rhs=qT[p][64:128, qt * 512:(qt + 1) * 512],
                    start=True, stop=True), sem="mm_s")
                mm_s_c = P.counts["mm_s"]
                eA = e_sl[(2 * kt) % 4]
                eB = e_sl[(2 * kt + 1) % 4]
                gk = len(pv_hist) - 2
                if gk >= 0:
                    P.wait("scalar", "mm_pv", pv_hist[gk])
                P.wait("scalar", "mm_s", mm_s_c)
                evA = P.op("scalar", lambda e, p=p, kt=kt, sA=sA, eA=eA: e.activation(
                    out=eA[:], in_=ps_mm[sA][:], func=AF.Exp,
                    bias=k2s[:, kt * 16 + 2 * p:kt * 16 + 2 * p + 1], scale=0.25),
                    sem="ev_e")
                evB = P.op("scalar", lambda e, p=p, kt=kt, sB=sB, eB=eB: e.activation(
                    out=eB[:], in_=ps_mm[sB][:], func=AF.Exp,
                    bias=k2s[:, kt * 16 + 2 * p + 1:kt * 16 + 2 * p + 2], scale=0.25),
                    sem="ev_e")
                pool.release(sA, "ev_e", evA)
                pool.release(sB, "ev_e", evB)
                P.wait("tensor", "ev_e", evA)
                P.op("tensor", lambda e, p=p, qt=qt, kt=kt, eA=eA: e.matmul(
                    out=ps_mm[4][0:65, :],
                    lhsT=v_ext[kt][:, (2 * p) * 65:(2 * p + 1) * 65],
                    rhs=eA[:], start=(kt == 0), stop=(kt == 15)), sem="mm_pv")
                P.wait("tensor", "ev_e", evB)
                P.op("tensor", lambda e, p=p, qt=qt, kt=kt, eB=eB: e.matmul(
                    out=ps_mm[5][0:65, :],
                    lhsT=v_ext[kt][:, (2 * p + 1) * 65:(2 * p + 2) * 65],
                    rhs=eB[:], start=(kt == 0), stop=(kt == 15)), sem="mm_pv")
                pv_hist.append(P.counts["mm_pv"])
            mm_pv_c = P.counts["mm_pv"]
            for hh in range(2):
                bank = 4 + hh
                P.wait("vector", "mm_pv", mm_pv_c)
                P.op("vector", lambda e, bank=bank: e.tensor_copy(
                    out=rs_row[:], in_=ps_mm[bank][64:65, :]))
                P.op("vector", lambda e: e.reciprocal(out=rs_row[:], in_=rs_row[:]))
                P.op("vector", lambda e: e.tensor_copy(
                    out=rsb_row[:], in_=rs_row[:]), sem="rs_done")
                P.wait("tensor", "rs_done", P.counts["rs_done"])
                bcb = 0 if hh == 0 else 1
                pool.acquire(bcb, "tensor")
                P.op("tensor", lambda e, bcb=bcb: e.matmul(
                    out=ps_mm[bcb][0:64, :],
                    lhsT=ones_row[0:1, 0:64],
                    rhs=rsb_row[0:1, :], start=True, stop=True), sem="mm_bc")
                P.wait("scalar", "mm_bc", P.counts["mm_bc"])
                if rb_hist[hh]:
                    P.wait("scalar", "attn_ev", rb_hist[hh])
                P.op("scalar", lambda e, hh=hh, bcb=bcb: e.copy(
                    out=rb_sb[hh][:], in_=ps_mm[bcb][0:64, :]), sem="rb_done")
                pool.release(bcb, "rb_done", P.counts["rb_done"])
                P.wait("vector", "rb_done", P.counts["rb_done"])
                ev = P.op("vector", lambda e, p=p, qt=qt, hh=hh, bank=bank: e.tensor_mul(
                    out=attn[p][hh * 64:(hh + 1) * 64, qt * 512:(qt + 1) * 512],
                    in0=ps_mm[bank][0:64, :], in1=rb_sb[hh][:]), sem="attn_ev")
                rb_hist[hh] = ev
                pool.release(bank, "attn_ev", ev)

    attn_total = P.counts["attn_ev"]
    if dbg:
        P.wait("gpsimd", "attn_ev", attn_total)
        P.op("gpsimd", lambda e: e.dma_start(out=g["dqT_h"], in_=qT[0][:]), sem="st_dbg", by=16)
        P.op("gpsimd", lambda e: e.dma_start(out=g["dkT_h"], in_=kT[0][:]), sem="st_dbg", by=16)
        P.op("gpsimd", lambda e: e.dma_start(out=g["dvx_h"], in_=v_ext[0][:]), sem="st_dbg", by=16)
        P.op("gpsimd", lambda e: e.dma_start(out=g["dat_h"], in_=attn[0][:]), sem="st_dbg", by=16)
        P.op("gpsimd", lambda e: e.dma_start(out=g["dk2_h"], in_=k2s[:]), sem="st_dbg", by=16)
        P.wait("gpsimd", "st_dbg", P.counts["st_dbg"])
    P.pop_scope()
    P.barrier()
    P.push_scope()
    woT_res = [P.sb(f"woTr{ic}", [128, D], BF16) for ic in range(ND)]
    xtile2 = [P.sb(f"xtile2{k}", [128, D], BF16) for k in range(2)]
    t1f = P.sb("t1f", [128, D], F32)
    sq_scr_b2 = P.sb("sq_scr_b2", [128, D], F32)
    n2c = P.sb("n2c", [128, 1], F32)
    wres_o = load_wres(g["woT_s"], "o", dst=woT_res)
    P.op("sync", lambda e: e.dma_start(
        out=brow_f[3][:], in_=g["bo_h"].rearrange("(a n) -> a n", a=1)), sem="bld", by=16)
    P.wait("vector", "bld", P.counts["bld"])
    P.op("vector", lambda e: e.tensor_copy(out=brow_b[3][:], in_=brow_f[3][:]), sem="bo_cast")
    P.wait("tensor", "wres", wres_o)
    P.wait("tensor", "attn_ev", attn_total)
    P.wait("tensor", "bo_cast", P.counts["bo_cast"])
    for t in range(NT):
        kx = t % 2
        if t >= 2:
            P.wait("sync", "xx2_done", t - 1)
        vx = P.op("sync", lambda e, t=t, kx=kx: e.dma_start(
            out=xtile2[kx][:], in_=xin_hbm[t * 128:(t + 1) * 128, :]),
            sem=f"xin2_{kx}", by=16)
        banks = [0, 1] if t % 2 == 0 else [2, 3]
        for p in range(8):
            for o2 in range(2):
                bk = banks[o2]
                if p == 0:
                    pool.acquire(bk, "tensor")
                P.op("tensor", lambda e, t=t, p=p, o2=o2, bk=bk: e.matmul(
                    out=ps_mm[bk][:],
                    lhsT=attn[p][:, t * 128:(t + 1) * 128],
                    rhs=woT_res[p][:, o2 * 512:(o2 + 1) * 512],
                    start=(p == 0), stop=False))
        for o2 in range(2):
            bk = banks[o2]
            P.op("tensor", lambda e, t=t, o2=o2, bk=bk: e.matmul(
                out=ps_mm[bk][:],
                lhsT=ones_row[0:1, :],
                rhs=brow_b[3][0:1, o2 * 512:(o2 + 1) * 512],
                start=False, stop=True), sem="mm_o")
        mmo_c = P.counts["mm_o"]
        P.wait("vector", "mm_o", mmo_c)
        P.wait("vector", f"xin2_{kx}", vx)
        for o2 in range(2):
            bk = banks[o2]
            ev = P.op("vector", lambda e, t=t, o2=o2, bk=bk, kx=kx: e.tensor_add(
                out=t1f[:, o2 * 512:(o2 + 1) * 512],
                in0=xtile2[kx][:, o2 * 512:(o2 + 1) * 512],
                in1=ps_mm[bk][:]), sem="t1_done")
            pool.release(bk, "t1_done", ev)
        P.op("vector", lambda e: e.tensor_mul(out=sq_scr_b2[:], in0=t1f[:], in1=t1f[:]))
        nv = P.op("vector", lambda e: e.reduce_sum(
            out=n2c[:], in_=sq_scr_b2[:], axis=AXX), sem="n2_o")
        P.wait("scalar", "n2_o", nv)
        na = P.op("scalar", lambda e: e.activation(
            out=n2c[:], in_=n2c[:], func=AF.Sqrt), sem="sq_o")
        P.wait("vector", "sq_o", na)
        P.op("vector", lambda e: e.tensor_scalar(
            out=n2c[:], in0=n2c[:], scalar1=1e-12, scalar2=None, op0=ALU.max))
        P.op("vector", lambda e: e.reciprocal(out=n2c[:], in_=n2c[:]))
        P.op("vector", lambda e: e.tensor_scalar(
            out=n2c[:], in0=n2c[:], scalar1=MAXN, scalar2=1.0,
            op0=ALU.mult, op1=ALU.min))
        P.op("vector", lambda e, t=t: e.tensor_scalar(
            out=xb[t][:], in0=t1f[:], scalar1=n2c[:, 0:1], scalar2=None,
            op0=ALU.mult), sem="xx2_done")
        P.op("vector", lambda e, t=t: e.reduce_sum(
            out=stat[:, SS + t:SS + t + 1], in_=xb[t][:], axis=AXX), sem="sv")
        P.wait("scalar", "xx2_done", P.counts["xx2_done"])
        P.op("scalar", lambda e, t=t: e.activation(
            out=junk_bf[:], in_=xb[t][:], func=AF.Square,
            accum_out=stat[:, SQ + t:SQ + t + 1]), sem="sa")
        if dbg:
            P.wait("gpsimd", "xx2_done", P.counts["xx2_done"])
            P.op("gpsimd", lambda e, t=t: e.dma_start(
                out=g["dxx2_h"][t * 128:(t + 1) * 128, :], in_=xb[t][:]), sem="st_dbg", by=16)
    P.wait("vector", "sa", P.counts["sa"])
    P.wait("vector", "sv", P.counts["sv"])
    P.op("vector", lambda e: e.tensor_scalar(
        out=stat[:, MU2:MU2 + NT], in0=stat[:, SS:SS + NT], scalar1=1.0 / D,
        scalar2=None, op0=ALU.mult))
    P.op("vector", lambda e: e.tensor_mul(
        out=stat[:, RSTD2:RSTD2 + NT], in0=stat[:, MU2:MU2 + NT],
        in1=stat[:, MU2:MU2 + NT]))
    vm = P.op("vector", lambda e: e.scalar_tensor_tensor(
        out=stat[:, RSTD2:RSTD2 + NT], in0=stat[:, SQ:SQ + NT], scalar=1.0 / D,
        in1=stat[:, RSTD2:RSTD2 + NT], op0=ALU.mult, op1=ALU.subtract), sem="sv")
    P.wait("scalar", "sv", vm)
    va = P.op("scalar", lambda e: e.activation(
        out=stat[:, RSTD2:RSTD2 + NT], in_=stat[:, RSTD2:RSTD2 + NT],
        func=AF.Sqrt, bias=EPS_LN, scale=1.0), sem="sa")
    P.wait("vector", "sa", va)
    P.op("vector", lambda e: e.reciprocal(
        out=stat[:, RSTD2:RSTD2 + NT], in_=stat[:, RSTD2:RSTD2 + NT]), sem="stats2_done")
    if dbg:
        P.wait("gpsimd", "st_dbg", P.counts["st_dbg"])
    P.pop_scope()
    P.pop_scope()
    P.pop_scope()
    P.barrier()


def _emit_l5_tail(P, env, dbg=False):
    g = env
    stat, g_col, ident_sb = g["stat"], g["g_col"], g["ident_sb"]
    ones_row, junk_bf = g["ones_row"], g["junk_bf"]
    ps_tr, ps_mm, pool = g["ps_tr"], g["ps_mm"], g["pool"]
    b2g = g["b2g"]
    b2g_bf = g["b2g_bf"]
    MU2, RSTD2, SQ, SS = g["MU2"], g["RSTD2"], g["SQ"], g["SS"]
    MU0, RSTD0 = g["MU0"], g["RSTD0"]
    xb = g["xb"]
    bf1col, fb1col = g["bf1col"], g["fb1col"]
    x2_hbm, x3_hbm, out_h = g["x2_hbm"], g["x3_hbm"], g["out_h"]
    MAXN = 1.0 - 1e-05
    NH = 32

    def ffn_block(tag, w1T_s, w2T_s, bias1_h, bias2_h, bcol, quant, src_fn,
                  out_cb, bufs):
        xnT_l = bufs["xnT_l"]; hT = bufs["hT"]; slab = bufs["slab"]
        w2ch = bufs["w2ch"]; brow4 = bufs["brow4"]; brow4b = bufs["brow4b"]
        b2row = bufs["b2row"]; row128 = bufs["row128"]
        tb_l = bufs["tb_l"]; ta_l = bufs["ta_l"]; hf32 = bufs.get("hf32")
        """h = act(xn@w1'.T + b1eff) ; out tiles = h@w2'.T + b2eff -> out_cb.

        quant=False: act = exact gelu; w1 gain-folded (g2); b1eff = bf1 + (b2/g2)@w1g.
        quant=True: act = relu(fixed_quant(.,8,4)); biases fixed_quant(8,6)."""
        base_tr = P.counts.get("trCf", 0)
        ntr = 0
        for i in range(NT):
            for c in range(ND):
                kk = ntr % 2
                if ntr >= 2:
                    P.wait("tensor", "trCf", base_tr + ntr - 1)
                P.op("tensor", lambda e, i=i, c=c, kk=kk: e.transpose(
                    out=ps_tr[kk][:], in_=src_fn(i, c),
                    identity=ident_sb[:]), sem="trp")
                P.wait("scalar", "trp", P.counts["trp"])
                P.op("scalar", lambda e, i=i, c=c, kk=kk: e.copy(
                    out=xnT_l[c][:, i * 128:(i + 1) * 128], in_=ps_tr[kk][:]),
                    sem="trCf")
                ntr += 1
        P.wait("tensor", "trCf", P.counts["trCf"])
        P.op("sync", lambda e, bias1_h=bias1_h: e.dma_start(
            out=brow4[:], in_=bias1_h.rearrange("(a n) -> a n", a=1)), sem="bld4", by=16)
        bld4_c = P.counts["bld4"]
        b1q_c = 0
        if quant:
            P.wait("vector", "bld4", bld4_c)
            for hseg in range(4):
                sl_ = slice(hseg * D, (hseg + 1) * D)
                def pre(tbx, sl_=sl_):
                    P.op("vector", lambda e, sl_=sl_: e.tensor_scalar(
                        out=tb_l[0:1, 0:D], in0=brow4[0:1, sl_], scalar1=64.0,
                        scalar2=None, op0=ALU.mult))
                def fin(tax, sl_=sl_):
                    P.op("vector", lambda e: e.tensor_scalar(
                        out=ta_l[0:1, 0:D], in0=ta_l[0:1, 0:D], scalar1=127.0,
                        scalar2=-128.0, op0=ALU.min, op1=ALU.max))
                    return P.op("vector", lambda e, sl_=sl_: e.tensor_scalar(
                        out=brow4[0:1, sl_], in0=ta_l[0:1, 0:D], scalar1=1.0 / 64.0,
                        scalar2=None, op0=ALU.mult), sem="b1q_done")
                tbv = type("S", (), {"__getitem__": lambda s_, k_: tb_l[0:1, 0:D]})()
                tav = type("S", (), {"__getitem__": lambda s_, k_: ta_l[0:1, 0:D]})()
                emit_round(P, fin, None, tbv, tav, g_col, pre=pre)
            b1q_c = P.counts["b1q_done"]
        for th in range(2):
            mm1_hist = []
            for r in range(NH):
                sl = r % 2
                if len(mm1_hist) >= 2:
                    P.wait("sync", "mm1f", mm1_hist[-2])
                for ic in range(ND):
                    P.op("sync", lambda e, r=r, ic=ic, sl=sl, w1T_s=w1T_s: e.dma_start(
                        out=slab[sl][ic][:],
                        in_=w1T_s[(r * ND + ic) * 128:(r * ND + ic + 1) * 128, :]),
                        sem=f"slab_ld{sl}", by=16)
                slab_c = P.counts[f"slab_ld{sl}"]
                if not quant:
                    P.wait("tensor", f"slab_ld{sl}", slab_c)
                    pool.acquire(4, "tensor")
                    for ic in range(ND):
                        P.op("tensor", lambda e, ic=ic, sl=sl: e.matmul(
                            out=ps_mm[4][0:1, 0:128],
                            lhsT=b2g_bf[:, ic:ic + 1],
                            rhs=slab[sl][ic][:],
                            start=(ic == 0), stop=(ic == ND - 1)),
                            sem="bmv4" if ic == ND - 1 else None)
                    bmv4_c = P.counts["bmv4"]
                    P.wait("vector", "bmv4", bmv4_c)
                    P.wait("vector", "bld4", bld4_c)
                    bv = P.op("vector", lambda e, r=r: e.tensor_add(
                        out=row128[:], in0=ps_mm[4][0:1, 0:128],
                        in1=brow4[0:1, r * 128:(r + 1) * 128]), sem="brow_v")
                    pool.release(4, "brow_v", bv)
                else:
                    P.wait("vector", "b1q_done", b1q_c)
                    bv = P.op("vector", lambda e, r=r: e.tensor_copy(
                        out=row128[:], in_=brow4[0:1, r * 128:(r + 1) * 128]), sem="brow_v")
                    P.wait("tensor", f"slab_ld{sl}", slab_c)
                P.wait("gpsimd", "brow_v", bv)
                P.wait("gpsimd", "bcol_d", P.counts.get("bcol_d", 0))
                P.op("gpsimd", lambda e, r=r, bcol=bcol: e.dma_start(
                    out=bcol[:, r:r + 1], in_=row128[:]),
                    sem="bcol_d", by=16)
                bcol_c = P.counts["bcol_d"]
                P.wait("vector", "bcol_d", bcol_c)
                P.wait("scalar", "bcol_d", bcol_c)
                for tt in range(2):
                    bk = tt
                    pool.acquire(bk, "tensor")
                    for ic in range(ND):
                        P.op("tensor", lambda e, r=r, ic=ic, tt=tt, bk=bk, sl=sl, th=th: e.matmul(
                            out=ps_mm[bk][:],
                            lhsT=slab[sl][ic][:],
                            rhs=xnT_l[ic][:, th * 1024 + tt * 512:th * 1024 + (tt + 1) * 512],
                            start=(ic == 0), stop=(ic == ND - 1)),
                            sem="mm1f" if ic == ND - 1 else None)
                mm1_c = P.counts["mm1f"]
                mm1_hist.append(mm1_c)
                for tt in range(2):
                    bk = tt
                    P.wait("scalar", "mm1f", mm1_c)
                    if not quant:
                        ev = P.op("scalar", lambda e, r=r, tt=tt, bk=bk, bcol=bcol: e.activation(
                            out=hT[r][:, tt * 512:(tt + 1) * 512], in_=ps_mm[bk][:],
                            func=AF.Gelu, bias=bcol[:, r:r + 1], scale=1.0), sem="ev1f")
                        pool.release(bk, "ev1f", ev)
                    else:
                        ev = P.op("scalar", lambda e, r=r, tt=tt, bk=bk, bcol=bcol: e.activation(
                            out=hf32[:, tt * 512:(tt + 1) * 512], in_=ps_mm[bk][:],
                            func=AF.Identity, bias=bcol[:, r:r + 1], scale=1.0), sem="ev1f")
                        pool.release(bk, "ev1f", ev)
                if quant:
                    P.wait("vector", "ev1f", P.counts["ev1f"])
                    def pre(tbx):
                        P.op("vector", lambda e: e.tensor_scalar(
                            out=tb_l[:], in0=hf32[:], scalar1=16.0, scalar2=None,
                            op0=ALU.mult))
                    def fin(tax, r=r):
                        P.op("vector", lambda e: e.tensor_scalar(
                            out=ta_l[:], in0=ta_l[:], scalar1=127.0, scalar2=-128.0,
                            op0=ALU.min, op1=ALU.max))
                        return P.op("vector", lambda e, r=r: e.tensor_scalar(
                            out=hT[r][:], in0=ta_l[:], scalar1=0.0, scalar2=1.0 / 16.0,
                            op0=ALU.max, op1=ALU.mult), sem="hq_done")
                    emit_round(P, fin, None, tb_l, ta_l, g_col, pre=pre)
            if th == 0:
                bias_consumed = P.counts["brow_v"]
                P.wait("sync", "brow_v", bias_consumed)
                P.op("sync", lambda e, bias2_h=bias2_h: e.dma_start(
                    out=b2row[:], in_=bias2_h.rearrange("(a n) -> a n", a=1)),
                    sem="bld4", by=16)
                bld4b = P.counts["bld4"]
                P.wait("vector", "bld4", bld4b)
                if quant:
                    def pre(tbx):
                        P.op("vector", lambda e: e.tensor_scalar(
                            out=tb_l[0:1, 0:D], in0=b2row[:], scalar1=64.0,
                            scalar2=None, op0=ALU.mult))
                    def fin(tax):
                        P.op("vector", lambda e: e.tensor_scalar(
                            out=ta_l[0:1, 0:D], in0=ta_l[0:1, 0:D], scalar1=127.0,
                            scalar2=-128.0, op0=ALU.min, op1=ALU.max))
                        return P.op("vector", lambda e: e.tensor_scalar(
                            out=b2row[:], in0=ta_l[0:1, 0:D], scalar1=1.0 / 64.0,
                            scalar2=None, op0=ALU.mult), sem="b1q_done")
                    tbv = type("S", (), {"__getitem__": lambda s_, k_: tb_l[0:1, 0:D]})()
                    tav = type("S", (), {"__getitem__": lambda s_, k_: ta_l[0:1, 0:D]})()
                    emit_round(P, fin, None, tbv, tav, g_col, pre=pre)
                bb = P.op("vector", lambda e: e.tensor_copy(
                    out=brow4b[0:1, 0:D], in_=b2row[:]), sem="brow_v")
                P.wait("tensor", "brow_v", bb)
            P.wait("tensor", "hq_done" if quant else "ev1f",
                   P.counts["hq_done" if quant else "ev1f"])
            w2_hist = []
            rounds = [(0, 3), (3, 3), (6, 2)]
            for rnd, (t0, ntile) in enumerate(rounds):
                for ic in range(NH):
                    sl = ic % 2
                    if len(w2_hist) >= 2:
                        P.wait("sync", "w2mmf", w2_hist[-2])
                    for r2 in range(ND):
                        P.op("sync", lambda e, ic=ic, r2=r2, sl=sl, w2T_s=w2T_s: e.dma_start(
                            out=w2ch[sl][:, r2 * 128:(r2 + 1) * 128],
                            in_=w2T_s[(r2 * NH + ic) * 128:(r2 * NH + ic + 1) * 128, :]),
                            sem=f"w2_ld{sl}", by=16)
                    w2c = P.counts[f"w2_ld{sl}"]
                    P.wait("tensor", f"w2_ld{sl}", w2c)
                    for tt in range(ntile):
                        for o2 in range(2):
                            bk = tt * 2 + o2
                            if ic == 0:
                                pool.acquire(bk, "tensor")
                            P.op("tensor", lambda e, ic=ic, tt=tt, o2=o2, bk=bk, sl=sl, t0=t0: e.matmul(
                                out=ps_mm[bk][:],
                                lhsT=hT[ic][:, (t0 + tt) * 128:(t0 + tt + 1) * 128],
                                rhs=w2ch[sl][:, o2 * 512:(o2 + 1) * 512],
                                start=(ic == 0), stop=False),
                                sem="w2mmf" if (tt == ntile - 1 and o2 == 1) else None)
                    w2_hist.append(P.counts["w2mmf"])
                for tt in range(ntile):
                    for o2 in range(2):
                        bk = tt * 2 + o2
                        P.op("tensor", lambda e, tt=tt, o2=o2, bk=bk: e.matmul(
                            out=ps_mm[bk][:],
                            lhsT=ones_row[0:1, :],
                            rhs=brow4b[0:1, o2 * 512:(o2 + 1) * 512],
                            start=False, stop=True), sem="mm2f")
                mm2_c = P.counts["mm2f"]
                for tt in range(ntile):
                    out_cb(th * 8 + t0 + tt, [tt * 2, tt * 2 + 1], mm2_c)

    P.push_scope()
    xnT_l = [P.sb(f"xnTl{c}", [128, T], BF16) for c in range(ND)]
    hT = [P.sb(f"hTl{r}", [128, T // 2], BF16) for r in range(NH)]
    slab = [[P.sb(f"slab{k}_{ic}", [128, 128], BF16) for ic in range(ND)] for k in range(2)]
    w2ch = [P.sb(f"w2ch{k}", [128, D], BF16) for k in range(2)]
    brow4 = P.sb("brow4", [1, 4 * D], F32)
    brow4b = P.sb("brow4b", [1, D], BF16)
    b2row = P.sb("b2row", [1, D], F32)
    row128 = P.sb("row128", [1, 128], F32)
    tb_l = P.sb("tb_l", [1, 128], F32)
    ta_l = P.sb("ta_l", [1, 128], F32)
    t1f = P.sb("t1fC", [128, D], F32)
    sqs = P.sb("sqsC", [128, D], F32)
    n2c = P.sb("n2cC", [128, 1], F32)
    xt32 = [P.sb(f"xt32{k}", [128, D], F32) for k in range(2)]
    x3t = P.sb("x3t", [128, D], F32)

    P.wait("vector", "stats2_done", P.counts["stats2_done"])
    for i in range(NT):
        P.op("vector", lambda e, i=i: e.tensor_scalar(
            out=hT[i][:], in0=xb[i][:],
            scalar1=stat[:, MU2 + i:MU2 + i + 1], scalar2=stat[:, RSTD2 + i:RSTD2 + i + 1],
            op0=ALU.subtract, op1=ALU.mult), sem="xnC_done")
    P.wait("tensor", "xnC_done", P.counts["xnC_done"])

    def srcC(i, c):
        return hT[i][:, c * 128:(c + 1) * 128]

    xg_C = {0: 0, 1: 0}

    def out_cb_C(t, banks, mm2_c):
        kx = t % 2
        if xg_C[kx]:
            P.wait("sync", "sv", xg_C[kx])
        vx = P.op("sync", lambda e, t=t, kx=kx: e.dma_start(
            out=xt32[kx][:], in_=x2_hbm[t * 128:(t + 1) * 128, :]), sem=f"xld{kx}", by=16)
        P.wait("vector", f"mm2f", mm2_c)
        if dbg and t == 0:
            for o2 in range(2):
                P.op("vector", lambda e, o2=o2, bk=banks[o2]: e.tensor_copy(
                    out=sqs[:, o2 * 512:(o2 + 1) * 512], in_=ps_mm[bk][:]), sem="dbgC")
            P.wait("gpsimd", "dbgC", P.counts["dbgC"])
            P.op("gpsimd", lambda e: e.dma_start(out=g["dcps_h"], in_=sqs[:]),
                 sem="st_dbg", by=16)
            P.wait("vector", "st_dbg", P.counts["st_dbg"])
        for o2 in range(2):
            bk = banks[o2]
            ev = P.op("vector", lambda e, t=t, o2=o2, bk=bk: e.tensor_add(
                out=t1f[:, o2 * 512:(o2 + 1) * 512],
                in0=xb[t][:, o2 * 512:(o2 + 1) * 512],
                in1=ps_mm[bk][:]), sem="t1C")
            pool.release(bk, "t1C", ev)
        if dbg and t == 0:
            P.wait("gpsimd", "t1C", P.counts["t1C"])
            P.op("gpsimd", lambda e: e.dma_start(out=g["dct1_h"], in_=t1f[:]),
                 sem="st_dbg", by=16)
            P.wait("vector", "st_dbg", P.counts["st_dbg"])
        P.op("vector", lambda e: e.tensor_mul(out=sqs[:], in0=t1f[:], in1=t1f[:]))
        nv = P.op("vector", lambda e: e.reduce_sum(out=n2c[:], in_=sqs[:], axis=AXX), sem="n2_qk")
        P.wait("scalar", "n2_qk", nv)
        na = P.op("scalar", lambda e: e.activation(out=n2c[:], in_=n2c[:], func=AF.Sqrt), sem="sq_qk")
        P.wait("vector", "sq_qk", na)
        P.op("vector", lambda e: e.tensor_scalar(
            out=n2c[:], in0=n2c[:], scalar1=1e-12, scalar2=None, op0=ALU.max))
        P.op("vector", lambda e: e.reciprocal(out=n2c[:], in_=n2c[:]))
        P.op("vector", lambda e: e.tensor_scalar(
            out=n2c[:], in0=n2c[:], scalar1=MAXN, scalar2=1.0, op0=ALU.mult, op1=ALU.min))
        if dbg and t == 0:
            P.op("vector", lambda e: e.tensor_copy(out=sqs[:, 0:1], in_=n2c[:]), sem="dbgC")
            P.wait("gpsimd", "dbgC", P.counts["dbgC"])
            P.op("gpsimd", lambda e: e.dma_start(
                out=g["dcn2_h"][:], in_=sqs[:, 0:4]), sem="st_dbg", by=16)
            P.wait("vector", "st_dbg", P.counts["st_dbg"])
        P.wait("vector", f"xld{kx}", vx)
        P.op("vector", lambda e, t=t: e.scalar_tensor_tensor(
            out=x3t[:], in0=t1f[:], scalar=n2c[:, 0:1], in1=xt32[kx][:],
            op0=ALU.mult, op1=ALU.add))
        P.op("vector", lambda e, t=t: e.reduce_sum(
            out=stat[:, SS + t:SS + t + 1], in_=x3t[:], axis=AXX), sem="sv")
        sv_c = P.counts["sv"]
        xg_C[kx] = sv_c
        P.wait("scalar", "sv", sv_c)
        P.op("scalar", lambda e, t=t: e.activation(
            out=junk_bf[:], in_=x3t[:], func=AF.Square,
            accum_out=stat[:, SQ + t:SQ + t + 1]), sem="sa")
        P.wait("gpsimd", "sa", P.counts["sa"])
        P.wait("gpsimd", "st_x3", P.counts.get("st_x3", 0))
        P.op("gpsimd", lambda e, t=t: e.dma_start(
            out=x3_hbm[t * 128:(t + 1) * 128, :], in_=x3t[:]), sem="st_x3", by=16)
        P.wait("vector", "st_x3", P.counts["st_x3"])

    ffn_block("C", g["w1T_s"], g["w2T_s"], g["bf1_h"], g["bf2_h"], bf1col,
              False, srcC, out_cb_C,
              dict(xnT_l=xnT_l, hT=hT, slab=slab, w2ch=w2ch, brow4=brow4,
                   brow4b=brow4b, b2row=b2row, row128=row128, tb_l=tb_l,
                   ta_l=ta_l))
    P.wait("gpsimd", "st_x3", P.counts["st_x3"])
    P.pop_scope()
    P.barrier()

    P.push_scope()
    xnT_ld = [P.sb(f"xnTd{c}", [128, T], BF16) for c in range(ND)]
    hTd = [P.sb(f"hTd{r}", [128, T // 2], BF16) for r in range(NH)]
    slabd = [[P.sb(f"slabd{k}_{ic}", [128, 128], BF16) for ic in range(ND)] for k in range(2)]
    w2chd = [P.sb(f"w2chd{k}", [128, D], BF16) for k in range(2)]
    brow4d = P.sb("brow4d", [1, 4 * D], F32)
    brow4bd = P.sb("brow4bd", [1, D], BF16)
    b2rowd = P.sb("b2rowd", [1, D], F32)
    row128d = P.sb("row128d", [1, 128], F32)
    tb_ld = P.sb("tb_ld", [128, T // 2], F32)
    ta_ld = P.sb("ta_ld", [128, T // 2], F32)
    hf32d = P.sb("hf32d", [128, T // 2], F32)
    xt32d = [P.sb(f"xt32d{k}", [128, D], F32) for k in range(2)]
    outt = P.sb("outt", [128, D], F32)

    P.op("vector", lambda e: e.tensor_scalar(
        out=stat[:, MU0:MU0 + NT], in0=stat[:, SS:SS + NT], scalar1=1.0 / D,
        scalar2=None, op0=ALU.mult))
    P.op("vector", lambda e: e.tensor_mul(
        out=stat[:, RSTD0:RSTD0 + NT], in0=stat[:, MU0:MU0 + NT],
        in1=stat[:, MU0:MU0 + NT]))
    vm = P.op("vector", lambda e: e.scalar_tensor_tensor(
        out=stat[:, RSTD0:RSTD0 + NT], in0=stat[:, SQ:SQ + NT], scalar=1.0 / D,
        in1=stat[:, RSTD0:RSTD0 + NT], op0=ALU.mult, op1=ALU.subtract), sem="sv")
    P.wait("scalar", "sv", vm)
    va = P.op("scalar", lambda e: e.activation(
        out=stat[:, RSTD0:RSTD0 + NT], in_=stat[:, RSTD0:RSTD0 + NT],
        func=AF.Sqrt, bias=EPS_LN, scale=1.0), sem="sa")
    P.wait("vector", "sa", va)
    P.op("vector", lambda e: e.reciprocal(
        out=stat[:, RSTD0:RSTD0 + NT], in_=stat[:, RSTD0:RSTD0 + NT]), sem="stats2_done")
    P.wait("vector", "stats2_done", P.counts["stats2_done"])
    base_xn = P.counts.get("xnC_done", 0)
    for i in range(NT):
        kx = i % 2
        if i >= 2:
            P.wait("sync", "xnC_done", base_xn + i - 1)
        vx = P.op("sync", lambda e, i=i, kx=kx: e.dma_start(
            out=xt32d[kx][:], in_=x3_hbm[i * 128:(i + 1) * 128, :]), sem=f"xld{kx}", by=16)
        P.wait("vector", f"xld{kx}", vx)
        def pre(tbx, i=i, kx=kx):
            P.op("vector", lambda e, i=i, kx=kx, tbx=tbx: e.tensor_scalar(
                out=tbx[:], in0=xt32d[kx][:],
                scalar1=stat[:, MU0 + i:MU0 + i + 1], scalar2=stat[:, RSTD0 + i:RSTD0 + i + 1],
                op0=ALU.subtract, op1=ALU.mult))
            P.op("vector", lambda e, tbx=tbx: e.tensor_scalar(
                out=tbx[:], in0=tbx[:], scalar1=16.0, scalar2=None, op0=ALU.mult))
        def fin(tax, i=i):
            P.op("vector", lambda e, tax=tax: e.tensor_scalar(
                out=tax[:], in0=tax[:], scalar1=127.0, scalar2=-128.0,
                op0=ALU.min, op1=ALU.max))
            return P.op("vector", lambda e, tax=tax, i=i: e.tensor_scalar(
                out=hTd[i][:], in0=tax[:], scalar1=1.0 / 16.0, scalar2=None,
                op0=ALU.mult), sem="xnC_done")
        emit_round(P, fin, None, tb_ld, ta_ld, g_col, pre=pre)
    P.wait("tensor", "xnC_done", P.counts["xnC_done"])

    xg_D = {0: 0, 1: 0}

    def out_cb_D(t, banks, mm2_c):
        kx = t % 2
        if xg_D[kx]:
            P.wait("sync", "t1C", xg_D[kx])
        vx = P.op("sync", lambda e, t=t, kx=kx: e.dma_start(
            out=xt32d[kx][:], in_=x3_hbm[t * 128:(t + 1) * 128, :]), sem=f"xld{kx}", by=16)
        P.wait("vector", f"mm2f", mm2_c)
        P.wait("vector", f"xld{kx}", vx)
        for o2 in range(2):
            bk = banks[o2]
            ev = P.op("vector", lambda e, t=t, o2=o2, bk=bk, kx=kx: e.tensor_add(
                out=outt[:, o2 * 512:(o2 + 1) * 512],
                in0=xt32d[kx][:, o2 * 512:(o2 + 1) * 512],
                in1=ps_mm[bk][:]), sem="t1C")
            pool.release(bk, "t1C", ev)
        xg_D[kx] = P.counts["t1C"]
        P.wait("gpsimd", "t1C", P.counts["t1C"])
        P.wait("gpsimd", "st_x3", P.counts.get("st_x3", 0))
        P.op("gpsimd", lambda e, t=t: e.dma_start(
            out=out_h[t * 128:(t + 1) * 128, :], in_=outt[:]), sem="st_x3", by=16)
        P.wait("vector", "st_x3", P.counts["st_x3"])

    ffn_block("D", g["fw1T_s"], g["fw2T_s"], g["fb1_h"], g["fb2_h"], fb1col,
              True, (lambda i, c, hTd=hTd: hTd[i][:, c * 128:(c + 1) * 128]), out_cb_D,
              dict(xnT_l=xnT_ld, hT=hTd, slab=slabd, w2ch=w2chd, brow4=brow4d,
                   brow4b=brow4bd, b2row=b2rowd, row128=row128d, tb_l=tb_ld,
                   ta_l=ta_ld, hf32=hf32d))
    P.wait("gpsimd", "st_x3", P.counts["st_x3"])
    P.pop_scope()


# version 03
# speedup vs baseline: 1.4681x; 1.4681x over previous
import numpy as np
import ml_dtypes
from contextlib import ExitStack

import concourse.bass as bass
import concourse.mybir as mybir
from concourse.bass_utils import run_bass_kernel_spmd

F32 = mybir.dt.float32
BF16 = mybir.dt.bfloat16
AF = mybir.ActivationFunctionType
ALU = mybir.AluOpType
AXX = mybir.AxisListType.X

T = 2048
D = 1024
NT = T // 128
ND = D // 128
RND_C = float(3 * 2.0 ** 22)
QMAX16 = 32767.0
EPS_LN = 1e-5

ENGINES = ("sync", "vector", "scalar", "tensor", "gpsimd")


class Prog:

    def __init__(self, nc):
        self.nc = nc
        self.q = {e: [] for e in ENGINES}
        self.ctx = ExitStack()
        self.sems = {}
        self.counts = {}
        self.const_vals = [0.0, 1.0]
        self.const_sb = self.nc.alloc_sbuf_tensor("const_sb", [128, 16], F32)

    def want_const(self, *vals):
        for v in vals:
            v = float(v)
            if v not in self.const_vals:
                self.const_vals.append(v)

    def const_ap(self, v):
        v = float(v)
        assert self.const_sb is not None, "const_ap only valid inside thunks (finish)"
        i = self.const_vals.index(v)
        return self.const_sb[:, i:i + 1]

    def sem(self, name):
        if name not in self.sems:
            self.sems[name] = self.ctx.enter_context(self.nc.semaphore(name))
            self.counts[name] = 0
        return self.sems[name]

    SELF_SYNC = ("vector", "scalar")

    def op(self, engine, fn, sem=None, by=1, tick=True):
        do_tick = tick and engine in self.SELF_SYNC
        if do_tick:
            prev = getattr(self, "_last_evt", {}).get(engine)
            if prev is not None:
                self.wait(engine, prev[0], prev[1])
        ret = None
        if sem is not None:
            s = self.sem(sem)
            self.counts[sem] += by
            ret = self.counts[sem]
            self.q[engine].append(lambda e, fn=fn, s=s, by=by: fn(e).then_inc(s, by))
            evt = (sem, ret)
        else:
            tname = f"tick_{engine}"
            if do_tick:
                ts = self.sem(tname)
                self.counts[tname] += 1
                tcur = self.counts[tname]
                self.q[engine].append(lambda e, fn=fn, ts=ts: fn(e).then_inc(ts, 1))
                evt = (tname, tcur)
            else:
                self.q[engine].append(fn)
                evt = None
        if do_tick and evt is not None:
            if not hasattr(self, "_last_evt"):
                self._last_evt = {}
            self._last_evt[engine] = evt
        return ret

    def wait(self, engine, sem, value):
        s = self.sem(sem)
        self.q[engine].append(lambda e, s=s, value=value: e.wait_ge(s, value))

    def barrier(self, engines=ENGINES):
        name = "bar"
        for eng in engines:
            self.op(eng, lambda e: e.nop(), sem=name)
        tot = self.counts[name]
        for eng in engines:
            self.wait(eng, name, tot)

    def wait_cur(self, engine, sem):
        self.wait(engine, sem, self.counts[sem])

    def sb(self, name, shape, dtype=F32, **kw):
        if getattr(self, "_scopes", None):
            return self._scopes[-1].enter_context(
                self.nc.sbuf_tensor(name, shape, dtype, **kw))
        return self.nc.alloc_sbuf_tensor(name, shape, dtype, **kw)

    def push_scope(self):
        if not hasattr(self, "_scopes"):
            self._scopes = []
        self._scopes.append(ExitStack())

    def pop_scope(self):
        self._scopes.pop().close()

    def psum(self, name, shape, dtype=F32):
        return self.nc.alloc_psum_tensor(name, shape, dtype)

    def finish(self):
        assert len(self.const_vals) <= 16
        vals = np.zeros(16, dtype=np.float32)
        vals[:len(self.const_vals)] = self.const_vals
        ctab = np.broadcast_to(vals, (128, 16)).copy()
        ctab_h = self.nc.inline_tensor(ctab, "const_tab").ap()
        for i, v in enumerate(self.const_vals):
            self.nc.const_aps.register(F32, float(v), self.const_sb[:, i:i + 1])
        s = self.sem("ld_const")
        self.q["sync"].insert(0, lambda e, s=s, h=ctab_h, d=self.const_sb: e.dma_start(
            out=d[:], in_=h).then_inc(s, 16))
        for eng in ENGINES:
            if self.q[eng] and eng != "sync":
                self.q[eng].insert(0, lambda e, s=s: e.wait_ge(s, 16))
        blk = self.ctx.enter_context(self.nc.Block())
        for eng in ENGINES:
            thunks = self.q[eng]
            if not thunks:
                continue
            getattr(blk, eng)(lambda e, th=thunks: [f(e) for f in th])
        self.ctx.close()
        return self.nc


class PsumPool:

    def __init__(self, P, banks):
        self.P = P
        self.banks = banks
        self.free_at = [None] * len(banks)

    def acquire(self, idx, engine):
        fa = self.free_at[idx]
        if fa is not None:
            self.P.wait(engine, fa[0], fa[1])
            self.free_at[idx] = None
        return self.banks[idx]

    def release(self, idx, sem, count):
        self.free_at[idx] = (sem, count)


def new_nc():
    return bass.Bass("TRN2", target_bir_lowering=False, detect_race_conditions=False)


def load_colvec(P, src_ap, dst_col, n_chunks, ld_sem, queue="scalar"):
    src = src_ap
    if len(src.shape) == 1:
        src = src.rearrange("(a n) -> a n", a=1)
    total = None
    for c in range(n_chunks):
        total = P.op(queue,
                     lambda e, c=c, src=src: e.dma_start(
                         out=dst_col[:, c:c + 1],
                         in_=src[0:1, c * 128:(c + 1) * 128].rearrange("a p -> p a")),
                     sem=ld_sem, by=16, tick=False)
    return total


def emit_ln_stats(P, x_tiles, mu, rstd, junk_bf, sumsq, ssum, vs, as_, eps=EPS_LN):
    for i in range(NT):
        P.op("vector", lambda e, i=i: e.reduce_sum(
            out=ssum[:, i:i + 1], in_=x_tiles[i][:], axis=AXX), sem=vs)
        P.op("scalar", lambda e, i=i: e.activation(
            out=junk_bf[:], in_=x_tiles[i][:], func=AF.Square,
            accum_out=sumsq[:, i:i + 1]), sem=as_)
    P.wait_cur("vector", as_)
    v = P.op("vector", lambda e: e.tensor_scalar(
        out=mu[:], in0=ssum[:], scalar1=1.0 / D, scalar2=None, op0=ALU.mult), sem=vs)
    P.wait("vector", vs, v)
    v = P.op("vector", lambda e: e.tensor_mul(out=rstd[:], in0=mu[:], in1=mu[:]), sem=vs)
    P.wait("vector", vs, v)
    v_mid = P.op("vector", lambda e: e.scalar_tensor_tensor(
        out=rstd[:], in0=sumsq[:], scalar=1.0 / D, in1=rstd[:],
        op0=ALU.mult, op1=ALU.subtract), sem=vs)
    P.wait("vector", vs, v_mid)
    P.wait("scalar", vs, v_mid)
    a_sqrt = P.op("scalar", lambda e: e.activation(
        out=rstd[:], in_=rstd[:], func=AF.Sqrt, bias=float(eps), scale=1.0), sem=as_)
    P.wait("vector", as_, a_sqrt)
    done = P.op("vector", lambda e: e.reciprocal(out=rstd[:], in_=rstd[:]), sem=vs)
    return done


def emit_partition_absmax(P, col_ap, row_sb, out_sb, vs, dsem, tag):
    d = P.op("gpsimd", lambda e: e.dma_start(out=row_sb[:], in_=col_ap), sem=dsem, by=16)
    P.wait("vector", dsem, d)
    return P.op("vector", lambda e: e.tensor_reduce(
        out=out_sb[:], in_=row_sb[:], axis=AXX, op=ALU.max), sem=vs)



def build_launch1(dbg=False, bars=()):
    nc = new_nc()
    P = Prog(nc)
    P.want_const(EPS_LN)
    ident_h = nc.inline_tensor(np.eye(128, dtype=ml_dtypes.bfloat16), "ident_bf16").ap()

    x_h = nc.dram_tensor("x", [T, D], F32, kind="ExternalInput").ap()
    win_h = nc.dram_tensor("W_in", [D, 2 * D], F32, kind="ExternalInput").ap()
    bin_h = nc.dram_tensor("b_in", [2 * D], F32, kind="ExternalInput").ap()
    a_h = nc.dram_tensor("A", [D], F32, kind="ExternalInput").ap()
    c_h = nc.dram_tensor("C", [D], F32, kind="ExternalInput").ap()
    wout_h = nc.dram_tensor("W_out", [D, D], F32, kind="ExternalInput").ap()
    bout_h = nc.dram_tensor("b_out", [D], F32, kind="ExternalInput").ap()

    x1_h = nc.dram_tensor("x1", [T, D], F32, kind="ExternalOutput").ap()
    if dbg:
        d_xn0_h = nc.dram_tensor("d_xn0", [128, T], BF16, kind="ExternalOutput").ap()
        d_x0_h = nc.dram_tensor("d_x0", [128, D], F32, kind="ExternalOutput").ap()
        d_st_h = nc.dram_tensor("d_st", [128, 4 * NT], F32, kind="ExternalOutput").ap()
        d_yall_h = [nc.dram_tensor(f"d_yall{c}", [128, T], BF16, kind="ExternalOutput").ap()
                    for c in range(ND)]
        d_xnT_h = nc.dram_tensor("d_xnT", [128, T], BF16, kind="ExternalOutput").ap()
        d_v0_h = nc.dram_tensor("d_v0", [128, T], BF16, kind="ExternalOutput").ap()
        d_h0_h = nc.dram_tensor("d_h0", [128, T], BF16, kind="ExternalOutput").ap()
        d_g0_h = nc.dram_tensor("d_g0", [128, T], BF16, kind="ExternalOutput").ap()
        d_y0_h = nc.dram_tensor("d_y0", [128, T], BF16, kind="ExternalOutput").ap()
    xn1_h = nc.dram_tensor("xn1", [T, D], BF16, kind="ExternalOutput").ap()
    m1_h = nc.dram_tensor("m1", [1, 1], F32, kind="ExternalOutput").ap()

    x_sb = [P.sb(f"xt{i}", [128, D], F32) for i in range(NT)]
    xnT = [P.sb(f"xnT{c}", [128, T], BF16) for c in range(ND)]
    big8 = [P.sb(f"big8_{c}", [128, T], BF16) for c in range(ND)]
    vtmp = [P.sb(f"vtmp{k}", [128, T], BF16) for k in range(2)]
    htmp = [P.sb(f"htmp{k}", [128, T], BF16) for k in range(2)]
    gtmp = [P.sb(f"gtmp{k}", [128, T], BF16) for k in range(2)]
    wstage = [P.sb(f"wstage{k}", [128, D], F32) for k in range(2)]
    wbf = [P.sb(f"wbf{c}", [128, 2 * D], BF16) for c in range(ND)]
    ident_sb = P.sb("ident_sb", [128, 128], BF16)
    bin_col = P.sb("bin_col", [128, 2 * ND], F32)
    dec_col = P.sb("dec_col", [128, ND], F32)
    c_col = P.sb("c_col", [128, ND], F32)
    bout_row = P.sb("bout_row", [1, D], F32)
    bout_row_bf = P.sb("bout_row_bf", [1, D], BF16)
    ones_row_bf = P.sb("ones_row_bf", [1, 128], BF16)
    stat_mu = P.sb("stat_mu", [128, NT], F32)
    stat_rstd = P.sb("stat_rstd", [128, NT], F32)
    stat_ssum = P.sb("stat_ssum", [128, NT], F32)
    stat_sumsq = P.sb("stat_sumsq", [128, NT], F32)
    junk_bf = P.sb("junk_bf", [128, D], BF16)
    amax_cols = P.sb("amax_cols", [128, NT], F32)
    amax_col = P.sb("amax_col", [128, 1], F32)
    amax_row = P.sb("amax_row", [1, 128], F32)
    m1_sb = P.sb("m1_sb", [1, 1], F32)

    ps_tr = [P.psum(f"ps_tr{k}", [128, 512], BF16) for k in range(2)]
    ps_mm = [P.psum(f"ps_mm{k}", [128, 512], F32) for k in range(6)]
    pool = PsumPool(P, ps_mm)

    for i in range(NT):
        P.op("sync", lambda e, i=i: e.dma_start(
            out=x_sb[i][:], in_=x_h[i * 128:(i + 1) * 128, :]), sem="ld_x", by=16)
    ld_x_total = P.counts["ld_x"]
    P.op("sync", lambda e: e.dma_start(out=ident_sb[:], in_=ident_h), sem="ld_misc", by=16)
    P.op("sync", lambda e: e.dma_start(
        out=bout_row[:], in_=bout_h.rearrange("(a n) -> a n", a=1)), sem="ld_misc", by=16)
    ld_misc_total = P.counts["ld_misc"]
    load_colvec(P, bin_h, bin_col, 2 * ND, "ld_vec")
    load_colvec(P, a_h, dec_col, ND, "ld_vec")
    ld_vec_total = load_colvec(P, c_h, c_col, ND, "ld_vec")
    P.wait("scalar", "ld_vec", ld_vec_total)
    dec_done = P.op("scalar", lambda e: e.activation(
        out=dec_col[:], in_=dec_col[:], func=AF.Sigmoid), sem="a_misc")
    P.op("vector", lambda e: e.memset(ones_row_bf[:], 1.0), sem="v_misc")
    P.wait("vector", "ld_misc", ld_misc_total)
    misc_done = P.op("vector", lambda e: e.tensor_copy(
        out=bout_row_bf[:], in_=bout_row[:]), sem="v_misc")

    nw = 0
    for ic in range(ND):
        for h in range(2):
            k = nw % 2
            if nw >= 2:
                P.wait("sync", "wcast1", nw - 1)
            v = P.op("sync", lambda e, ic=ic, h=h, k=k: e.dma_start(
                out=wstage[k][:], in_=win_h[ic * 128:(ic + 1) * 128, h * D:(h + 1) * D]),
                sem=f"wld1_{k}", by=16)
            P.wait("scalar", f"wld1_{k}", v)
            P.op("scalar", lambda e, ic=ic, h=h, k=k: e.copy(
                out=wbf[ic][:, h * D:(h + 1) * D], in_=wstage[k][:]), sem="wcast1")
            nw += 1
    wcast1_total = P.counts["wcast1"]

    if 0 in bars:
        P.barrier()
    P.wait("vector", "ld_x", ld_x_total)
    P.wait("scalar", "ld_x", ld_x_total)
    ln0_done = emit_ln_stats(P, x_sb, stat_mu, stat_rstd, junk_bf,
                             stat_sumsq, stat_ssum, "v1", "a1")
    P.wait("vector", "v1", ln0_done)
    if dbg:
        P.wait("gpsimd", "v1", ln0_done)
        P.op("gpsimd", lambda e: e.dma_start(out=d_st_h[:, 0:NT], in_=stat_mu[:]), sem="st_dbg", by=16)
        P.op("gpsimd", lambda e: e.dma_start(out=d_st_h[:, NT:2*NT], in_=stat_rstd[:]), sem="st_dbg", by=16)
        P.op("gpsimd", lambda e: e.dma_start(out=d_st_h[:, 2*NT:3*NT], in_=stat_ssum[:]), sem="st_dbg", by=16)
        P.op("gpsimd", lambda e: e.dma_start(out=d_st_h[:, 3*NT:4*NT], in_=stat_sumsq[:]), sem="st_dbg", by=16)
    for i in range(NT):
        dst = big8[i // 2]
        off = (i % 2) * D
        P.op("vector", lambda e, i=i, dst=dst, off=off: e.tensor_scalar(
            out=dst[:, off:off + D], in0=x_sb[i][:],
            scalar1=stat_mu[:, i:i + 1], scalar2=stat_rstd[:, i:i + 1],
            op0=ALU.subtract, op1=ALU.mult), sem="xn0_done")
    if dbg:
        P.wait("gpsimd", "ld_x", ld_x_total)
        P.op("gpsimd", lambda e: e.dma_start(out=d_x0_h, in_=x_sb[0][:]), sem="st_dbg", by=16)
        P.wait("gpsimd", "xn0_done", P.counts["xn0_done"])
        P.op("gpsimd", lambda e: e.dma_start(out=d_xn0_h, in_=big8[0][:]), sem="st_dbg", by=16)
    P.wait("tensor", "ld_misc", ld_misc_total)
    ntr = 0
    for i in range(NT):
        P.wait("tensor", "xn0_done", i + 1)
        src = big8[i // 2]
        soff = (i % 2) * D
        for c in range(ND):
            k = ntr % 2
            if ntr >= 2:
                P.wait("tensor", "trev1", ntr - 1)
            P.op("tensor", lambda e, i=i, c=c, k=k, src=src, soff=soff: e.transpose(
                out=ps_tr[k][:, 0:128], in_=src[:, soff + c * 128:soff + (c + 1) * 128],
                identity=ident_sb[:]), sem="tr1")
            P.wait("scalar", "tr1", P.counts["tr1"])
            P.op("scalar", lambda e, i=i, c=c, k=k: e.copy(
                out=xnT[c][:, i * 128:(i + 1) * 128], in_=ps_tr[k][:, 0:128]), sem="trev1")
            ntr += 1
    trev1_total = P.counts["trev1"]

    if 1 in bars:
        P.barrier()
    if dbg:
        P.wait("gpsimd", "trev1", trev1_total)
        P.op("gpsimd", lambda e: e.dma_start(out=d_xnT_h, in_=xnT[0][:]), sem="st_dbg", by=16)
    P.wait("tensor", "trev1", trev1_total)
    P.wait("scalar", "ld_vec", ld_vec_total)
    P.wait("vector", "ld_vec", ld_vec_total)
    scan_counts = []
    y_counts = []
    for s in range(16):
        c = s // 2
        is_gate = (s % 2 == 1)
        o = c + ND if is_gate else c
        banks = [0, 1, 2, 3] if s % 2 == 0 else [4, 5, 0, 1]
        dstslot = gtmp[c % 2] if is_gate else vtmp[c % 2]
        if c >= 2:
            if is_gate:
                P.wait("scalar", "y1", y_counts[c - 2])
            else:
                P.wait("scalar", "scan1", scan_counts[c - 2])
        for ic in range(ND):
            P.wait("tensor", "wcast1", 2 * (ic + 1))
            for t in range(4):
                bk = banks[t]
                if ic == 0:
                    pool.acquire(bk, "tensor")
                last = (ic == ND - 1)
                P.op("tensor", lambda e, o=o, ic=ic, t=t, bk=bk, last=last: e.matmul(
                    out=ps_mm[bk][:],
                    lhsT=wbf[ic][:, o * 128:(o + 1) * 128],
                    rhs=xnT[ic][:, t * 512:(t + 1) * 512],
                    start=(ic == 0), stop=last),
                    sem="mm1" if last else None)
        mm_after = P.counts["mm1"]
        for t in range(4):
            bk = banks[t]
            P.wait("scalar", "mm1", mm_after)
            if is_gate:
                ev = P.op("scalar", lambda e, c=c, t=t, bk=bk, dstslot=dstslot: e.activation(
                    out=dstslot[:, t * 512:(t + 1) * 512], in_=ps_mm[bk][:],
                    func=AF.Sigmoid, bias=bin_col[:, (c + ND):(c + ND) + 1], scale=1.0),
                    sem="ev1")
            else:
                ev = P.op("scalar", lambda e, c=c, t=t, bk=bk, dstslot=dstslot: e.activation(
                    out=dstslot[:, t * 512:(t + 1) * 512], in_=ps_mm[bk][:],
                    func=AF.Identity, bias=bin_col[:, c:c + 1], scale=1.0), sem="ev1")
            pool.release(bk, "ev1", ev)
        if not is_gate:
            P.wait("vector", "ev1", P.counts["ev1"])
            P.wait("vector", "a_misc", dec_done)
            sc = P.op("vector", lambda e, c=c: e.tensor_tensor_scan(
                out=htmp[c % 2][:], data0=dec_col[:, c:c + 1].broadcast_to((128, T)),
                data1=vtmp[c % 2][:], initial=0.0, op0=ALU.mult, op1=ALU.add),
                sem="scan1")
            scan_counts.append(sc)
        else:
            P.wait("vector", "ev1", P.counts["ev1"])
            P.wait("vector", "scan1", scan_counts[c])
            yc = P.op("vector", lambda e, c=c: e.scalar_tensor_tensor(
                out=big8[c][:], in0=htmp[c % 2][:], scalar=c_col[:, c:c + 1],
                in1=gtmp[c % 2][:], op0=ALU.mult, op1=ALU.mult), sem="y1")
            y_counts.append(yc)
            if dbg and c == 0:
                P.wait("gpsimd", "y1", yc)
                P.op("gpsimd", lambda e: e.dma_start(out=d_v0_h, in_=vtmp[0][:]), sem="st_dbg", by=16)
                P.op("gpsimd", lambda e: e.dma_start(out=d_h0_h, in_=htmp[0][:]), sem="st_dbg", by=16)
                P.op("gpsimd", lambda e: e.dma_start(out=d_g0_h, in_=gtmp[0][:]), sem="st_dbg", by=16)
                P.op("gpsimd", lambda e: e.dma_start(out=d_y0_h, in_=big8[0][:]), sem="st_dbg", by=16)
    y1_total = P.counts["y1"]

    if dbg:
        P.wait("gpsimd", "y1", y1_total)
        for c in range(ND):
            P.op("gpsimd", lambda e, c=c: e.dma_start(out=d_yall_h[c], in_=big8[c][:]),
                 sem="st_dbg", by=16)

    mm1_total = P.counts["mm1"]
    P.wait("sync", "mm1", mm1_total)
    P.wait("scalar", "mm1", mm1_total)
    for ic in range(ND):
        k = ic % 2
        if ic >= 2:
            P.wait("sync", "wcast2", ic - 1)
        v = P.op("sync", lambda e, ic=ic, k=k: e.dma_start(
            out=wstage[k][:], in_=wout_h[ic * 128:(ic + 1) * 128, :]),
            sem=f"wld2_{k}", by=16)
        P.wait("scalar", f"wld2_{k}", v)
        P.op("scalar", lambda e, ic=ic, k=k: e.copy(
            out=wbf[ic][:, 0:D], in_=wstage[k][:]), sem="wcast2")

    P.wait("tensor", "y1", y1_total)
    P.wait("tensor", "v_misc", misc_done)
    for t in range(NT):
        banks = [0, 1] if t % 2 == 0 else [2, 3]
        for ic in range(ND):
            P.wait("tensor", "wcast2", ic + 1)
            for o2 in range(2):
                bk = banks[o2]
                if ic == 0:
                    pool.acquire(bk, "tensor")
                P.op("tensor", lambda e, t=t, ic=ic, o2=o2, bk=bk: e.matmul(
                    out=ps_mm[bk][:],
                    lhsT=big8[ic][:, t * 128:(t + 1) * 128],
                    rhs=wbf[ic][:, o2 * 512:(o2 + 1) * 512],
                    start=(ic == 0), stop=False))
        for o2 in range(2):
            bk = banks[o2]
            P.op("tensor", lambda e, t=t, o2=o2, bk=bk: e.matmul(
                out=ps_mm[bk][:],
                lhsT=ones_row_bf[0:1, :],
                rhs=bout_row_bf[0:1, o2 * 512:(o2 + 1) * 512],
                start=False, stop=True), sem="mm2")
        mm2_after = P.counts["mm2"]
        for o2 in range(2):
            bk = banks[o2]
            P.wait("vector", "mm2", mm2_after)
            ev = P.op("vector", lambda e, t=t, o2=o2, bk=bk: e.tensor_add(
                out=x_sb[t][:, o2 * 512:(o2 + 1) * 512],
                in0=x_sb[t][:, o2 * 512:(o2 + 1) * 512],
                in1=ps_mm[bk][:]), sem="ev2")
            pool.release(bk, "ev2", ev)
        P.wait("gpsimd", "ev2", P.counts["ev2"])
        P.op("gpsimd", lambda e, t=t: e.dma_start(
            out=x1_h[t * 128:(t + 1) * 128, :], in_=x_sb[t][:]), sem="st_x1", by=16)

    if 3 in bars:
        P.barrier()
    ev2_total = P.counts["ev2"]
    P.wait("vector", "ev2", ev2_total)
    P.wait("scalar", "ev2", ev2_total)
    ln1_done = emit_ln_stats(P, x_sb, stat_mu, stat_rstd, junk_bf,
                             stat_sumsq, stat_ssum, "v2", "a2")
    P.wait("vector", "v2", ln1_done)
    for i in range(NT):
        dst = xnT[i // 2]
        off = (i % 2) * D
        P.op("vector", lambda e, i=i, dst=dst, off=off: e.tensor_scalar(
            out=dst[:, off:off + D], in0=x_sb[i][:],
            scalar1=stat_mu[:, i:i + 1], scalar2=stat_rstd[:, i:i + 1],
            op0=ALU.subtract, op1=ALU.mult), sem="xn1_done")
        P.op("vector", lambda e, i=i, dst=dst, off=off: e.tensor_reduce(
            out=amax_cols[:, i:i + 1], in_=dst[:, off:off + D], axis=AXX,
            op=ALU.max, apply_absolute_value=True), sem="xn1_done")
        P.wait("gpsimd", "xn1_done", 2 * (i + 1))
        P.op("gpsimd", lambda e, i=i, dst=dst, off=off: e.dma_start(
            out=xn1_h[i * 128:(i + 1) * 128, :], in_=dst[:, off:off + D]),
            sem="st_xn1", by=16)
    am = P.op("vector", lambda e: e.tensor_reduce(
        out=amax_col[:], in_=amax_cols[:], axis=AXX, op=ALU.max), sem="amax_v")
    P.wait("gpsimd", "amax_v", am)
    m_fin = emit_partition_absmax(P, amax_col[:], amax_row, m1_sb, "amax_v", "amax_d", "m1")
    P.wait("gpsimd", "amax_v", m_fin)
    P.op("gpsimd", lambda e: e.dma_start(out=m1_h[:], in_=m1_sb[:]), sem="st_m1", by=16)

    if dbg:
        P.wait("gpsimd", "st_dbg", P.counts["st_dbg"])
    P.wait("gpsimd", "st_x1", P.counts["st_x1"])
    P.wait("gpsimd", "st_xn1", P.counts["st_xn1"])
    P.wait("gpsimd", "st_m1", P.counts["st_m1"])
    return P.finish()


def kernel(**inputs):
    B = 8
    cores = list(range(B))

    def contig(a):
        return np.ascontiguousarray(np.asarray(a))

    x = contig(inputs["L0_attn_x"])

    nc1 = build_launch1()
    wk1 = {k: contig(inputs["L0_attn_" + k])
           for k in ("W_in", "b_in", "A", "C", "W_out", "b_out")}
    r1 = run_bass_kernel_spmd(
        nc1, [dict(wk1, x=contig(x[b])) for b in range(B)], core_ids=cores)
    x1 = [contig(r1.results[b]["x1"]) for b in range(B)]
    xn1 = [contig(r1.results[b]["xn1"]) for b in range(B)]
    m1_all = contig(np.concatenate(
        [np.asarray(r1.results[b]["m1"]).reshape(1, 1) for b in range(B)], axis=1))

    nc2 = build_launch2()
    wk2 = {"w_up": contig(inputs["L0_ffn_w_up"]),
           "w_gate": contig(inputs["L0_ffn_w_gate"]), "m1_all": m1_all}
    r2 = run_bass_kernel_spmd(
        nc2, [dict(wk2, xn1=xn1[b]) for b in range(B)], core_ids=cores)
    up_raw = [contig(r2.results[b]["up_raw"]) for b in range(B)]
    gate_raw = [contig(r2.results[b]["gate_raw"]) for b in range(B)]
    mup_all = contig(np.concatenate(
        [np.asarray(r2.results[b]["m_up"]).reshape(1, 1) for b in range(B)], axis=1))
    mg_all = contig(np.concatenate(
        [np.asarray(r2.results[b]["m_g"]).reshape(1, 1) for b in range(B)], axis=1))

    nc3 = build_launch3()
    r3 = run_bass_kernel_spmd(
        nc3, [{"gate_raw": gate_raw[b], "m_g_all": mg_all} for b in range(B)],
        core_ids=cores)
    mg2_all = contig(np.concatenate(
        [np.asarray(r3.results[b]["m_g2"]).reshape(1, 1) for b in range(B)], axis=1))

    nc4 = build_launch4()
    r4 = run_bass_kernel_spmd(
        nc4, [{"gate_raw": gate_raw[b], "up_raw": up_raw[b], "m_g_all": mg_all,
               "m_g2_all": mg2_all, "m_up_all": mup_all} for b in range(B)],
        core_ids=cores)
    t_q = [contig(r4.results[b]["t_q"]) for b in range(B)]
    mo_all = contig(np.concatenate(
        [np.asarray(r4.results[b]["m_o"]).reshape(1, 1) for b in range(B)], axis=1))

    nc5 = build_launch5_full()
    wk5 = {"w_down": contig(inputs["L0_ffn_w_down"]), "m_o_all": mo_all}
    for k in ("wq", "bq", "wk", "bk", "wv", "bv", "wo", "bo",
              "g1", "b1", "g2", "b2", "w1", "bf1", "w2", "bf2"):
        wk5[k] = contig(inputs["L1_attn_" + k])
    for src, dst in (("w1", "f_w1"), ("b1", "f_b1"), ("w2", "f_w2"), ("b2", "f_b2")):
        wk5[dst] = contig(inputs["L1_ffn_" + src])
    r5 = run_bass_kernel_spmd(
        nc5, [dict(wk5, x1=x1[b], t_q=t_q[b]) for b in range(B)], core_ids=cores)
    return np.stack([np.asarray(r5.results[b]["out"], dtype=np.float32)
                     for b in range(B)], axis=0)


C32 = float(3 * 2.0 ** 22)
C64 = float(3 * 2.0 ** 51)
G_A = float(2.0 ** -75)
G_B = float(2.0 ** 75)


def emit_g_gadget(P, g_col):
    P.op("vector", lambda e: e.memset(g_col[:], G_A))
    P.op("vector", lambda e: e.tensor_scalar(
        out=g_col[:], in0=g_col[:], scalar1=G_A, scalar2=G_B,
        op0=ALU.mult, op1=ALU.mult))
    return P.op("vector", lambda e: e.tensor_scalar(
        out=g_col[:], in0=g_col[:], scalar1=G_B, scalar2=None,
        op0=ALU.mult), sem="g_done")


def emit_round(P, dst_fn, src_ap, tb, ta, g_col, pre=None):
    if pre is not None:
        pre(tb)
        tsrc = tb[:]
    else:
        tsrc = src_ap
    P.op("vector", lambda e, tsrc=tsrc: e.tensor_scalar(
        out=tb[:], in0=tsrc, scalar1=C32, scalar2=C32,
        op0=ALU.add, op1=ALU.subtract))
    P.op("vector", lambda e: e.tensor_scalar(
        out=ta[:], in0=tb[:], scalar1=C64, scalar2=C64,
        op0=ALU.add, op1=ALU.subtract))
    P.op("vector", lambda e: e.tensor_sub(out=ta[:], in0=ta[:], in1=tb[:]))
    npart = ta[:].shape[0]
    P.op("vector", lambda e, npart=npart: e.scalar_tensor_tensor(
        out=ta[:], in0=ta[:], scalar=g_col[0:npart, 0:1], in1=tb[:],
        op0=ALU.mult, op1=ALU.add))
    return dst_fn(ta)



def build_launch2():
    nc = new_nc()
    P = Prog(nc)
    ident_h = nc.inline_tensor(np.eye(128, dtype=ml_dtypes.bfloat16), "ident_bf16").ap()

    xn1_h = nc.dram_tensor("xn1", [T, D], BF16, kind="ExternalInput").ap()
    m1a_h = nc.dram_tensor("m1_all", [1, 8], F32, kind="ExternalInput").ap()
    wup_h = nc.dram_tensor("w_up", [3 * D, D], F32, kind="ExternalInput").ap()
    wgate_h = nc.dram_tensor("w_gate", [3 * D, D], F32, kind="ExternalInput").ap()

    up_h = nc.dram_tensor("up_raw", [3 * D, T], BF16, kind="ExternalOutput").ap()
    gate_h = nc.dram_tensor("gate_raw", [3 * D, T], BF16, kind="ExternalOutput").ap()
    mup_h = nc.dram_tensor("m_up", [1, 1], F32, kind="ExternalOutput").ap()
    mg_h = nc.dram_tensor("m_g", [1, 1], F32, kind="ExternalOutput").ap()

    NO = 24

    xn1 = [P.sb(f"xn1_{i}", [128, D], BF16) for i in range(NT)]
    xqT = [P.sb(f"xqT{c}", [128, T], BF16) for c in range(ND)]
    xq_tm = [P.sb(f"xq{i}", [128, D], BF16) for i in range(NT)]
    tmp1 = P.sb("tmp1", [128, D], F32)
    tmp2 = P.sb("tmp2", [128, D], F32)
    wstage = [P.sb(f"wstage{k}", [128, D], F32) for k in range(2)]
    wq_nat = [P.sb(f"wq_nat{k}", [128, D], BF16) for k in range(2)]
    wT_sl = [P.sb(f"wT_{k}", [128, D], BF16) for k in range(2)]
    strip = [P.sb(f"strip{k}", [128, T], BF16) for k in range(2)]
    ident_sb = P.sb("ident_sb", [128, 128], BF16)
    g_col = P.sb("g_col", [128, 1], F32)
    mrow = P.sb("mrow", [128, 8], F32)
    s_col = P.sb("s_col", [128, 1], F32)
    invs_col = P.sb("invs_col", [128, 1], F32)
    bmax = P.sb("bmax", [128, 32], F32)
    sblk = P.sb("sblk", [128, 32], F32)
    invblk = P.sb("invblk", [128, 32], F32)
    amax_acc = P.sb("amax_acc", [128, 2], F32)
    amax_row = P.sb("amax_row", [1, 128], F32)
    m_sb = P.sb("m_sb", [1, 2], F32)

    ps_tr = [P.psum(f"ps_tr{k}", [128, 512], BF16) for k in range(2)]
    ps_mm = [P.psum(f"ps_mm{k}", [128, 512], F32) for k in range(6)]
    pool = PsumPool(P, ps_mm)

    for i in range(NT):
        P.op("sync", lambda e, i=i: e.dma_start(
            out=xn1[i][:], in_=xn1_h[i * 128:(i + 1) * 128, :]), sem="ld_x", by=16)
    ld_x_total = P.counts["ld_x"]
    P.op("sync", lambda e: e.dma_start(out=ident_sb[:], in_=ident_h), sem="ld_misc", by=16)
    P.op("sync", lambda e: e.dma_start(
        out=mrow[:], in_=m1a_h.partition_broadcast(128)), sem="ld_misc", by=16)
    ld_misc_total = P.counts["ld_misc"]

    emit_g_gadget(P, g_col)
    P.wait("vector", "ld_misc", ld_misc_total)
    P.op("vector", lambda e: e.tensor_reduce(
        out=s_col[:], in_=mrow[:], axis=AXX, op=ALU.max))
    P.op("vector", lambda e: e.tensor_scalar(
        out=s_col[:], in0=s_col[:], scalar1=1.0 / QMAX16, scalar2=1e-12,
        op0=ALU.mult, op1=ALU.max))
    sc_done = P.op("vector", lambda e: e.reciprocal(
        out=invs_col[:], in_=s_col[:]), sem="sc_done")

    P.wait("vector", "ld_x", ld_x_total)
    for i in range(NT):
        def pre(tb, i=i):
            P.op("vector", lambda e, i=i, tb=tb: e.tensor_scalar(
                out=tb[:], in0=xn1[i][:], scalar1=invs_col[:, 0:1], scalar2=None,
                op0=ALU.mult))
        def fin(ta, i=i):
            return P.op("vector", lambda e, i=i, ta=ta: e.tensor_scalar(
                out=xq_tm[i][:], in0=ta[:], scalar1=s_col[:, 0:1], scalar2=None,
                op0=ALU.mult), sem="xq_done")
        emit_round(P, fin, None, tmp1, tmp2, g_col, pre=pre)
    P.wait("tensor", "ld_misc", ld_misc_total)
    ntr = 0
    for i in range(NT):
        P.wait("tensor", "xq_done", i + 1)
        for c in range(ND):
            k = ntr % 2
            if ntr >= 2:
                P.wait("tensor", "trev", ntr - 1)
            P.op("tensor", lambda e, i=i, c=c, k=k: e.transpose(
                out=ps_tr[k][:, 0:128], in_=xq_tm[i][:, c * 128:(c + 1) * 128],
                identity=ident_sb[:]), sem="tr")
            P.wait("scalar", "tr", P.counts["tr"])
            P.op("scalar", lambda e, i=i, c=c, k=k: e.copy(
                out=xqT[c][:, i * 128:(i + 1) * 128], in_=ps_tr[k][:, 0:128]), sem="trev")
            ntr += 1
    trev_total = P.counts["trev"]
    P.op("vector", lambda e: e.memset(amax_acc[:], 0.0), sem="amax_init")

    P.wait("tensor", "trev", trev_total)
    wtr_cp = {0: 0, 1: 0}
    wtr_hist = []
    for wi, (w_h, out_h, acol) in enumerate(((wup_h, up_h, 0), (wgate_h, gate_h, 1))):
        for r in range(NO):
            k = r % 2
            nw = wi * NO + r
            if nw >= 2:
                P.wait("sync", "wdq", nw - 1)
            v = P.op("sync", lambda e, w_h=w_h, r=r, k=k: e.dma_start(
                out=wstage[k][:], in_=w_h[r * 128:(r + 1) * 128, :]),
                sem=f"wld_{k}", by=16)
            P.wait("vector", f"wld_{k}", v)
            if nw >= 2 and wtr_hist[nw - 2]:
                P.wait("vector", "wtr_pe", wtr_hist[nw - 2])
            P.op("vector", lambda e, k=k: e.tensor_reduce(
                out=bmax[:], in_=wstage[k][:].rearrange("p (b i) -> p b i", i=32),
                axis=AXX, op=ALU.max, apply_absolute_value=True))
            P.op("vector", lambda e: e.tensor_scalar(
                out=sblk[:], in0=bmax[:], scalar1=1.0 / 7.0, scalar2=1e-12,
                op0=ALU.mult, op1=ALU.max))
            P.op("vector", lambda e: e.reciprocal(out=invblk[:], in_=sblk[:]))
            def pre(tb, k=k):
                P.op("vector", lambda e, k=k, tb=tb: e.tensor_tensor(
                    out=tb[:].rearrange("p (b i) -> p b i", i=32),
                    in0=wstage[k][:].rearrange("p (b i) -> p b i", i=32),
                    in1=invblk[:].rearrange("p b -> p b ()").broadcast_to((128, 32, 32)),
                    op=ALU.mult))
            def fin(ta, k=k):
                return P.op("vector", lambda e, k=k, ta=ta: e.tensor_tensor(
                    out=wq_nat[k][:].rearrange("p (b i) -> p b i", i=32),
                    in0=ta[:].rearrange("p (b i) -> p b i", i=32),
                    in1=sblk[:].rearrange("p b -> p b ()").broadcast_to((128, 32, 32)),
                    op=ALU.mult), sem="wdq")
            emit_round(P, fin, None, tmp1, tmp2, g_col, pre=pre)
            wdq_c = P.counts["wdq"]
            for gno in range(2):
                kk = gno
                if wtr_cp[kk]:
                    P.wait("tensor", f"wtr_cp{kk}", wtr_cp[kk])
                if gno == 0:
                    P.wait("tensor", "wdq", wdq_c)
                v = 0
                for j in range(4):
                    ic = gno * 4 + j
                    v = P.op("tensor", lambda e, k=k, ic=ic, j=j, kk=kk: e.transpose(
                        out=ps_tr[kk][:, j * 128:(j + 1) * 128],
                        in_=wq_nat[k][:, ic * 128:(ic + 1) * 128],
                        identity=ident_sb[:]), sem="wtr_pe")
                P.wait("scalar", "wtr_pe", v)
                c = P.op("scalar", lambda e, k=k, gno=gno, kk=kk: e.copy(
                    out=wT_sl[k][:, gno * 512:(gno + 1) * 512], in_=ps_tr[kk][:]),
                    sem=f"wtr_cp{kk}")
                wtr_cp[kk] = c
            wtr_hist.append(P.counts["wtr_pe"])
            banks = [0, 1, 2, 3] if nw % 2 == 0 else [4, 5, 0, 1]
            for kk in range(2):
                P.wait("tensor", f"wtr_cp{kk}", wtr_cp[kk])
            for ic in range(ND):
                for t in range(4):
                    bk = banks[t]
                    if ic == 0:
                        pool.acquire(bk, "tensor")
                    P.op("tensor", lambda e, k=k, ic=ic, t=t, bk=bk: e.matmul(
                        out=ps_mm[bk][:],
                        lhsT=wT_sl[k][:, ic * 128:(ic + 1) * 128],
                        rhs=xqT[ic][:, t * 512:(t + 1) * 512],
                        start=(ic == 0), stop=(ic == ND - 1)),
                        sem="mm" if ic == ND - 1 else None)
            mm_c = P.counts["mm"]
            if nw >= 2:
                P.wait("scalar", "strip_st", (nw - 2) * 16 + 16)
            for t in range(4):
                bk = banks[t]
                P.wait("scalar", "mm", mm_c)
                ev = P.op("scalar", lambda e, k=k, t=t, bk=bk: e.copy(
                    out=strip[k][:, t * 512:(t + 1) * 512], in_=ps_mm[bk][:]), sem="ev")
                pool.release(bk, "ev", ev)
            ev_c = P.counts["ev"]
            P.wait("vector", "ev", ev_c)
            P.wait("vector", "amax_init", 1)
            P.op("vector", lambda e, k=k, acol=acol: e.tensor_reduce(
                out=bmax[:, 0:1], in_=strip[k][:], axis=AXX, op=ALU.max,
                apply_absolute_value=True))
            vm = P.op("vector", lambda e, acol=acol: e.tensor_tensor(
                out=amax_acc[:, acol:acol + 1], in0=amax_acc[:, acol:acol + 1],
                in1=bmax[:, 0:1], op=ALU.max), sem="amax_v")
            P.wait("gpsimd", "ev", ev_c)
            P.op("gpsimd", lambda e, out_h=out_h, r=r, k=k: e.dma_start(
                out=out_h[r * 128:(r + 1) * 128, :], in_=strip[k][:]),
                sem="strip_st", by=16)

    P.wait("gpsimd", "amax_v", P.counts["amax_v"])
    d = P.op("gpsimd", lambda e: e.dma_start(
        out=amax_row[:], in_=amax_acc[:, 0:1]), sem="amax_d", by=16)
    P.wait("vector", "amax_d", d)
    r1 = P.op("vector", lambda e: e.tensor_reduce(
        out=m_sb[:, 0:1], in_=amax_row[:], axis=AXX, op=ALU.max), sem="amax_f")
    P.wait("gpsimd", "amax_f", r1)
    d = P.op("gpsimd", lambda e: e.dma_start(
        out=amax_row[:], in_=amax_acc[:, 1:2]), sem="amax_d", by=16)
    P.wait("vector", "amax_d", d)
    r2 = P.op("vector", lambda e: e.tensor_reduce(
        out=m_sb[:, 1:2], in_=amax_row[:], axis=AXX, op=ALU.max), sem="amax_f")
    P.wait("gpsimd", "amax_f", r2)
    P.op("gpsimd", lambda e: e.dma_start(out=mup_h, in_=m_sb[0:1, 0:1]), sem="st_m", by=16)
    P.op("gpsimd", lambda e: e.dma_start(out=mg_h, in_=m_sb[0:1, 1:2]), sem="st_m", by=16)

    P.wait("gpsimd", "strip_st", P.counts["strip_st"])
    P.wait("gpsimd", "st_m", P.counts["st_m"])
    return P.finish()


def emit_scale_cols(P, rows_ap, s_col, invs_col, tmp_col, qmax, ld_sem_val, name):
    P.op("vector", lambda e: e.tensor_reduce(
        out=s_col[:], in_=tmp_col[:], axis=AXX, op=ALU.max))
    P.op("vector", lambda e: e.tensor_scalar(
        out=s_col[:], in0=s_col[:], scalar1=1.0 / qmax, scalar2=1e-12,
        op0=ALU.mult, op1=ALU.max))
    return P.op("vector", lambda e: e.reciprocal(
        out=invs_col[:], in_=s_col[:]), sem=name)



def build_launch3():
    nc = new_nc()
    P = Prog(nc)
    gate_h = nc.dram_tensor("gate_raw", [3 * D, T], BF16, kind="ExternalInput").ap()
    mga_h = nc.dram_tensor("m_g_all", [1, 8], F32, kind="ExternalInput").ap()
    mg2_h = nc.dram_tensor("m_g2", [1, 1], F32, kind="ExternalOutput").ap()

    NO = 24
    chunk = [P.sb(f"chunk{k}", [128, T], BF16) for k in range(2)]
    g1 = P.sb("g1", [128, T], F32)
    sg = P.sb("sg", [128, T], F32)
    tb = P.sb("tb", [128, T], F32)
    ta = P.sb("ta", [128, T], F32)
    g_col = P.sb("g_col", [128, 1], F32)
    mrow8 = P.sb("mrow8", [128, 8], F32)
    s_col = P.sb("s_col", [128, 1], F32)
    invs_col = P.sb("invs_col", [128, 1], F32)
    amax_acc = P.sb("amax_acc", [128, 2], F32)
    amax_row = P.sb("amax_row", [1, 128], F32)
    m_sb = P.sb("m_sb", [1, 1], F32)

    P.op("sync", lambda e: e.dma_start(
        out=mrow8[:], in_=mga_h.partition_broadcast(128)), sem="ld_misc", by=16)
    emit_g_gadget(P, g_col)
    P.wait("vector", "ld_misc", 16)
    emit_scale_cols(P, None, s_col, invs_col, mrow8, QMAX16, None, "sc_done")
    P.op("vector", lambda e: e.memset(amax_acc[:], 0.0))

    for r in range(NO):
        k = r % 2
        if r >= 2:
            P.wait("sync", "ch_done", r - 1)
        v = P.op("sync", lambda e, r=r, k=k: e.dma_start(
            out=chunk[k][:], in_=gate_h[r * 128:(r + 1) * 128, :]),
            sem=f"ld_{k}", by=16)
        P.wait("vector", f"ld_{k}", v)
        def pre(tbx, k=k):
            P.op("vector", lambda e, k=k, tbx=tbx: e.tensor_scalar(
                out=tbx[:], in0=chunk[k][:], scalar1=invs_col[:, 0:1], scalar2=None,
                op0=ALU.mult))
        def fin(tax):
            return P.op("vector", lambda e, tax=tax: e.tensor_scalar(
                out=g1[:], in0=tax[:], scalar1=s_col[:, 0:1], scalar2=None,
                op0=ALU.mult), sem="g1_done")
        emit_round(P, fin, None, tb, ta, g_col, pre=pre)
        P.wait("scalar", "g1_done", P.counts["g1_done"])
        P.op("scalar", lambda e: e.activation(out=sg[:], in_=g1[:], func=AF.Sigmoid),
             sem="sg_done")
        P.wait("vector", "sg_done", P.counts["sg_done"])
        def pre2(tbx):
            P.op("vector", lambda e, tbx=tbx: e.tensor_scalar(
                out=tbx[:], in0=sg[:], scalar1=65536.0, scalar2=None, op0=ALU.mult))
        def fin2(tax):
            return P.op("vector", lambda e, tax=tax: e.tensor_scalar(
                out=sg[:], in0=tax[:], scalar1=65535.0, scalar2=float(2.0 ** -16),
                op0=ALU.min, op1=ALU.mult), sem="sig_done")
        emit_round(P, fin2, None, tb, ta, g_col, pre=pre2)
        P.op("vector", lambda e: e.tensor_mul(out=g1[:], in0=g1[:], in1=sg[:]))
        P.op("vector", lambda e: e.tensor_reduce(
            out=amax_acc[:, 1:2], in_=g1[:], axis=AXX, op=ALU.max,
            apply_absolute_value=True))
        cd = P.op("vector", lambda e: e.tensor_tensor(
            out=amax_acc[:, 0:1], in0=amax_acc[:, 0:1], in1=amax_acc[:, 1:2],
            op=ALU.max), sem="ch_done")

    P.wait("gpsimd", "ch_done", P.counts["ch_done"])
    d = P.op("gpsimd", lambda e: e.dma_start(
        out=amax_row[:], in_=amax_acc[:, 0:1]), sem="amax_d", by=16)
    P.wait("vector", "amax_d", d)
    rf = P.op("vector", lambda e: e.tensor_reduce(
        out=m_sb[:], in_=amax_row[:], axis=AXX, op=ALU.max), sem="amax_f")
    P.wait("gpsimd", "amax_f", rf)
    P.op("gpsimd", lambda e: e.dma_start(out=mg2_h, in_=m_sb[:]), sem="st_m", by=16)
    P.wait("gpsimd", "st_m", 16)
    return P.finish()



def build_launch4():
    nc = new_nc()
    P = Prog(nc)
    gate_h = nc.dram_tensor("gate_raw", [3 * D, T], BF16, kind="ExternalInput").ap()
    up_h = nc.dram_tensor("up_raw", [3 * D, T], BF16, kind="ExternalInput").ap()
    mga_h = nc.dram_tensor("m_g_all", [1, 8], F32, kind="ExternalInput").ap()
    mg2a_h = nc.dram_tensor("m_g2_all", [1, 8], F32, kind="ExternalInput").ap()
    mua_h = nc.dram_tensor("m_up_all", [1, 8], F32, kind="ExternalInput").ap()
    tq_h = nc.dram_tensor("t_q", [3 * D, T], BF16, kind="ExternalOutput").ap()
    mo_h = nc.dram_tensor("m_o", [1, 1], F32, kind="ExternalOutput").ap()

    NO = 24
    gch = [P.sb(f"gch{k}", [128, T], BF16) for k in range(2)]
    uch = [P.sb(f"uch{k}", [128, T], BF16) for k in range(2)]
    tqb = [P.sb(f"tqb{k}", [128, T], BF16) for k in range(2)]
    g1 = P.sb("g1", [128, T], F32)
    sg = P.sb("sg", [128, T], F32)
    tb = P.sb("tb", [128, T], F32)
    ta = P.sb("ta", [128, T], F32)
    g_col = P.sb("g_col", [128, 1], F32)
    mrow8 = P.sb("mrow8", [128, 8], F32)
    sg1_col = P.sb("sg1_col", [128, 1], F32)
    invg1_col = P.sb("invg1_col", [128, 1], F32)
    sg2_col = P.sb("sg2_col", [128, 1], F32)
    invg2_col = P.sb("invg2_col", [128, 1], F32)
    su_col = P.sb("su_col", [128, 1], F32)
    invu_col = P.sb("invu_col", [128, 1], F32)
    amax_acc = P.sb("amax_acc", [128, 2], F32)
    amax_row = P.sb("amax_row", [1, 128], F32)
    m_sb = P.sb("m_sb", [1, 1], F32)

    P.op("sync", lambda e: e.dma_start(
        out=mrow8[:], in_=mga_h.partition_broadcast(128)), sem="ld_m1", by=16)
    emit_g_gadget(P, g_col)
    P.wait("vector", "ld_m1", 16)
    emit_scale_cols(P, None, sg1_col, invg1_col, mrow8, QMAX16, None, "sc1")
    P.op("sync", lambda e: e.dma_start(
        out=mrow8[:], in_=mg2a_h.partition_broadcast(128)), sem="ld_m2", by=16)
    P.wait("sync", "sc1", P.counts["sc1"])
    P.wait("vector", "ld_m2", 16)
    emit_scale_cols(P, None, sg2_col, invg2_col, mrow8, QMAX16, None, "sc2")
    P.op("sync", lambda e: e.dma_start(
        out=mrow8[:], in_=mua_h.partition_broadcast(128)), sem="ld_m3", by=16)
    P.wait("sync", "sc2", P.counts["sc2"])
    P.wait("vector", "ld_m3", 16)
    emit_scale_cols(P, None, su_col, invu_col, mrow8, QMAX16, None, "sc3")
    P.op("vector", lambda e: e.memset(amax_acc[:], 0.0))

    for r in range(NO):
        k = r % 2
        if r >= 2:
            P.wait("sync", "ch_done", r - 1)
        v = P.op("sync", lambda e, r=r, k=k: e.dma_start(
            out=gch[k][:], in_=gate_h[r * 128:(r + 1) * 128, :]),
            sem=f"ldg_{k}", by=16)
        v2 = P.op("sync", lambda e, r=r, k=k: e.dma_start(
            out=uch[k][:], in_=up_h[r * 128:(r + 1) * 128, :]),
            sem=f"ldu_{k}", by=16)
        P.wait("vector", f"ldg_{k}", v)
        P.wait("vector", f"ldu_{k}", v2)
        def pre(tbx, k=k):
            P.op("vector", lambda e, k=k, tbx=tbx: e.tensor_scalar(
                out=tbx[:], in0=gch[k][:], scalar1=invg1_col[:, 0:1], scalar2=None,
                op0=ALU.mult))
        def fin(tax):
            return P.op("vector", lambda e, tax=tax: e.tensor_scalar(
                out=g1[:], in0=tax[:], scalar1=sg1_col[:, 0:1], scalar2=None,
                op0=ALU.mult), sem="g1_done")
        emit_round(P, fin, None, tb, ta, g_col, pre=pre)
        P.wait("scalar", "g1_done", P.counts["g1_done"])
        P.op("scalar", lambda e: e.activation(out=sg[:], in_=g1[:], func=AF.Sigmoid),
             sem="sg_done")
        P.wait("vector", "sg_done", P.counts["sg_done"])
        def pre2(tbx):
            P.op("vector", lambda e, tbx=tbx: e.tensor_scalar(
                out=tbx[:], in0=sg[:], scalar1=65536.0, scalar2=None, op0=ALU.mult))
        def fin2(tax):
            return P.op("vector", lambda e, tax=tax: e.tensor_scalar(
                out=sg[:], in0=tax[:], scalar1=65535.0, scalar2=float(2.0 ** -16),
                op0=ALU.min, op1=ALU.mult), sem="sig_done")
        emit_round(P, fin2, None, tb, ta, g_col, pre=pre2)
        P.op("vector", lambda e: e.tensor_mul(out=g1[:], in0=g1[:], in1=sg[:]))
        def pre3(tbx):
            P.op("vector", lambda e, tbx=tbx: e.tensor_scalar(
                out=tbx[:], in0=g1[:], scalar1=invg2_col[:, 0:1], scalar2=None,
                op0=ALU.mult))
        def fin3(tax):
            return P.op("vector", lambda e, tax=tax: e.tensor_scalar(
                out=g1[:], in0=tax[:], scalar1=sg2_col[:, 0:1], scalar2=None,
                op0=ALU.mult), sem="g2_done")
        emit_round(P, fin3, None, tb, ta, g_col, pre=pre3)
        def pre4(tbx, k=k):
            P.op("vector", lambda e, k=k, tbx=tbx: e.tensor_scalar(
                out=tbx[:], in0=uch[k][:], scalar1=invu_col[:, 0:1], scalar2=None,
                op0=ALU.mult))
        def fin4(tax):
            return P.op("vector", lambda e, tax=tax: e.tensor_scalar(
                out=sg[:], in0=tax[:], scalar1=su_col[:, 0:1], scalar2=None,
                op0=ALU.mult), sem="u1_done")
        emit_round(P, fin4, None, tb, ta, g_col, pre=pre4)
        if r >= 2:
            P.wait("vector", "tq_st", (r - 2) * 16 + 16)
        P.op("vector", lambda e, k=k: e.tensor_mul(out=tqb[k][:], in0=g1[:], in1=sg[:]))
        P.op("vector", lambda e, k=k: e.tensor_reduce(
            out=amax_acc[:, 1:2], in_=tqb[k][:], axis=AXX, op=ALU.max,
            apply_absolute_value=True))
        cd = P.op("vector", lambda e: e.tensor_tensor(
            out=amax_acc[:, 0:1], in0=amax_acc[:, 0:1], in1=amax_acc[:, 1:2],
            op=ALU.max), sem="ch_done")
        P.wait("gpsimd", "ch_done", cd)
        P.op("gpsimd", lambda e, r=r, k=k: e.dma_start(
            out=tq_h[r * 128:(r + 1) * 128, :], in_=tqb[k][:]), sem="tq_st", by=16)

    P.wait("gpsimd", "ch_done", P.counts["ch_done"])
    d = P.op("gpsimd", lambda e: e.dma_start(
        out=amax_row[:], in_=amax_acc[:, 0:1]), sem="amax_d", by=16)
    P.wait("vector", "amax_d", d)
    rf = P.op("vector", lambda e: e.tensor_reduce(
        out=m_sb[:], in_=amax_row[:], axis=AXX, op=ALU.max), sem="amax_f")
    P.wait("gpsimd", "amax_f", rf)
    P.op("gpsimd", lambda e: e.dma_start(out=mo_h, in_=m_sb[:]), sem="st_m", by=16)
    P.wait("gpsimd", "tq_st", P.counts["tq_st"])
    P.wait("gpsimd", "st_m", 16)
    return P.finish()


def emit_wprep(P, w_h, O, I, scratch_h, stage, stage_bf, rowbuf, ident,
               gain_col=None, lpbq=False, fixq=None, g_col=None,
               bmax=None, sblk=None, invblk=None, tb=None, ta=None, tag="",
               pe_tr=None, order="r"):
    NR, NI = O // 128, I // 128
    base_cast = P.counts.get("wp_cast", 0)
    base_st = [P.counts.get(f"wp_st{sl}", 0) for sl in range(2)]
    if not hasattr(P, "_wp_hist"):
        P._wp_hist = []
        P._wp_cp = {0: 0, 1: 0}
        P._wp_row = {0: 0, 1: 0}
    for r in range(NR):
        nw = r
        g_row = base_cast + r
        ldsem = f"wp_ld{r % 2}"
        if g_row >= 2:
            P.wait("sync", "wp_cast", g_row - 1)
        v = P.op("sync", lambda e, r=r: e.dma_start(
            out=stage[r % 2][:, 0:I], in_=w_h[r * 128:(r + 1) * 128, :]), sem=ldsem, by=16)
        P.wait("vector", ldsem, v)
        if g_row >= 2:
            hp = P._wp_hist[g_row - 2]
            if hp:
                P.wait("vector", "wp_pe", hp)
        if lpbq:
            nb = I // 32
            P.op("vector", lambda e, r=r: e.tensor_reduce(
                out=bmax[:, 0:nb], in_=stage[r % 2][:].rearrange("p (b i) -> p b i", i=32),
                axis=AXX, op=ALU.max, apply_absolute_value=True))
            P.op("vector", lambda e: e.tensor_scalar(
                out=sblk[:, 0:nb], in0=bmax[:, 0:nb], scalar1=1.0 / 7.0, scalar2=1e-12,
                op0=ALU.mult, op1=ALU.max))
            P.op("vector", lambda e: e.reciprocal(out=invblk[:, 0:nb], in_=sblk[:, 0:nb]))
            def pre(tbx, r=r):
                P.op("vector", lambda e, r=r, tbx=tbx: e.tensor_tensor(
                    out=tbx[:].rearrange("p (b i) -> p b i", i=32),
                    in0=stage[r % 2][:].rearrange("p (b i) -> p b i", i=32),
                    in1=invblk[:, 0:nb].rearrange("p b -> p b ()").broadcast_to((128, nb, 32)),
                    op=ALU.mult))
            def fin(tax, r=r):
                return P.op("vector", lambda e, r=r, tax=tax: e.tensor_tensor(
                    out=stage_bf[r % 2][:].rearrange("p (b i) -> p b i", i=32),
                    in0=tax[:].rearrange("p (b i) -> p b i", i=32),
                    in1=sblk[:, 0:nb].rearrange("p b -> p b ()").broadcast_to((128, nb, 32)),
                    op=ALU.mult), sem="wp_cast")
            emit_round(P, fin, None, tb, ta, g_col, pre=pre)
        elif fixq is not None:
            width, frac = fixq
            sc = float(2.0 ** frac)
            imax, imin = float(2 ** (width - 1) - 1), float(-2 ** (width - 1))
            def pre(tbx, r=r):
                P.op("vector", lambda e, r=r: e.tensor_scalar(
                    out=tb[:, 0:I], in0=stage[r % 2][:, 0:I], scalar1=sc, scalar2=None,
                    op0=ALU.mult))
            def fin(tax, r=r):
                P.op("vector", lambda e: e.tensor_scalar(
                    out=ta[:, 0:I], in0=ta[:, 0:I], scalar1=imax, scalar2=imin,
                    op0=ALU.min, op1=ALU.max))
                return P.op("vector", lambda e, r=r: e.tensor_scalar(
                    out=stage_bf[r % 2][:, 0:I], in0=ta[:, 0:I], scalar1=1.0 / sc,
                    scalar2=None, op0=ALU.mult), sem="wp_cast")
            tbv = type("S", (), {"__getitem__": lambda s_, k_: tb[:, 0:I]})()
            tav = type("S", (), {"__getitem__": lambda s_, k_: ta[:, 0:I]})()
            emit_round(P, fin, None, tbv, tav, g_col, pre=pre)
        else:
            P.op("vector", lambda e, r=r: e.tensor_copy(
                out=stage_bf[r % 2][:, 0:I], in_=stage[r % 2][:, 0:I]), sem="wp_cast")
        cast_c = P.counts["wp_cast"]
        rb = g_row % 2
        NG = NI // 4
        grp_ev = [0, 0]
        for gno in range(NG):
            kk = gno % 2
            cpsem = f"wp_cp{kk}"
            if P._wp_cp[kk]:
                P.wait("tensor", cpsem, P._wp_cp[kk])
            if gno < 2:
                P.wait("tensor", "wp_cast", cast_c)
            v = 0
            for j in range(4):
                ic = gno * 4 + j
                v = P.op("tensor", lambda e, r=r, ic=ic, j=j, kk=kk: e.transpose(
                    out=pe_tr[kk][:, j * 128:(j + 1) * 128],
                    in_=stage_bf[r % 2][:, ic * 128:(ic + 1) * 128],
                    identity=ident[:]), sem="wp_pe")
            if gno == 0 and P._wp_row[rb]:
                P.wait("scalar", "wp_st", P._wp_row[rb])
            P.wait("scalar", "wp_pe", v)
            c = P.op("scalar", lambda e, kk=kk, gno=gno, rb=rb: e.copy(
                out=rowbuf[rb][:, gno * 512:(gno + 1) * 512], in_=pe_tr[kk][:]),
                sem=cpsem)
            P._wp_cp[kk] = c
            grp_ev[kk] = c
        if gain_col is not None:
            for kk in range(2):
                if grp_ev[kk]:
                    P.wait("vector", f"wp_cp{kk}", grp_ev[kk])
            P.op("vector", lambda e, rb=rb, NIx=NI: e.tensor_tensor(
                out=rowbuf[rb][:, 0:NI * 128].rearrange("p (a c) -> p a c", c=128),
                in0=rowbuf[rb][:, 0:NI * 128].rearrange("p (a c) -> p a c", c=128),
                in1=gain_col[:, 0:NIx].rearrange("p a -> p a ()").broadcast_to((128, NIx, 128)),
                op=ALU.mult), sem="wp_g")
            P.wait("gpsimd", "wp_g", P.counts["wp_g"])
        else:
            for kk in range(2):
                if grp_ev[kk]:
                    P.wait("gpsimd", f"wp_cp{kk}", grp_ev[kk])
        P.wait("gpsimd", "wp_st", P.counts.get("wp_st", 0))
        if order == "r":
            st = P.op("gpsimd", lambda e, r=r, rb=rb, NIx=NI: e.dma_start(
                out=scratch_h[:, r * NIx * 128:(r + 1) * NIx * 128],
                in_=rowbuf[rb][:, 0:NIx * 128]), sem="wp_st", by=16)
        else:
            sc4 = scratch_h.rearrange("p (a b c) -> p a b c", b=NR, c=128)
            st = P.op("gpsimd", lambda e, r=r, rb=rb, NIx=NI, sc4=sc4: e.dma_start(
                out=sc4[:, :, r, :],
                in_=rowbuf[rb][:, 0:NIx * 128].rearrange("p (a c) -> p a c", c=128)),
                sem="wp_st", by=16)
        P._wp_row[rb] = st
        P._wp_hist.append(P.counts.get("wp_pe", 0))
    if "wp_st" in P.counts:
        P.wait("gpsimd", "wp_st", P.counts["wp_st"])


def _qdq_tile(P, out_fn, src_fn, invs, s, tb, ta, g_col, sem=None):
    def pre(tbx):
        P.op("vector", lambda e, tbx=tbx: src_fn(e, tbx))
    def fin(tax):
        return P.op("vector", lambda e, tax=tax: out_fn(e, tax), sem=sem)
    return emit_round(P, fin, None, tb, ta, g_col, pre=pre)


def build_launch5(dbg=False, upto=None):
    nc = new_nc()
    P = Prog(nc)
    P.want_const(EPS_LN)
    ident_h = nc.inline_tensor(np.eye(128, dtype=ml_dtypes.bfloat16), "ident_bf16").ap()

    x1_h = nc.dram_tensor("x1", [T, D], F32, kind="ExternalInput").ap()
    tq_h = nc.dram_tensor("t_q", [3 * D, T], BF16, kind="ExternalInput").ap()
    moa_h = nc.dram_tensor("m_o_all", [1, 8], F32, kind="ExternalInput").ap()
    wdown_h = nc.dram_tensor("w_down", [D, 3 * D], F32, kind="ExternalInput").ap()
    wq_h = nc.dram_tensor("wq", [D, D], F32, kind="ExternalInput").ap()
    bq_h = nc.dram_tensor("bq", [D], F32, kind="ExternalInput").ap()
    wk_h = nc.dram_tensor("wk", [D, D], F32, kind="ExternalInput").ap()
    bk_h = nc.dram_tensor("bk", [D], F32, kind="ExternalInput").ap()
    wv_h = nc.dram_tensor("wv", [D, D], F32, kind="ExternalInput").ap()
    bv_h = nc.dram_tensor("bv", [D], F32, kind="ExternalInput").ap()
    wo_h = nc.dram_tensor("wo", [D, D], F32, kind="ExternalInput").ap()
    bo_h = nc.dram_tensor("bo", [D], F32, kind="ExternalInput").ap()
    g1_h = nc.dram_tensor("g1", [D], F32, kind="ExternalInput").ap()
    b1_h = nc.dram_tensor("b1", [D], F32, kind="ExternalInput").ap()
    g2_h = nc.dram_tensor("g2", [D], F32, kind="ExternalInput").ap()
    b2_h = nc.dram_tensor("b2", [D], F32, kind="ExternalInput").ap()
    w1_h = nc.dram_tensor("w1", [4 * D, D], F32, kind="ExternalInput").ap()
    bf1_h = nc.dram_tensor("bf1", [4 * D], F32, kind="ExternalInput").ap()
    w2_h = nc.dram_tensor("w2", [D, 4 * D], F32, kind="ExternalInput").ap()
    bf2_h = nc.dram_tensor("bf2", [D], F32, kind="ExternalInput").ap()
    fw1_h = nc.dram_tensor("f_w1", [4 * D, D], F32, kind="ExternalInput").ap()
    fb1_h = nc.dram_tensor("f_b1", [4 * D], F32, kind="ExternalInput").ap()
    fw2_h = nc.dram_tensor("f_w2", [D, 4 * D], F32, kind="ExternalInput").ap()
    fb2_h = nc.dram_tensor("f_b2", [D], F32, kind="ExternalInput").ap()
    out_h = nc.dram_tensor("out", [T, D], F32, kind="ExternalOutput").ap()

    x2_hbm = nc.dram_tensor("x2_hbm", [T, D], F32, kind="Internal").ap()
    xin_hbm = nc.dram_tensor("xin_hbm", [T, D], BF16, kind="Internal").ap()
    x3_hbm = nc.dram_tensor("x3_hbm", [T, D], F32, kind="Internal").ap()
    wqT_s = nc.dram_tensor("wqT_s", [128, D * D // 128], BF16, kind="Internal").ap()
    wkT_s = nc.dram_tensor("wkT_s", [128, D * D // 128], BF16, kind="Internal").ap()
    wvT_s = nc.dram_tensor("wvT_s", [128, D * D // 128], BF16, kind="Internal").ap()
    woT_s = nc.dram_tensor("woT_s", [128, D * D // 128], BF16, kind="Internal").ap()
    w1T_s = nc.dram_tensor("w1T_s", [128, 4 * D * D // 128], BF16, kind="Internal").ap()
    w2T_s = nc.dram_tensor("w2T_s", [128, 4 * D * D // 128], BF16, kind="Internal").ap()
    fw1T_s = nc.dram_tensor("fw1T_s", [128, 4 * D * D // 128], BF16, kind="Internal").ap()
    fw2T_s = nc.dram_tensor("fw2T_s", [128, 4 * D * D // 128], BF16, kind="Internal").ap()
    if dbg:
        dx2_h = nc.dram_tensor("d_x2", [T, D], F32, kind="ExternalOutput").ap()
        dxx2_h = nc.dram_tensor("d_xx2", [T, D], F32, kind="ExternalOutput").ap()
        dqT_h = nc.dram_tensor("d_qT", [128, T], BF16, kind="ExternalOutput").ap()
        dkT_h = nc.dram_tensor("d_kT", [128, T], BF16, kind="ExternalOutput").ap()
        dvx_h = nc.dram_tensor("d_vx", [128, 1040], BF16, kind="ExternalOutput").ap()
        dat_h = nc.dram_tensor("d_at", [128, T], BF16, kind="ExternalOutput").ap()
        dk2_h = nc.dram_tensor("d_k2", [128, 256], F32, kind="ExternalOutput").ap()
        dcps_h = nc.dram_tensor("d_cps", [128, D], F32, kind="ExternalOutput").ap()
        dct1_h = nc.dram_tensor("d_ct1", [128, D], F32, kind="ExternalOutput").ap()
        dcn2_h = nc.dram_tensor("d_cn2", [128, 4], F32, kind="ExternalOutput").ap()

    ident_sb = P.sb("ident_sb", [128, 128], BF16)
    g_col = P.sb("g_col", [128, 1], F32)
    mrow8 = P.sb("mrow8", [128, 8], F32)
    so_col = P.sb("so_col", [128, 1], F32)
    invo_col = P.sb("invo_col", [128, 1], F32)
    g1c = P.sb("g1c", [128, ND], F32)
    b1c = P.sb("b1c", [128, ND], F32)
    g2c = P.sb("g2c", [128, ND], F32)
    b2c = P.sb("b2c", [128, ND], F32)
    b1g = P.sb("b1g", [128, ND], F32)
    b2g = P.sb("b2g", [128, ND], F32)
    b1g_bf = P.sb("b1g_bf", [128, ND], BF16)
    b2g_bf = P.sb("b2g_bf", [128, ND], BF16)
    ones_row = P.sb("ones_row", [1, 128], BF16)
    ones512 = P.sb("ones512", [1, 512], BF16)
    bf1col = P.sb("bf1col", [128, 32], F32)
    fb1col = P.sb("fb1col", [128, 32], F32)
    stat = P.sb("stat", [128, NT * 8], F32)
    k2s = P.sb("k2s", [128, NT * 16], F32)
    f16 = P.sb("f16", [128, 16], F32)
    n2t = P.sb("n2t", [128, 16], F32)
    junk_bf = P.sb("junk_bf", [128, D], BF16)
    amax_row = P.sb("amax_row", [1, 128], F32)

    ps_tr = [P.psum(f"ps_tr{k}", [128, 512], BF16) for k in range(2)]
    ps_mm = [P.psum(f"ps_mm{k}", [128, 512], F32) for k in range(6)]
    pool = PsumPool(P, ps_mm)

    MU0, RSTD0, MU1, RSTD1, SQ, SS = 0, NT, 2 * NT, 3 * NT, 4 * NT, 5 * NT
    MU2, RSTD2 = 6 * NT, 7 * NT

    P.op("sync", lambda e: e.dma_start(out=ident_sb[:], in_=ident_h), sem="ld_misc", by=16)
    P.op("sync", lambda e: e.dma_start(
        out=mrow8[:], in_=moa_h.partition_broadcast(128)), sem="ld_misc", by=16)
    ld_misc_total = P.counts["ld_misc"]
    emit_g_gadget(P, g_col)
    P.op("vector", lambda e: e.memset(ones_row[:], 1.0), sem="v_misc")
    P.op("vector", lambda e: e.memset(ones512[:], 1.0), sem="v_misc")
    load_colvec(P, g1_h, g1c, ND, "ld_vec")
    load_colvec(P, b1_h, b1c, ND, "ld_vec")
    load_colvec(P, g2_h, g2c, ND, "ld_vec")
    ld_vec_total = load_colvec(P, b2_h, b2c, ND, "ld_vec")
    P.wait("vector", "ld_vec", ld_vec_total)
    P.op("vector", lambda e: e.reciprocal(out=b1g[:], in_=g1c[:]))
    P.op("vector", lambda e: e.tensor_mul(out=b1g[:], in0=b1g[:], in1=b1c[:]))
    P.op("vector", lambda e: e.reciprocal(out=b2g[:], in_=g2c[:]))
    P.op("vector", lambda e: e.tensor_mul(out=b2g[:], in0=b2g[:], in1=b2c[:]))
    P.op("vector", lambda e: e.tensor_copy(out=b1g_bf[:], in_=b1g[:]))
    P.op("vector", lambda e: e.tensor_copy(out=b2g_bf[:], in_=b2g[:]), sem="bg_done")
    P.wait("vector", "ld_misc", ld_misc_total)
    emit_scale_cols(P, None, so_col, invo_col, mrow8, QMAX16, None, "sc_o")

    P.push_scope()
    wstage = [P.sb(f"wstage{k}", [128, 4 * D], F32) for k in range(2)]
    wstage_bf = [P.sb(f"wstage_bf{k}", [128, 4 * D], BF16) for k in range(2)]
    wrow = [P.sb(f"wrow{k}", [128, 4 * D], BF16) for k in range(2)]
    tbw = P.sb("tbw", [128, 4 * D], F32)
    taw = P.sb("taw", [128, 4 * D], F32)
    P.wait("vector", "bg_done", P.counts["bg_done"])
    P.wait("tensor", "ld_misc", ld_misc_total)
    emit_wprep(P, wq_h, D, D, wqT_s, wstage, wstage_bf, wrow, ident_sb,
               gain_col=g1c, tag="q", pe_tr=ps_tr, order="ic")
    emit_wprep(P, wk_h, D, D, wkT_s, wstage, wstage_bf, wrow, ident_sb,
               gain_col=g1c, tag="k", pe_tr=ps_tr, order="ic")
    emit_wprep(P, wv_h, D, D, wvT_s, wstage, wstage_bf, wrow, ident_sb,
               gain_col=g1c, tag="v", pe_tr=ps_tr, order="ic")
    emit_wprep(P, wo_h, D, D, woT_s, wstage, wstage_bf, wrow, ident_sb, tag="o",
               pe_tr=ps_tr, order="ic")
    emit_wprep(P, w1_h, 4 * D, D, w1T_s, wstage, wstage_bf, wrow, ident_sb,
               gain_col=g2c, tag="w1", pe_tr=ps_tr, order="r")
    emit_wprep(P, w2_h, D, 4 * D, w2T_s, wstage, wstage_bf, wrow, ident_sb,
               tag="w2", pe_tr=ps_tr, order="ic")
    emit_wprep(P, fw1_h, 4 * D, D, fw1T_s, wstage, wstage_bf, wrow, ident_sb,
               fixq=(8, 6), g_col=g_col, tb=tbw, ta=taw, tag="f1", pe_tr=ps_tr,
               order="r")
    emit_wprep(P, fw2_h, D, 4 * D, fw2T_s, wstage, wstage_bf, wrow, ident_sb,
               fixq=(8, 6), g_col=g_col, tb=tbw, ta=taw, tag="f2", pe_tr=ps_tr,
               order="ic")
    P.pop_scope()
    P.barrier()
    if upto == "P0":
        return P, dict(locals())

    P.push_scope()
    x_sb = [P.sb(f"xA{i}", [128, D], F32) for i in range(NT)]
    wdT = [P.sb(f"wdT{ic}", [128, D], BF16) for ic in range(24)]
    oq = [[P.sb(f"oq0_{ic}", [128, 512], BF16) for ic in range(24)]]
    HW = 3 * D // 2
    stA = [P.sb(f"stA{k}", [128, HW], F32) for k in range(2)]
    stA_bf = [P.sb(f"stA_bf{k}", [128, HW], BF16) for k in range(2)]
    tqst = [P.sb(f"tqst{k}", [128, 512], BF16) for k in range(2)]
    tbA = P.sb("tbA", [128, HW], F32)
    taA = P.sb("taA", [128, HW], F32)
    xinb = P.sb("xinb", [128, D], BF16)

    for i in range(NT):
        P.op("sync", lambda e, i=i: e.dma_start(
            out=x_sb[i][:], in_=x1_h[i * 128:(i + 1) * 128, :]), sem="ld_x", by=16)
    ld_x_total = P.counts["ld_x"]

    bmaxA = P.sb("bmaxA", [128, 48], F32)
    sblkA = P.sb("sblkA", [128, 48], F32)
    invblkA = P.sb("invblkA", [128, 48], F32)
    nwch = 0
    wd_trhist = []
    wd_cp = {0: 0, 1: 0}
    for r in range(ND):
        for h in range(2):
            k = nwch % 2
            if nwch >= 2:
                P.wait("sync", "wd_cast", nwch - 1)
            v = P.op("sync", lambda e, r=r, h=h, k=k: e.dma_start(
                out=stA[k][:], in_=wdown_h[r * 128:(r + 1) * 128, h * HW:(h + 1) * HW]),
                sem=f"wd_ld{k}", by=16)
            P.wait("vector", f"wd_ld{k}", v)
            if nwch >= 2:
                if wd_trhist[nwch - 2]:
                    P.wait("vector", "wd_pe", wd_trhist[nwch - 2])
            P.op("vector", lambda e, k=k: e.tensor_reduce(
                out=bmaxA[:], in_=stA[k][:].rearrange("p (b i) -> p b i", i=32),
                axis=AXX, op=ALU.max, apply_absolute_value=True))
            P.op("vector", lambda e: e.tensor_scalar(
                out=sblkA[:], in0=bmaxA[:], scalar1=1.0 / 7.0, scalar2=1e-12,
                op0=ALU.mult, op1=ALU.max))
            P.op("vector", lambda e: e.reciprocal(out=invblkA[:], in_=sblkA[:]))
            def pre(tbx, k=k):
                P.op("vector", lambda e, k=k, tbx=tbx: e.tensor_tensor(
                    out=tbx[:].rearrange("p (b i) -> p b i", i=32),
                    in0=stA[k][:].rearrange("p (b i) -> p b i", i=32),
                    in1=invblkA[:].rearrange("p b -> p b ()").broadcast_to((128, 48, 32)),
                    op=ALU.mult))
            def fin(tax, k=k):
                return P.op("vector", lambda e, k=k, tax=tax: e.tensor_tensor(
                    out=stA_bf[k][:].rearrange("p (b i) -> p b i", i=32),
                    in0=tax[:].rearrange("p (b i) -> p b i", i=32),
                    in1=sblkA[:].rearrange("p b -> p b ()").broadcast_to((128, 48, 32)),
                    op=ALU.mult), sem="wd_cast")
            emit_round(P, fin, None, tbA, taA, g_col, pre=pre)
            cast_c = P.counts["wd_cast"]
            for icl in range(12):
                ic = h * 12 + icl
                ntr = nwch * 12 + icl
                kk = ntr % 2
                if wd_cp[kk]:
                    P.wait("tensor", f"wd_cp{kk}", wd_cp[kk])
                P.wait("tensor", "wd_cast", cast_c)
                v = P.op("tensor", lambda e, icl=icl, kk=kk, k=k: e.transpose(
                    out=ps_tr[kk][:, 0:128], in_=stA_bf[k][:, icl * 128:(icl + 1) * 128],
                    identity=ident_sb[:]), sem="wd_pe")
                P.wait("scalar", "wd_pe", v)
                c = P.op("scalar", lambda e, r=r, ic=ic, kk=kk: e.copy(
                    out=wdT[ic][:, r * 128:(r + 1) * 128], in_=ps_tr[kk][:, 0:128]),
                    sem=f"wd_cp{kk}")
                wd_cp[kk] = c
            wd_trhist.append(P.counts.get("wd_pe", 0))
            nwch += 1
    for kk in range(2):
        P.wait("tensor", f"wd_cp{kk}", P.counts.get(f"wd_cp{kk}", 0))

    P.wait("vector", "sc_o", P.counts["sc_o"])
    for qt in range(4):
        kq = 0
        if qt >= 1:
            P.wait("vector", "mmA", P.counts["mmA"])
        for ic in range(24):
            sl = ic % 2
            nld = qt * 24 + ic
            if nld >= 2:
                P.wait("sync", "oq_done", nld - 1)
            v = P.op("sync", lambda e, qt=qt, ic=ic, sl=sl: e.dma_start(
                out=tqst[sl][:], in_=tq_h[ic * 128:(ic + 1) * 128,
                                          qt * 512:(qt + 1) * 512]),
                sem=f"tq_ld{sl}", by=16)
            P.wait("vector", f"tq_ld{sl}", v)
            def src_fn(e, tbx, sl=sl):
                return e.tensor_scalar(out=tbx[:, 0:512], in0=tqst[sl][:],
                                       scalar1=invo_col[:, 0:1], scalar2=None,
                                       op0=ALU.mult)
            def out_fn(e, tax, qt=qt, ic=ic, kq=kq):
                return e.tensor_scalar(out=oq[kq][ic][:], in0=tax[:, 0:512],
                                       scalar1=so_col[:, 0:1], scalar2=None,
                                       op0=ALU.mult)
            def pre(tbx, sl=sl):
                P.op("vector", lambda e, tbx=tbx, sl=sl: e.tensor_scalar(
                    out=tbx[:, 0:512], in0=tqst[sl][:], scalar1=invo_col[:, 0:1],
                    scalar2=None, op0=ALU.mult))
            def fin(tax, qt=qt, ic=ic, kq=kq):
                return P.op("vector", lambda e, tax=tax, qt=qt, ic=ic, kq=kq: e.tensor_scalar(
                    out=oq[kq][ic][:], in0=tax[:, 0:512], scalar1=so_col[:, 0:1],
                    scalar2=None, op0=ALU.mult), sem="oq_done")
            class _V:
                pass
            tbv = type("S", (), {"__getitem__": lambda s, k: tbA[:, 0:512]})()
            tav = type("S", (), {"__getitem__": lambda s, k: taA[:, 0:512]})()
            emit_round(P, fin, None, tbv, tav, g_col, pre=pre)
        oq_ready = P.counts["oq_done"]
        P.wait("tensor", "oq_done", oq_ready)
        for tt in range(4):
            t_glob = qt * 4 + tt
            banks = [0, 1] if tt % 2 == 0 else [2, 3]
            for ic in range(24):
                for o2 in range(2):
                    bk = banks[o2]
                    if ic == 0:
                        pool.acquire(bk, "tensor")
                    P.op("tensor", lambda e, qt=qt, tt=tt, ic=ic, o2=o2, bk=bk, kq=kq: e.matmul(
                        out=ps_mm[bk][:],
                        lhsT=oq[kq][ic][:, tt * 128:(tt + 1) * 128],
                        rhs=wdT[ic][:, o2 * 512:(o2 + 1) * 512],
                        start=(ic == 0), stop=(ic == 23)),
                        sem="mmA" if ic == 23 else None)
            mm_c = P.counts["mmA"]
            P.wait("vector", "ld_x", ld_x_total)
            for o2 in range(2):
                bk = banks[o2]
                P.wait("vector", "mmA", mm_c)
                ev = P.op("vector", lambda e, t_glob=t_glob, o2=o2, bk=bk: e.tensor_add(
                    out=x_sb[t_glob][:, o2 * 512:(o2 + 1) * 512],
                    in0=x_sb[t_glob][:, o2 * 512:(o2 + 1) * 512],
                    in1=ps_mm[bk][:]), sem="evA")
                pool.release(bk, "evA", ev)
            P.op("vector", lambda e, t_glob=t_glob: e.reduce_sum(
                out=stat[:, SS + t_glob:SS + t_glob + 1], in_=x_sb[t_glob][:],
                axis=AXX), sem="sv")
            P.wait("scalar", "evA", P.counts["evA"])
            P.op("scalar", lambda e, t_glob=t_glob: e.activation(
                out=junk_bf[:], in_=x_sb[t_glob][:], func=AF.Square,
                accum_out=stat[:, SQ + t_glob:SQ + t_glob + 1]), sem="sa")
            P.wait("gpsimd", "evA", P.counts["evA"])
            P.op("gpsimd", lambda e, t_glob=t_glob: e.dma_start(
                out=x2_hbm[t_glob * 128:(t_glob + 1) * 128, :], in_=x_sb[t_glob][:]),
                sem="st_x2", by=16)
            if dbg:
                P.op("gpsimd", lambda e, t_glob=t_glob: e.dma_start(
                    out=dx2_h[t_glob * 128:(t_glob + 1) * 128, :], in_=x_sb[t_glob][:]),
                    sem="st_x2", by=16)

    P.wait("vector", "sa", P.counts["sa"])
    P.wait("vector", "sv", P.counts["sv"])
    P.op("vector", lambda e: e.tensor_scalar(
        out=stat[:, MU0:MU0 + NT], in0=stat[:, SS:SS + NT], scalar1=1.0 / D,
        scalar2=None, op0=ALU.mult))
    P.op("vector", lambda e: e.tensor_mul(
        out=stat[:, RSTD0:RSTD0 + NT], in0=stat[:, MU0:MU0 + NT],
        in1=stat[:, MU0:MU0 + NT]))
    vm = P.op("vector", lambda e: e.scalar_tensor_tensor(
        out=stat[:, RSTD0:RSTD0 + NT], in0=stat[:, SQ:SQ + NT], scalar=1.0 / D,
        in1=stat[:, RSTD0:RSTD0 + NT], op0=ALU.mult, op1=ALU.subtract), sem="sv")
    P.wait("scalar", "sv", vm)
    va = P.op("scalar", lambda e: e.activation(
        out=stat[:, RSTD0:RSTD0 + NT], in_=stat[:, RSTD0:RSTD0 + NT],
        func=AF.Sqrt, bias=EPS_LN, scale=1.0), sem="sa")
    P.wait("vector", "sa", va)
    P.op("vector", lambda e: e.reciprocal(
        out=stat[:, RSTD0:RSTD0 + NT], in_=stat[:, RSTD0:RSTD0 + NT]), sem="sv")
    for i in range(NT):
        P.op("vector", lambda e, i=i: e.tensor_scalar(
            out=xinb[:], in0=x_sb[i][:],
            scalar1=stat[:, MU0 + i:MU0 + i + 1], scalar2=stat[:, RSTD0 + i:RSTD0 + i + 1],
            op0=ALU.subtract, op1=ALU.mult))
        P.op("vector", lambda e, i=i: e.reduce_sum(
            out=stat[:, SS + i:SS + i + 1], in_=xinb[:], axis=AXX), sem="sv")
        P.wait("scalar", "sv", P.counts["sv"])
        P.op("scalar", lambda e, i=i: e.activation(
            out=junk_bf[:], in_=xinb[:], func=AF.Square,
            accum_out=stat[:, SQ + i:SQ + i + 1]), sem="sa")
        P.wait("gpsimd", "sa", P.counts["sa"])
        P.op("gpsimd", lambda e, i=i: e.dma_start(
            out=xin_hbm[i * 128:(i + 1) * 128, :], in_=xinb[:]), sem="st_xin", by=16)
        P.wait("vector", "st_xin", P.counts["st_xin"])
    P.op("vector", lambda e: e.tensor_scalar(
        out=stat[:, MU1:MU1 + NT], in0=stat[:, SS:SS + NT], scalar1=1.0 / D,
        scalar2=None, op0=ALU.mult))
    P.op("vector", lambda e: e.tensor_mul(
        out=stat[:, RSTD1:RSTD1 + NT], in0=stat[:, MU1:MU1 + NT],
        in1=stat[:, MU1:MU1 + NT]))
    vm = P.op("vector", lambda e: e.scalar_tensor_tensor(
        out=stat[:, RSTD1:RSTD1 + NT], in0=stat[:, SQ:SQ + NT], scalar=1.0 / D,
        in1=stat[:, RSTD1:RSTD1 + NT], op0=ALU.mult, op1=ALU.subtract), sem="sv")
    P.wait("scalar", "sv", vm)
    va = P.op("scalar", lambda e: e.activation(
        out=stat[:, RSTD1:RSTD1 + NT], in_=stat[:, RSTD1:RSTD1 + NT],
        func=AF.Sqrt, bias=EPS_LN, scale=1.0), sem="sa")
    P.wait("vector", "sa", va)
    P.op("vector", lambda e: e.reciprocal(
        out=stat[:, RSTD1:RSTD1 + NT], in_=stat[:, RSTD1:RSTD1 + NT]), sem="stats1_done")
    P.wait("gpsimd", "st_x2", P.counts["st_x2"])
    P.wait("gpsimd", "st_xin", P.counts["st_xin"])
    P.pop_scope()
    P.barrier()
    return P, dict(locals())


def build_launch5_full(dbg=False, upto=None):
    P, env = build_launch5(dbg=dbg, upto=upto)
    if upto in ("P0", "A"):
        return P.finish()
    _emit_l5_attention(P, env, dbg=dbg)
    if upto == "B":
        return P.finish()
    _emit_l5_tail(P, env, dbg=dbg)
    return P.finish()


def _emit_l5_attention(P, env, dbg=False):
    nc = P.nc
    g = env
    stat, g_col, ident_sb, k2s, f16, n2t = g["stat"], g["g_col"], g["ident_sb"], g["k2s"], g["f16"], g["n2t"]
    ones_row, junk_bf = g["ones_row"], g["junk_bf"]
    ps_tr, ps_mm, pool = g["ps_tr"], g["ps_mm"], g["pool"]
    b1g = g["b1g"]
    b1g_bf = g["b1g_bf"]
    MU1, RSTD1, SQ, SS = g["MU1"], g["RSTD1"], g["SQ"], g["SS"]
    MU2, RSTD2 = g["MU2"], g["RSTD2"]
    xin_hbm = g["xin_hbm"]
    MAXN = 1.0 - 1e-05

    P.push_scope()
    xb = [P.sb(f"xb{i}", [128, D], BF16) for i in range(NT)]
    g["xb"] = xb
    brow_f = [P.sb("brow_f0", [1, D], F32)] * 4
    brow_b = [P.sb("brow_b0", [1, D], BF16)] * 4
    P.push_scope()
    qT = [P.sb(f"qT{p}", [128, T], BF16) for p in range(8)]
    kT = [P.sb(f"kT{p}", [128, T], BF16) for p in range(8)]
    v_ext = [P.sb(f"vext{i}", [128, 16 * 65], BF16) for i in range(NT)]

    P.push_scope()
    xnT = [P.sb(f"xnT{c}", [128, T], BF16) for c in range(ND)]
    wT_res = [P.sb(f"wTr{ic}", [128, D], BF16) for ic in range(ND)]
    qk_tm = xb
    xtile = [P.sb(f"xtile{k}", [128, D], BF16) for k in range(2)]

    P.wait("vector", "stats1_done", P.counts["stats1_done"])
    for i in range(NT):
        k = i % 2
        if i >= 2:
            P.wait("sync", "xnstage", i - 1)
        v = P.op("sync", lambda e, i=i, k=k: e.dma_start(
            out=xtile[k][:], in_=xin_hbm[i * 128:(i + 1) * 128, :]),
            sem=f"xin_ld{k}", by=16)
        P.wait("vector", f"xin_ld{k}", v)
        P.op("vector", lambda e, i=i, k=k: e.tensor_scalar(
            out=qk_tm[i][:], in0=xtile[k][:],
            scalar1=stat[:, MU1 + i:MU1 + i + 1], scalar2=stat[:, RSTD1 + i:RSTD1 + i + 1],
            op0=ALU.subtract, op1=ALU.mult), sem="xnstage")
    xn_c = P.counts["xnstage"]
    P.wait("tensor", "xnstage", xn_c)
    P.wait("tensor", "ld_misc", 16)
    ntr = 0
    for i in range(NT):
        for c in range(ND):
            kk = ntr % 2
            if ntr >= 2:
                P.wait("tensor", "trB1", ntr - 1)
            P.op("tensor", lambda e, i=i, c=c, kk=kk: e.transpose(
                out=ps_tr[kk][:, 0:128], in_=qk_tm[i][:, c * 128:(c + 1) * 128],
                identity=ident_sb[:]), sem="trp")
            P.wait("scalar", "trp", P.counts["trp"])
            P.op("scalar", lambda e, i=i, c=c, kk=kk: e.copy(
                out=xnT[c][:, i * 128:(i + 1) * 128], in_=ps_tr[kk][:, 0:128]), sem="trB1")
            ntr += 1
    trB1_total = P.counts["trB1"]

    def load_wres(scr, tag, dst=None):
        if dst is None:
            dst = wT_res
        for ic in range(ND):
            P.op("sync", lambda e, ic=ic, scr=scr, dst=dst: e.dma_start(
                out=dst[ic][:], in_=scr[:, ic * ND * 128:(ic + 1) * ND * 128]),
                sem="wres", by=16)
        return P.counts["wres"]

    def bias_matvec(bvec_col, bias_h, w_row, tag, wres_c=None):
        P.op("sync", lambda e, bias_h=bias_h, w_row=w_row: e.dma_start(
            out=brow_f[w_row][:], in_=bias_h.rearrange("(a n) -> a n", a=1)),
            sem="bld", by=16)
        if wres_c is not None:
            P.wait("tensor", "wres", wres_c)
        for o2 in range(2):
            bk = o2
            pool.acquire(bk, "tensor")
            for ic in range(ND):
                P.op("tensor", lambda e, ic=ic, o2=o2, bk=bk: e.matmul(
                    out=ps_mm[bk][0:1, :],
                    lhsT=b1g_bf[:, ic:ic + 1],
                    rhs=wT_res[ic][:, o2 * 512:(o2 + 1) * 512],
                    start=(ic == 0), stop=(ic == ND - 1)),
                    sem="bmv" if ic == ND - 1 else None)
        P.wait("vector", "bmv", P.counts["bmv"])
        P.wait("vector", "bld", P.counts["bld"])
        for o2 in range(2):
            ev = P.op("vector", lambda e, o2=o2, w_row=w_row: e.tensor_add(
                out=brow_b[w_row][0:1, o2 * 512:(o2 + 1) * 512],
                in0=ps_mm[o2][0:1, :],
                in1=brow_f[w_row][0:1, o2 * 512:(o2 + 1) * 512]), sem="bev")
            pool.release(o2, "bev", ev)
        return P.counts["bev"]

    def qkv_mm(w_row, tag, evac):
        for t in range(NT):
            banks = [0, 1] if t % 2 == 0 else [2, 3]
            for ic in range(ND):
                for o2 in range(2):
                    bk = banks[o2]
                    if ic == 0:
                        pool.acquire(bk, "tensor")
                    P.op("tensor", lambda e, t=t, ic=ic, o2=o2, bk=bk: e.matmul(
                        out=ps_mm[bk][:],
                        lhsT=xnT[ic][:, t * 128:(t + 1) * 128],
                        rhs=wT_res[ic][:, o2 * 512:(o2 + 1) * 512],
                        start=(ic == 0), stop=False))
            for o2 in range(2):
                bk = banks[o2]
                P.op("tensor", lambda e, t=t, o2=o2, bk=bk, w_row=w_row: e.matmul(
                    out=ps_mm[bk][:],
                    lhsT=ones_row[0:1, :],
                    rhs=brow_b[w_row][0:1, o2 * 512:(o2 + 1) * 512],
                    start=False, stop=True), sem="mm_qkv")
            evac(t, P.counts["mm_qkv"], banks)

    wres_c = load_wres(g["wqT_s"], "q")
    bias_matvec(b1g, g["bq_h"], 0, "q", wres_c=wres_c)
    P.wait("tensor", "trB1", trB1_total)
    P.wait("tensor", "wres", wres_c)
    sq_scr = P.sb("sq_scr", [128, D], BF16)

    def evac_qk(which):
        def evac(t, mm_c, banks):
            for o2 in range(2):
                bk = banks[o2]
                P.wait("scalar", "mm_qkv", mm_c)
                ev = P.op("scalar", lambda e, t=t, o2=o2, bk=bk: e.copy(
                    out=qk_tm[t][:, o2 * 512:(o2 + 1) * 512], in_=ps_mm[bk][:]),
                    sem="ev_qkv")
                pool.release(bk, "ev_qkv", ev)
            P.wait("vector", "ev_qkv", P.counts["ev_qkv"])
            P.op("vector", lambda e, t=t: e.tensor_mul(
                out=sq_scr[:], in0=qk_tm[t][:], in1=qk_tm[t][:]))
            nv = P.op("vector", lambda e: e.tensor_reduce(
                out=n2t[:], in_=sq_scr[:].rearrange("p (h d) -> p h d", d=64),
                axis=AXX, op=ALU.add), sem="n2_qk")
            P.wait("scalar", "n2_qk", nv)
            na = P.op("scalar", lambda e: e.activation(
                out=f16[:], in_=n2t[:], func=AF.Sqrt), sem="sq_qk")
            P.wait("vector", "sq_qk", na)
            P.op("vector", lambda e: e.tensor_scalar(
                out=f16[:], in0=f16[:], scalar1=1e-12, scalar2=None, op0=ALU.max))
            P.op("vector", lambda e: e.reciprocal(out=f16[:], in_=f16[:]))
            P.op("vector", lambda e: e.tensor_scalar(
                out=f16[:], in0=f16[:], scalar1=MAXN, scalar2=1.0,
                op0=ALU.mult, op1=ALU.min))
            if which == "k":
                P.op("vector", lambda e: e.tensor_mul(
                    out=n2t[:], in0=n2t[:], in1=f16[:]))
                P.op("vector", lambda e: e.tensor_mul(
                    out=n2t[:], in0=n2t[:], in1=f16[:]))
                P.op("vector", lambda e, t=t: e.tensor_scalar(
                    out=k2s[:, t * 16:(t + 1) * 16], in0=n2t[:], scalar1=-0.125,
                    scalar2=None, op0=ALU.mult), sem="k2s_done")
            P.op("vector", lambda e, t=t: e.tensor_tensor(
                out=qk_tm[t][:].rearrange("p (h d) -> p h d", d=64),
                in0=qk_tm[t][:].rearrange("p (h d) -> p h d", d=64),
                in1=f16[:].rearrange("p h -> p h ()").broadcast_to((128, 16, 64)),
                op=ALU.mult), sem="proj_qk")
        return evac
    qkv_mm(0, "q", evac_qk("q"))
    P.wait("tensor", "proj_qk", P.counts["proj_qk"])
    ntr = 0
    for i in range(NT):
        for c in range(ND):
            kk = ntr % 2
            if ntr >= 2:
                P.wait("tensor", "trQ", ntr - 1)
            P.op("tensor", lambda e, i=i, c=c, kk=kk: e.transpose(
                out=ps_tr[kk][:, 0:128], in_=qk_tm[i][:, c * 128:(c + 1) * 128],
                identity=ident_sb[:]), sem="trp")
            P.wait("scalar", "trp", P.counts["trp"])
            P.op("scalar", lambda e, i=i, c=c, kk=kk: e.copy(
                out=qT[c][:, i * 128:(i + 1) * 128], in_=ps_tr[kk][:, 0:128]), sem="trQ")
            ntr += 1
    trQ_total = P.counts["trQ"]
    P.wait("sync", "trQ", trQ_total)
    wres_k = load_wres(g["wkT_s"], "k")
    bias_matvec(b1g, g["bk_h"], 1, "k", wres_c=wres_k)
    P.wait("tensor", "wres", wres_k)
    qkv_mm(1, "k", evac_qk("k"))
    P.wait("tensor", "proj_qk", P.counts["proj_qk"])
    ntr = 0
    for i in range(NT):
        for c in range(ND):
            kk = ntr % 2
            if ntr >= 2:
                P.wait("tensor", "trK", ntr - 1)
            P.op("tensor", lambda e, i=i, c=c, kk=kk: e.transpose(
                out=ps_tr[kk][:, 0:128], in_=qk_tm[i][:, c * 128:(c + 1) * 128],
                identity=ident_sb[:]), sem="trp")
            P.wait("scalar", "trp", P.counts["trp"])
            P.op("scalar", lambda e, i=i, c=c, kk=kk: e.copy(
                out=kT[c][:, i * 128:(i + 1) * 128], in_=ps_tr[kk][:, 0:128]), sem="trK")
            ntr += 1
    for i in range(NT):
        P.op("vector", lambda e, i=i: e.memset(
            v_ext[i][:].rearrange("p (h c) -> p h c", c=65)[:, :, 64:65], 1.0),
            sem="vone")
    P.wait("sync", "trK", P.counts["trK"])
    wres_v = load_wres(g["wvT_s"], "v")
    bias_matvec(b1g, g["bv_h"], 2, "v", wres_c=wres_v)
    P.wait("tensor", "wres", wres_v)

    def evac_v(t, mm_c, banks):
        for o2 in range(2):
            bk = banks[o2]
            P.wait("scalar", "mm_qkv", mm_c)
            ev = P.op("scalar", lambda e, t=t, o2=o2, bk=bk: e.copy(
                out=v_ext[t][:].rearrange("p (h c) -> p h c", c=65)[:, o2 * 8:(o2 + 1) * 8, 0:64],
                in_=ps_mm[bk][:].rearrange("p (h d) -> p h d", d=64)), sem="ev_v")
            pool.release(bk, "ev_v", ev)
    qkv_mm(2, "v", evac_v)
    P.wait("tensor", "ev_v", P.counts["ev_v"])
    P.wait("tensor", "vone", P.counts["vone"])
    P.pop_scope()
    P.barrier()

    P.push_scope()
    attn = [P.sb(f"attn{p}", [128, T], BF16) for p in range(8)]
    P.push_scope()
    e_sl = [P.sb(f"esl{k}", [128, 512], BF16) for k in range(4)]
    rb_sb = [P.sb(f"rbsb{k}", [64, 512], BF16) for k in range(2)]
    rs_row = P.sb("rs_row", [1, 512], F32)
    rsb_row = P.sb("rsb_row", [1, 512], BF16)

    P.wait("scalar", "k2s_done", P.counts["k2s_done"])
    P.wait("tensor", "trK", P.counts["trK"])
    pv_hist = []
    rb_hist = [0, 0]
    for p in range(8):
        for qt in range(4):
            pool.acquire(4, "tensor")
            pool.acquire(5, "tensor")
            for kt in range(16):
                sA = 0 if kt % 2 == 0 else 2
                sB = 1 if kt % 2 == 0 else 3
                pool.acquire(sA, "tensor")
                pool.acquire(sB, "tensor")
                P.op("tensor", lambda e, p=p, qt=qt, kt=kt, sA=sA: e.matmul(
                    out=ps_mm[sA][:],
                    lhsT=kT[p][0:64, kt * 128:(kt + 1) * 128],
                    rhs=qT[p][0:64, qt * 512:(qt + 1) * 512],
                    start=True, stop=True), sem="mm_s")
                P.op("tensor", lambda e, p=p, qt=qt, kt=kt, sB=sB: e.matmul(
                    out=ps_mm[sB][:],
                    lhsT=kT[p][64:128, kt * 128:(kt + 1) * 128],
                    rhs=qT[p][64:128, qt * 512:(qt + 1) * 512],
                    start=True, stop=True), sem="mm_s")
                mm_s_c = P.counts["mm_s"]
                eA = e_sl[(2 * kt) % 4]
                eB = e_sl[(2 * kt + 1) % 4]
                gk = len(pv_hist) - 2
                if gk >= 0:
                    P.wait("scalar", "mm_pv", pv_hist[gk])
                P.wait("scalar", "mm_s", mm_s_c)
                evA = P.op("scalar", lambda e, p=p, kt=kt, sA=sA, eA=eA: e.activation(
                    out=eA[:], in_=ps_mm[sA][:], func=AF.Exp,
                    bias=k2s[:, kt * 16 + 2 * p:kt * 16 + 2 * p + 1], scale=0.25),
                    sem="ev_e")
                evB = P.op("scalar", lambda e, p=p, kt=kt, sB=sB, eB=eB: e.activation(
                    out=eB[:], in_=ps_mm[sB][:], func=AF.Exp,
                    bias=k2s[:, kt * 16 + 2 * p + 1:kt * 16 + 2 * p + 2], scale=0.25),
                    sem="ev_e")
                pool.release(sA, "ev_e", evA)
                pool.release(sB, "ev_e", evB)
                P.wait("tensor", "ev_e", evA)
                P.op("tensor", lambda e, p=p, qt=qt, kt=kt, eA=eA: e.matmul(
                    out=ps_mm[4][0:65, :],
                    lhsT=v_ext[kt][:, (2 * p) * 65:(2 * p + 1) * 65],
                    rhs=eA[:], start=(kt == 0), stop=(kt == 15)), sem="mm_pv")
                P.wait("tensor", "ev_e", evB)
                P.op("tensor", lambda e, p=p, qt=qt, kt=kt, eB=eB: e.matmul(
                    out=ps_mm[5][0:65, :],
                    lhsT=v_ext[kt][:, (2 * p + 1) * 65:(2 * p + 2) * 65],
                    rhs=eB[:], start=(kt == 0), stop=(kt == 15)), sem="mm_pv")
                pv_hist.append(P.counts["mm_pv"])
            mm_pv_c = P.counts["mm_pv"]
            for hh in range(2):
                bank = 4 + hh
                P.wait("vector", "mm_pv", mm_pv_c)
                P.op("vector", lambda e, bank=bank: e.tensor_copy(
                    out=rs_row[:], in_=ps_mm[bank][64:65, :]))
                P.op("vector", lambda e: e.reciprocal(out=rs_row[:], in_=rs_row[:]))
                P.op("vector", lambda e: e.tensor_copy(
                    out=rsb_row[:], in_=rs_row[:]), sem="rs_done")
                P.wait("tensor", "rs_done", P.counts["rs_done"])
                bcb = 0 if hh == 0 else 1
                pool.acquire(bcb, "tensor")
                P.op("tensor", lambda e, bcb=bcb: e.matmul(
                    out=ps_mm[bcb][0:64, :],
                    lhsT=ones_row[0:1, 0:64],
                    rhs=rsb_row[0:1, :], start=True, stop=True), sem="mm_bc")
                P.wait("scalar", "mm_bc", P.counts["mm_bc"])
                if rb_hist[hh]:
                    P.wait("scalar", "attn_ev", rb_hist[hh])
                P.op("scalar", lambda e, hh=hh, bcb=bcb: e.copy(
                    out=rb_sb[hh][:], in_=ps_mm[bcb][0:64, :]), sem="rb_done")
                pool.release(bcb, "rb_done", P.counts["rb_done"])
                P.wait("vector", "rb_done", P.counts["rb_done"])
                ev = P.op("vector", lambda e, p=p, qt=qt, hh=hh, bank=bank: e.tensor_mul(
                    out=attn[p][hh * 64:(hh + 1) * 64, qt * 512:(qt + 1) * 512],
                    in0=ps_mm[bank][0:64, :], in1=rb_sb[hh][:]), sem="attn_ev")
                rb_hist[hh] = ev
                pool.release(bank, "attn_ev", ev)

    attn_total = P.counts["attn_ev"]
    if dbg:
        P.wait("gpsimd", "attn_ev", attn_total)
        P.op("gpsimd", lambda e: e.dma_start(out=g["dqT_h"], in_=qT[0][:]), sem="st_dbg", by=16)
        P.op("gpsimd", lambda e: e.dma_start(out=g["dkT_h"], in_=kT[0][:]), sem="st_dbg", by=16)
        P.op("gpsimd", lambda e: e.dma_start(out=g["dvx_h"], in_=v_ext[0][:]), sem="st_dbg", by=16)
        P.op("gpsimd", lambda e: e.dma_start(out=g["dat_h"], in_=attn[0][:]), sem="st_dbg", by=16)
        P.op("gpsimd", lambda e: e.dma_start(out=g["dk2_h"], in_=k2s[:]), sem="st_dbg", by=16)
        P.wait("gpsimd", "st_dbg", P.counts["st_dbg"])
    P.pop_scope()
    P.barrier()
    P.push_scope()
    woT_res = [P.sb(f"woTr{ic}", [128, D], BF16) for ic in range(ND)]
    xtile2 = [P.sb(f"xtile2{k}", [128, D], BF16) for k in range(2)]
    t1f = P.sb("t1f", [128, D], F32)
    sq_scr_b2 = P.sb("sq_scr_b2", [128, D], F32)
    n2c = P.sb("n2c", [128, 1], F32)
    wres_o = load_wres(g["woT_s"], "o", dst=woT_res)
    P.op("sync", lambda e: e.dma_start(
        out=brow_f[3][:], in_=g["bo_h"].rearrange("(a n) -> a n", a=1)), sem="bld", by=16)
    P.wait("vector", "bld", P.counts["bld"])
    P.op("vector", lambda e: e.tensor_copy(out=brow_b[3][:], in_=brow_f[3][:]), sem="bo_cast")
    P.wait("tensor", "wres", wres_o)
    P.wait("tensor", "attn_ev", attn_total)
    P.wait("tensor", "bo_cast", P.counts["bo_cast"])
    for t in range(NT):
        kx = t % 2
        if t >= 2:
            P.wait("sync", "xx2_done", t - 1)
        vx = P.op("sync", lambda e, t=t, kx=kx: e.dma_start(
            out=xtile2[kx][:], in_=xin_hbm[t * 128:(t + 1) * 128, :]),
            sem=f"xin2_{kx}", by=16)
        banks = [0, 1] if t % 2 == 0 else [2, 3]
        for p in range(8):
            for o2 in range(2):
                bk = banks[o2]
                if p == 0:
                    pool.acquire(bk, "tensor")
                P.op("tensor", lambda e, t=t, p=p, o2=o2, bk=bk: e.matmul(
                    out=ps_mm[bk][:],
                    lhsT=attn[p][:, t * 128:(t + 1) * 128],
                    rhs=woT_res[p][:, o2 * 512:(o2 + 1) * 512],
                    start=(p == 0), stop=False))
        for o2 in range(2):
            bk = banks[o2]
            P.op("tensor", lambda e, t=t, o2=o2, bk=bk: e.matmul(
                out=ps_mm[bk][:],
                lhsT=ones_row[0:1, :],
                rhs=brow_b[3][0:1, o2 * 512:(o2 + 1) * 512],
                start=False, stop=True), sem="mm_o")
        mmo_c = P.counts["mm_o"]
        P.wait("vector", "mm_o", mmo_c)
        P.wait("vector", f"xin2_{kx}", vx)
        for o2 in range(2):
            bk = banks[o2]
            ev = P.op("vector", lambda e, t=t, o2=o2, bk=bk, kx=kx: e.tensor_add(
                out=t1f[:, o2 * 512:(o2 + 1) * 512],
                in0=xtile2[kx][:, o2 * 512:(o2 + 1) * 512],
                in1=ps_mm[bk][:]), sem="t1_done")
            pool.release(bk, "t1_done", ev)
        P.op("vector", lambda e: e.tensor_mul(out=sq_scr_b2[:], in0=t1f[:], in1=t1f[:]))
        nv = P.op("vector", lambda e: e.reduce_sum(
            out=n2c[:], in_=sq_scr_b2[:], axis=AXX), sem="n2_o")
        P.wait("scalar", "n2_o", nv)
        na = P.op("scalar", lambda e: e.activation(
            out=n2c[:], in_=n2c[:], func=AF.Sqrt), sem="sq_o")
        P.wait("vector", "sq_o", na)
        P.op("vector", lambda e: e.tensor_scalar(
            out=n2c[:], in0=n2c[:], scalar1=1e-12, scalar2=None, op0=ALU.max))
        P.op("vector", lambda e: e.reciprocal(out=n2c[:], in_=n2c[:]))
        P.op("vector", lambda e: e.tensor_scalar(
            out=n2c[:], in0=n2c[:], scalar1=MAXN, scalar2=1.0,
            op0=ALU.mult, op1=ALU.min))
        P.op("vector", lambda e, t=t: e.tensor_scalar(
            out=xb[t][:], in0=t1f[:], scalar1=n2c[:, 0:1], scalar2=None,
            op0=ALU.mult), sem="xx2_done")
        P.op("vector", lambda e, t=t: e.reduce_sum(
            out=stat[:, SS + t:SS + t + 1], in_=xb[t][:], axis=AXX), sem="sv")
        P.wait("scalar", "xx2_done", P.counts["xx2_done"])
        P.op("scalar", lambda e, t=t: e.activation(
            out=junk_bf[:], in_=xb[t][:], func=AF.Square,
            accum_out=stat[:, SQ + t:SQ + t + 1]), sem="sa")
        if dbg:
            P.wait("gpsimd", "xx2_done", P.counts["xx2_done"])
            P.op("gpsimd", lambda e, t=t: e.dma_start(
                out=g["dxx2_h"][t * 128:(t + 1) * 128, :], in_=xb[t][:]), sem="st_dbg", by=16)
    P.wait("vector", "sa", P.counts["sa"])
    P.wait("vector", "sv", P.counts["sv"])
    P.op("vector", lambda e: e.tensor_scalar(
        out=stat[:, MU2:MU2 + NT], in0=stat[:, SS:SS + NT], scalar1=1.0 / D,
        scalar2=None, op0=ALU.mult))
    P.op("vector", lambda e: e.tensor_mul(
        out=stat[:, RSTD2:RSTD2 + NT], in0=stat[:, MU2:MU2 + NT],
        in1=stat[:, MU2:MU2 + NT]))
    vm = P.op("vector", lambda e: e.scalar_tensor_tensor(
        out=stat[:, RSTD2:RSTD2 + NT], in0=stat[:, SQ:SQ + NT], scalar=1.0 / D,
        in1=stat[:, RSTD2:RSTD2 + NT], op0=ALU.mult, op1=ALU.subtract), sem="sv")
    P.wait("scalar", "sv", vm)
    va = P.op("scalar", lambda e: e.activation(
        out=stat[:, RSTD2:RSTD2 + NT], in_=stat[:, RSTD2:RSTD2 + NT],
        func=AF.Sqrt, bias=EPS_LN, scale=1.0), sem="sa")
    P.wait("vector", "sa", va)
    P.op("vector", lambda e: e.reciprocal(
        out=stat[:, RSTD2:RSTD2 + NT], in_=stat[:, RSTD2:RSTD2 + NT]), sem="stats2_done")
    if dbg:
        P.wait("gpsimd", "st_dbg", P.counts["st_dbg"])
    P.pop_scope()
    P.pop_scope()
    P.pop_scope()
    P.barrier()


def _emit_l5_tail(P, env, dbg=False):
    g = env
    stat, g_col, ident_sb = g["stat"], g["g_col"], g["ident_sb"]
    ones_row, junk_bf = g["ones_row"], g["junk_bf"]
    ones512 = g["ones512"]
    ps_tr, ps_mm, pool = g["ps_tr"], g["ps_mm"], g["pool"]
    b2g = g["b2g"]
    b2g_bf = g["b2g_bf"]
    MU2, RSTD2, SQ, SS = g["MU2"], g["RSTD2"], g["SQ"], g["SS"]
    MU0, RSTD0 = g["MU0"], g["RSTD0"]
    xb = g["xb"]
    bf1col, fb1col = g["bf1col"], g["fb1col"]
    x2_hbm, x3_hbm, out_h = g["x2_hbm"], g["x3_hbm"], g["out_h"]
    MAXN = 1.0 - 1e-05
    NH = 32

    def ffn_block(tag, w1T_s, w2T_s, bias1_h, bias2_h, bcol, quant, src_fn,
                  out_cb, bufs):
        xnT_l = bufs["xnT_l"]; hT = bufs["hT"]; slab = bufs["slab"]
        w2ch = bufs["w2ch"]; brow4 = bufs["brow4"]; brow4b = bufs["brow4b"]
        b2row = bufs["b2row"]; row128 = bufs["row128"]
        b1bf = bufs.get("b1bf")
        tb_l = bufs["tb_l"]; ta_l = bufs["ta_l"]; hf32 = bufs.get("hf32")
        """h = act(xn@w1'.T + b1eff) ; out tiles = h@w2'.T + b2eff -> out_cb.

        quant=False: act = exact gelu; w1 gain-folded (g2); b1eff = bf1 + (b2/g2)@w1g.
        quant=True: act = relu(fixed_quant(.,8,4)); biases fixed_quant(8,6)."""
        last_hqrd = 0
        base_tr = P.counts.get("trCf", 0)
        ntr = 0
        for i in range(NT):
            for c in range(ND):
                kk = ntr % 2
                if ntr >= 2:
                    P.wait("tensor", "trCf", base_tr + ntr - 1)
                P.op("tensor", lambda e, i=i, c=c, kk=kk: e.transpose(
                    out=ps_tr[kk][:, 0:128], in_=src_fn(i, c),
                    identity=ident_sb[:]), sem="trp")
                P.wait("scalar", "trp", P.counts["trp"])
                P.op("scalar", lambda e, i=i, c=c, kk=kk: e.copy(
                    out=xnT_l[c][:, i * 128:(i + 1) * 128], in_=ps_tr[kk][:, 0:128]),
                    sem="trCf")
                ntr += 1
        P.wait("tensor", "trCf", P.counts["trCf"])
        P.op("sync", lambda e, bias1_h=bias1_h: e.dma_start(
            out=brow4[:], in_=bias1_h.rearrange("(a n) -> a n", a=1)), sem="bld4", by=16)
        bld4_c = P.counts["bld4"]
        b1q_c = 0
        if quant:
            P.wait("vector", "bld4", bld4_c)
            for hseg in range(4):
                sl_ = slice(hseg * D, (hseg + 1) * D)
                def pre(tbx, sl_=sl_):
                    P.op("vector", lambda e, sl_=sl_: e.tensor_scalar(
                        out=tb_l[0:1, 0:D], in0=brow4[0:1, sl_], scalar1=64.0,
                        scalar2=None, op0=ALU.mult))
                def fin(tax, sl_=sl_):
                    P.op("vector", lambda e: e.tensor_scalar(
                        out=ta_l[0:1, 0:D], in0=ta_l[0:1, 0:D], scalar1=127.0,
                        scalar2=-128.0, op0=ALU.min, op1=ALU.max))
                    return P.op("vector", lambda e, sl_=sl_: e.tensor_scalar(
                        out=brow4[0:1, sl_], in0=ta_l[0:1, 0:D], scalar1=1.0 / 64.0,
                        scalar2=None, op0=ALU.mult), sem="b1q_done")
                tbv = type("S", (), {"__getitem__": lambda s_, k_: tb_l[0:1, 0:D]})()
                tav = type("S", (), {"__getitem__": lambda s_, k_: ta_l[0:1, 0:D]})()
                emit_round(P, fin, None, tbv, tav, g_col, pre=pre)
            b1q_c = P.counts["b1q_done"]
            P.wait("vector", "b1q_done", b1q_c)
            bq = P.op("vector", lambda e: e.tensor_copy(
                out=b1bf[0:1, 0:4 * D], in_=brow4[:]), sem="b1bf_c")
            P.wait("tensor", "b1bf_c", bq)
        for th in range(2):
            mm1_hist = []
            for r in range(NH):
                sl = r % 2
                if len(mm1_hist) >= 2:
                    P.wait("sync", "mm1f", mm1_hist[-2])
                P.op("sync", lambda e, r=r, sl=sl, w1T_s=w1T_s: e.dma_start(
                    out=slab[sl][:],
                    in_=w1T_s[:, r * ND * 128:(r + 1) * ND * 128]),
                    sem=f"slab_ld{sl}", by=16)
                slab_c = P.counts[f"slab_ld{sl}"]
                if not quant:
                    P.wait("tensor", f"slab_ld{sl}", slab_c)
                    pool.acquire(4, "tensor")
                    for ic in range(ND):
                        P.op("tensor", lambda e, ic=ic, sl=sl: e.matmul(
                            out=ps_mm[4][0:1, 0:128],
                            lhsT=b2g_bf[:, ic:ic + 1],
                            rhs=slab[sl][:, ic * 128:(ic + 1) * 128],
                            start=(ic == 0), stop=(ic == ND - 1)),
                            sem="bmv4" if ic == ND - 1 else None)
                    bmv4_c = P.counts["bmv4"]
                    P.wait("vector", "bmv4", bmv4_c)
                    P.wait("vector", "bld4", bld4_c)
                    if mm1_hist:
                        P.wait("vector", "mm1f", mm1_hist[-1])
                    bv = P.op("vector", lambda e, r=r: e.tensor_add(
                        out=row128[:], in0=ps_mm[4][0:1, 0:128],
                        in1=brow4[0:1, r * 128:(r + 1) * 128]), sem="brow_v")
                    pool.release(4, "brow_v", bv)
                else:
                    bv = None
                    P.wait("tensor", f"slab_ld{sl}", slab_c)
                for tt in range(2):
                    bk = tt
                    pool.acquire(bk, "tensor")
                    for ic in range(ND):
                        P.op("tensor", lambda e, r=r, ic=ic, tt=tt, bk=bk, sl=sl, th=th: e.matmul(
                            out=ps_mm[bk][:],
                            lhsT=slab[sl][:, ic * 128:(ic + 1) * 128],
                            rhs=xnT_l[ic][:, th * 1024 + tt * 512:th * 1024 + (tt + 1) * 512],
                            start=(ic == 0), stop=False))
                    if bv is not None:
                        P.wait("tensor", "brow_v", bv)
                        P.op("tensor", lambda e, bk=bk: e.matmul(
                            out=ps_mm[bk][:], lhsT=row128[0:1, :], rhs=ones512[0:1, :],
                            start=False, stop=True), sem="mm1f")
                    else:
                        P.op("tensor", lambda e, bk=bk, r=r: e.matmul(
                            out=ps_mm[bk][:], lhsT=b1bf[0:1, r * 128:(r + 1) * 128],
                            rhs=ones512[0:1, :], start=False, stop=True), sem="mm1f")
                mm1_c = P.counts["mm1f"]
                mm1_hist.append(mm1_c)
                if quant and last_hqrd:
                    P.wait("scalar", "hq_rd", last_hqrd)
                for tt in range(2):
                    bk = tt
                    P.wait("scalar", "mm1f", mm1_c)
                    if not quant:
                        ev = P.op("scalar", lambda e, r=r, tt=tt, bk=bk: e.activation(
                            out=hT[r][:, tt * 512:(tt + 1) * 512], in_=ps_mm[bk][:],
                            func=AF.Gelu, bias=0.0, scale=1.0), sem="ev1f")
                        pool.release(bk, "ev1f", ev)
                    else:
                        ev = P.op("scalar", lambda e, r=r, tt=tt, bk=bk: e.copy(
                            out=hf32[:, tt * 512:(tt + 1) * 512], in_=ps_mm[bk][:]),
                            sem="ev1f")
                        pool.release(bk, "ev1f", ev)
                if quant:
                    P.wait("vector", "ev1f", P.counts["ev1f"])
                    def pre(tbx):
                        P.op("vector", lambda e: e.tensor_scalar(
                            out=tb_l[:], in0=hf32[:], scalar1=16.0, scalar2=None,
                            op0=ALU.mult), sem="hq_rd")
                    def fin(tax, r=r):
                        P.op("vector", lambda e: e.tensor_scalar(
                            out=ta_l[:], in0=ta_l[:], scalar1=127.0, scalar2=-128.0,
                            op0=ALU.min, op1=ALU.max))
                        return P.op("vector", lambda e, r=r: e.tensor_scalar(
                            out=hT[r][:], in0=ta_l[:], scalar1=0.0, scalar2=1.0 / 16.0,
                            op0=ALU.max, op1=ALU.mult), sem="hq_done")
                    emit_round(P, fin, None, tb_l, ta_l, g_col, pre=pre)
                    last_hqrd = P.counts["hq_rd"]
            if th == 0:
                bias_consumed = P.counts["brow_v"]
                P.wait("sync", "brow_v", bias_consumed)
                P.op("sync", lambda e, bias2_h=bias2_h: e.dma_start(
                    out=b2row[:], in_=bias2_h.rearrange("(a n) -> a n", a=1)),
                    sem="bld4", by=16)
                bld4b = P.counts["bld4"]
                P.wait("vector", "bld4", bld4b)
                if quant:
                    def pre(tbx):
                        P.op("vector", lambda e: e.tensor_scalar(
                            out=tb_l[0:1, 0:D], in0=b2row[:], scalar1=64.0,
                            scalar2=None, op0=ALU.mult))
                    def fin(tax):
                        P.op("vector", lambda e: e.tensor_scalar(
                            out=ta_l[0:1, 0:D], in0=ta_l[0:1, 0:D], scalar1=127.0,
                            scalar2=-128.0, op0=ALU.min, op1=ALU.max))
                        return P.op("vector", lambda e: e.tensor_scalar(
                            out=b2row[:], in0=ta_l[0:1, 0:D], scalar1=1.0 / 64.0,
                            scalar2=None, op0=ALU.mult), sem="b1q_done")
                    tbv = type("S", (), {"__getitem__": lambda s_, k_: tb_l[0:1, 0:D]})()
                    tav = type("S", (), {"__getitem__": lambda s_, k_: ta_l[0:1, 0:D]})()
                    emit_round(P, fin, None, tbv, tav, g_col, pre=pre)
                bb = P.op("vector", lambda e: e.tensor_copy(
                    out=brow4b[0:1, 0:D], in_=b2row[:]), sem="brow_v")
                P.wait("tensor", "brow_v", bb)
            P.wait("tensor", "hq_done" if quant else "ev1f",
                   P.counts["hq_done" if quant else "ev1f"])
            w2_hist = []
            rounds = [(0, 3), (3, 3), (6, 2)]
            for rnd, (t0, ntile) in enumerate(rounds):
                for ic in range(NH):
                    sl = ic % 2
                    if len(w2_hist) >= 2:
                        P.wait("sync", "w2mmf", w2_hist[-2])
                    P.op("sync", lambda e, ic=ic, sl=sl, w2T_s=w2T_s: e.dma_start(
                        out=w2ch[sl][:],
                        in_=w2T_s[:, ic * ND * 128:(ic + 1) * ND * 128]),
                        sem=f"w2_ld{sl}", by=16)
                    w2c = P.counts[f"w2_ld{sl}"]
                    P.wait("tensor", f"w2_ld{sl}", w2c)
                    for tt in range(ntile):
                        for o2 in range(2):
                            bk = tt * 2 + o2
                            if ic == 0:
                                pool.acquire(bk, "tensor")
                            P.op("tensor", lambda e, ic=ic, tt=tt, o2=o2, bk=bk, sl=sl, t0=t0: e.matmul(
                                out=ps_mm[bk][:],
                                lhsT=hT[ic][:, (t0 + tt) * 128:(t0 + tt + 1) * 128],
                                rhs=w2ch[sl][:, o2 * 512:(o2 + 1) * 512],
                                start=(ic == 0), stop=False),
                                sem="w2mmf" if (tt == ntile - 1 and o2 == 1) else None)
                    w2_hist.append(P.counts["w2mmf"])
                for tt in range(ntile):
                    for o2 in range(2):
                        bk = tt * 2 + o2
                        P.op("tensor", lambda e, tt=tt, o2=o2, bk=bk: e.matmul(
                            out=ps_mm[bk][:],
                            lhsT=ones_row[0:1, :],
                            rhs=brow4b[0:1, o2 * 512:(o2 + 1) * 512],
                            start=False, stop=True), sem="mm2f")
                mm2_c = P.counts["mm2f"]
                for tt in range(ntile):
                    out_cb(th * 8 + t0 + tt, [tt * 2, tt * 2 + 1], mm2_c)

    P.push_scope()
    xnT_l = [P.sb(f"xnTl{c}", [128, T], BF16) for c in range(ND)]
    hT = [P.sb(f"hTl{r}", [128, T // 2], BF16) for r in range(NH)]
    slab = [P.sb(f"slab{k}", [128, ND * 128], BF16) for k in range(2)]
    w2ch = [P.sb(f"w2ch{k}", [128, D], BF16) for k in range(2)]
    brow4 = P.sb("brow4", [1, 4 * D], F32)
    brow4b = P.sb("brow4b", [1, D], BF16)
    b2row = P.sb("b2row", [1, D], F32)
    row128 = P.sb("row128", [1, 128], BF16)
    tb_l = P.sb("tb_l", [1, 128], F32)
    ta_l = P.sb("ta_l", [1, 128], F32)
    t1f = P.sb("t1fC", [128, D], F32)
    sqs = P.sb("sqsC", [128, D], F32)
    n2c = P.sb("n2cC", [128, 1], F32)
    xt32 = [P.sb(f"xt32{k}", [128, D], F32) for k in range(2)]
    x3t = P.sb("x3t", [128, D], F32)

    P.wait("vector", "stats2_done", P.counts["stats2_done"])
    for i in range(NT):
        P.op("vector", lambda e, i=i: e.tensor_scalar(
            out=hT[i][:], in0=xb[i][:],
            scalar1=stat[:, MU2 + i:MU2 + i + 1], scalar2=stat[:, RSTD2 + i:RSTD2 + i + 1],
            op0=ALU.subtract, op1=ALU.mult), sem="xnC_done")
    P.wait("tensor", "xnC_done", P.counts["xnC_done"])

    def srcC(i, c):
        return hT[i][:, c * 128:(c + 1) * 128]

    xg_C = {0: 0, 1: 0}

    def out_cb_C(t, banks, mm2_c):
        kx = t % 2
        if xg_C[kx]:
            P.wait("sync", "sv", xg_C[kx])
        vx = P.op("sync", lambda e, t=t, kx=kx: e.dma_start(
            out=xt32[kx][:], in_=x2_hbm[t * 128:(t + 1) * 128, :]), sem=f"xld{kx}", by=16)
        P.wait("vector", f"mm2f", mm2_c)
        if dbg and t == 0:
            for o2 in range(2):
                P.op("vector", lambda e, o2=o2, bk=banks[o2]: e.tensor_copy(
                    out=sqs[:, o2 * 512:(o2 + 1) * 512], in_=ps_mm[bk][:]), sem="dbgC")
            P.wait("gpsimd", "dbgC", P.counts["dbgC"])
            P.op("gpsimd", lambda e: e.dma_start(out=g["dcps_h"], in_=sqs[:]),
                 sem="st_dbg", by=16)
            P.wait("vector", "st_dbg", P.counts["st_dbg"])
        for o2 in range(2):
            bk = banks[o2]
            ev = P.op("vector", lambda e, t=t, o2=o2, bk=bk: e.tensor_add(
                out=t1f[:, o2 * 512:(o2 + 1) * 512],
                in0=xb[t][:, o2 * 512:(o2 + 1) * 512],
                in1=ps_mm[bk][:]), sem="t1C")
            pool.release(bk, "t1C", ev)
        if dbg and t == 0:
            P.wait("gpsimd", "t1C", P.counts["t1C"])
            P.op("gpsimd", lambda e: e.dma_start(out=g["dct1_h"], in_=t1f[:]),
                 sem="st_dbg", by=16)
            P.wait("vector", "st_dbg", P.counts["st_dbg"])
        P.op("vector", lambda e: e.tensor_mul(out=sqs[:], in0=t1f[:], in1=t1f[:]))
        nv = P.op("vector", lambda e: e.reduce_sum(out=n2c[:], in_=sqs[:], axis=AXX), sem="n2_qk")
        P.wait("scalar", "n2_qk", nv)
        na = P.op("scalar", lambda e: e.activation(out=n2c[:], in_=n2c[:], func=AF.Sqrt), sem="sq_qk")
        P.wait("vector", "sq_qk", na)
        P.op("vector", lambda e: e.tensor_scalar(
            out=n2c[:], in0=n2c[:], scalar1=1e-12, scalar2=None, op0=ALU.max))
        P.op("vector", lambda e: e.reciprocal(out=n2c[:], in_=n2c[:]))
        P.op("vector", lambda e: e.tensor_scalar(
            out=n2c[:], in0=n2c[:], scalar1=MAXN, scalar2=1.0, op0=ALU.mult, op1=ALU.min))
        if dbg and t == 0:
            P.op("vector", lambda e: e.tensor_copy(out=sqs[:, 0:1], in_=n2c[:]), sem="dbgC")
            P.wait("gpsimd", "dbgC", P.counts["dbgC"])
            P.op("gpsimd", lambda e: e.dma_start(
                out=g["dcn2_h"][:], in_=sqs[:, 0:4]), sem="st_dbg", by=16)
            P.wait("vector", "st_dbg", P.counts["st_dbg"])
        P.wait("vector", f"xld{kx}", vx)
        P.op("vector", lambda e, t=t: e.scalar_tensor_tensor(
            out=x3t[:], in0=t1f[:], scalar=n2c[:, 0:1], in1=xt32[kx][:],
            op0=ALU.mult, op1=ALU.add))
        P.op("vector", lambda e, t=t: e.reduce_sum(
            out=stat[:, SS + t:SS + t + 1], in_=x3t[:], axis=AXX), sem="sv")
        sv_c = P.counts["sv"]
        xg_C[kx] = sv_c
        P.wait("scalar", "sv", sv_c)
        P.op("scalar", lambda e, t=t: e.activation(
            out=junk_bf[:], in_=x3t[:], func=AF.Square,
            accum_out=stat[:, SQ + t:SQ + t + 1]), sem="sa")
        P.wait("gpsimd", "sa", P.counts["sa"])
        P.wait("gpsimd", "st_x3", P.counts.get("st_x3", 0))
        P.op("gpsimd", lambda e, t=t: e.dma_start(
            out=x3_hbm[t * 128:(t + 1) * 128, :], in_=x3t[:]), sem="st_x3", by=16)
        P.wait("vector", "st_x3", P.counts["st_x3"])

    ffn_block("C", g["w1T_s"], g["w2T_s"], g["bf1_h"], g["bf2_h"], bf1col,
              False, srcC, out_cb_C,
              dict(xnT_l=xnT_l, hT=hT, slab=slab, w2ch=w2ch, brow4=brow4,
                   brow4b=brow4b, b2row=b2row, row128=row128, tb_l=tb_l,
                   ta_l=ta_l))
    P.wait("gpsimd", "st_x3", P.counts["st_x3"])
    P.pop_scope()
    P.barrier()

    P.push_scope()
    xnT_ld = [P.sb(f"xnTd{c}", [128, T], BF16) for c in range(ND)]
    hTd = [P.sb(f"hTd{r}", [128, T // 2], BF16) for r in range(NH)]
    slabd = [P.sb(f"slabd{k}", [128, ND * 128], BF16) for k in range(2)]
    w2chd = [P.sb(f"w2chd{k}", [128, D], BF16) for k in range(2)]
    brow4d = P.sb("brow4d", [1, 4 * D], F32)
    brow4bd = P.sb("brow4bd", [1, D], BF16)
    b1bfd = P.sb("b1bfd", [1, 4 * D], BF16)
    b2rowd = P.sb("b2rowd", [1, D], F32)
    row128d = P.sb("row128d", [1, 128], F32)
    tb_ld = P.sb("tb_ld", [128, T // 2], F32)
    ta_ld = P.sb("ta_ld", [128, T // 2], F32)
    hf32d = P.sb("hf32d", [128, T // 2], F32)
    xt32d = [P.sb(f"xt32d{k}", [128, D], F32) for k in range(2)]
    outt = P.sb("outt", [128, D], F32)

    P.op("vector", lambda e: e.tensor_scalar(
        out=stat[:, MU0:MU0 + NT], in0=stat[:, SS:SS + NT], scalar1=1.0 / D,
        scalar2=None, op0=ALU.mult))
    P.op("vector", lambda e: e.tensor_mul(
        out=stat[:, RSTD0:RSTD0 + NT], in0=stat[:, MU0:MU0 + NT],
        in1=stat[:, MU0:MU0 + NT]))
    vm = P.op("vector", lambda e: e.scalar_tensor_tensor(
        out=stat[:, RSTD0:RSTD0 + NT], in0=stat[:, SQ:SQ + NT], scalar=1.0 / D,
        in1=stat[:, RSTD0:RSTD0 + NT], op0=ALU.mult, op1=ALU.subtract), sem="sv")
    P.wait("scalar", "sv", vm)
    va = P.op("scalar", lambda e: e.activation(
        out=stat[:, RSTD0:RSTD0 + NT], in_=stat[:, RSTD0:RSTD0 + NT],
        func=AF.Sqrt, bias=EPS_LN, scale=1.0), sem="sa")
    P.wait("vector", "sa", va)
    P.op("vector", lambda e: e.reciprocal(
        out=stat[:, RSTD0:RSTD0 + NT], in_=stat[:, RSTD0:RSTD0 + NT]), sem="stats2_done")
    P.wait("vector", "stats2_done", P.counts["stats2_done"])
    base_xn = P.counts.get("xnC_done", 0)
    for i in range(NT):
        kx = i % 2
        if i >= 2:
            P.wait("sync", "xnC_done", base_xn + i - 1)
        vx = P.op("sync", lambda e, i=i, kx=kx: e.dma_start(
            out=xt32d[kx][:], in_=x3_hbm[i * 128:(i + 1) * 128, :]), sem=f"xld{kx}", by=16)
        P.wait("vector", f"xld{kx}", vx)
        def pre(tbx, i=i, kx=kx):
            P.op("vector", lambda e, i=i, kx=kx, tbx=tbx: e.tensor_scalar(
                out=tbx[:], in0=xt32d[kx][:],
                scalar1=stat[:, MU0 + i:MU0 + i + 1], scalar2=stat[:, RSTD0 + i:RSTD0 + i + 1],
                op0=ALU.subtract, op1=ALU.mult))
            P.op("vector", lambda e, tbx=tbx: e.tensor_scalar(
                out=tbx[:], in0=tbx[:], scalar1=16.0, scalar2=None, op0=ALU.mult))
        def fin(tax, i=i):
            P.op("vector", lambda e, tax=tax: e.tensor_scalar(
                out=tax[:], in0=tax[:], scalar1=127.0, scalar2=-128.0,
                op0=ALU.min, op1=ALU.max))
            return P.op("vector", lambda e, tax=tax, i=i: e.tensor_scalar(
                out=hTd[i][:], in0=tax[:], scalar1=1.0 / 16.0, scalar2=None,
                op0=ALU.mult), sem="xnC_done")
        emit_round(P, fin, None, tb_ld, ta_ld, g_col, pre=pre)
    P.wait("tensor", "xnC_done", P.counts["xnC_done"])

    xg_D = {0: 0, 1: 0}

    def out_cb_D(t, banks, mm2_c):
        kx = t % 2
        if xg_D[kx]:
            P.wait("sync", "t1C", xg_D[kx])
        vx = P.op("sync", lambda e, t=t, kx=kx: e.dma_start(
            out=xt32d[kx][:], in_=x3_hbm[t * 128:(t + 1) * 128, :]), sem=f"xld{kx}", by=16)
        P.wait("vector", f"mm2f", mm2_c)
        P.wait("vector", f"xld{kx}", vx)
        for o2 in range(2):
            bk = banks[o2]
            ev = P.op("vector", lambda e, t=t, o2=o2, bk=bk, kx=kx: e.tensor_add(
                out=outt[:, o2 * 512:(o2 + 1) * 512],
                in0=xt32d[kx][:, o2 * 512:(o2 + 1) * 512],
                in1=ps_mm[bk][:]), sem="t1C")
            pool.release(bk, "t1C", ev)
        xg_D[kx] = P.counts["t1C"]
        P.wait("gpsimd", "t1C", P.counts["t1C"])
        P.wait("gpsimd", "st_x3", P.counts.get("st_x3", 0))
        P.op("gpsimd", lambda e, t=t: e.dma_start(
            out=out_h[t * 128:(t + 1) * 128, :], in_=outt[:]), sem="st_x3", by=16)
        P.wait("vector", "st_x3", P.counts["st_x3"])

    ffn_block("D", g["fw1T_s"], g["fw2T_s"], g["fb1_h"], g["fb2_h"], fb1col,
              True, (lambda i, c, hTd=hTd: hTd[i][:, c * 128:(c + 1) * 128]), out_cb_D,
              dict(xnT_l=xnT_ld, hT=hTd, slab=slabd, w2ch=w2chd, brow4=brow4d,
                   brow4b=brow4bd, b2row=b2rowd, row128=row128d, tb_l=tb_ld,
                   ta_l=ta_ld, hf32=hf32d, b1bf=b1bfd))
    if dbg:
        dh_h = P.nc.dram_tensor("d_hD", [128, 1024], BF16, kind="ExternalOutput").ap()
        db_h = P.nc.dram_tensor("d_b1bf", [1, 4096], BF16, kind="ExternalOutput").ap()
        dxn_h = P.nc.dram_tensor("d_xnD", [128, T], BF16, kind="ExternalOutput").ap()
        P.wait("gpsimd", "hq_done", P.counts["hq_done"])
        P.op("gpsimd", lambda e: e.dma_start(out=dh_h, in_=hTd[0][:]), sem="st_dbg", by=16)
        P.op("gpsimd", lambda e: e.dma_start(out=db_h, in_=b1bfd[0:1, :]), sem="st_dbg", by=16)
        P.op("gpsimd", lambda e: e.dma_start(out=dxn_h, in_=xnT_ld[0][:]), sem="st_dbg", by=16)
        P.wait("gpsimd", "st_dbg", P.counts["st_dbg"])
    P.wait("gpsimd", "st_x3", P.counts["st_x3"])
    P.pop_scope()


# version 04
# speedup vs baseline: 1.4960x; 1.0190x over previous
import numpy as np
import ml_dtypes
from contextlib import ExitStack

import concourse.bass as bass
import concourse.mybir as mybir
from concourse.bass_utils import run_bass_kernel_spmd

F32 = mybir.dt.float32
BF16 = mybir.dt.bfloat16
AF = mybir.ActivationFunctionType
ALU = mybir.AluOpType
AXX = mybir.AxisListType.X

T = 2048
D = 1024
NT = T // 128
ND = D // 128
RND_C = float(3 * 2.0 ** 22)
QMAX16 = 32767.0
EPS_LN = 1e-5

ENGINES = ("sync", "vector", "scalar", "tensor", "gpsimd")


class Prog:

    def __init__(self, nc):
        self.nc = nc
        self.q = {e: [] for e in ENGINES}
        self.ctx = ExitStack()
        self.sems = {}
        self.counts = {}
        self.const_vals = [0.0, 1.0]
        self.const_sb = self.nc.alloc_sbuf_tensor("const_sb", [128, 16], F32)

    def want_const(self, *vals):
        for v in vals:
            v = float(v)
            if v not in self.const_vals:
                self.const_vals.append(v)

    def const_ap(self, v):
        v = float(v)
        assert self.const_sb is not None, "const_ap only valid inside thunks (finish)"
        i = self.const_vals.index(v)
        return self.const_sb[:, i:i + 1]

    def sem(self, name):
        if name not in self.sems:
            self.sems[name] = self.ctx.enter_context(self.nc.semaphore(name))
            self.counts[name] = 0
        return self.sems[name]

    SELF_SYNC = ("vector", "scalar")

    def op(self, engine, fn, sem=None, by=1, tick=True):
        do_tick = tick and engine in self.SELF_SYNC
        if do_tick:
            prev = getattr(self, "_last_evt", {}).get(engine)
            if prev is not None:
                self.wait(engine, prev[0], prev[1])
        ret = None
        if sem is not None:
            s = self.sem(sem)
            self.counts[sem] += by
            ret = self.counts[sem]
            self.q[engine].append(lambda e, fn=fn, s=s, by=by: fn(e).then_inc(s, by))
            evt = (sem, ret)
        else:
            tname = f"tick_{engine}"
            if do_tick:
                ts = self.sem(tname)
                self.counts[tname] += 1
                tcur = self.counts[tname]
                self.q[engine].append(lambda e, fn=fn, ts=ts: fn(e).then_inc(ts, 1))
                evt = (tname, tcur)
            else:
                self.q[engine].append(fn)
                evt = None
        if do_tick and evt is not None:
            if not hasattr(self, "_last_evt"):
                self._last_evt = {}
            self._last_evt[engine] = evt
        return ret

    def wait(self, engine, sem, value):
        s = self.sem(sem)
        self.q[engine].append(lambda e, s=s, value=value: e.wait_ge(s, value))

    def barrier(self, engines=ENGINES):
        name = "bar"
        for eng in engines:
            self.op(eng, lambda e: e.nop(), sem=name)
        tot = self.counts[name]
        for eng in engines:
            self.wait(eng, name, tot)

    def wait_cur(self, engine, sem):
        self.wait(engine, sem, self.counts[sem])

    def sb(self, name, shape, dtype=F32, **kw):
        if getattr(self, "_scopes", None):
            return self._scopes[-1].enter_context(
                self.nc.sbuf_tensor(name, shape, dtype, **kw))
        return self.nc.alloc_sbuf_tensor(name, shape, dtype, **kw)

    def push_scope(self):
        if not hasattr(self, "_scopes"):
            self._scopes = []
        self._scopes.append(ExitStack())

    def pop_scope(self):
        self._scopes.pop().close()

    def psum(self, name, shape, dtype=F32):
        return self.nc.alloc_psum_tensor(name, shape, dtype)

    def finish(self):
        assert len(self.const_vals) <= 16
        vals = np.zeros(16, dtype=np.float32)
        vals[:len(self.const_vals)] = self.const_vals
        ctab = np.broadcast_to(vals, (128, 16)).copy()
        ctab_h = self.nc.inline_tensor(ctab, "const_tab").ap()
        for i, v in enumerate(self.const_vals):
            self.nc.const_aps.register(F32, float(v), self.const_sb[:, i:i + 1])
        s = self.sem("ld_const")
        self.q["sync"].insert(0, lambda e, s=s, h=ctab_h, d=self.const_sb: e.dma_start(
            out=d[:], in_=h).then_inc(s, 16))
        for eng in ENGINES:
            if self.q[eng] and eng != "sync":
                self.q[eng].insert(0, lambda e, s=s: e.wait_ge(s, 16))
        blk = self.ctx.enter_context(self.nc.Block())
        for eng in ENGINES:
            thunks = self.q[eng]
            if not thunks:
                continue
            getattr(blk, eng)(lambda e, th=thunks: [f(e) for f in th])
        self.ctx.close()
        return self.nc


class PsumPool:

    def __init__(self, P, banks):
        self.P = P
        self.banks = banks
        self.free_at = [None] * len(banks)

    def acquire(self, idx, engine):
        fa = self.free_at[idx]
        if fa is not None:
            self.P.wait(engine, fa[0], fa[1])
            self.free_at[idx] = None
        return self.banks[idx]

    def release(self, idx, sem, count):
        self.free_at[idx] = (sem, count)


def new_nc():
    return bass.Bass("TRN2", target_bir_lowering=False, detect_race_conditions=False)


def load_colvec(P, src_ap, dst_col, n_chunks, ld_sem, queue="scalar"):
    src = src_ap
    if len(src.shape) == 1:
        src = src.rearrange("(a n) -> a n", a=1)
    total = None
    for c in range(n_chunks):
        total = P.op(queue,
                     lambda e, c=c, src=src: e.dma_start(
                         out=dst_col[:, c:c + 1],
                         in_=src[0:1, c * 128:(c + 1) * 128].rearrange("a p -> p a")),
                     sem=ld_sem, by=16, tick=False)
    return total


def emit_ln_stats(P, x_tiles, mu, rstd, junk_bf, sumsq, ssum, vs, as_, eps=EPS_LN):
    for i in range(NT):
        P.op("vector", lambda e, i=i: e.reduce_sum(
            out=ssum[:, i:i + 1], in_=x_tiles[i][:], axis=AXX), sem=vs)
        P.op("scalar", lambda e, i=i: e.activation(
            out=junk_bf[:], in_=x_tiles[i][:], func=AF.Square,
            accum_out=sumsq[:, i:i + 1]), sem=as_)
    P.wait_cur("vector", as_)
    v = P.op("vector", lambda e: e.tensor_scalar(
        out=mu[:], in0=ssum[:], scalar1=1.0 / D, scalar2=None, op0=ALU.mult), sem=vs)
    P.wait("vector", vs, v)
    v = P.op("vector", lambda e: e.tensor_mul(out=rstd[:], in0=mu[:], in1=mu[:]), sem=vs)
    P.wait("vector", vs, v)
    v_mid = P.op("vector", lambda e: e.scalar_tensor_tensor(
        out=rstd[:], in0=sumsq[:], scalar=1.0 / D, in1=rstd[:],
        op0=ALU.mult, op1=ALU.subtract), sem=vs)
    P.wait("vector", vs, v_mid)
    P.wait("scalar", vs, v_mid)
    a_sqrt = P.op("scalar", lambda e: e.activation(
        out=rstd[:], in_=rstd[:], func=AF.Sqrt, bias=float(eps), scale=1.0), sem=as_)
    P.wait("vector", as_, a_sqrt)
    done = P.op("vector", lambda e: e.reciprocal(out=rstd[:], in_=rstd[:]), sem=vs)
    return done


def emit_partition_absmax(P, col_ap, row_sb, out_sb, vs, dsem, tag):
    d = P.op("gpsimd", lambda e: e.dma_start(out=row_sb[:], in_=col_ap), sem=dsem, by=16)
    P.wait("vector", dsem, d)
    return P.op("vector", lambda e: e.tensor_reduce(
        out=out_sb[:], in_=row_sb[:], axis=AXX, op=ALU.max), sem=vs)



def build_launch1(dbg=False, bars=()):
    nc = new_nc()
    P = Prog(nc)
    P.want_const(EPS_LN)
    ident_h = nc.inline_tensor(np.eye(128, dtype=ml_dtypes.bfloat16), "ident_bf16").ap()

    x_h = nc.dram_tensor("x", [T, D], F32, kind="ExternalInput").ap()
    win_h = nc.dram_tensor("W_in", [D, 2 * D], F32, kind="ExternalInput").ap()
    bin_h = nc.dram_tensor("b_in", [2 * D], F32, kind="ExternalInput").ap()
    a_h = nc.dram_tensor("A", [D], F32, kind="ExternalInput").ap()
    c_h = nc.dram_tensor("C", [D], F32, kind="ExternalInput").ap()
    wout_h = nc.dram_tensor("W_out", [D, D], F32, kind="ExternalInput").ap()
    bout_h = nc.dram_tensor("b_out", [D], F32, kind="ExternalInput").ap()

    x1_h = nc.dram_tensor("x1", [T, D], F32, kind="ExternalOutput").ap()
    if dbg:
        d_xn0_h = nc.dram_tensor("d_xn0", [128, T], BF16, kind="ExternalOutput").ap()
        d_x0_h = nc.dram_tensor("d_x0", [128, D], F32, kind="ExternalOutput").ap()
        d_st_h = nc.dram_tensor("d_st", [128, 4 * NT], F32, kind="ExternalOutput").ap()
        d_yall_h = [nc.dram_tensor(f"d_yall{c}", [128, T], BF16, kind="ExternalOutput").ap()
                    for c in range(ND)]
        d_xnT_h = nc.dram_tensor("d_xnT", [128, T], BF16, kind="ExternalOutput").ap()
        d_v0_h = nc.dram_tensor("d_v0", [128, T], BF16, kind="ExternalOutput").ap()
        d_h0_h = nc.dram_tensor("d_h0", [128, T], BF16, kind="ExternalOutput").ap()
        d_g0_h = nc.dram_tensor("d_g0", [128, T], BF16, kind="ExternalOutput").ap()
        d_y0_h = nc.dram_tensor("d_y0", [128, T], BF16, kind="ExternalOutput").ap()
    xn1_h = nc.dram_tensor("xn1", [T, D], BF16, kind="ExternalOutput").ap()
    m1_h = nc.dram_tensor("m1", [1, 1], F32, kind="ExternalOutput").ap()

    x_sb = [P.sb(f"xt{i}", [128, D], F32) for i in range(NT)]
    xnT = [P.sb(f"xnT{c}", [128, T], BF16) for c in range(ND)]
    big8 = [P.sb(f"big8_{c}", [128, T], BF16) for c in range(ND)]
    vtmp = [P.sb(f"vtmp{k}", [128, T], BF16) for k in range(2)]
    htmp = [P.sb(f"htmp{k}", [128, T], BF16) for k in range(2)]
    gtmp = [P.sb(f"gtmp{k}", [128, T], BF16) for k in range(2)]
    wstage = [P.sb(f"wstage{k}", [128, D], F32) for k in range(2)]
    wbf = [P.sb(f"wbf{c}", [128, 2 * D], BF16) for c in range(ND)]
    ident_sb = P.sb("ident_sb", [128, 128], BF16)
    bin_col = P.sb("bin_col", [128, 2 * ND], F32)
    dec_col = P.sb("dec_col", [128, ND], F32)
    c_col = P.sb("c_col", [128, ND], F32)
    bout_row = P.sb("bout_row", [1, D], F32)
    bout_row_bf = P.sb("bout_row_bf", [1, D], BF16)
    ones_row_bf = P.sb("ones_row_bf", [1, 128], BF16)
    stat_mu = P.sb("stat_mu", [128, NT], F32)
    stat_rstd = P.sb("stat_rstd", [128, NT], F32)
    stat_ssum = P.sb("stat_ssum", [128, NT], F32)
    stat_sumsq = P.sb("stat_sumsq", [128, NT], F32)
    junk_bf = P.sb("junk_bf", [128, D], BF16)
    amax_cols = P.sb("amax_cols", [128, NT], F32)
    amax_col = P.sb("amax_col", [128, 1], F32)
    amax_row = P.sb("amax_row", [1, 128], F32)
    m1_sb = P.sb("m1_sb", [1, 1], F32)

    ps_tr = [P.psum(f"ps_tr{k}", [128, 512], BF16) for k in range(2)]
    ps_mm = [P.psum(f"ps_mm{k}", [128, 512], F32) for k in range(6)]
    pool = PsumPool(P, ps_mm)

    for i in range(NT):
        P.op("sync", lambda e, i=i: e.dma_start(
            out=x_sb[i][:], in_=x_h[i * 128:(i + 1) * 128, :]), sem="ld_x", by=16)
    ld_x_total = P.counts["ld_x"]
    P.op("sync", lambda e: e.dma_start(out=ident_sb[:], in_=ident_h), sem="ld_misc", by=16)
    P.op("sync", lambda e: e.dma_start(
        out=bout_row[:], in_=bout_h.rearrange("(a n) -> a n", a=1)), sem="ld_misc", by=16)
    ld_misc_total = P.counts["ld_misc"]
    load_colvec(P, bin_h, bin_col, 2 * ND, "ld_vec")
    load_colvec(P, a_h, dec_col, ND, "ld_vec")
    ld_vec_total = load_colvec(P, c_h, c_col, ND, "ld_vec")
    P.wait("scalar", "ld_vec", ld_vec_total)
    dec_done = P.op("scalar", lambda e: e.activation(
        out=dec_col[:], in_=dec_col[:], func=AF.Sigmoid), sem="a_misc")
    P.op("vector", lambda e: e.memset(ones_row_bf[:], 1.0), sem="v_misc")
    P.wait("vector", "ld_misc", ld_misc_total)
    misc_done = P.op("vector", lambda e: e.tensor_copy(
        out=bout_row_bf[:], in_=bout_row[:]), sem="v_misc")

    nw = 0
    for ic in range(ND):
        for h in range(2):
            k = nw % 2
            if nw >= 2:
                P.wait("sync", "wcast1", nw - 1)
            v = P.op("sync", lambda e, ic=ic, h=h, k=k: e.dma_start(
                out=wstage[k][:], in_=win_h[ic * 128:(ic + 1) * 128, h * D:(h + 1) * D]),
                sem=f"wld1_{k}", by=16)
            P.wait("scalar", f"wld1_{k}", v)
            P.op("scalar", lambda e, ic=ic, h=h, k=k: e.copy(
                out=wbf[ic][:, h * D:(h + 1) * D], in_=wstage[k][:]), sem="wcast1")
            nw += 1
    wcast1_total = P.counts["wcast1"]

    if 0 in bars:
        P.barrier()
    P.wait("vector", "ld_x", ld_x_total)
    P.wait("scalar", "ld_x", ld_x_total)
    ln0_done = emit_ln_stats(P, x_sb, stat_mu, stat_rstd, junk_bf,
                             stat_sumsq, stat_ssum, "v1", "a1")
    P.wait("vector", "v1", ln0_done)
    if dbg:
        P.wait("gpsimd", "v1", ln0_done)
        P.op("gpsimd", lambda e: e.dma_start(out=d_st_h[:, 0:NT], in_=stat_mu[:]), sem="st_dbg", by=16)
        P.op("gpsimd", lambda e: e.dma_start(out=d_st_h[:, NT:2*NT], in_=stat_rstd[:]), sem="st_dbg", by=16)
        P.op("gpsimd", lambda e: e.dma_start(out=d_st_h[:, 2*NT:3*NT], in_=stat_ssum[:]), sem="st_dbg", by=16)
        P.op("gpsimd", lambda e: e.dma_start(out=d_st_h[:, 3*NT:4*NT], in_=stat_sumsq[:]), sem="st_dbg", by=16)
    for i in range(NT):
        dst = big8[i // 2]
        off = (i % 2) * D
        P.op("vector", lambda e, i=i, dst=dst, off=off: e.tensor_scalar(
            out=dst[:, off:off + D], in0=x_sb[i][:],
            scalar1=stat_mu[:, i:i + 1], scalar2=stat_rstd[:, i:i + 1],
            op0=ALU.subtract, op1=ALU.mult), sem="xn0_done")
    if dbg:
        P.wait("gpsimd", "ld_x", ld_x_total)
        P.op("gpsimd", lambda e: e.dma_start(out=d_x0_h, in_=x_sb[0][:]), sem="st_dbg", by=16)
        P.wait("gpsimd", "xn0_done", P.counts["xn0_done"])
        P.op("gpsimd", lambda e: e.dma_start(out=d_xn0_h, in_=big8[0][:]), sem="st_dbg", by=16)
    P.wait("tensor", "ld_misc", ld_misc_total)
    ntr = 0
    for i in range(NT):
        P.wait("tensor", "xn0_done", i + 1)
        src = big8[i // 2]
        soff = (i % 2) * D
        for c in range(ND):
            k = ntr % 2
            if ntr >= 2:
                P.wait("tensor", "trev1", ntr - 1)
            P.op("tensor", lambda e, i=i, c=c, k=k, src=src, soff=soff: e.transpose(
                out=ps_tr[k][:, 0:128], in_=src[:, soff + c * 128:soff + (c + 1) * 128],
                identity=ident_sb[:]), sem="tr1")
            P.wait("scalar", "tr1", P.counts["tr1"])
            P.op("scalar", lambda e, i=i, c=c, k=k: e.copy(
                out=xnT[c][:, i * 128:(i + 1) * 128], in_=ps_tr[k][:, 0:128]), sem="trev1")
            ntr += 1
    trev1_total = P.counts["trev1"]

    if 1 in bars:
        P.barrier()
    if dbg:
        P.wait("gpsimd", "trev1", trev1_total)
        P.op("gpsimd", lambda e: e.dma_start(out=d_xnT_h, in_=xnT[0][:]), sem="st_dbg", by=16)
    P.wait("tensor", "trev1", trev1_total)
    P.wait("scalar", "ld_vec", ld_vec_total)
    P.wait("vector", "ld_vec", ld_vec_total)
    scan_counts = []
    y_counts = []
    for s in range(16):
        c = s // 2
        is_gate = (s % 2 == 1)
        o = c + ND if is_gate else c
        banks = [0, 1, 2, 3] if s % 2 == 0 else [4, 5, 0, 1]
        dstslot = gtmp[c % 2] if is_gate else vtmp[c % 2]
        if c >= 2:
            if is_gate:
                P.wait("scalar", "y1", y_counts[c - 2])
            else:
                P.wait("scalar", "scan1", scan_counts[c - 2])
        for ic in range(ND):
            P.wait("tensor", "wcast1", 2 * (ic + 1))
            for t in range(4):
                bk = banks[t]
                if ic == 0:
                    pool.acquire(bk, "tensor")
                last = (ic == ND - 1)
                P.op("tensor", lambda e, o=o, ic=ic, t=t, bk=bk, last=last: e.matmul(
                    out=ps_mm[bk][:],
                    lhsT=wbf[ic][:, o * 128:(o + 1) * 128],
                    rhs=xnT[ic][:, t * 512:(t + 1) * 512],
                    start=(ic == 0), stop=last),
                    sem="mm1" if last else None)
        mm_after = P.counts["mm1"]
        for t in range(4):
            bk = banks[t]
            P.wait("scalar", "mm1", mm_after)
            if is_gate:
                ev = P.op("scalar", lambda e, c=c, t=t, bk=bk, dstslot=dstslot: e.activation(
                    out=dstslot[:, t * 512:(t + 1) * 512], in_=ps_mm[bk][:],
                    func=AF.Sigmoid, bias=bin_col[:, (c + ND):(c + ND) + 1], scale=1.0),
                    sem="ev1")
            else:
                ev = P.op("scalar", lambda e, c=c, t=t, bk=bk, dstslot=dstslot: e.activation(
                    out=dstslot[:, t * 512:(t + 1) * 512], in_=ps_mm[bk][:],
                    func=AF.Identity, bias=bin_col[:, c:c + 1], scale=1.0), sem="ev1")
            pool.release(bk, "ev1", ev)
        if not is_gate:
            P.wait("vector", "ev1", P.counts["ev1"])
            P.wait("vector", "a_misc", dec_done)
            sc = P.op("vector", lambda e, c=c: e.tensor_tensor_scan(
                out=htmp[c % 2][:], data0=dec_col[:, c:c + 1].broadcast_to((128, T)),
                data1=vtmp[c % 2][:], initial=0.0, op0=ALU.mult, op1=ALU.add),
                sem="scan1")
            scan_counts.append(sc)
        else:
            P.wait("vector", "ev1", P.counts["ev1"])
            P.wait("vector", "scan1", scan_counts[c])
            yc = P.op("vector", lambda e, c=c: e.scalar_tensor_tensor(
                out=big8[c][:], in0=htmp[c % 2][:], scalar=c_col[:, c:c + 1],
                in1=gtmp[c % 2][:], op0=ALU.mult, op1=ALU.mult), sem="y1")
            y_counts.append(yc)
            if dbg and c == 0:
                P.wait("gpsimd", "y1", yc)
                P.op("gpsimd", lambda e: e.dma_start(out=d_v0_h, in_=vtmp[0][:]), sem="st_dbg", by=16)
                P.op("gpsimd", lambda e: e.dma_start(out=d_h0_h, in_=htmp[0][:]), sem="st_dbg", by=16)
                P.op("gpsimd", lambda e: e.dma_start(out=d_g0_h, in_=gtmp[0][:]), sem="st_dbg", by=16)
                P.op("gpsimd", lambda e: e.dma_start(out=d_y0_h, in_=big8[0][:]), sem="st_dbg", by=16)
    y1_total = P.counts["y1"]

    if dbg:
        P.wait("gpsimd", "y1", y1_total)
        for c in range(ND):
            P.op("gpsimd", lambda e, c=c: e.dma_start(out=d_yall_h[c], in_=big8[c][:]),
                 sem="st_dbg", by=16)

    mm1_total = P.counts["mm1"]
    P.wait("sync", "mm1", mm1_total)
    P.wait("scalar", "mm1", mm1_total)
    for ic in range(ND):
        k = ic % 2
        if ic >= 2:
            P.wait("sync", "wcast2", ic - 1)
        v = P.op("sync", lambda e, ic=ic, k=k: e.dma_start(
            out=wstage[k][:], in_=wout_h[ic * 128:(ic + 1) * 128, :]),
            sem=f"wld2_{k}", by=16)
        P.wait("scalar", f"wld2_{k}", v)
        P.op("scalar", lambda e, ic=ic, k=k: e.copy(
            out=wbf[ic][:, 0:D], in_=wstage[k][:]), sem="wcast2")

    P.wait("tensor", "y1", y1_total)
    P.wait("tensor", "v_misc", misc_done)
    for t in range(NT):
        banks = [0, 1] if t % 2 == 0 else [2, 3]
        for ic in range(ND):
            P.wait("tensor", "wcast2", ic + 1)
            for o2 in range(2):
                bk = banks[o2]
                if ic == 0:
                    pool.acquire(bk, "tensor")
                P.op("tensor", lambda e, t=t, ic=ic, o2=o2, bk=bk: e.matmul(
                    out=ps_mm[bk][:],
                    lhsT=big8[ic][:, t * 128:(t + 1) * 128],
                    rhs=wbf[ic][:, o2 * 512:(o2 + 1) * 512],
                    start=(ic == 0), stop=False))
        for o2 in range(2):
            bk = banks[o2]
            P.op("tensor", lambda e, t=t, o2=o2, bk=bk: e.matmul(
                out=ps_mm[bk][:],
                lhsT=ones_row_bf[0:1, :],
                rhs=bout_row_bf[0:1, o2 * 512:(o2 + 1) * 512],
                start=False, stop=True), sem="mm2")
        mm2_after = P.counts["mm2"]
        for o2 in range(2):
            bk = banks[o2]
            P.wait("vector", "mm2", mm2_after)
            ev = P.op("vector", lambda e, t=t, o2=o2, bk=bk: e.tensor_add(
                out=x_sb[t][:, o2 * 512:(o2 + 1) * 512],
                in0=x_sb[t][:, o2 * 512:(o2 + 1) * 512],
                in1=ps_mm[bk][:]), sem="ev2")
            pool.release(bk, "ev2", ev)
        P.wait("gpsimd", "ev2", P.counts["ev2"])
        P.op("gpsimd", lambda e, t=t: e.dma_start(
            out=x1_h[t * 128:(t + 1) * 128, :], in_=x_sb[t][:]), sem="st_x1", by=16)

    if 3 in bars:
        P.barrier()
    ev2_total = P.counts["ev2"]
    P.wait("vector", "ev2", ev2_total)
    P.wait("scalar", "ev2", ev2_total)
    ln1_done = emit_ln_stats(P, x_sb, stat_mu, stat_rstd, junk_bf,
                             stat_sumsq, stat_ssum, "v2", "a2")
    P.wait("vector", "v2", ln1_done)
    for i in range(NT):
        dst = xnT[i // 2]
        off = (i % 2) * D
        P.op("vector", lambda e, i=i, dst=dst, off=off: e.tensor_scalar(
            out=dst[:, off:off + D], in0=x_sb[i][:],
            scalar1=stat_mu[:, i:i + 1], scalar2=stat_rstd[:, i:i + 1],
            op0=ALU.subtract, op1=ALU.mult), sem="xn1_done")
        P.op("vector", lambda e, i=i, dst=dst, off=off: e.tensor_reduce(
            out=amax_cols[:, i:i + 1], in_=dst[:, off:off + D], axis=AXX,
            op=ALU.max, apply_absolute_value=True), sem="xn1_done")
        P.wait("gpsimd", "xn1_done", 2 * (i + 1))
        P.op("gpsimd", lambda e, i=i, dst=dst, off=off: e.dma_start(
            out=xn1_h[i * 128:(i + 1) * 128, :], in_=dst[:, off:off + D]),
            sem="st_xn1", by=16)
    am = P.op("vector", lambda e: e.tensor_reduce(
        out=amax_col[:], in_=amax_cols[:], axis=AXX, op=ALU.max), sem="amax_v")
    P.wait("gpsimd", "amax_v", am)
    m_fin = emit_partition_absmax(P, amax_col[:], amax_row, m1_sb, "amax_v", "amax_d", "m1")
    P.wait("gpsimd", "amax_v", m_fin)
    P.op("gpsimd", lambda e: e.dma_start(out=m1_h[:], in_=m1_sb[:]), sem="st_m1", by=16)

    if dbg:
        P.wait("gpsimd", "st_dbg", P.counts["st_dbg"])
    P.wait("gpsimd", "st_x1", P.counts["st_x1"])
    P.wait("gpsimd", "st_xn1", P.counts["st_xn1"])
    P.wait("gpsimd", "st_m1", P.counts["st_m1"])
    return P.finish()


def kernel(**inputs):
    B = 8
    cores = list(range(B))

    def contig(a):
        return np.ascontiguousarray(np.asarray(a))

    x = contig(inputs["L0_attn_x"])

    nc1 = build_launch1()
    wk1 = {k: contig(inputs["L0_attn_" + k])
           for k in ("W_in", "b_in", "A", "C", "W_out", "b_out")}
    r1 = run_bass_kernel_spmd(
        nc1, [dict(wk1, x=contig(x[b])) for b in range(B)], core_ids=cores)
    x1 = [contig(r1.results[b]["x1"]) for b in range(B)]
    xn1 = [contig(r1.results[b]["xn1"]) for b in range(B)]
    m1_all = contig(np.concatenate(
        [np.asarray(r1.results[b]["m1"]).reshape(1, 1) for b in range(B)], axis=1))

    nc2 = build_launch2()
    wk2 = {"w_up": contig(inputs["L0_ffn_w_up"]),
           "w_gate": contig(inputs["L0_ffn_w_gate"]), "m1_all": m1_all}
    r2 = run_bass_kernel_spmd(
        nc2, [dict(wk2, xn1=xn1[b]) for b in range(B)], core_ids=cores)
    up_raw = [contig(r2.results[b]["up_raw"]) for b in range(B)]
    gate_raw = [contig(r2.results[b]["gate_raw"]) for b in range(B)]
    mup_all = contig(np.concatenate(
        [np.asarray(r2.results[b]["m_up"]).reshape(1, 1) for b in range(B)], axis=1))
    mg_all = contig(np.concatenate(
        [np.asarray(r2.results[b]["m_g"]).reshape(1, 1) for b in range(B)], axis=1))

    nc3 = build_launch3()
    r3 = run_bass_kernel_spmd(
        nc3, [{"gate_raw": gate_raw[b], "m_g_all": mg_all} for b in range(B)],
        core_ids=cores)
    mg2_all = contig(np.concatenate(
        [np.asarray(r3.results[b]["m_g2"]).reshape(1, 1) for b in range(B)], axis=1))

    nc4 = build_launch4()
    r4 = run_bass_kernel_spmd(
        nc4, [{"gate_raw": gate_raw[b], "up_raw": up_raw[b], "m_g_all": mg_all,
               "m_g2_all": mg2_all, "m_up_all": mup_all} for b in range(B)],
        core_ids=cores)
    t_q = [contig(r4.results[b]["t_q"]) for b in range(B)]
    mo_all = contig(np.concatenate(
        [np.asarray(r4.results[b]["m_o"]).reshape(1, 1) for b in range(B)], axis=1))

    nc5 = build_launch5_full()
    wk5 = {"w_down": contig(inputs["L0_ffn_w_down"]), "m_o_all": mo_all}
    for k in ("wq", "bq", "wk", "bk", "wv", "bv", "wo", "bo",
              "g1", "b1", "g2", "b2", "w1", "bf1", "w2", "bf2"):
        wk5[k] = contig(inputs["L1_attn_" + k])
    for src, dst in (("w1", "f_w1"), ("b1", "f_b1"), ("w2", "f_w2"), ("b2", "f_b2")):
        wk5[dst] = contig(inputs["L1_ffn_" + src])
    r5 = run_bass_kernel_spmd(
        nc5, [dict(wk5, x1=x1[b], t_q=t_q[b]) for b in range(B)], core_ids=cores)
    return np.stack([np.asarray(r5.results[b]["out"], dtype=np.float32)
                     for b in range(B)], axis=0)


C32 = float(3 * 2.0 ** 22)
C64 = float(3 * 2.0 ** 51)
G_A = float(2.0 ** -75)
G_B = float(2.0 ** 75)


def emit_g_gadget(P, g_col):
    P.op("vector", lambda e: e.memset(g_col[:], G_A))
    P.op("vector", lambda e: e.tensor_scalar(
        out=g_col[:], in0=g_col[:], scalar1=G_A, scalar2=G_B,
        op0=ALU.mult, op1=ALU.mult))
    return P.op("vector", lambda e: e.tensor_scalar(
        out=g_col[:], in0=g_col[:], scalar1=G_B, scalar2=None,
        op0=ALU.mult), sem="g_done")


def emit_round(P, dst_fn, src_ap, tb, ta, g_col, pre=None):
    if pre is not None:
        pre(tb)
        tsrc = tb[:]
    else:
        tsrc = src_ap
    P.op("vector", lambda e, tsrc=tsrc: e.tensor_scalar(
        out=tb[:], in0=tsrc, scalar1=C32, scalar2=C32,
        op0=ALU.add, op1=ALU.subtract))
    P.op("vector", lambda e: e.tensor_scalar(
        out=ta[:], in0=tb[:], scalar1=C64, scalar2=C64,
        op0=ALU.add, op1=ALU.subtract))
    P.op("vector", lambda e: e.tensor_sub(out=ta[:], in0=ta[:], in1=tb[:]))
    npart = ta[:].shape[0]
    P.op("vector", lambda e, npart=npart: e.scalar_tensor_tensor(
        out=ta[:], in0=ta[:], scalar=g_col[0:npart, 0:1], in1=tb[:],
        op0=ALU.mult, op1=ALU.add))
    return dst_fn(ta)


def emit_round_sv(P, dst_fn, fin_sem, tb, ta, slot, g_col, pre_s):
    if not hasattr(P, "_rsv_last"):
        P._rsv_last = {}
    lastv = P._rsv_last.get(slot)
    if lastv:
        P.wait("scalar", lastv[0], lastv[1])
    pre_s(tb)
    P.op("scalar", lambda e: e.activation(out=tb[:], in_=tb[:], func=AF.Identity,
                                          bias=-C32, scale=1.0))
    P.op("scalar", lambda e: e.activation(out=ta[:], in_=tb[:], func=AF.Identity,
                                          bias=C64, scale=1.0))
    a = P.op("scalar", lambda e: e.activation(out=ta[:], in_=ta[:], func=AF.Identity,
                                              bias=-C64, scale=1.0), sem="rsv_s")
    P.wait("vector", "rsv_s", a)
    P.op("vector", lambda e: e.tensor_sub(out=ta[:], in0=ta[:], in1=tb[:]))
    npart = ta[:].shape[0]
    P.op("vector", lambda e, npart=npart: e.scalar_tensor_tensor(
        out=ta[:], in0=ta[:], scalar=g_col[0:npart, 0:1], in1=tb[:],
        op0=ALU.mult, op1=ALU.add))
    fc = dst_fn(ta)
    P._rsv_last[slot] = (fin_sem, fc)
    return fc



def build_launch2():
    nc = new_nc()
    P = Prog(nc)
    ident_h = nc.inline_tensor(np.eye(128, dtype=ml_dtypes.bfloat16), "ident_bf16").ap()

    xn1_h = nc.dram_tensor("xn1", [T, D], BF16, kind="ExternalInput").ap()
    m1a_h = nc.dram_tensor("m1_all", [1, 8], F32, kind="ExternalInput").ap()
    wup_h = nc.dram_tensor("w_up", [3 * D, D], F32, kind="ExternalInput").ap()
    wgate_h = nc.dram_tensor("w_gate", [3 * D, D], F32, kind="ExternalInput").ap()

    up_h = nc.dram_tensor("up_raw", [3 * D, T], BF16, kind="ExternalOutput").ap()
    gate_h = nc.dram_tensor("gate_raw", [3 * D, T], BF16, kind="ExternalOutput").ap()
    mup_h = nc.dram_tensor("m_up", [1, 1], F32, kind="ExternalOutput").ap()
    mg_h = nc.dram_tensor("m_g", [1, 1], F32, kind="ExternalOutput").ap()

    NO = 24

    xn1 = [P.sb(f"xn1_{i}", [128, D], BF16) for i in range(NT)]
    xqT = [P.sb(f"xqT{c}", [128, T], BF16) for c in range(ND)]
    xq_tm = [P.sb(f"xq{i}", [128, D], BF16) for i in range(NT)]
    tmp1 = P.sb("tmp1", [128, D], F32)
    tmp2 = P.sb("tmp2", [128, D], F32)
    wstage = [P.sb(f"wstage{k}", [128, D], F32) for k in range(2)]
    wq_nat = [P.sb(f"wq_nat{k}", [128, D], BF16) for k in range(2)]
    wT_sl = [P.sb(f"wT_{k}", [128, D], BF16) for k in range(2)]
    strip = [P.sb(f"strip{k}", [128, T], BF16) for k in range(2)]
    ident_sb = P.sb("ident_sb", [128, 128], BF16)
    g_col = P.sb("g_col", [128, 1], F32)
    mrow = P.sb("mrow", [128, 8], F32)
    s_col = P.sb("s_col", [128, 1], F32)
    invs_col = P.sb("invs_col", [128, 1], F32)
    bmax = P.sb("bmax", [128, 32], F32)
    sblk = P.sb("sblk", [128, 32], F32)
    invblk = P.sb("invblk", [128, 32], F32)
    amax_acc = P.sb("amax_acc", [128, 2], F32)
    amax_row = P.sb("amax_row", [1, 128], F32)
    m_sb = P.sb("m_sb", [1, 2], F32)

    ps_tr = [P.psum(f"ps_tr{k}", [128, 512], BF16) for k in range(2)]
    ps_mm = [P.psum(f"ps_mm{k}", [128, 512], F32) for k in range(6)]
    pool = PsumPool(P, ps_mm)

    for i in range(NT):
        P.op("sync", lambda e, i=i: e.dma_start(
            out=xn1[i][:], in_=xn1_h[i * 128:(i + 1) * 128, :]), sem="ld_x", by=16)
    ld_x_total = P.counts["ld_x"]
    P.op("sync", lambda e: e.dma_start(out=ident_sb[:], in_=ident_h), sem="ld_misc", by=16)
    P.op("sync", lambda e: e.dma_start(
        out=mrow[:], in_=m1a_h.partition_broadcast(128)), sem="ld_misc", by=16)
    ld_misc_total = P.counts["ld_misc"]

    emit_g_gadget(P, g_col)
    P.wait("vector", "ld_misc", ld_misc_total)
    P.op("vector", lambda e: e.tensor_reduce(
        out=s_col[:], in_=mrow[:], axis=AXX, op=ALU.max))
    P.op("vector", lambda e: e.tensor_scalar(
        out=s_col[:], in0=s_col[:], scalar1=1.0 / QMAX16, scalar2=1e-12,
        op0=ALU.mult, op1=ALU.max))
    sc_done = P.op("vector", lambda e: e.reciprocal(
        out=invs_col[:], in_=s_col[:]), sem="sc_done")

    P.wait("vector", "ld_x", ld_x_total)
    for i in range(NT):
        def pre(tb, i=i):
            P.op("vector", lambda e, i=i, tb=tb: e.tensor_scalar(
                out=tb[:], in0=xn1[i][:], scalar1=invs_col[:, 0:1], scalar2=None,
                op0=ALU.mult))
        def fin(ta, i=i):
            return P.op("vector", lambda e, i=i, ta=ta: e.tensor_scalar(
                out=xq_tm[i][:], in0=ta[:], scalar1=s_col[:, 0:1], scalar2=None,
                op0=ALU.mult), sem="xq_done")
        emit_round(P, fin, None, tmp1, tmp2, g_col, pre=pre)
    P.wait("tensor", "ld_misc", ld_misc_total)
    ntr = 0
    for i in range(NT):
        P.wait("tensor", "xq_done", i + 1)
        for c in range(ND):
            k = ntr % 2
            if ntr >= 2:
                P.wait("tensor", "trev", ntr - 1)
            P.op("tensor", lambda e, i=i, c=c, k=k: e.transpose(
                out=ps_tr[k][:, 0:128], in_=xq_tm[i][:, c * 128:(c + 1) * 128],
                identity=ident_sb[:]), sem="tr")
            P.wait("scalar", "tr", P.counts["tr"])
            P.op("scalar", lambda e, i=i, c=c, k=k: e.copy(
                out=xqT[c][:, i * 128:(i + 1) * 128], in_=ps_tr[k][:, 0:128]), sem="trev")
            ntr += 1
    trev_total = P.counts["trev"]
    P.op("vector", lambda e: e.memset(amax_acc[:], 0.0), sem="amax_init")

    P.wait("tensor", "trev", trev_total)
    wtr_cp = {0: 0, 1: 0}
    wtr_hist = []
    for wi, (w_h, out_h, acol) in enumerate(((wup_h, up_h, 0), (wgate_h, gate_h, 1))):
        for r in range(NO):
            k = r % 2
            nw = wi * NO + r
            if nw >= 2:
                P.wait("sync", "wdq", nw - 1)
            v = P.op("sync", lambda e, w_h=w_h, r=r, k=k: e.dma_start(
                out=wstage[k][:], in_=w_h[r * 128:(r + 1) * 128, :]),
                sem=f"wld_{k}", by=16)
            P.wait("vector", f"wld_{k}", v)
            if nw >= 2 and wtr_hist[nw - 2]:
                P.wait("vector", "wtr_pe", wtr_hist[nw - 2])
            P.op("vector", lambda e, k=k: e.tensor_reduce(
                out=bmax[:], in_=wstage[k][:].rearrange("p (b i) -> p b i", i=32),
                axis=AXX, op=ALU.max, apply_absolute_value=True))
            P.op("vector", lambda e: e.tensor_scalar(
                out=sblk[:], in0=bmax[:], scalar1=1.0 / 7.0, scalar2=1e-12,
                op0=ALU.mult, op1=ALU.max))
            P.op("vector", lambda e: e.reciprocal(out=invblk[:], in_=sblk[:]))
            def pre(tb, k=k):
                P.op("vector", lambda e, k=k, tb=tb: e.tensor_tensor(
                    out=tb[:].rearrange("p (b i) -> p b i", i=32),
                    in0=wstage[k][:].rearrange("p (b i) -> p b i", i=32),
                    in1=invblk[:].rearrange("p b -> p b ()").broadcast_to((128, 32, 32)),
                    op=ALU.mult))
            def fin(ta, k=k):
                return P.op("vector", lambda e, k=k, ta=ta: e.tensor_tensor(
                    out=wq_nat[k][:].rearrange("p (b i) -> p b i", i=32),
                    in0=ta[:].rearrange("p (b i) -> p b i", i=32),
                    in1=sblk[:].rearrange("p b -> p b ()").broadcast_to((128, 32, 32)),
                    op=ALU.mult), sem="wdq")
            emit_round(P, fin, None, tmp1, tmp2, g_col, pre=pre)
            wdq_c = P.counts["wdq"]
            for gno in range(2):
                kk = gno
                if wtr_cp[kk]:
                    P.wait("tensor", f"wtr_cp{kk}", wtr_cp[kk])
                if gno == 0:
                    P.wait("tensor", "wdq", wdq_c)
                v = 0
                for j in range(4):
                    ic = gno * 4 + j
                    v = P.op("tensor", lambda e, k=k, ic=ic, j=j, kk=kk: e.transpose(
                        out=ps_tr[kk][:, j * 128:(j + 1) * 128],
                        in_=wq_nat[k][:, ic * 128:(ic + 1) * 128],
                        identity=ident_sb[:]), sem="wtr_pe")
                P.wait("scalar", "wtr_pe", v)
                c = P.op("scalar", lambda e, k=k, gno=gno, kk=kk: e.copy(
                    out=wT_sl[k][:, gno * 512:(gno + 1) * 512], in_=ps_tr[kk][:]),
                    sem=f"wtr_cp{kk}")
                wtr_cp[kk] = c
            wtr_hist.append(P.counts["wtr_pe"])
            banks = [0, 1, 2, 3] if nw % 2 == 0 else [4, 5, 0, 1]
            for kk in range(2):
                P.wait("tensor", f"wtr_cp{kk}", wtr_cp[kk])
            for ic in range(ND):
                for t in range(4):
                    bk = banks[t]
                    if ic == 0:
                        pool.acquire(bk, "tensor")
                    P.op("tensor", lambda e, k=k, ic=ic, t=t, bk=bk: e.matmul(
                        out=ps_mm[bk][:],
                        lhsT=wT_sl[k][:, ic * 128:(ic + 1) * 128],
                        rhs=xqT[ic][:, t * 512:(t + 1) * 512],
                        start=(ic == 0), stop=(ic == ND - 1)),
                        sem="mm" if ic == ND - 1 else None)
            mm_c = P.counts["mm"]
            if nw >= 2:
                P.wait("scalar", "strip_st", (nw - 2) * 16 + 16)
            for t in range(4):
                bk = banks[t]
                P.wait("scalar", "mm", mm_c)
                ev = P.op("scalar", lambda e, k=k, t=t, bk=bk: e.copy(
                    out=strip[k][:, t * 512:(t + 1) * 512], in_=ps_mm[bk][:]), sem="ev")
                pool.release(bk, "ev", ev)
            ev_c = P.counts["ev"]
            P.wait("vector", "ev", ev_c)
            P.wait("vector", "amax_init", 1)
            P.op("vector", lambda e, k=k, acol=acol: e.tensor_reduce(
                out=bmax[:, 0:1], in_=strip[k][:], axis=AXX, op=ALU.max,
                apply_absolute_value=True))
            vm = P.op("vector", lambda e, acol=acol: e.tensor_tensor(
                out=amax_acc[:, acol:acol + 1], in0=amax_acc[:, acol:acol + 1],
                in1=bmax[:, 0:1], op=ALU.max), sem="amax_v")
            P.wait("gpsimd", "ev", ev_c)
            P.op("gpsimd", lambda e, out_h=out_h, r=r, k=k: e.dma_start(
                out=out_h[r * 128:(r + 1) * 128, :], in_=strip[k][:]),
                sem="strip_st", by=16)

    P.wait("gpsimd", "amax_v", P.counts["amax_v"])
    d = P.op("gpsimd", lambda e: e.dma_start(
        out=amax_row[:], in_=amax_acc[:, 0:1]), sem="amax_d", by=16)
    P.wait("vector", "amax_d", d)
    r1 = P.op("vector", lambda e: e.tensor_reduce(
        out=m_sb[:, 0:1], in_=amax_row[:], axis=AXX, op=ALU.max), sem="amax_f")
    P.wait("gpsimd", "amax_f", r1)
    d = P.op("gpsimd", lambda e: e.dma_start(
        out=amax_row[:], in_=amax_acc[:, 1:2]), sem="amax_d", by=16)
    P.wait("vector", "amax_d", d)
    r2 = P.op("vector", lambda e: e.tensor_reduce(
        out=m_sb[:, 1:2], in_=amax_row[:], axis=AXX, op=ALU.max), sem="amax_f")
    P.wait("gpsimd", "amax_f", r2)
    P.op("gpsimd", lambda e: e.dma_start(out=mup_h, in_=m_sb[0:1, 0:1]), sem="st_m", by=16)
    P.op("gpsimd", lambda e: e.dma_start(out=mg_h, in_=m_sb[0:1, 1:2]), sem="st_m", by=16)

    P.wait("gpsimd", "strip_st", P.counts["strip_st"])
    P.wait("gpsimd", "st_m", P.counts["st_m"])
    return P.finish()


def emit_scale_cols(P, rows_ap, s_col, invs_col, tmp_col, qmax, ld_sem_val, name):
    P.op("vector", lambda e: e.tensor_reduce(
        out=s_col[:], in_=tmp_col[:], axis=AXX, op=ALU.max))
    P.op("vector", lambda e: e.tensor_scalar(
        out=s_col[:], in0=s_col[:], scalar1=1.0 / qmax, scalar2=1e-12,
        op0=ALU.mult, op1=ALU.max))
    return P.op("vector", lambda e: e.reciprocal(
        out=invs_col[:], in_=s_col[:]), sem=name)



def build_launch3():
    nc = new_nc()
    P = Prog(nc)
    gate_h = nc.dram_tensor("gate_raw", [3 * D, T], BF16, kind="ExternalInput").ap()
    mga_h = nc.dram_tensor("m_g_all", [1, 8], F32, kind="ExternalInput").ap()
    mg2_h = nc.dram_tensor("m_g2", [1, 1], F32, kind="ExternalOutput").ap()

    NO = 24
    P.want_const(C32, -C32, C64, -C64)
    chunk = [P.sb(f"chunk{k}", [128, T], BF16) for k in range(2)]
    g1 = P.sb("g1", [128, T], F32)
    sg = P.sb("sg", [128, T], F32)
    tb = P.sb("tb", [128, T], F32)
    ta = P.sb("ta", [128, T], F32)
    tb2 = P.sb("tb2", [128, T], F32)
    ta2 = P.sb("ta2", [128, T], F32)
    g_col = P.sb("g_col", [128, 1], F32)
    mrow8 = P.sb("mrow8", [128, 8], F32)
    s_col = P.sb("s_col", [128, 1], F32)
    invs_col = P.sb("invs_col", [128, 1], F32)
    amax_acc = P.sb("amax_acc", [128, 2], F32)
    amax_row = P.sb("amax_row", [1, 128], F32)
    m_sb = P.sb("m_sb", [1, 1], F32)

    P.op("sync", lambda e: e.dma_start(
        out=mrow8[:], in_=mga_h.partition_broadcast(128)), sem="ld_misc", by=16)
    emit_g_gadget(P, g_col)
    P.wait("vector", "ld_misc", 16)
    emit_scale_cols(P, None, s_col, invs_col, mrow8, QMAX16, None, "sc_done")
    P.op("vector", lambda e: e.memset(amax_acc[:], 0.0))

    prs = [(tb, ta), (tb2, ta2)]
    gi = 0
    t1m_last = 0
    for r in range(NO):
        k = r % 2
        if r >= 2:
            P.wait("sync", "ch_done", r - 1)
        v = P.op("sync", lambda e, r=r, k=k: e.dma_start(
            out=chunk[k][:], in_=gate_h[r * 128:(r + 1) * 128, :]),
            sem=f"ld_{k}", by=16)
        P.wait("scalar", f"ld_{k}", v)
        def pre1(tbx, k=k):
            P.op("scalar", lambda e, k=k, tbx=tbx: e.activation(
                out=tbx[:], in_=chunk[k][:], func=AF.Identity, bias=C32,
                scale=invs_col[:, 0:1]))
        def fin(tax):
            return P.op("vector", lambda e, tax=tax: e.tensor_scalar(
                out=g1[:], in0=tax[:], scalar1=s_col[:, 0:1], scalar2=None,
                op0=ALU.mult), sem="g1_done")
        tb_, ta_ = prs[gi % 2]
        emit_round_sv(P, fin, "g1_done", tb_, ta_, gi % 2, g_col, pre1)
        gi += 1
        P.wait("scalar", "g1_done", P.counts["g1_done"])
        if t1m_last:
            P.wait("scalar", "t1m", t1m_last)
        P.op("scalar", lambda e: e.activation(out=sg[:], in_=g1[:], func=AF.Sigmoid),
             sem="sg_done")
        def pre2(tbx):
            P.op("scalar", lambda e, tbx=tbx: e.activation(
                out=tbx[:], in_=sg[:], func=AF.Identity, bias=C32, scale=65536.0))
        def fin2(tax):
            return P.op("vector", lambda e, tax=tax: e.tensor_scalar(
                out=sg[:], in0=tax[:], scalar1=65535.0, scalar2=float(2.0 ** -16),
                op0=ALU.min, op1=ALU.mult), sem="sig_done")
        tb_, ta_ = prs[gi % 2]
        emit_round_sv(P, fin2, "sig_done", tb_, ta_, gi % 2, g_col, pre2)
        gi += 1
        t1m_last = P.op("vector", lambda e: e.tensor_mul(
            out=g1[:], in0=g1[:], in1=sg[:]), sem="t1m")
        P.op("vector", lambda e: e.tensor_reduce(
            out=amax_acc[:, 1:2], in_=g1[:], axis=AXX, op=ALU.max,
            apply_absolute_value=True))
        cd = P.op("vector", lambda e: e.tensor_tensor(
            out=amax_acc[:, 0:1], in0=amax_acc[:, 0:1], in1=amax_acc[:, 1:2],
            op=ALU.max), sem="ch_done")

    P.wait("gpsimd", "ch_done", P.counts["ch_done"])
    d = P.op("gpsimd", lambda e: e.dma_start(
        out=amax_row[:], in_=amax_acc[:, 0:1]), sem="amax_d", by=16)
    P.wait("vector", "amax_d", d)
    rf = P.op("vector", lambda e: e.tensor_reduce(
        out=m_sb[:], in_=amax_row[:], axis=AXX, op=ALU.max), sem="amax_f")
    P.wait("gpsimd", "amax_f", rf)
    P.op("gpsimd", lambda e: e.dma_start(out=mg2_h, in_=m_sb[:]), sem="st_m", by=16)
    P.wait("gpsimd", "st_m", 16)
    return P.finish()



def build_launch4():
    nc = new_nc()
    P = Prog(nc)
    gate_h = nc.dram_tensor("gate_raw", [3 * D, T], BF16, kind="ExternalInput").ap()
    up_h = nc.dram_tensor("up_raw", [3 * D, T], BF16, kind="ExternalInput").ap()
    mga_h = nc.dram_tensor("m_g_all", [1, 8], F32, kind="ExternalInput").ap()
    mg2a_h = nc.dram_tensor("m_g2_all", [1, 8], F32, kind="ExternalInput").ap()
    mua_h = nc.dram_tensor("m_up_all", [1, 8], F32, kind="ExternalInput").ap()
    tq_h = nc.dram_tensor("t_q", [3 * D, T], BF16, kind="ExternalOutput").ap()
    mo_h = nc.dram_tensor("m_o", [1, 1], F32, kind="ExternalOutput").ap()

    NO = 24
    P.want_const(C32, -C32, C64, -C64)
    gch = [P.sb(f"gch{k}", [128, T], BF16) for k in range(2)]
    uch = [P.sb(f"uch{k}", [128, T], BF16) for k in range(2)]
    tqb = [P.sb(f"tqb{k}", [128, T], BF16) for k in range(2)]
    g1 = P.sb("g1", [128, T], F32)
    sg = P.sb("sg", [128, T], F32)
    tb = P.sb("tb", [128, T], F32)
    ta = P.sb("ta", [128, T], F32)
    tb2 = P.sb("tb2", [128, T], F32)
    ta2 = P.sb("ta2", [128, T], F32)
    g_col = P.sb("g_col", [128, 1], F32)
    mrow8 = P.sb("mrow8", [128, 8], F32)
    sg1_col = P.sb("sg1_col", [128, 1], F32)
    invg1_col = P.sb("invg1_col", [128, 1], F32)
    sg2_col = P.sb("sg2_col", [128, 1], F32)
    invg2_col = P.sb("invg2_col", [128, 1], F32)
    su_col = P.sb("su_col", [128, 1], F32)
    invu_col = P.sb("invu_col", [128, 1], F32)
    amax_acc = P.sb("amax_acc", [128, 2], F32)
    amax_row = P.sb("amax_row", [1, 128], F32)
    m_sb = P.sb("m_sb", [1, 1], F32)

    P.op("sync", lambda e: e.dma_start(
        out=mrow8[:], in_=mga_h.partition_broadcast(128)), sem="ld_m1", by=16)
    emit_g_gadget(P, g_col)
    P.wait("vector", "ld_m1", 16)
    emit_scale_cols(P, None, sg1_col, invg1_col, mrow8, QMAX16, None, "sc1")
    P.op("sync", lambda e: e.dma_start(
        out=mrow8[:], in_=mg2a_h.partition_broadcast(128)), sem="ld_m2", by=16)
    P.wait("sync", "sc1", P.counts["sc1"])
    P.wait("vector", "ld_m2", 16)
    emit_scale_cols(P, None, sg2_col, invg2_col, mrow8, QMAX16, None, "sc2")
    P.op("sync", lambda e: e.dma_start(
        out=mrow8[:], in_=mua_h.partition_broadcast(128)), sem="ld_m3", by=16)
    P.wait("sync", "sc2", P.counts["sc2"])
    P.wait("vector", "ld_m3", 16)
    emit_scale_cols(P, None, su_col, invu_col, mrow8, QMAX16, None, "sc3")
    P.op("vector", lambda e: e.memset(amax_acc[:], 0.0))

    prs = [(tb, ta), (tb2, ta2)]
    gi = 0
    t1m_last = 0
    for r in range(NO):
        k = r % 2
        if r >= 2:
            P.wait("sync", "ch_done", r - 1)
        v = P.op("sync", lambda e, r=r, k=k: e.dma_start(
            out=gch[k][:], in_=gate_h[r * 128:(r + 1) * 128, :]),
            sem=f"ldg_{k}", by=16)
        v2 = P.op("sync", lambda e, r=r, k=k: e.dma_start(
            out=uch[k][:], in_=up_h[r * 128:(r + 1) * 128, :]),
            sem=f"ldu_{k}", by=16)
        P.wait("scalar", f"ldg_{k}", v)
        def pre1(tbx, k=k):
            P.op("scalar", lambda e, k=k, tbx=tbx: e.activation(
                out=tbx[:], in_=gch[k][:], func=AF.Identity, bias=C32,
                scale=invg1_col[:, 0:1]))
        def fin(tax):
            return P.op("vector", lambda e, tax=tax: e.tensor_scalar(
                out=g1[:], in0=tax[:], scalar1=sg1_col[:, 0:1], scalar2=None,
                op0=ALU.mult), sem="g1_done")
        tb_, ta_ = prs[gi % 2]
        emit_round_sv(P, fin, "g1_done", tb_, ta_, gi % 2, g_col, pre1)
        gi += 1
        P.wait("scalar", "g1_done", P.counts["g1_done"])
        if t1m_last:
            P.wait("scalar", "tqm", t1m_last)
        P.op("scalar", lambda e: e.activation(out=sg[:], in_=g1[:], func=AF.Sigmoid),
             sem="sg_done")
        def pre2(tbx):
            P.op("scalar", lambda e, tbx=tbx: e.activation(
                out=tbx[:], in_=sg[:], func=AF.Identity, bias=C32, scale=65536.0))
        def fin2(tax):
            return P.op("vector", lambda e, tax=tax: e.tensor_scalar(
                out=sg[:], in0=tax[:], scalar1=65535.0, scalar2=float(2.0 ** -16),
                op0=ALU.min, op1=ALU.mult), sem="sig_done")
        tb_, ta_ = prs[gi % 2]
        emit_round_sv(P, fin2, "sig_done", tb_, ta_, gi % 2, g_col, pre2)
        gi += 1
        t1m = P.op("vector", lambda e: e.tensor_mul(
            out=g1[:], in0=g1[:], in1=sg[:]), sem="t1m")
        def pre4(tbx, k=k):
            P.wait("scalar", f"ldu_{k}", v2)
            P.op("scalar", lambda e, k=k, tbx=tbx: e.activation(
                out=tbx[:], in_=uch[k][:], func=AF.Identity, bias=C32,
                scale=invu_col[:, 0:1]))
        def fin4(tax):
            return P.op("vector", lambda e, tax=tax: e.tensor_scalar(
                out=sg[:], in0=tax[:], scalar1=su_col[:, 0:1], scalar2=None,
                op0=ALU.mult), sem="u1_done")
        tb_, ta_ = prs[gi % 2]
        emit_round_sv(P, fin4, "u1_done", tb_, ta_, gi % 2, g_col, pre4)
        gi += 1
        def pre3(tbx):
            P.wait("scalar", "t1m", t1m)
            P.op("scalar", lambda e, tbx=tbx: e.activation(
                out=tbx[:], in_=g1[:], func=AF.Identity, bias=C32,
                scale=invg2_col[:, 0:1]))
        def fin3(tax):
            return P.op("vector", lambda e, tax=tax: e.tensor_scalar(
                out=g1[:], in0=tax[:], scalar1=sg2_col[:, 0:1], scalar2=None,
                op0=ALU.mult), sem="g2_done")
        tb_, ta_ = prs[gi % 2]
        emit_round_sv(P, fin3, "g2_done", tb_, ta_, gi % 2, g_col, pre3)
        gi += 1
        if r >= 2:
            P.wait("vector", "tq_st", (r - 2) * 16 + 16)
        t1m_last = P.op("vector", lambda e, k=k: e.tensor_mul(
            out=tqb[k][:], in0=g1[:], in1=sg[:]), sem="tqm")
        P.op("vector", lambda e, k=k: e.tensor_reduce(
            out=amax_acc[:, 1:2], in_=tqb[k][:], axis=AXX, op=ALU.max,
            apply_absolute_value=True))
        cd = P.op("vector", lambda e: e.tensor_tensor(
            out=amax_acc[:, 0:1], in0=amax_acc[:, 0:1], in1=amax_acc[:, 1:2],
            op=ALU.max), sem="ch_done")
        P.wait("gpsimd", "ch_done", cd)
        P.op("gpsimd", lambda e, r=r, k=k: e.dma_start(
            out=tq_h[r * 128:(r + 1) * 128, :], in_=tqb[k][:]), sem="tq_st", by=16)

    P.wait("gpsimd", "ch_done", P.counts["ch_done"])
    d = P.op("gpsimd", lambda e: e.dma_start(
        out=amax_row[:], in_=amax_acc[:, 0:1]), sem="amax_d", by=16)
    P.wait("vector", "amax_d", d)
    rf = P.op("vector", lambda e: e.tensor_reduce(
        out=m_sb[:], in_=amax_row[:], axis=AXX, op=ALU.max), sem="amax_f")
    P.wait("gpsimd", "amax_f", rf)
    P.op("gpsimd", lambda e: e.dma_start(out=mo_h, in_=m_sb[:]), sem="st_m", by=16)
    P.wait("gpsimd", "tq_st", P.counts["tq_st"])
    P.wait("gpsimd", "st_m", 16)
    return P.finish()


def emit_wprep(P, w_h, O, I, scratch_h, stage, stage_bf, rowbuf, ident,
               gain_col=None, lpbq=False, fixq=None, g_col=None,
               bmax=None, sblk=None, invblk=None, tb=None, ta=None, tag="",
               pe_tr=None, order="r"):
    NR, NI = O // 128, I // 128
    base_cast = P.counts.get("wp_cast", 0)
    base_st = [P.counts.get(f"wp_st{sl}", 0) for sl in range(2)]
    if not hasattr(P, "_wp_hist"):
        P._wp_hist = []
        P._wp_cp = {0: 0, 1: 0}
        P._wp_row = {0: 0, 1: 0}
    for r in range(NR):
        nw = r
        g_row = base_cast + r
        ldsem = f"wp_ld{r % 2}"
        if g_row >= 2:
            P.wait("sync", "wp_cast", g_row - 1)
        v = P.op("sync", lambda e, r=r: e.dma_start(
            out=stage[r % 2][:, 0:I], in_=w_h[r * 128:(r + 1) * 128, :]), sem=ldsem, by=16)
        P.wait("vector", ldsem, v)
        if g_row >= 2:
            hp = P._wp_hist[g_row - 2]
            if hp:
                P.wait("vector", "wp_pe", hp)
        if lpbq:
            nb = I // 32
            P.op("vector", lambda e, r=r: e.tensor_reduce(
                out=bmax[:, 0:nb], in_=stage[r % 2][:].rearrange("p (b i) -> p b i", i=32),
                axis=AXX, op=ALU.max, apply_absolute_value=True))
            P.op("vector", lambda e: e.tensor_scalar(
                out=sblk[:, 0:nb], in0=bmax[:, 0:nb], scalar1=1.0 / 7.0, scalar2=1e-12,
                op0=ALU.mult, op1=ALU.max))
            P.op("vector", lambda e: e.reciprocal(out=invblk[:, 0:nb], in_=sblk[:, 0:nb]))
            def pre(tbx, r=r):
                P.op("vector", lambda e, r=r, tbx=tbx: e.tensor_tensor(
                    out=tbx[:].rearrange("p (b i) -> p b i", i=32),
                    in0=stage[r % 2][:].rearrange("p (b i) -> p b i", i=32),
                    in1=invblk[:, 0:nb].rearrange("p b -> p b ()").broadcast_to((128, nb, 32)),
                    op=ALU.mult))
            def fin(tax, r=r):
                return P.op("vector", lambda e, r=r, tax=tax: e.tensor_tensor(
                    out=stage_bf[r % 2][:].rearrange("p (b i) -> p b i", i=32),
                    in0=tax[:].rearrange("p (b i) -> p b i", i=32),
                    in1=sblk[:, 0:nb].rearrange("p b -> p b ()").broadcast_to((128, nb, 32)),
                    op=ALU.mult), sem="wp_cast")
            emit_round(P, fin, None, tb, ta, g_col, pre=pre)
        elif fixq is not None:
            width, frac = fixq
            sc = float(2.0 ** frac)
            imax, imin = float(2 ** (width - 1) - 1), float(-2 ** (width - 1))
            def pre(tbx, r=r):
                P.op("vector", lambda e, r=r: e.tensor_scalar(
                    out=tb[:, 0:I], in0=stage[r % 2][:, 0:I], scalar1=sc, scalar2=None,
                    op0=ALU.mult))
            def fin(tax, r=r):
                P.op("vector", lambda e: e.tensor_scalar(
                    out=ta[:, 0:I], in0=ta[:, 0:I], scalar1=imax, scalar2=imin,
                    op0=ALU.min, op1=ALU.max))
                return P.op("vector", lambda e, r=r: e.tensor_scalar(
                    out=stage_bf[r % 2][:, 0:I], in0=ta[:, 0:I], scalar1=1.0 / sc,
                    scalar2=None, op0=ALU.mult), sem="wp_cast")
            tbv = type("S", (), {"__getitem__": lambda s_, k_: tb[:, 0:I]})()
            tav = type("S", (), {"__getitem__": lambda s_, k_: ta[:, 0:I]})()
            emit_round(P, fin, None, tbv, tav, g_col, pre=pre)
        else:
            P.op("vector", lambda e, r=r: e.tensor_copy(
                out=stage_bf[r % 2][:, 0:I], in_=stage[r % 2][:, 0:I]), sem="wp_cast")
        cast_c = P.counts["wp_cast"]
        rb = g_row % 2
        NG = NI // 4
        grp_ev = [0, 0]
        for gno in range(NG):
            kk = gno % 2
            cpsem = f"wp_cp{kk}"
            if P._wp_cp[kk]:
                P.wait("tensor", cpsem, P._wp_cp[kk])
            if gno < 2:
                P.wait("tensor", "wp_cast", cast_c)
            v = 0
            for j in range(4):
                ic = gno * 4 + j
                v = P.op("tensor", lambda e, r=r, ic=ic, j=j, kk=kk: e.transpose(
                    out=pe_tr[kk][:, j * 128:(j + 1) * 128],
                    in_=stage_bf[r % 2][:, ic * 128:(ic + 1) * 128],
                    identity=ident[:]), sem="wp_pe")
            if gno == 0 and P._wp_row[rb]:
                P.wait("scalar", "wp_st", P._wp_row[rb])
            P.wait("scalar", "wp_pe", v)
            c = P.op("scalar", lambda e, kk=kk, gno=gno, rb=rb: e.copy(
                out=rowbuf[rb][:, gno * 512:(gno + 1) * 512], in_=pe_tr[kk][:]),
                sem=cpsem)
            P._wp_cp[kk] = c
            grp_ev[kk] = c
        if gain_col is not None:
            for kk in range(2):
                if grp_ev[kk]:
                    P.wait("vector", f"wp_cp{kk}", grp_ev[kk])
            P.op("vector", lambda e, rb=rb, NIx=NI: e.tensor_tensor(
                out=rowbuf[rb][:, 0:NI * 128].rearrange("p (a c) -> p a c", c=128),
                in0=rowbuf[rb][:, 0:NI * 128].rearrange("p (a c) -> p a c", c=128),
                in1=gain_col[:, 0:NIx].rearrange("p a -> p a ()").broadcast_to((128, NIx, 128)),
                op=ALU.mult), sem="wp_g")
            P.wait("gpsimd", "wp_g", P.counts["wp_g"])
        else:
            for kk in range(2):
                if grp_ev[kk]:
                    P.wait("gpsimd", f"wp_cp{kk}", grp_ev[kk])
        P.wait("gpsimd", "wp_st", P.counts.get("wp_st", 0))
        if order == "r":
            st = P.op("gpsimd", lambda e, r=r, rb=rb, NIx=NI: e.dma_start(
                out=scratch_h[:, r * NIx * 128:(r + 1) * NIx * 128],
                in_=rowbuf[rb][:, 0:NIx * 128]), sem="wp_st", by=16)
        else:
            sc4 = scratch_h.rearrange("p (a b c) -> p a b c", b=NR, c=128)
            st = P.op("gpsimd", lambda e, r=r, rb=rb, NIx=NI, sc4=sc4: e.dma_start(
                out=sc4[:, :, r, :],
                in_=rowbuf[rb][:, 0:NIx * 128].rearrange("p (a c) -> p a c", c=128)),
                sem="wp_st", by=16)
        P._wp_row[rb] = st
        P._wp_hist.append(P.counts.get("wp_pe", 0))
    if "wp_st" in P.counts:
        P.wait("gpsimd", "wp_st", P.counts["wp_st"])


def _qdq_tile(P, out_fn, src_fn, invs, s, tb, ta, g_col, sem=None):
    def pre(tbx):
        P.op("vector", lambda e, tbx=tbx: src_fn(e, tbx))
    def fin(tax):
        return P.op("vector", lambda e, tax=tax: out_fn(e, tax), sem=sem)
    return emit_round(P, fin, None, tb, ta, g_col, pre=pre)


def build_launch5(dbg=False, upto=None):
    nc = new_nc()
    P = Prog(nc)
    P.want_const(EPS_LN)
    ident_h = nc.inline_tensor(np.eye(128, dtype=ml_dtypes.bfloat16), "ident_bf16").ap()

    x1_h = nc.dram_tensor("x1", [T, D], F32, kind="ExternalInput").ap()
    tq_h = nc.dram_tensor("t_q", [3 * D, T], BF16, kind="ExternalInput").ap()
    moa_h = nc.dram_tensor("m_o_all", [1, 8], F32, kind="ExternalInput").ap()
    wdown_h = nc.dram_tensor("w_down", [D, 3 * D], F32, kind="ExternalInput").ap()
    wq_h = nc.dram_tensor("wq", [D, D], F32, kind="ExternalInput").ap()
    bq_h = nc.dram_tensor("bq", [D], F32, kind="ExternalInput").ap()
    wk_h = nc.dram_tensor("wk", [D, D], F32, kind="ExternalInput").ap()
    bk_h = nc.dram_tensor("bk", [D], F32, kind="ExternalInput").ap()
    wv_h = nc.dram_tensor("wv", [D, D], F32, kind="ExternalInput").ap()
    bv_h = nc.dram_tensor("bv", [D], F32, kind="ExternalInput").ap()
    wo_h = nc.dram_tensor("wo", [D, D], F32, kind="ExternalInput").ap()
    bo_h = nc.dram_tensor("bo", [D], F32, kind="ExternalInput").ap()
    g1_h = nc.dram_tensor("g1", [D], F32, kind="ExternalInput").ap()
    b1_h = nc.dram_tensor("b1", [D], F32, kind="ExternalInput").ap()
    g2_h = nc.dram_tensor("g2", [D], F32, kind="ExternalInput").ap()
    b2_h = nc.dram_tensor("b2", [D], F32, kind="ExternalInput").ap()
    w1_h = nc.dram_tensor("w1", [4 * D, D], F32, kind="ExternalInput").ap()
    bf1_h = nc.dram_tensor("bf1", [4 * D], F32, kind="ExternalInput").ap()
    w2_h = nc.dram_tensor("w2", [D, 4 * D], F32, kind="ExternalInput").ap()
    bf2_h = nc.dram_tensor("bf2", [D], F32, kind="ExternalInput").ap()
    fw1_h = nc.dram_tensor("f_w1", [4 * D, D], F32, kind="ExternalInput").ap()
    fb1_h = nc.dram_tensor("f_b1", [4 * D], F32, kind="ExternalInput").ap()
    fw2_h = nc.dram_tensor("f_w2", [D, 4 * D], F32, kind="ExternalInput").ap()
    fb2_h = nc.dram_tensor("f_b2", [D], F32, kind="ExternalInput").ap()
    out_h = nc.dram_tensor("out", [T, D], F32, kind="ExternalOutput").ap()

    x2_hbm = nc.dram_tensor("x2_hbm", [T, D], F32, kind="Internal").ap()
    xin_hbm = nc.dram_tensor("xin_hbm", [T, D], BF16, kind="Internal").ap()
    x3_hbm = nc.dram_tensor("x3_hbm", [T, D], F32, kind="Internal").ap()
    wqT_s = nc.dram_tensor("wqT_s", [128, D * D // 128], BF16, kind="Internal").ap()
    wkT_s = nc.dram_tensor("wkT_s", [128, D * D // 128], BF16, kind="Internal").ap()
    wvT_s = nc.dram_tensor("wvT_s", [128, D * D // 128], BF16, kind="Internal").ap()
    woT_s = nc.dram_tensor("woT_s", [128, D * D // 128], BF16, kind="Internal").ap()
    w1T_s = nc.dram_tensor("w1T_s", [128, 4 * D * D // 128], BF16, kind="Internal").ap()
    w2T_s = nc.dram_tensor("w2T_s", [128, 4 * D * D // 128], BF16, kind="Internal").ap()
    fw1T_s = nc.dram_tensor("fw1T_s", [128, 4 * D * D // 128], BF16, kind="Internal").ap()
    fw2T_s = nc.dram_tensor("fw2T_s", [128, 4 * D * D // 128], BF16, kind="Internal").ap()
    if dbg:
        dx2_h = nc.dram_tensor("d_x2", [T, D], F32, kind="ExternalOutput").ap()
        dxx2_h = nc.dram_tensor("d_xx2", [T, D], F32, kind="ExternalOutput").ap()
        dqT_h = nc.dram_tensor("d_qT", [128, T], BF16, kind="ExternalOutput").ap()
        dkT_h = nc.dram_tensor("d_kT", [128, T], BF16, kind="ExternalOutput").ap()
        dvx_h = nc.dram_tensor("d_vx", [128, 1040], BF16, kind="ExternalOutput").ap()
        dat_h = nc.dram_tensor("d_at", [128, T], BF16, kind="ExternalOutput").ap()
        dk2_h = nc.dram_tensor("d_k2", [128, 256], F32, kind="ExternalOutput").ap()
        dcps_h = nc.dram_tensor("d_cps", [128, D], F32, kind="ExternalOutput").ap()
        dct1_h = nc.dram_tensor("d_ct1", [128, D], F32, kind="ExternalOutput").ap()
        dcn2_h = nc.dram_tensor("d_cn2", [128, 4], F32, kind="ExternalOutput").ap()

    ident_sb = P.sb("ident_sb", [128, 128], BF16)
    g_col = P.sb("g_col", [128, 1], F32)
    mrow8 = P.sb("mrow8", [128, 8], F32)
    so_col = P.sb("so_col", [128, 1], F32)
    invo_col = P.sb("invo_col", [128, 1], F32)
    g1c = P.sb("g1c", [128, ND], F32)
    b1c = P.sb("b1c", [128, ND], F32)
    g2c = P.sb("g2c", [128, ND], F32)
    b2c = P.sb("b2c", [128, ND], F32)
    b1g = P.sb("b1g", [128, ND], F32)
    b2g = P.sb("b2g", [128, ND], F32)
    b1g_bf = P.sb("b1g_bf", [128, ND], BF16)
    b2g_bf = P.sb("b2g_bf", [128, ND], BF16)
    ones_row = P.sb("ones_row", [1, 128], BF16)
    ones512 = P.sb("ones512", [1, 512], BF16)
    bf1col = P.sb("bf1col", [128, 32], F32)
    fb1col = P.sb("fb1col", [128, 32], F32)
    stat = P.sb("stat", [128, NT * 8], F32)
    k2s = P.sb("k2s", [128, NT * 16], F32)
    f16 = P.sb("f16", [128, 16], F32)
    n2t = P.sb("n2t", [128, 16], F32)
    junk_bf = P.sb("junk_bf", [128, D], BF16)
    amax_row = P.sb("amax_row", [1, 128], F32)

    ps_tr = [P.psum(f"ps_tr{k}", [128, 512], BF16) for k in range(2)]
    ps_mm = [P.psum(f"ps_mm{k}", [128, 512], F32) for k in range(6)]
    pool = PsumPool(P, ps_mm)

    MU0, RSTD0, MU1, RSTD1, SQ, SS = 0, NT, 2 * NT, 3 * NT, 4 * NT, 5 * NT
    MU2, RSTD2 = 6 * NT, 7 * NT

    P.op("sync", lambda e: e.dma_start(out=ident_sb[:], in_=ident_h), sem="ld_misc", by=16)
    P.op("sync", lambda e: e.dma_start(
        out=mrow8[:], in_=moa_h.partition_broadcast(128)), sem="ld_misc", by=16)
    ld_misc_total = P.counts["ld_misc"]
    emit_g_gadget(P, g_col)
    P.op("vector", lambda e: e.memset(ones_row[:], 1.0), sem="v_misc")
    P.op("vector", lambda e: e.memset(ones512[:], 1.0), sem="v_misc")
    load_colvec(P, g1_h, g1c, ND, "ld_vec")
    load_colvec(P, b1_h, b1c, ND, "ld_vec")
    load_colvec(P, g2_h, g2c, ND, "ld_vec")
    ld_vec_total = load_colvec(P, b2_h, b2c, ND, "ld_vec")
    P.wait("vector", "ld_vec", ld_vec_total)
    P.op("vector", lambda e: e.reciprocal(out=b1g[:], in_=g1c[:]))
    P.op("vector", lambda e: e.tensor_mul(out=b1g[:], in0=b1g[:], in1=b1c[:]))
    P.op("vector", lambda e: e.reciprocal(out=b2g[:], in_=g2c[:]))
    P.op("vector", lambda e: e.tensor_mul(out=b2g[:], in0=b2g[:], in1=b2c[:]))
    P.op("vector", lambda e: e.tensor_copy(out=b1g_bf[:], in_=b1g[:]))
    P.op("vector", lambda e: e.tensor_copy(out=b2g_bf[:], in_=b2g[:]), sem="bg_done")
    P.wait("vector", "ld_misc", ld_misc_total)
    emit_scale_cols(P, None, so_col, invo_col, mrow8, QMAX16, None, "sc_o")

    P.push_scope()
    wstage = [P.sb(f"wstage{k}", [128, 4 * D], F32) for k in range(2)]
    wstage_bf = [P.sb(f"wstage_bf{k}", [128, 4 * D], BF16) for k in range(2)]
    wrow = [P.sb(f"wrow{k}", [128, 4 * D], BF16) for k in range(2)]
    tbw = P.sb("tbw", [128, 4 * D], F32)
    taw = P.sb("taw", [128, 4 * D], F32)
    P.wait("vector", "bg_done", P.counts["bg_done"])
    P.wait("tensor", "ld_misc", ld_misc_total)
    emit_wprep(P, wq_h, D, D, wqT_s, wstage, wstage_bf, wrow, ident_sb,
               gain_col=g1c, tag="q", pe_tr=ps_tr, order="ic")
    emit_wprep(P, wk_h, D, D, wkT_s, wstage, wstage_bf, wrow, ident_sb,
               gain_col=g1c, tag="k", pe_tr=ps_tr, order="ic")
    emit_wprep(P, wv_h, D, D, wvT_s, wstage, wstage_bf, wrow, ident_sb,
               gain_col=g1c, tag="v", pe_tr=ps_tr, order="ic")
    emit_wprep(P, wo_h, D, D, woT_s, wstage, wstage_bf, wrow, ident_sb, tag="o",
               pe_tr=ps_tr, order="ic")
    emit_wprep(P, w1_h, 4 * D, D, w1T_s, wstage, wstage_bf, wrow, ident_sb,
               gain_col=g2c, tag="w1", pe_tr=ps_tr, order="r")
    emit_wprep(P, w2_h, D, 4 * D, w2T_s, wstage, wstage_bf, wrow, ident_sb,
               tag="w2", pe_tr=ps_tr, order="ic")
    emit_wprep(P, fw1_h, 4 * D, D, fw1T_s, wstage, wstage_bf, wrow, ident_sb,
               fixq=(8, 6), g_col=g_col, tb=tbw, ta=taw, tag="f1", pe_tr=ps_tr,
               order="r")
    emit_wprep(P, fw2_h, D, 4 * D, fw2T_s, wstage, wstage_bf, wrow, ident_sb,
               fixq=(8, 6), g_col=g_col, tb=tbw, ta=taw, tag="f2", pe_tr=ps_tr,
               order="ic")
    P.pop_scope()
    P.barrier()
    if upto == "P0":
        return P, dict(locals())

    P.push_scope()
    x_sb = [P.sb(f"xA{i}", [128, D], F32) for i in range(NT)]
    wdT = [P.sb(f"wdT{ic}", [128, D], BF16) for ic in range(24)]
    oq = [[P.sb(f"oq0_{ic}", [128, 512], BF16) for ic in range(24)]]
    HW = 3 * D // 2
    stA = [P.sb(f"stA{k}", [128, HW], F32) for k in range(2)]
    stA_bf = [P.sb(f"stA_bf{k}", [128, HW], BF16) for k in range(2)]
    tqst = [P.sb(f"tqst{k}", [128, 512], BF16) for k in range(2)]
    tbA = P.sb("tbA", [128, HW], F32)
    taA = P.sb("taA", [128, HW], F32)
    xinb = P.sb("xinb", [128, D], BF16)

    for i in range(NT):
        P.op("sync", lambda e, i=i: e.dma_start(
            out=x_sb[i][:], in_=x1_h[i * 128:(i + 1) * 128, :]), sem="ld_x", by=16)
    ld_x_total = P.counts["ld_x"]

    bmaxA = P.sb("bmaxA", [128, 48], F32)
    sblkA = P.sb("sblkA", [128, 48], F32)
    invblkA = P.sb("invblkA", [128, 48], F32)
    nwch = 0
    wd_trhist = []
    wd_cp = {0: 0, 1: 0}
    for r in range(ND):
        for h in range(2):
            k = nwch % 2
            if nwch >= 2:
                P.wait("sync", "wd_cast", nwch - 1)
            v = P.op("sync", lambda e, r=r, h=h, k=k: e.dma_start(
                out=stA[k][:], in_=wdown_h[r * 128:(r + 1) * 128, h * HW:(h + 1) * HW]),
                sem=f"wd_ld{k}", by=16)
            P.wait("vector", f"wd_ld{k}", v)
            if nwch >= 2:
                if wd_trhist[nwch - 2]:
                    P.wait("vector", "wd_pe", wd_trhist[nwch - 2])
            P.op("vector", lambda e, k=k: e.tensor_reduce(
                out=bmaxA[:], in_=stA[k][:].rearrange("p (b i) -> p b i", i=32),
                axis=AXX, op=ALU.max, apply_absolute_value=True))
            P.op("vector", lambda e: e.tensor_scalar(
                out=sblkA[:], in0=bmaxA[:], scalar1=1.0 / 7.0, scalar2=1e-12,
                op0=ALU.mult, op1=ALU.max))
            P.op("vector", lambda e: e.reciprocal(out=invblkA[:], in_=sblkA[:]))
            def pre(tbx, k=k):
                P.op("vector", lambda e, k=k, tbx=tbx: e.tensor_tensor(
                    out=tbx[:].rearrange("p (b i) -> p b i", i=32),
                    in0=stA[k][:].rearrange("p (b i) -> p b i", i=32),
                    in1=invblkA[:].rearrange("p b -> p b ()").broadcast_to((128, 48, 32)),
                    op=ALU.mult))
            def fin(tax, k=k):
                return P.op("vector", lambda e, k=k, tax=tax: e.tensor_tensor(
                    out=stA_bf[k][:].rearrange("p (b i) -> p b i", i=32),
                    in0=tax[:].rearrange("p (b i) -> p b i", i=32),
                    in1=sblkA[:].rearrange("p b -> p b ()").broadcast_to((128, 48, 32)),
                    op=ALU.mult), sem="wd_cast")
            emit_round(P, fin, None, tbA, taA, g_col, pre=pre)
            cast_c = P.counts["wd_cast"]
            for icl in range(12):
                ic = h * 12 + icl
                ntr = nwch * 12 + icl
                kk = ntr % 2
                if wd_cp[kk]:
                    P.wait("tensor", f"wd_cp{kk}", wd_cp[kk])
                P.wait("tensor", "wd_cast", cast_c)
                v = P.op("tensor", lambda e, icl=icl, kk=kk, k=k: e.transpose(
                    out=ps_tr[kk][:, 0:128], in_=stA_bf[k][:, icl * 128:(icl + 1) * 128],
                    identity=ident_sb[:]), sem="wd_pe")
                P.wait("scalar", "wd_pe", v)
                c = P.op("scalar", lambda e, r=r, ic=ic, kk=kk: e.copy(
                    out=wdT[ic][:, r * 128:(r + 1) * 128], in_=ps_tr[kk][:, 0:128]),
                    sem=f"wd_cp{kk}")
                wd_cp[kk] = c
            wd_trhist.append(P.counts.get("wd_pe", 0))
            nwch += 1
    for kk in range(2):
        P.wait("tensor", f"wd_cp{kk}", P.counts.get(f"wd_cp{kk}", 0))

    P.wait("vector", "sc_o", P.counts["sc_o"])
    for qt in range(4):
        kq = 0
        if qt >= 1:
            P.wait("vector", "mmA", P.counts["mmA"])
        for ic in range(24):
            sl = ic % 2
            nld = qt * 24 + ic
            if nld >= 2:
                P.wait("sync", "oq_done", nld - 1)
            v = P.op("sync", lambda e, qt=qt, ic=ic, sl=sl: e.dma_start(
                out=tqst[sl][:], in_=tq_h[ic * 128:(ic + 1) * 128,
                                          qt * 512:(qt + 1) * 512]),
                sem=f"tq_ld{sl}", by=16)
            P.wait("vector", f"tq_ld{sl}", v)
            def src_fn(e, tbx, sl=sl):
                return e.tensor_scalar(out=tbx[:, 0:512], in0=tqst[sl][:],
                                       scalar1=invo_col[:, 0:1], scalar2=None,
                                       op0=ALU.mult)
            def out_fn(e, tax, qt=qt, ic=ic, kq=kq):
                return e.tensor_scalar(out=oq[kq][ic][:], in0=tax[:, 0:512],
                                       scalar1=so_col[:, 0:1], scalar2=None,
                                       op0=ALU.mult)
            def pre(tbx, sl=sl):
                P.op("vector", lambda e, tbx=tbx, sl=sl: e.tensor_scalar(
                    out=tbx[:, 0:512], in0=tqst[sl][:], scalar1=invo_col[:, 0:1],
                    scalar2=None, op0=ALU.mult))
            def fin(tax, qt=qt, ic=ic, kq=kq):
                return P.op("vector", lambda e, tax=tax, qt=qt, ic=ic, kq=kq: e.tensor_scalar(
                    out=oq[kq][ic][:], in0=tax[:, 0:512], scalar1=so_col[:, 0:1],
                    scalar2=None, op0=ALU.mult), sem="oq_done")
            class _V:
                pass
            tbv = type("S", (), {"__getitem__": lambda s, k: tbA[:, 0:512]})()
            tav = type("S", (), {"__getitem__": lambda s, k: taA[:, 0:512]})()
            emit_round(P, fin, None, tbv, tav, g_col, pre=pre)
        oq_ready = P.counts["oq_done"]
        P.wait("tensor", "oq_done", oq_ready)
        for tt in range(4):
            t_glob = qt * 4 + tt
            banks = [0, 1] if tt % 2 == 0 else [2, 3]
            for ic in range(24):
                for o2 in range(2):
                    bk = banks[o2]
                    if ic == 0:
                        pool.acquire(bk, "tensor")
                    P.op("tensor", lambda e, qt=qt, tt=tt, ic=ic, o2=o2, bk=bk, kq=kq: e.matmul(
                        out=ps_mm[bk][:],
                        lhsT=oq[kq][ic][:, tt * 128:(tt + 1) * 128],
                        rhs=wdT[ic][:, o2 * 512:(o2 + 1) * 512],
                        start=(ic == 0), stop=(ic == 23)),
                        sem="mmA" if ic == 23 else None)
            mm_c = P.counts["mmA"]
            P.wait("vector", "ld_x", ld_x_total)
            for o2 in range(2):
                bk = banks[o2]
                P.wait("vector", "mmA", mm_c)
                ev = P.op("vector", lambda e, t_glob=t_glob, o2=o2, bk=bk: e.tensor_add(
                    out=x_sb[t_glob][:, o2 * 512:(o2 + 1) * 512],
                    in0=x_sb[t_glob][:, o2 * 512:(o2 + 1) * 512],
                    in1=ps_mm[bk][:]), sem="evA")
                pool.release(bk, "evA", ev)
            P.op("vector", lambda e, t_glob=t_glob: e.reduce_sum(
                out=stat[:, SS + t_glob:SS + t_glob + 1], in_=x_sb[t_glob][:],
                axis=AXX), sem="sv")
            P.wait("scalar", "evA", P.counts["evA"])
            P.op("scalar", lambda e, t_glob=t_glob: e.activation(
                out=junk_bf[:], in_=x_sb[t_glob][:], func=AF.Square,
                accum_out=stat[:, SQ + t_glob:SQ + t_glob + 1]), sem="sa")
            P.wait("gpsimd", "evA", P.counts["evA"])
            P.op("gpsimd", lambda e, t_glob=t_glob: e.dma_start(
                out=x2_hbm[t_glob * 128:(t_glob + 1) * 128, :], in_=x_sb[t_glob][:]),
                sem="st_x2", by=16)
            if dbg:
                P.op("gpsimd", lambda e, t_glob=t_glob: e.dma_start(
                    out=dx2_h[t_glob * 128:(t_glob + 1) * 128, :], in_=x_sb[t_glob][:]),
                    sem="st_x2", by=16)

    P.wait("vector", "sa", P.counts["sa"])
    P.wait("vector", "sv", P.counts["sv"])
    P.op("vector", lambda e: e.tensor_scalar(
        out=stat[:, MU0:MU0 + NT], in0=stat[:, SS:SS + NT], scalar1=1.0 / D,
        scalar2=None, op0=ALU.mult))
    P.op("vector", lambda e: e.tensor_mul(
        out=stat[:, RSTD0:RSTD0 + NT], in0=stat[:, MU0:MU0 + NT],
        in1=stat[:, MU0:MU0 + NT]))
    vm = P.op("vector", lambda e: e.scalar_tensor_tensor(
        out=stat[:, RSTD0:RSTD0 + NT], in0=stat[:, SQ:SQ + NT], scalar=1.0 / D,
        in1=stat[:, RSTD0:RSTD0 + NT], op0=ALU.mult, op1=ALU.subtract), sem="sv")
    P.wait("scalar", "sv", vm)
    va = P.op("scalar", lambda e: e.activation(
        out=stat[:, RSTD0:RSTD0 + NT], in_=stat[:, RSTD0:RSTD0 + NT],
        func=AF.Sqrt, bias=EPS_LN, scale=1.0), sem="sa")
    P.wait("vector", "sa", va)
    P.op("vector", lambda e: e.reciprocal(
        out=stat[:, RSTD0:RSTD0 + NT], in_=stat[:, RSTD0:RSTD0 + NT]), sem="sv")
    for i in range(NT):
        P.op("vector", lambda e, i=i: e.tensor_scalar(
            out=xinb[:], in0=x_sb[i][:],
            scalar1=stat[:, MU0 + i:MU0 + i + 1], scalar2=stat[:, RSTD0 + i:RSTD0 + i + 1],
            op0=ALU.subtract, op1=ALU.mult))
        P.op("vector", lambda e, i=i: e.reduce_sum(
            out=stat[:, SS + i:SS + i + 1], in_=xinb[:], axis=AXX), sem="sv")
        P.wait("scalar", "sv", P.counts["sv"])
        P.op("scalar", lambda e, i=i: e.activation(
            out=junk_bf[:], in_=xinb[:], func=AF.Square,
            accum_out=stat[:, SQ + i:SQ + i + 1]), sem="sa")
        P.wait("gpsimd", "sa", P.counts["sa"])
        P.op("gpsimd", lambda e, i=i: e.dma_start(
            out=xin_hbm[i * 128:(i + 1) * 128, :], in_=xinb[:]), sem="st_xin", by=16)
        P.wait("vector", "st_xin", P.counts["st_xin"])
    P.op("vector", lambda e: e.tensor_scalar(
        out=stat[:, MU1:MU1 + NT], in0=stat[:, SS:SS + NT], scalar1=1.0 / D,
        scalar2=None, op0=ALU.mult))
    P.op("vector", lambda e: e.tensor_mul(
        out=stat[:, RSTD1:RSTD1 + NT], in0=stat[:, MU1:MU1 + NT],
        in1=stat[:, MU1:MU1 + NT]))
    vm = P.op("vector", lambda e: e.scalar_tensor_tensor(
        out=stat[:, RSTD1:RSTD1 + NT], in0=stat[:, SQ:SQ + NT], scalar=1.0 / D,
        in1=stat[:, RSTD1:RSTD1 + NT], op0=ALU.mult, op1=ALU.subtract), sem="sv")
    P.wait("scalar", "sv", vm)
    va = P.op("scalar", lambda e: e.activation(
        out=stat[:, RSTD1:RSTD1 + NT], in_=stat[:, RSTD1:RSTD1 + NT],
        func=AF.Sqrt, bias=EPS_LN, scale=1.0), sem="sa")
    P.wait("vector", "sa", va)
    P.op("vector", lambda e: e.reciprocal(
        out=stat[:, RSTD1:RSTD1 + NT], in_=stat[:, RSTD1:RSTD1 + NT]), sem="stats1_done")
    P.wait("gpsimd", "st_x2", P.counts["st_x2"])
    P.wait("gpsimd", "st_xin", P.counts["st_xin"])
    P.pop_scope()
    P.barrier()
    return P, dict(locals())


def build_launch5_full(dbg=False, upto=None):
    P, env = build_launch5(dbg=dbg, upto=upto)
    if upto in ("P0", "A"):
        return P.finish()
    _emit_l5_attention(P, env, dbg=dbg, cut=("B1" if upto == "B1" else None))
    if upto in ("B", "B1"):
        return P.finish()
    _emit_l5_tail(P, env, dbg=dbg)
    return P.finish()


def _emit_l5_attention(P, env, dbg=False, cut=None):
    nc = P.nc
    g = env
    stat, g_col, ident_sb, k2s, f16, n2t = g["stat"], g["g_col"], g["ident_sb"], g["k2s"], g["f16"], g["n2t"]
    ones_row, junk_bf = g["ones_row"], g["junk_bf"]
    ps_tr, ps_mm, pool = g["ps_tr"], g["ps_mm"], g["pool"]
    b1g = g["b1g"]
    b1g_bf = g["b1g_bf"]
    MU1, RSTD1, SQ, SS = g["MU1"], g["RSTD1"], g["SQ"], g["SS"]
    MU2, RSTD2 = g["MU2"], g["RSTD2"]
    xin_hbm = g["xin_hbm"]
    MAXN = 1.0 - 1e-05

    P.push_scope()
    xb = [P.sb(f"xb{i}", [128, D], BF16) for i in range(NT)]
    g["xb"] = xb
    brow_f = [P.sb("brow_f0", [1, D], F32)] * 4
    brow_b = [P.sb("brow_b0", [1, D], BF16)] * 4
    P.push_scope()
    qT = [P.sb(f"qT{p}", [128, T], BF16) for p in range(8)]
    kT = [P.sb(f"kT{p}", [128, T], BF16) for p in range(8)]
    v_ext = [P.sb(f"vext{i}", [128, 16 * 65], BF16) for i in range(NT)]

    P.push_scope()
    xnT = [P.sb(f"xnT{c}", [128, T], BF16) for c in range(ND)]
    wT_res = [P.sb(f"wTr{ic}", [128, D], BF16) for ic in range(ND)]
    qk_tm = xb
    xtile = [P.sb(f"xtile{k}", [128, D], BF16) for k in range(2)]

    P.wait("vector", "stats1_done", P.counts["stats1_done"])
    for i in range(NT):
        k = i % 2
        if i >= 2:
            P.wait("sync", "xnstage", i - 1)
        v = P.op("sync", lambda e, i=i, k=k: e.dma_start(
            out=xtile[k][:], in_=xin_hbm[i * 128:(i + 1) * 128, :]),
            sem=f"xin_ld{k}", by=16)
        P.wait("vector", f"xin_ld{k}", v)
        P.op("vector", lambda e, i=i, k=k: e.tensor_scalar(
            out=qk_tm[i][:], in0=xtile[k][:],
            scalar1=stat[:, MU1 + i:MU1 + i + 1], scalar2=stat[:, RSTD1 + i:RSTD1 + i + 1],
            op0=ALU.subtract, op1=ALU.mult), sem="xnstage")
    xn_c = P.counts["xnstage"]
    P.wait("tensor", "xnstage", xn_c)
    P.wait("tensor", "ld_misc", 16)
    ntr = 0
    for i in range(NT):
        for c in range(ND):
            kk = ntr % 2
            if ntr >= 2:
                P.wait("tensor", "trB1", ntr - 1)
            P.op("tensor", lambda e, i=i, c=c, kk=kk: e.transpose(
                out=ps_tr[kk][:, 0:128], in_=qk_tm[i][:, c * 128:(c + 1) * 128],
                identity=ident_sb[:]), sem="trp")
            P.wait("scalar", "trp", P.counts["trp"])
            P.op("scalar", lambda e, i=i, c=c, kk=kk: e.copy(
                out=xnT[c][:, i * 128:(i + 1) * 128], in_=ps_tr[kk][:, 0:128]), sem="trB1")
            ntr += 1
    trB1_total = P.counts["trB1"]

    def load_wres(scr, tag, dst=None):
        if dst is None:
            dst = wT_res
        for ic in range(ND):
            P.op("sync", lambda e, ic=ic, scr=scr, dst=dst: e.dma_start(
                out=dst[ic][:], in_=scr[:, ic * ND * 128:(ic + 1) * ND * 128]),
                sem="wres", by=16)
        return P.counts["wres"]

    def bias_matvec(bvec_col, bias_h, w_row, tag, wres_c=None):
        P.op("sync", lambda e, bias_h=bias_h, w_row=w_row: e.dma_start(
            out=brow_f[w_row][:], in_=bias_h.rearrange("(a n) -> a n", a=1)),
            sem="bld", by=16)
        if wres_c is not None:
            P.wait("tensor", "wres", wres_c)
        for o2 in range(2):
            bk = o2
            pool.acquire(bk, "tensor")
            for ic in range(ND):
                P.op("tensor", lambda e, ic=ic, o2=o2, bk=bk: e.matmul(
                    out=ps_mm[bk][0:1, :],
                    lhsT=b1g_bf[:, ic:ic + 1],
                    rhs=wT_res[ic][:, o2 * 512:(o2 + 1) * 512],
                    start=(ic == 0), stop=(ic == ND - 1)),
                    sem="bmv" if ic == ND - 1 else None)
        P.wait("vector", "bmv", P.counts["bmv"])
        P.wait("vector", "bld", P.counts["bld"])
        for o2 in range(2):
            ev = P.op("vector", lambda e, o2=o2, w_row=w_row: e.tensor_add(
                out=brow_b[w_row][0:1, o2 * 512:(o2 + 1) * 512],
                in0=ps_mm[o2][0:1, :],
                in1=brow_f[w_row][0:1, o2 * 512:(o2 + 1) * 512]), sem="bev")
            pool.release(o2, "bev", ev)
        return P.counts["bev"]

    def qkv_mm(w_row, tag, evac):
        for t in range(NT):
            banks = [0, 1] if t % 2 == 0 else [2, 3]
            for ic in range(ND):
                for o2 in range(2):
                    bk = banks[o2]
                    if ic == 0:
                        pool.acquire(bk, "tensor")
                    P.op("tensor", lambda e, t=t, ic=ic, o2=o2, bk=bk: e.matmul(
                        out=ps_mm[bk][:],
                        lhsT=xnT[ic][:, t * 128:(t + 1) * 128],
                        rhs=wT_res[ic][:, o2 * 512:(o2 + 1) * 512],
                        start=(ic == 0), stop=False))
            for o2 in range(2):
                bk = banks[o2]
                P.op("tensor", lambda e, t=t, o2=o2, bk=bk, w_row=w_row: e.matmul(
                    out=ps_mm[bk][:],
                    lhsT=ones_row[0:1, :],
                    rhs=brow_b[w_row][0:1, o2 * 512:(o2 + 1) * 512],
                    start=False, stop=True), sem="mm_qkv")
            evac(t, P.counts["mm_qkv"], banks)

    wres_c = load_wres(g["wqT_s"], "q")
    bias_matvec(b1g, g["bq_h"], 0, "q", wres_c=wres_c)
    P.wait("tensor", "trB1", trB1_total)
    P.wait("tensor", "wres", wres_c)
    sq_scr = P.sb("sq_scr", [128, D], BF16)

    def evac_qk(which):
        def evac(t, mm_c, banks):
            for o2 in range(2):
                bk = banks[o2]
                P.wait("scalar", "mm_qkv", mm_c)
                ev = P.op("scalar", lambda e, t=t, o2=o2, bk=bk: e.copy(
                    out=qk_tm[t][:, o2 * 512:(o2 + 1) * 512], in_=ps_mm[bk][:]),
                    sem="ev_qkv")
                pool.release(bk, "ev_qkv", ev)
            P.wait("vector", "ev_qkv", P.counts["ev_qkv"])
            P.op("vector", lambda e, t=t: e.tensor_mul(
                out=sq_scr[:], in0=qk_tm[t][:], in1=qk_tm[t][:]))
            nv = P.op("vector", lambda e: e.tensor_reduce(
                out=n2t[:], in_=sq_scr[:].rearrange("p (h d) -> p h d", d=64),
                axis=AXX, op=ALU.add), sem="n2_qk")
            P.wait("scalar", "n2_qk", nv)
            na = P.op("scalar", lambda e: e.activation(
                out=f16[:], in_=n2t[:], func=AF.Sqrt), sem="sq_qk")
            P.wait("vector", "sq_qk", na)
            P.op("vector", lambda e: e.tensor_scalar(
                out=f16[:], in0=f16[:], scalar1=1e-12, scalar2=None, op0=ALU.max))
            P.op("vector", lambda e: e.reciprocal(out=f16[:], in_=f16[:]))
            P.op("vector", lambda e: e.tensor_scalar(
                out=f16[:], in0=f16[:], scalar1=MAXN, scalar2=1.0,
                op0=ALU.mult, op1=ALU.min))
            if which == "k":
                P.op("vector", lambda e: e.tensor_mul(
                    out=n2t[:], in0=n2t[:], in1=f16[:]))
                P.op("vector", lambda e: e.tensor_mul(
                    out=n2t[:], in0=n2t[:], in1=f16[:]))
                P.op("vector", lambda e, t=t: e.tensor_scalar(
                    out=k2s[:, t * 16:(t + 1) * 16], in0=n2t[:], scalar1=-0.125,
                    scalar2=None, op0=ALU.mult), sem="k2s_done")
            P.op("vector", lambda e, t=t: e.tensor_tensor(
                out=qk_tm[t][:].rearrange("p (h d) -> p h d", d=64),
                in0=qk_tm[t][:].rearrange("p (h d) -> p h d", d=64),
                in1=f16[:].rearrange("p h -> p h ()").broadcast_to((128, 16, 64)),
                op=ALU.mult), sem="proj_qk")
        return evac
    qkv_mm(0, "q", evac_qk("q"))
    P.wait("tensor", "proj_qk", P.counts["proj_qk"])
    ntr = 0
    for i in range(NT):
        for c in range(ND):
            kk = ntr % 2
            if ntr >= 2:
                P.wait("tensor", "trQ", ntr - 1)
            P.op("tensor", lambda e, i=i, c=c, kk=kk: e.transpose(
                out=ps_tr[kk][:, 0:128], in_=qk_tm[i][:, c * 128:(c + 1) * 128],
                identity=ident_sb[:]), sem="trp")
            P.wait("scalar", "trp", P.counts["trp"])
            P.op("scalar", lambda e, i=i, c=c, kk=kk: e.copy(
                out=qT[c][:, i * 128:(i + 1) * 128], in_=ps_tr[kk][:, 0:128]), sem="trQ")
            ntr += 1
    trQ_total = P.counts["trQ"]
    P.wait("sync", "trQ", trQ_total)
    wres_k = load_wres(g["wkT_s"], "k")
    bias_matvec(b1g, g["bk_h"], 1, "k", wres_c=wres_k)
    P.wait("tensor", "wres", wres_k)
    qkv_mm(1, "k", evac_qk("k"))
    P.wait("tensor", "proj_qk", P.counts["proj_qk"])
    ntr = 0
    for i in range(NT):
        for c in range(ND):
            kk = ntr % 2
            if ntr >= 2:
                P.wait("tensor", "trK", ntr - 1)
            P.op("tensor", lambda e, i=i, c=c, kk=kk: e.transpose(
                out=ps_tr[kk][:, 0:128], in_=qk_tm[i][:, c * 128:(c + 1) * 128],
                identity=ident_sb[:]), sem="trp")
            P.wait("scalar", "trp", P.counts["trp"])
            P.op("scalar", lambda e, i=i, c=c, kk=kk: e.copy(
                out=kT[c][:, i * 128:(i + 1) * 128], in_=ps_tr[kk][:, 0:128]), sem="trK")
            ntr += 1
    for i in range(NT):
        P.op("vector", lambda e, i=i: e.memset(
            v_ext[i][:].rearrange("p (h c) -> p h c", c=65)[:, :, 64:65], 1.0),
            sem="vone")
    P.wait("sync", "trK", P.counts["trK"])
    wres_v = load_wres(g["wvT_s"], "v")
    bias_matvec(b1g, g["bv_h"], 2, "v", wres_c=wres_v)
    P.wait("tensor", "wres", wres_v)

    def evac_v(t, mm_c, banks):
        for o2 in range(2):
            bk = banks[o2]
            P.wait("scalar", "mm_qkv", mm_c)
            ev = P.op("scalar", lambda e, t=t, o2=o2, bk=bk: e.copy(
                out=v_ext[t][:].rearrange("p (h c) -> p h c", c=65)[:, o2 * 8:(o2 + 1) * 8, 0:64],
                in_=ps_mm[bk][:].rearrange("p (h d) -> p h d", d=64)), sem="ev_v")
            pool.release(bk, "ev_v", ev)
    qkv_mm(2, "v", evac_v)
    P.wait("tensor", "ev_v", P.counts["ev_v"])
    P.wait("tensor", "vone", P.counts["vone"])
    P.pop_scope()
    P.barrier()
    if cut == "B1":
        return

    P.push_scope()
    attn = [P.sb(f"attn{p}", [128, T], BF16) for p in range(8)]
    P.push_scope()
    e_sl = [P.sb(f"esl{k}", [128, 512], BF16) for k in range(4)]
    pvs = [P.sb(f"pvs{k}", [128, 512], F32) for k in range(2)]
    rs_row = P.sb("rs_row", [1, 512], F32)
    rsb_row = P.sb("rsb_row", [1, 512], BF16)

    P.wait("scalar", "k2s_done", P.counts["k2s_done"])
    P.wait("tensor", "trK", P.counts["trK"])
    pv_hist = []
    rb_hist = [0, 0]
    for p in range(8):
        for qt in range(4):
            pool.acquire(4, "tensor")
            pool.acquire(5, "tensor")
            for kt in range(16):
                sA = 0 if kt % 2 == 0 else 2
                sB = 1 if kt % 2 == 0 else 3
                pool.acquire(sA, "tensor")
                pool.acquire(sB, "tensor")
                P.op("tensor", lambda e, p=p, qt=qt, kt=kt, sA=sA: e.matmul(
                    out=ps_mm[sA][:],
                    lhsT=kT[p][0:64, kt * 128:(kt + 1) * 128],
                    rhs=qT[p][0:64, qt * 512:(qt + 1) * 512],
                    start=True, stop=True), sem="mm_s")
                P.op("tensor", lambda e, p=p, qt=qt, kt=kt, sB=sB: e.matmul(
                    out=ps_mm[sB][:],
                    lhsT=kT[p][64:128, kt * 128:(kt + 1) * 128],
                    rhs=qT[p][64:128, qt * 512:(qt + 1) * 512],
                    start=True, stop=True), sem="mm_s")
                mm_s_c = P.counts["mm_s"]
                eA = e_sl[(2 * kt) % 4]
                eB = e_sl[(2 * kt + 1) % 4]
                gk = len(pv_hist) - 2
                if gk >= 0:
                    P.wait("scalar", "mm_pv", pv_hist[gk])
                P.wait("scalar", "mm_s", mm_s_c)
                evA = P.op("scalar", lambda e, p=p, kt=kt, sA=sA, eA=eA: e.activation(
                    out=eA[:], in_=ps_mm[sA][:], func=AF.Exp,
                    bias=k2s[:, kt * 16 + 2 * p:kt * 16 + 2 * p + 1], scale=0.25),
                    sem="ev_e")
                evB = P.op("scalar", lambda e, p=p, kt=kt, sB=sB, eB=eB: e.activation(
                    out=eB[:], in_=ps_mm[sB][:], func=AF.Exp,
                    bias=k2s[:, kt * 16 + 2 * p + 1:kt * 16 + 2 * p + 2], scale=0.25),
                    sem="ev_e")
                pool.release(sA, "ev_e", evA)
                pool.release(sB, "ev_e", evB)
                P.wait("tensor", "ev_e", evA)
                P.op("tensor", lambda e, p=p, qt=qt, kt=kt, eA=eA: e.matmul(
                    out=ps_mm[4][0:65, :],
                    lhsT=v_ext[kt][:, (2 * p) * 65:(2 * p + 1) * 65],
                    rhs=eA[:], start=(kt == 0), stop=(kt == 15)), sem="mm_pv")
                P.wait("tensor", "ev_e", evB)
                P.op("tensor", lambda e, p=p, qt=qt, kt=kt, eB=eB: e.matmul(
                    out=ps_mm[5][0:65, :],
                    lhsT=v_ext[kt][:, (2 * p + 1) * 65:(2 * p + 2) * 65],
                    rhs=eB[:], start=(kt == 0), stop=(kt == 15)), sem="mm_pv")
                pv_hist.append(P.counts["mm_pv"])
            mm_pv_c = P.counts["mm_pv"]
            for hh in range(2):
                bank = 4 + hh
                P.wait("scalar", "mm_pv", mm_pv_c)
                if rb_hist[hh]:
                    P.wait("scalar", "attn_ev", rb_hist[hh])
                pe_ = P.op("scalar", lambda e, hh=hh, bank=bank: e.copy(
                    out=pvs[hh][0:65, :], in_=ps_mm[bank][0:65, :]), sem="pv_ev")
                pool.release(bank, "pv_ev", pe_)
                P.wait("vector", "pv_ev", pe_)
                P.op("vector", lambda e, hh=hh: e.tensor_copy(
                    out=rs_row[:], in_=pvs[hh][64:65, :]))
                P.op("vector", lambda e: e.reciprocal(out=rs_row[:], in_=rs_row[:]))
                rd = P.op("vector", lambda e: e.tensor_copy(
                    out=rsb_row[:], in_=rs_row[:]), sem="rs_done")
                P.wait("tensor", "rs_done", rd)
                bcb = 0 if hh == 0 else 1
                pool.acquire(bcb, "tensor")
                P.op("tensor", lambda e, bcb=bcb: e.matmul(
                    out=ps_mm[bcb][0:64, :],
                    lhsT=ones_row[0:1, 0:64],
                    rhs=rsb_row[0:1, :], start=True, stop=True), sem="mm_bc")
                P.wait("vector", "mm_bc", P.counts["mm_bc"])
                ev = P.op("vector", lambda e, p=p, qt=qt, hh=hh, bcb=bcb: e.tensor_mul(
                    out=attn[p][hh * 64:(hh + 1) * 64, qt * 512:(qt + 1) * 512],
                    in0=pvs[hh][0:64, :], in1=ps_mm[bcb][0:64, :]), sem="attn_ev")
                rb_hist[hh] = ev
                pool.release(bcb, "attn_ev", ev)

    attn_total = P.counts["attn_ev"]
    if dbg:
        P.wait("gpsimd", "attn_ev", attn_total)
        P.op("gpsimd", lambda e: e.dma_start(out=g["dqT_h"], in_=qT[0][:]), sem="st_dbg", by=16)
        P.op("gpsimd", lambda e: e.dma_start(out=g["dkT_h"], in_=kT[0][:]), sem="st_dbg", by=16)
        P.op("gpsimd", lambda e: e.dma_start(out=g["dvx_h"], in_=v_ext[0][:]), sem="st_dbg", by=16)
        P.op("gpsimd", lambda e: e.dma_start(out=g["dat_h"], in_=attn[0][:]), sem="st_dbg", by=16)
        P.op("gpsimd", lambda e: e.dma_start(out=g["dk2_h"], in_=k2s[:]), sem="st_dbg", by=16)
        P.wait("gpsimd", "st_dbg", P.counts["st_dbg"])
    P.pop_scope()
    P.barrier()
    P.push_scope()
    woT_res = [P.sb(f"woTr{ic}", [128, D], BF16) for ic in range(ND)]
    xtile2 = [P.sb(f"xtile2{k}", [128, D], BF16) for k in range(2)]
    t1f = P.sb("t1f", [128, D], F32)
    sq_scr_b2 = P.sb("sq_scr_b2", [128, D], F32)
    n2c = P.sb("n2c", [128, 1], F32)
    wres_o = load_wres(g["woT_s"], "o", dst=woT_res)
    P.op("sync", lambda e: e.dma_start(
        out=brow_f[3][:], in_=g["bo_h"].rearrange("(a n) -> a n", a=1)), sem="bld", by=16)
    P.wait("vector", "bld", P.counts["bld"])
    P.op("vector", lambda e: e.tensor_copy(out=brow_b[3][:], in_=brow_f[3][:]), sem="bo_cast")
    P.wait("tensor", "wres", wres_o)
    P.wait("tensor", "attn_ev", attn_total)
    P.wait("tensor", "bo_cast", P.counts["bo_cast"])
    for t in range(NT):
        kx = t % 2
        if t >= 2:
            P.wait("sync", "xx2_done", t - 1)
        vx = P.op("sync", lambda e, t=t, kx=kx: e.dma_start(
            out=xtile2[kx][:], in_=xin_hbm[t * 128:(t + 1) * 128, :]),
            sem=f"xin2_{kx}", by=16)
        banks = [0, 1] if t % 2 == 0 else [2, 3]
        for p in range(8):
            for o2 in range(2):
                bk = banks[o2]
                if p == 0:
                    pool.acquire(bk, "tensor")
                P.op("tensor", lambda e, t=t, p=p, o2=o2, bk=bk: e.matmul(
                    out=ps_mm[bk][:],
                    lhsT=attn[p][:, t * 128:(t + 1) * 128],
                    rhs=woT_res[p][:, o2 * 512:(o2 + 1) * 512],
                    start=(p == 0), stop=False))
        for o2 in range(2):
            bk = banks[o2]
            P.op("tensor", lambda e, t=t, o2=o2, bk=bk: e.matmul(
                out=ps_mm[bk][:],
                lhsT=ones_row[0:1, :],
                rhs=brow_b[3][0:1, o2 * 512:(o2 + 1) * 512],
                start=False, stop=True), sem="mm_o")
        mmo_c = P.counts["mm_o"]
        P.wait("vector", "mm_o", mmo_c)
        P.wait("vector", f"xin2_{kx}", vx)
        for o2 in range(2):
            bk = banks[o2]
            ev = P.op("vector", lambda e, t=t, o2=o2, bk=bk, kx=kx: e.tensor_add(
                out=t1f[:, o2 * 512:(o2 + 1) * 512],
                in0=xtile2[kx][:, o2 * 512:(o2 + 1) * 512],
                in1=ps_mm[bk][:]), sem="t1_done")
            pool.release(bk, "t1_done", ev)
        P.op("vector", lambda e: e.tensor_mul(out=sq_scr_b2[:], in0=t1f[:], in1=t1f[:]))
        nv = P.op("vector", lambda e: e.reduce_sum(
            out=n2c[:], in_=sq_scr_b2[:], axis=AXX), sem="n2_o")
        P.wait("scalar", "n2_o", nv)
        na = P.op("scalar", lambda e: e.activation(
            out=n2c[:], in_=n2c[:], func=AF.Sqrt), sem="sq_o")
        P.wait("vector", "sq_o", na)
        P.op("vector", lambda e: e.tensor_scalar(
            out=n2c[:], in0=n2c[:], scalar1=1e-12, scalar2=None, op0=ALU.max))
        P.op("vector", lambda e: e.reciprocal(out=n2c[:], in_=n2c[:]))
        P.op("vector", lambda e: e.tensor_scalar(
            out=n2c[:], in0=n2c[:], scalar1=MAXN, scalar2=1.0,
            op0=ALU.mult, op1=ALU.min))
        P.op("vector", lambda e, t=t: e.tensor_scalar(
            out=xb[t][:], in0=t1f[:], scalar1=n2c[:, 0:1], scalar2=None,
            op0=ALU.mult), sem="xx2_done")
        P.op("vector", lambda e, t=t: e.reduce_sum(
            out=stat[:, SS + t:SS + t + 1], in_=xb[t][:], axis=AXX), sem="sv")
        P.wait("scalar", "xx2_done", P.counts["xx2_done"])
        P.op("scalar", lambda e, t=t: e.activation(
            out=junk_bf[:], in_=xb[t][:], func=AF.Square,
            accum_out=stat[:, SQ + t:SQ + t + 1]), sem="sa")
        if dbg:
            P.wait("gpsimd", "xx2_done", P.counts["xx2_done"])
            P.op("gpsimd", lambda e, t=t: e.dma_start(
                out=g["dxx2_h"][t * 128:(t + 1) * 128, :], in_=xb[t][:]), sem="st_dbg", by=16)
    P.wait("vector", "sa", P.counts["sa"])
    P.wait("vector", "sv", P.counts["sv"])
    P.op("vector", lambda e: e.tensor_scalar(
        out=stat[:, MU2:MU2 + NT], in0=stat[:, SS:SS + NT], scalar1=1.0 / D,
        scalar2=None, op0=ALU.mult))
    P.op("vector", lambda e: e.tensor_mul(
        out=stat[:, RSTD2:RSTD2 + NT], in0=stat[:, MU2:MU2 + NT],
        in1=stat[:, MU2:MU2 + NT]))
    vm = P.op("vector", lambda e: e.scalar_tensor_tensor(
        out=stat[:, RSTD2:RSTD2 + NT], in0=stat[:, SQ:SQ + NT], scalar=1.0 / D,
        in1=stat[:, RSTD2:RSTD2 + NT], op0=ALU.mult, op1=ALU.subtract), sem="sv")
    P.wait("scalar", "sv", vm)
    va = P.op("scalar", lambda e: e.activation(
        out=stat[:, RSTD2:RSTD2 + NT], in_=stat[:, RSTD2:RSTD2 + NT],
        func=AF.Sqrt, bias=EPS_LN, scale=1.0), sem="sa")
    P.wait("vector", "sa", va)
    P.op("vector", lambda e: e.reciprocal(
        out=stat[:, RSTD2:RSTD2 + NT], in_=stat[:, RSTD2:RSTD2 + NT]), sem="stats2_done")
    if dbg:
        P.wait("gpsimd", "st_dbg", P.counts["st_dbg"])
    P.pop_scope()
    P.pop_scope()
    P.pop_scope()
    P.barrier()


def _emit_l5_tail(P, env, dbg=False):
    g = env
    stat, g_col, ident_sb = g["stat"], g["g_col"], g["ident_sb"]
    ones_row, junk_bf = g["ones_row"], g["junk_bf"]
    ones512 = g["ones512"]
    ps_tr, ps_mm, pool = g["ps_tr"], g["ps_mm"], g["pool"]
    b2g = g["b2g"]
    b2g_bf = g["b2g_bf"]
    MU2, RSTD2, SQ, SS = g["MU2"], g["RSTD2"], g["SQ"], g["SS"]
    MU0, RSTD0 = g["MU0"], g["RSTD0"]
    xb = g["xb"]
    bf1col, fb1col = g["bf1col"], g["fb1col"]
    x2_hbm, x3_hbm, out_h = g["x2_hbm"], g["x3_hbm"], g["out_h"]
    MAXN = 1.0 - 1e-05
    NH = 32

    def ffn_block(tag, w1T_s, w2T_s, bias1_h, bias2_h, bcol, quant, src_fn,
                  out_cb, bufs):
        xnT_l = bufs["xnT_l"]; hT = bufs["hT"]; slab = bufs["slab"]
        w2ch = bufs["w2ch"]; brow4 = bufs["brow4"]; brow4b = bufs["brow4b"]
        b2row = bufs["b2row"]; row128 = bufs["row128"]
        b1bf = bufs.get("b1bf")
        tb_l = bufs["tb_l"]; ta_l = bufs["ta_l"]; hf32 = bufs.get("hf32")
        """h = act(xn@w1'.T + b1eff) ; out tiles = h@w2'.T + b2eff -> out_cb.

        quant=False: act = exact gelu; w1 gain-folded (g2); b1eff = bf1 + (b2/g2)@w1g.
        quant=True: act = relu(fixed_quant(.,8,4)); biases fixed_quant(8,6)."""
        last_hqrd = 0
        base_tr = P.counts.get("trCf", 0)
        ntr = 0
        for i in range(NT):
            for c in range(ND):
                kk = ntr % 2
                if ntr >= 2:
                    P.wait("tensor", "trCf", base_tr + ntr - 1)
                P.op("tensor", lambda e, i=i, c=c, kk=kk: e.transpose(
                    out=ps_tr[kk][:, 0:128], in_=src_fn(i, c),
                    identity=ident_sb[:]), sem="trp")
                P.wait("scalar", "trp", P.counts["trp"])
                P.op("scalar", lambda e, i=i, c=c, kk=kk: e.copy(
                    out=xnT_l[c][:, i * 128:(i + 1) * 128], in_=ps_tr[kk][:, 0:128]),
                    sem="trCf")
                ntr += 1
        P.wait("tensor", "trCf", P.counts["trCf"])
        P.op("sync", lambda e, bias1_h=bias1_h: e.dma_start(
            out=brow4[:], in_=bias1_h.rearrange("(a n) -> a n", a=1)), sem="bld4", by=16)
        bld4_c = P.counts["bld4"]
        b1q_c = 0
        if quant:
            P.wait("vector", "bld4", bld4_c)
            for hseg in range(4):
                sl_ = slice(hseg * D, (hseg + 1) * D)
                def pre(tbx, sl_=sl_):
                    P.op("vector", lambda e, sl_=sl_: e.tensor_scalar(
                        out=tb_l[0:1, 0:D], in0=brow4[0:1, sl_], scalar1=64.0,
                        scalar2=None, op0=ALU.mult))
                def fin(tax, sl_=sl_):
                    P.op("vector", lambda e: e.tensor_scalar(
                        out=ta_l[0:1, 0:D], in0=ta_l[0:1, 0:D], scalar1=127.0,
                        scalar2=-128.0, op0=ALU.min, op1=ALU.max))
                    return P.op("vector", lambda e, sl_=sl_: e.tensor_scalar(
                        out=brow4[0:1, sl_], in0=ta_l[0:1, 0:D], scalar1=1.0 / 64.0,
                        scalar2=None, op0=ALU.mult), sem="b1q_done")
                tbv = type("S", (), {"__getitem__": lambda s_, k_: tb_l[0:1, 0:D]})()
                tav = type("S", (), {"__getitem__": lambda s_, k_: ta_l[0:1, 0:D]})()
                emit_round(P, fin, None, tbv, tav, g_col, pre=pre)
            b1q_c = P.counts["b1q_done"]
            P.wait("vector", "b1q_done", b1q_c)
            bq = P.op("vector", lambda e: e.tensor_copy(
                out=b1bf[0:1, 0:4 * D], in_=brow4[:]), sem="b1bf_c")
            P.wait("tensor", "b1bf_c", bq)
        for th in range(2):
            mm1_hist = []
            for r in range(NH):
                sl = r % 2
                if len(mm1_hist) >= 2:
                    P.wait("sync", "mm1f", mm1_hist[-2])
                P.op("sync", lambda e, r=r, sl=sl, w1T_s=w1T_s: e.dma_start(
                    out=slab[sl][:],
                    in_=w1T_s[:, r * ND * 128:(r + 1) * ND * 128]),
                    sem=f"slab_ld{sl}", by=16)
                slab_c = P.counts[f"slab_ld{sl}"]
                if not quant:
                    P.wait("tensor", f"slab_ld{sl}", slab_c)
                    pool.acquire(4, "tensor")
                    for ic in range(ND):
                        P.op("tensor", lambda e, ic=ic, sl=sl: e.matmul(
                            out=ps_mm[4][0:1, 0:128],
                            lhsT=b2g_bf[:, ic:ic + 1],
                            rhs=slab[sl][:, ic * 128:(ic + 1) * 128],
                            start=(ic == 0), stop=(ic == ND - 1)),
                            sem="bmv4" if ic == ND - 1 else None)
                    bmv4_c = P.counts["bmv4"]
                    P.wait("vector", "bmv4", bmv4_c)
                    P.wait("vector", "bld4", bld4_c)
                    if mm1_hist:
                        P.wait("vector", "mm1f", mm1_hist[-1])
                    bv = P.op("vector", lambda e, r=r: e.tensor_add(
                        out=row128[:], in0=ps_mm[4][0:1, 0:128],
                        in1=brow4[0:1, r * 128:(r + 1) * 128]), sem="brow_v")
                    pool.release(4, "brow_v", bv)
                else:
                    bv = None
                    P.wait("tensor", f"slab_ld{sl}", slab_c)
                for tt in range(2):
                    bk = tt
                    pool.acquire(bk, "tensor")
                    for ic in range(ND):
                        P.op("tensor", lambda e, r=r, ic=ic, tt=tt, bk=bk, sl=sl, th=th: e.matmul(
                            out=ps_mm[bk][:],
                            lhsT=slab[sl][:, ic * 128:(ic + 1) * 128],
                            rhs=xnT_l[ic][:, th * 1024 + tt * 512:th * 1024 + (tt + 1) * 512],
                            start=(ic == 0), stop=False))
                    if bv is not None:
                        P.wait("tensor", "brow_v", bv)
                        P.op("tensor", lambda e, bk=bk: e.matmul(
                            out=ps_mm[bk][:], lhsT=row128[0:1, :], rhs=ones512[0:1, :],
                            start=False, stop=True), sem="mm1f")
                    else:
                        P.op("tensor", lambda e, bk=bk, r=r: e.matmul(
                            out=ps_mm[bk][:], lhsT=b1bf[0:1, r * 128:(r + 1) * 128],
                            rhs=ones512[0:1, :], start=False, stop=True), sem="mm1f")
                mm1_c = P.counts["mm1f"]
                mm1_hist.append(mm1_c)
                if quant and last_hqrd:
                    P.wait("scalar", "hq_rd", last_hqrd)
                for tt in range(2):
                    bk = tt
                    P.wait("scalar", "mm1f", mm1_c)
                    if not quant:
                        ev = P.op("scalar", lambda e, r=r, tt=tt, bk=bk: e.activation(
                            out=hT[r][:, tt * 512:(tt + 1) * 512], in_=ps_mm[bk][:],
                            func=AF.Gelu, bias=0.0, scale=1.0), sem="ev1f")
                        pool.release(bk, "ev1f", ev)
                    else:
                        ev = P.op("scalar", lambda e, r=r, tt=tt, bk=bk: e.copy(
                            out=hf32[:, tt * 512:(tt + 1) * 512], in_=ps_mm[bk][:]),
                            sem="ev1f")
                        pool.release(bk, "ev1f", ev)
                if quant:
                    P.wait("vector", "ev1f", P.counts["ev1f"])
                    def pre(tbx):
                        P.op("vector", lambda e: e.tensor_scalar(
                            out=tb_l[:], in0=hf32[:], scalar1=16.0, scalar2=None,
                            op0=ALU.mult), sem="hq_rd")
                    def fin(tax, r=r):
                        P.op("vector", lambda e: e.tensor_scalar(
                            out=ta_l[:], in0=ta_l[:], scalar1=127.0, scalar2=-128.0,
                            op0=ALU.min, op1=ALU.max))
                        return P.op("vector", lambda e, r=r: e.tensor_scalar(
                            out=hT[r][:], in0=ta_l[:], scalar1=0.0, scalar2=1.0 / 16.0,
                            op0=ALU.max, op1=ALU.mult), sem="hq_done")
                    emit_round(P, fin, None, tb_l, ta_l, g_col, pre=pre)
                    last_hqrd = P.counts["hq_rd"]
            if th == 0:
                bias_consumed = P.counts["brow_v"]
                P.wait("sync", "brow_v", bias_consumed)
                P.op("sync", lambda e, bias2_h=bias2_h: e.dma_start(
                    out=b2row[:], in_=bias2_h.rearrange("(a n) -> a n", a=1)),
                    sem="bld4", by=16)
                bld4b = P.counts["bld4"]
                P.wait("vector", "bld4", bld4b)
                if quant:
                    def pre(tbx):
                        P.op("vector", lambda e: e.tensor_scalar(
                            out=tb_l[0:1, 0:D], in0=b2row[:], scalar1=64.0,
                            scalar2=None, op0=ALU.mult))
                    def fin(tax):
                        P.op("vector", lambda e: e.tensor_scalar(
                            out=ta_l[0:1, 0:D], in0=ta_l[0:1, 0:D], scalar1=127.0,
                            scalar2=-128.0, op0=ALU.min, op1=ALU.max))
                        return P.op("vector", lambda e: e.tensor_scalar(
                            out=b2row[:], in0=ta_l[0:1, 0:D], scalar1=1.0 / 64.0,
                            scalar2=None, op0=ALU.mult), sem="b1q_done")
                    tbv = type("S", (), {"__getitem__": lambda s_, k_: tb_l[0:1, 0:D]})()
                    tav = type("S", (), {"__getitem__": lambda s_, k_: ta_l[0:1, 0:D]})()
                    emit_round(P, fin, None, tbv, tav, g_col, pre=pre)
                bb = P.op("vector", lambda e: e.tensor_copy(
                    out=brow4b[0:1, 0:D], in_=b2row[:]), sem="brow_v")
                P.wait("tensor", "brow_v", bb)
            P.wait("tensor", "hq_done" if quant else "ev1f",
                   P.counts["hq_done" if quant else "ev1f"])
            w2_hist = []
            rounds = [(0, 3), (3, 3), (6, 2)]
            for rnd, (t0, ntile) in enumerate(rounds):
                for ic in range(NH):
                    sl = ic % 2
                    if len(w2_hist) >= 2:
                        P.wait("sync", "w2mmf", w2_hist[-2])
                    P.op("sync", lambda e, ic=ic, sl=sl, w2T_s=w2T_s: e.dma_start(
                        out=w2ch[sl][:],
                        in_=w2T_s[:, ic * ND * 128:(ic + 1) * ND * 128]),
                        sem=f"w2_ld{sl}", by=16)
                    w2c = P.counts[f"w2_ld{sl}"]
                    P.wait("tensor", f"w2_ld{sl}", w2c)
                    for tt in range(ntile):
                        for o2 in range(2):
                            bk = tt * 2 + o2
                            if ic == 0:
                                pool.acquire(bk, "tensor")
                            P.op("tensor", lambda e, ic=ic, tt=tt, o2=o2, bk=bk, sl=sl, t0=t0: e.matmul(
                                out=ps_mm[bk][:],
                                lhsT=hT[ic][:, (t0 + tt) * 128:(t0 + tt + 1) * 128],
                                rhs=w2ch[sl][:, o2 * 512:(o2 + 1) * 512],
                                start=(ic == 0), stop=False),
                                sem="w2mmf" if (tt == ntile - 1 and o2 == 1) else None)
                    w2_hist.append(P.counts["w2mmf"])
                for tt in range(ntile):
                    for o2 in range(2):
                        bk = tt * 2 + o2
                        P.op("tensor", lambda e, tt=tt, o2=o2, bk=bk: e.matmul(
                            out=ps_mm[bk][:],
                            lhsT=ones_row[0:1, :],
                            rhs=brow4b[0:1, o2 * 512:(o2 + 1) * 512],
                            start=False, stop=True), sem="mm2f")
                mm2_c = P.counts["mm2f"]
                for tt in range(ntile):
                    out_cb(th * 8 + t0 + tt, [tt * 2, tt * 2 + 1], mm2_c)

    P.push_scope()
    xnT_l = [P.sb(f"xnTl{c}", [128, T], BF16) for c in range(ND)]
    hT = [P.sb(f"hTl{r}", [128, T // 2], BF16) for r in range(NH)]
    slab = [P.sb(f"slab{k}", [128, ND * 128], BF16) for k in range(2)]
    w2ch = [P.sb(f"w2ch{k}", [128, D], BF16) for k in range(2)]
    brow4 = P.sb("brow4", [1, 4 * D], F32)
    brow4b = P.sb("brow4b", [1, D], BF16)
    b2row = P.sb("b2row", [1, D], F32)
    row128 = P.sb("row128", [1, 128], BF16)
    tb_l = P.sb("tb_l", [1, 128], F32)
    ta_l = P.sb("ta_l", [1, 128], F32)
    t1f = P.sb("t1fC", [128, D], F32)
    sqs = P.sb("sqsC", [128, D], F32)
    n2c = P.sb("n2cC", [128, 1], F32)
    xt32 = [P.sb(f"xt32{k}", [128, D], F32) for k in range(2)]
    x3t = P.sb("x3t", [128, D], F32)

    P.wait("vector", "stats2_done", P.counts["stats2_done"])
    for i in range(NT):
        P.op("vector", lambda e, i=i: e.tensor_scalar(
            out=hT[i][:], in0=xb[i][:],
            scalar1=stat[:, MU2 + i:MU2 + i + 1], scalar2=stat[:, RSTD2 + i:RSTD2 + i + 1],
            op0=ALU.subtract, op1=ALU.mult), sem="xnC_done")
    P.wait("tensor", "xnC_done", P.counts["xnC_done"])

    def srcC(i, c):
        return hT[i][:, c * 128:(c + 1) * 128]

    xg_C = {0: 0, 1: 0}

    def out_cb_C(t, banks, mm2_c):
        kx = t % 2
        if xg_C[kx]:
            P.wait("sync", "sv", xg_C[kx])
        vx = P.op("sync", lambda e, t=t, kx=kx: e.dma_start(
            out=xt32[kx][:], in_=x2_hbm[t * 128:(t + 1) * 128, :]), sem=f"xld{kx}", by=16)
        P.wait("vector", f"mm2f", mm2_c)
        if dbg and t == 0:
            for o2 in range(2):
                P.op("vector", lambda e, o2=o2, bk=banks[o2]: e.tensor_copy(
                    out=sqs[:, o2 * 512:(o2 + 1) * 512], in_=ps_mm[bk][:]), sem="dbgC")
            P.wait("gpsimd", "dbgC", P.counts["dbgC"])
            P.op("gpsimd", lambda e: e.dma_start(out=g["dcps_h"], in_=sqs[:]),
                 sem="st_dbg", by=16)
            P.wait("vector", "st_dbg", P.counts["st_dbg"])
        for o2 in range(2):
            bk = banks[o2]
            ev = P.op("vector", lambda e, t=t, o2=o2, bk=bk: e.tensor_add(
                out=t1f[:, o2 * 512:(o2 + 1) * 512],
                in0=xb[t][:, o2 * 512:(o2 + 1) * 512],
                in1=ps_mm[bk][:]), sem="t1C")
            pool.release(bk, "t1C", ev)
        if dbg and t == 0:
            P.wait("gpsimd", "t1C", P.counts["t1C"])
            P.op("gpsimd", lambda e: e.dma_start(out=g["dct1_h"], in_=t1f[:]),
                 sem="st_dbg", by=16)
            P.wait("vector", "st_dbg", P.counts["st_dbg"])
        P.op("vector", lambda e: e.tensor_mul(out=sqs[:], in0=t1f[:], in1=t1f[:]))
        nv = P.op("vector", lambda e: e.reduce_sum(out=n2c[:], in_=sqs[:], axis=AXX), sem="n2_qk")
        P.wait("scalar", "n2_qk", nv)
        na = P.op("scalar", lambda e: e.activation(out=n2c[:], in_=n2c[:], func=AF.Sqrt), sem="sq_qk")
        P.wait("vector", "sq_qk", na)
        P.op("vector", lambda e: e.tensor_scalar(
            out=n2c[:], in0=n2c[:], scalar1=1e-12, scalar2=None, op0=ALU.max))
        P.op("vector", lambda e: e.reciprocal(out=n2c[:], in_=n2c[:]))
        P.op("vector", lambda e: e.tensor_scalar(
            out=n2c[:], in0=n2c[:], scalar1=MAXN, scalar2=1.0, op0=ALU.mult, op1=ALU.min))
        if dbg and t == 0:
            P.op("vector", lambda e: e.tensor_copy(out=sqs[:, 0:1], in_=n2c[:]), sem="dbgC")
            P.wait("gpsimd", "dbgC", P.counts["dbgC"])
            P.op("gpsimd", lambda e: e.dma_start(
                out=g["dcn2_h"][:], in_=sqs[:, 0:4]), sem="st_dbg", by=16)
            P.wait("vector", "st_dbg", P.counts["st_dbg"])
        P.wait("vector", f"xld{kx}", vx)
        P.op("vector", lambda e, t=t: e.scalar_tensor_tensor(
            out=x3t[:], in0=t1f[:], scalar=n2c[:, 0:1], in1=xt32[kx][:],
            op0=ALU.mult, op1=ALU.add))
        P.op("vector", lambda e, t=t: e.reduce_sum(
            out=stat[:, SS + t:SS + t + 1], in_=x3t[:], axis=AXX), sem="sv")
        sv_c = P.counts["sv"]
        xg_C[kx] = sv_c
        P.wait("scalar", "sv", sv_c)
        P.op("scalar", lambda e, t=t: e.activation(
            out=junk_bf[:], in_=x3t[:], func=AF.Square,
            accum_out=stat[:, SQ + t:SQ + t + 1]), sem="sa")
        P.wait("gpsimd", "sa", P.counts["sa"])
        P.wait("gpsimd", "st_x3", P.counts.get("st_x3", 0))
        P.op("gpsimd", lambda e, t=t: e.dma_start(
            out=x3_hbm[t * 128:(t + 1) * 128, :], in_=x3t[:]), sem="st_x3", by=16)
        P.wait("vector", "st_x3", P.counts["st_x3"])

    ffn_block("C", g["w1T_s"], g["w2T_s"], g["bf1_h"], g["bf2_h"], bf1col,
              False, srcC, out_cb_C,
              dict(xnT_l=xnT_l, hT=hT, slab=slab, w2ch=w2ch, brow4=brow4,
                   brow4b=brow4b, b2row=b2row, row128=row128, tb_l=tb_l,
                   ta_l=ta_l))
    P.wait("gpsimd", "st_x3", P.counts["st_x3"])
    P.pop_scope()
    P.barrier()

    P.push_scope()
    xnT_ld = [P.sb(f"xnTd{c}", [128, T], BF16) for c in range(ND)]
    hTd = [P.sb(f"hTd{r}", [128, T // 2], BF16) for r in range(NH)]
    slabd = [P.sb(f"slabd{k}", [128, ND * 128], BF16) for k in range(2)]
    w2chd = [P.sb(f"w2chd{k}", [128, D], BF16) for k in range(2)]
    brow4d = P.sb("brow4d", [1, 4 * D], F32)
    brow4bd = P.sb("brow4bd", [1, D], BF16)
    b1bfd = P.sb("b1bfd", [1, 4 * D], BF16)
    b2rowd = P.sb("b2rowd", [1, D], F32)
    row128d = P.sb("row128d", [1, 128], F32)
    tb_ld = P.sb("tb_ld", [128, T // 2], F32)
    ta_ld = P.sb("ta_ld", [128, T // 2], F32)
    hf32d = P.sb("hf32d", [128, T // 2], F32)
    xt32d = [P.sb(f"xt32d{k}", [128, D], F32) for k in range(2)]
    outt = P.sb("outt", [128, D], F32)

    P.op("vector", lambda e: e.tensor_scalar(
        out=stat[:, MU0:MU0 + NT], in0=stat[:, SS:SS + NT], scalar1=1.0 / D,
        scalar2=None, op0=ALU.mult))
    P.op("vector", lambda e: e.tensor_mul(
        out=stat[:, RSTD0:RSTD0 + NT], in0=stat[:, MU0:MU0 + NT],
        in1=stat[:, MU0:MU0 + NT]))
    vm = P.op("vector", lambda e: e.scalar_tensor_tensor(
        out=stat[:, RSTD0:RSTD0 + NT], in0=stat[:, SQ:SQ + NT], scalar=1.0 / D,
        in1=stat[:, RSTD0:RSTD0 + NT], op0=ALU.mult, op1=ALU.subtract), sem="sv")
    P.wait("scalar", "sv", vm)
    va = P.op("scalar", lambda e: e.activation(
        out=stat[:, RSTD0:RSTD0 + NT], in_=stat[:, RSTD0:RSTD0 + NT],
        func=AF.Sqrt, bias=EPS_LN, scale=1.0), sem="sa")
    P.wait("vector", "sa", va)
    P.op("vector", lambda e: e.reciprocal(
        out=stat[:, RSTD0:RSTD0 + NT], in_=stat[:, RSTD0:RSTD0 + NT]), sem="stats2_done")
    P.wait("vector", "stats2_done", P.counts["stats2_done"])
    base_xn = P.counts.get("xnC_done", 0)
    for i in range(NT):
        kx = i % 2
        if i >= 2:
            P.wait("sync", "xnC_done", base_xn + i - 1)
        vx = P.op("sync", lambda e, i=i, kx=kx: e.dma_start(
            out=xt32d[kx][:], in_=x3_hbm[i * 128:(i + 1) * 128, :]), sem=f"xld{kx}", by=16)
        P.wait("vector", f"xld{kx}", vx)
        def pre(tbx, i=i, kx=kx):
            P.op("vector", lambda e, i=i, kx=kx, tbx=tbx: e.tensor_scalar(
                out=tbx[:], in0=xt32d[kx][:],
                scalar1=stat[:, MU0 + i:MU0 + i + 1], scalar2=stat[:, RSTD0 + i:RSTD0 + i + 1],
                op0=ALU.subtract, op1=ALU.mult))
            P.op("vector", lambda e, tbx=tbx: e.tensor_scalar(
                out=tbx[:], in0=tbx[:], scalar1=16.0, scalar2=None, op0=ALU.mult))
        def fin(tax, i=i):
            P.op("vector", lambda e, tax=tax: e.tensor_scalar(
                out=tax[:], in0=tax[:], scalar1=127.0, scalar2=-128.0,
                op0=ALU.min, op1=ALU.max))
            return P.op("vector", lambda e, tax=tax, i=i: e.tensor_scalar(
                out=hTd[i][:], in0=tax[:], scalar1=1.0 / 16.0, scalar2=None,
                op0=ALU.mult), sem="xnC_done")
        emit_round(P, fin, None, tb_ld, ta_ld, g_col, pre=pre)
    P.wait("tensor", "xnC_done", P.counts["xnC_done"])

    xg_D = {0: 0, 1: 0}

    def out_cb_D(t, banks, mm2_c):
        kx = t % 2
        if xg_D[kx]:
            P.wait("sync", "t1C", xg_D[kx])
        vx = P.op("sync", lambda e, t=t, kx=kx: e.dma_start(
            out=xt32d[kx][:], in_=x3_hbm[t * 128:(t + 1) * 128, :]), sem=f"xld{kx}", by=16)
        P.wait("vector", f"mm2f", mm2_c)
        P.wait("vector", f"xld{kx}", vx)
        for o2 in range(2):
            bk = banks[o2]
            ev = P.op("vector", lambda e, t=t, o2=o2, bk=bk, kx=kx: e.tensor_add(
                out=outt[:, o2 * 512:(o2 + 1) * 512],
                in0=xt32d[kx][:, o2 * 512:(o2 + 1) * 512],
                in1=ps_mm[bk][:]), sem="t1C")
            pool.release(bk, "t1C", ev)
        xg_D[kx] = P.counts["t1C"]
        P.wait("gpsimd", "t1C", P.counts["t1C"])
        P.wait("gpsimd", "st_x3", P.counts.get("st_x3", 0))
        P.op("gpsimd", lambda e, t=t: e.dma_start(
            out=out_h[t * 128:(t + 1) * 128, :], in_=outt[:]), sem="st_x3", by=16)
        P.wait("vector", "st_x3", P.counts["st_x3"])

    ffn_block("D", g["fw1T_s"], g["fw2T_s"], g["fb1_h"], g["fb2_h"], fb1col,
              True, (lambda i, c, hTd=hTd: hTd[i][:, c * 128:(c + 1) * 128]), out_cb_D,
              dict(xnT_l=xnT_ld, hT=hTd, slab=slabd, w2ch=w2chd, brow4=brow4d,
                   brow4b=brow4bd, b2row=b2rowd, row128=row128d, tb_l=tb_ld,
                   ta_l=ta_ld, hf32=hf32d, b1bf=b1bfd))
    if dbg:
        dh_h = P.nc.dram_tensor("d_hD", [128, 1024], BF16, kind="ExternalOutput").ap()
        db_h = P.nc.dram_tensor("d_b1bf", [1, 4096], BF16, kind="ExternalOutput").ap()
        dxn_h = P.nc.dram_tensor("d_xnD", [128, T], BF16, kind="ExternalOutput").ap()
        P.wait("gpsimd", "hq_done", P.counts["hq_done"])
        P.op("gpsimd", lambda e: e.dma_start(out=dh_h, in_=hTd[0][:]), sem="st_dbg", by=16)
        P.op("gpsimd", lambda e: e.dma_start(out=db_h, in_=b1bfd[0:1, :]), sem="st_dbg", by=16)
        P.op("gpsimd", lambda e: e.dma_start(out=dxn_h, in_=xnT_ld[0][:]), sem="st_dbg", by=16)
        P.wait("gpsimd", "st_dbg", P.counts["st_dbg"])
    P.wait("gpsimd", "st_x3", P.counts["st_x3"])
    P.pop_scope()
